# version 02
# baseline (speedup 1.0000x reference)
import contextlib
import hashlib
import os
import shutil
import numpy as np
import ml_dtypes
import concourse.bass as bass
import concourse.mybir as mybir
from concourse.bass_utils import run_bass_kernel_spmd

import concourse.bass2jax as _b2j

if not getattr(_b2j, "_neff_cache_installed", False):
    _ORIG_COMPILE = _b2j.compile_bir_kernel
    _NEFF_CACHE_DIR = os.environ.get("BASS_NEFF_CACHE", "/root/.bass_neff_cache")

    def _cached_compile_bir_kernel(ant_bir_str, compile_dir_path, neff_name="model.neff"):
        cpath = None
        try:
            os.makedirs(_NEFF_CACHE_DIR, exist_ok=True)
            key = hashlib.sha256(ant_bir_str).hexdigest()
            cpath = os.path.join(_NEFF_CACHE_DIR, key + ".neff")
            if os.path.exists(cpath):
                out = os.path.join(compile_dir_path, neff_name)
                shutil.copyfile(cpath, out)
                return out
        except Exception:
            cpath = None
        out = _ORIG_COMPILE(ant_bir_str, compile_dir_path, neff_name=neff_name)
        if cpath is not None:
            try:
                shutil.copyfile(out, cpath + ".tmp")
                os.replace(cpath + ".tmp", cpath)
            except Exception:
                pass
        return out

    _b2j.compile_bir_kernel = _cached_compile_bir_kernel
    _b2j._neff_cache_installed = True

F32 = mybir.dt.float32
BF16 = mybir.dt.bfloat16
U8 = mybir.dt.uint8
AF = mybir.ActivationFunctionType
ALU = mybir.AluOpType
AX = mybir.AxisListType

NC = 8
B = 2
L = 8192
D = 1024
T = B * L
TC = T // NC
NT = TC // 128
CORE_IDS = list(range(NC))

_BUILD_CACHE = {}


def _nc():
    return bass.Bass("TRN2", target_bir_lowering=False, debug=False)


class W:

    def __init__(self, nc, name, sem):
        self._nc = nc
        self._name = name
        self.sem = sem
        self.n = 0

    @property
    def e(self):
        return getattr(self._nc, self._name)

    def wait(self, sem, val):
        self.e.wait_ge(sem, val)

    def op(self, name, *a, **k):
        if self.n > 0:
            self.e.wait_ge(self.sem, self.n)
        inst = getattr(self.e, name)(*a, **k)
        inst.then_inc(self.sem, 1)
        self.n += 1
        return self.n


def _ap3_tokmaj(handle, ntile, d, offset=0):
    return bass.AP(handle, offset, [[d, 128], [128 * d, ntile], [1, d]])


def _row_ap(handle, n, offset=0):
    return bass.AP(handle, offset, [[n, 1], [1, n]])


def _build_ident(gp, ident_bf, ident_f32=None):
    gp.op("memset", ident_bf[:, :], 1.0)
    gp.op("affine_select", out=ident_bf[:, :], in_=ident_bf[:, :], pattern=[[-1, 128]],
          compare_op=ALU.is_equal, fill=0.0, base=0, channel_multiplier=1)
    if ident_f32 is not None:
        gp.op("memset", ident_f32[:, :], 1.0)
        gp.op("affine_select", out=ident_f32[:, :], in_=ident_f32[:, :], pattern=[[-1, 128]],
              compare_op=ALU.is_equal, fill=0.0, base=0, channel_multiplier=1)


def _emit_ln(nc, VE, SC, x_sb, xn_sb, lns, ntile, d, normalize=True):
    sq_a, sq_b = lns["sq_a"], lns["sq_b"]
    m_sb, v_sb, sd_sb, rs_sb = lns["m"], lns["v"], lns["sd"], lns["rs"]
    with nc.Block() as blk:
        @blk.vector
        def _(e):
            for j in range(ntile):
                VE.op("reduce_sum", out=m_sb[:, j:j + 1], in_=x_sb[:, j * d:(j + 1) * d], axis=AX.X)
    with nc.Block() as blk:
        @blk.scalar
        def _(e):
            SC.op("mul", m_sb[:, 0:ntile], m_sb[:, 0:ntile], 1.0 / d)
    with nc.Block() as blk:
        @blk.vector
        def _(e):
            for j in range(ntile):
                VE.op("tensor_scalar", out=sq_a[:, :], in0=x_sb[:, j * d:(j + 1) * d],
                      scalar1=m_sb[:, j:j + 1], scalar2=None, op0=ALU.subtract)
                VE.op("tensor_tensor", out=sq_b[:, :], in0=sq_a[:, :], in1=sq_a[:, :], op=ALU.mult)
                VE.op("reduce_sum", out=v_sb[:, j:j + 1], in_=sq_b[:, :], axis=AX.X)
    with nc.Block() as blk:
        @blk.scalar
        def _(e):
            SC.op("activation", out=sd_sb[:, 0:ntile], in_=v_sb[:, 0:ntile], func=AF.Sqrt,
                  bias=lns["eps"][:, 0:1], scale=1.0 / d)
    with nc.Block() as blk:
        @blk.vector
        def _(e):
            VE.op("reciprocal", out=rs_sb[:, 0:ntile], in_=sd_sb[:, 0:ntile])
            if normalize:
                for j in range(ntile):
                    VE.op("tensor_scalar", out=xn_sb[:, j * d:(j + 1) * d],
                          in0=x_sb[:, j * d:(j + 1) * d],
                          scalar1=m_sb[:, j:j + 1], scalar2=rs_sb[:, j:j + 1],
                          op0=ALU.subtract, op1=ALU.mult)


def _ln_scratch(sb, tag, d, ntile=NT):
    return {"sq_a": sb(f"lnsqa{tag}", [128, d]), "sq_b": sb(f"lnsqb{tag}", [128, d]),
            "m": sb(f"lnm{tag}", [128, ntile]), "v": sb(f"lnv{tag}", [128, ntile]),
            "sd": sb(f"lnsd{tag}", [128, ntile]), "rs": sb(f"lnrs{tag}", [128, ntile]),
            "eps": sb(f"lneps{tag}", [128, 1])}



class Seq:

    def __init__(self, nc, VE, SC, GP, pe_sem, pe_count, dma_counts=None):
        self.nc = nc
        self.W = {"V": VE, "S": SC, "G": GP}
        self.pe_sem = pe_sem
        self.pe_count = pe_count
        self.dma_counts = dma_counts if dma_counts is not None else {}
        self.steps = []

    def step(self, eng, deps, fn, dma_sem=None):
        h = len(self.steps)
        self.steps.append([eng, fn, list(deps), dma_sem, None])
        return h

    def emit(self, blk):
        base = {k: w.n for k, w in self.W.items()}
        cnt = dict(base)
        pe = self.pe_count[0]
        dma_cnt = self.dma_counts
        for st_ in self.steps:
            eng, fn, deps, dsem, _ = st_
            if eng in self.W:
                cnt[eng] += 1
                st_[4] = ("W", self.W[eng].sem, cnt[eng])
            elif eng == "T":
                pe += 1
                st_[4] = ("T", self.pe_sem, pe)
            else:
                key = id(dsem)
                dma_cnt.setdefault(key, 0)
                dma_cnt[key] += 16
                st_[4] = ("D", dsem, dma_cnt[key])
        emitted_pe = [self.pe_count[0]]

        def run_engine(engine_names, e):
            for st_ in self.steps:
                eng, fn, deps, dsem, mark = st_
                if eng not in engine_names:
                    continue
                for d in deps:
                    dm = self.steps[d][4] if isinstance(d, int) else d
                    kind = dm[0]
                    if kind in ("W", "T", "D"):
                        sem_, val_ = dm[1], dm[2]
                    else:
                        sem_, val_ = dm
                    if eng in self.W and kind == "W" and sem_ is self.W[eng].sem:
                        continue
                    if eng == "T" and kind == "T":
                        continue
                    if eng in self.W:
                        self.W[eng].wait(sem_, val_)
                    else:
                        e.wait_ge(sem_, val_)
                if eng in self.W:
                    got = fn(self.W[eng])
                    assert got is None or got == st_[4][2], (got, st_[4])
                elif eng == "T":
                    inst = fn(e)
                    inst.then_inc(self.pe_sem, 1)
                    emitted_pe[0] += 1
                else:
                    inst = fn(e)
                    inst.then_inc(st_[4][1], 16)

        @blk.vector
        def _(e):
            run_engine(("V",), e)

        @blk.scalar
        def _(e):
            run_engine(("S",), e)

        @blk.tensor
        def _(e):
            run_engine(("T",), e)

        @blk.gpsimd
        def _(e):
            run_engine(("G", "GD"), e)

        @blk.sync
        def _(e):
            run_engine(("SD",), e)

        self.pe_count[0] = emitted_pe[0]
        for k, w in self.W.items():
            pass
        return {i: st_[4] for i, st_ in enumerate(self.steps)}


def _mk_ctx(nc):
    es = contextlib.ExitStack()
    sbf = lambda name, shape, dt=F32: es.enter_context(nc.sbuf_tensor(name, shape, dt))
    semf = lambda name: es.enter_context(nc.semaphore(name))
    VE = W(nc, "vector", semf("_sv"))
    SC = W(nc, "scalar", semf("_ss"))
    GP = W(nc, "gpsimd", semf("_sg"))
    return es, sbf, semf, VE, SC, GP



def build_l0a():
    if "l0a" in _BUILD_CACHE:
        return _BUILD_CACHE["l0a"]
    nc = _nc()
    x_d = nc.dram_tensor("x", [TC, D], F32, kind="ExternalInput")
    rwv_d = nc.dram_tensor("rwv", [D], F32, kind="ExternalInput")
    rbv_d = nc.dram_tensor("rbv", [1], F32, kind="ExternalInput")
    xn_d = nc.dram_tensor("xn", [TC, D], F32, kind="ExternalOutput")
    rw_d = nc.dram_tensor("rw2d", [128, NT], F32, kind="ExternalOutput")

    es, sb, sem, VE, SC, GP = _mk_ctx(nc)

    x_sb = sb("x_sb", [128, NT * D])
    xn_sb = sb("xn_sb", [128, NT * D])
    lns = _ln_scratch(sb, "a", D)
    rw_sb = sb("rw_sb", [128, NT])
    wrow = sb("wrow", [1, D])
    rb1 = sb("rb1", [1, 1])
    ones_row = sb("ones_row", [1, 128])
    wb_sb = sb("wb_sb", [128, D])
    rbb = sb("rbb", [128, 1])
    wb_ps = es.enter_context(nc.psum_tensor("wb_ps", [128, D], F32))
    rb_ps = es.enter_context(nc.psum_tensor("rb_ps", [128, 1], F32))
    ld = sem("ld")
    ldg = sem("ldg")
    st = sem("st")

    with nc.Block() as blk:
        @blk.sync
        def _(e):
            e.dma_start(out=x_sb[:, :], in_=_ap3_tokmaj(x_d, NT, D)).then_inc(ld, 16)

        @blk.gpsimd
        def _(e):
            e.dma_start(out=wrow[0:1, :], in_=_row_ap(rwv_d, D)).then_inc(ldg, 16)
            e.dma_start(out=rb1[0:1, :], in_=_row_ap(rbv_d, 1)).then_inc(ldg, 16)
            GP.op("memset", ones_row[:, :], 1.0)
            GP.op("memset", lns["eps"][:, :], 1e-5)

        @blk.vector
        def _(e):
            e.wait_ge(ld, 16)

    with nc.Block() as blk:
        @blk.tensor
        def _(e):
            e.wait_ge(ldg, 32)
            e.wait_ge(GP.sem, 2)
            e.matmul(wb_ps[:, 0:512], ones_row[0:1, 0:128], wrow[0:1, 0:512], start=True, stop=True)
            e.matmul(wb_ps[:, 512:1024], ones_row[0:1, 0:128], wrow[0:1, 512:1024], start=True, stop=True)
            e.matmul(rb_ps[:, 0:1], ones_row[0:1, 0:128], rb1[0:1, 0:1], start=True, stop=True)

    _emit_ln(nc, VE, SC, x_sb, xn_sb, lns, NT, D)

    with nc.Block() as blk:
        @blk.vector
        def _(e):
            VE.op("tensor_copy", out=wb_sb[:, :], in_=wb_ps[:, :])
            VE.op("tensor_copy", out=rbb[:, :], in_=rb_ps[:, :])
            for j in range(NT):
                VE.op("scalar_tensor_tensor", out=lns["sq_a"][:, :],
                      in0=xn_sb[:, j * D:(j + 1) * D],
                      scalar=1.0, in1=wb_sb[:, :], op0=ALU.bypass,
                      op1=ALU.mult, accum_out=rw_sb[:, j:j + 1])
            VE.op("tensor_scalar", out=rw_sb[:, :], in0=rw_sb[:, :], scalar1=rbb[:, 0:1],
                  scalar2=None, op0=ALU.add)

    with nc.Block() as blk:
        @blk.sync
        def _(e):
            e.dma_start(out=_ap3_tokmaj(xn_d, NT, D), in_=xn_sb[:, :]).then_inc(st, 16)
            e.dma_start(out=rw_d.ap(), in_=rw_sb[:, :]).then_inc(st, 16)
            e.wait_ge(st, 32)

    es.close()
    _BUILD_CACHE["l0a"] = nc
    return nc



def build_l0b():
    if "l0b" in _BUILD_CACHE:
        return _BUILD_CACHE["l0b"]
    HH = 2048
    NHC = HH // 128
    NDC = D // 128
    NG = 4
    nc = _nc()
    x_d = nc.dram_tensor("x", [TC, D], F32, kind="ExternalInput")
    xn_d = nc.dram_tensor("xn", [TC, D], F32, kind="ExternalInput")
    xr_d = nc.dram_tensor("xnrev", [TC, D], F32, kind="ExternalInput")
    rwb_d = nc.dram_tensor("rwb", [L], F32, kind="ExternalInput")
    rwo_d = nc.dram_tensor("rwown2d", [128, NT], F32, kind="ExternalInput")
    wg_d = nc.dram_tensor("wg", [D, 8], F32, kind="ExternalInput")
    w1_d = nc.dram_tensor("w1", [8, D, HH], F32, kind="ExternalInput")
    w3_d = nc.dram_tensor("w3", [8, D, HH], F32, kind="ExternalInput")
    w2_d = nc.dram_tensor("w2", [8, HH, D], F32, kind="ExternalInput")
    x1_d = nc.dram_tensor("x1", [TC, D], F32, kind="ExternalOutput")

    es, sb, sem, VE, SC, GP = _mk_ctx(nc)

    xnT = sb("xnT", [128, NDC * TC], BF16)
    h_sb = sb("h_sb", [128, NHC * TC], BF16)
    g_acc = sb("g_acc", [128, NT * D], BF16)
    w1c = [sb(f"w1c{p}", [128, NDC * 128], BF16) for p in range(2)]
    w3c = [sb(f"w3c{p}", [128, NDC * 128], BF16) for p in range(2)]
    w2_sb = sb("w2_sb", [128, NHC * D], BF16)
    wg_sb = sb("wg_sb", [128, NDC * 8], BF16)
    sil = [sb(f"sil{p}", [128, 512]) for p in range(2)]
    xstage = sb("xstage", [128, D], BF16)
    ident_bf = sb("ident_bf", [128, 128], BF16)
    ident_f = sb("ident_f", [128, 128], F32)
    ones_row = sb("ones_row", [1, 128])
    ones_col = sb("ones_col", [128, 1])
    rw_sb = sb("rw_sb", [128, 64])
    rwo_sb = sb("rwo_sb", [128, NT])
    cmp_sb = sb("cmp_sb", [128, 64])
    cnt_col = sb("cnt_col", [128, 1])
    lo = sb("lo", [1, 1])
    hi = sb("hi", [1, 1])
    mid = sb("mid", [1, 1])
    t1x = sb("t1x", [1, 1])
    t2x = sb("t2x", [1, 1])
    pred = sb("pred", [1, 1])
    midb = sb("midb", [128, 1])
    mask_sb = sb("mask_sb", [128, NT])
    neg_sb = sb("neg_sb", [128, 64])
    red_col = sb("red_col", [128, 1])
    red_row_sb = sb("red_row_sb", [1, 128])
    mx_sb = sb("mx_sb", [128, NT * 8])
    logits_sb = sb("logits_sb", [128, NT * 8])
    d_sb = sb("d_sb", [128, NT])
    p1_sb = sb("p1_sb", [128, NT])
    p2_sb = sb("p2_sb", [128, NT])
    wsel_sb = sb("wsel_sb", [128, NT * 8])
    wtmp_sb = sb("wtmp_sb", [128, 8])
    xf = [sb(f"xf{p}", [128, D]) for p in range(2)]
    xr = [sb(f"xr{p}", [128, D]) for p in range(2)]
    gs = [sb(f"gs{p}", [128, D]) for p in range(2)]

    ld = sem("ld")
    ldg = sem("ldg")
    ltr = sem("ltr")
    str_t = sem("str_t")
    spb = sem("spb")
    spc = sem("spc")
    lw = [sem("lw0"), sem("lw1")]
    spe = sem("spe")
    lw2 = sem("lw2")
    sty = sem("sty")
    slg = sem("slg")
    lx = [sem("lx0"), sem("lx1")]
    st = [sem("st0"), sem("st1")]

    with nc.Block() as blk:
        @blk.sync
        def _(e):
            e.dma_start(out=rw_sb[:, :], in_=bass.AP(rwb_d, 0, [[64, 128], [1, 64]])).then_inc(ld, 16)
            e.dma_start(out=rwo_sb[:, :], in_=rwo_d.ap()).then_inc(ld, 16)

        @blk.gpsimd
        def _(e):
            _build_ident(GP, ident_bf, ident_f)
            GP.op("memset", ones_row[:, :], 1.0)
            GP.op("memset", ones_col[:, :], 1.0)
            e.dma_start(out=wg_sb[:, :],
                        in_=bass.AP(wg_d, 0, [[8, 128], [8 * 128, NDC], [1, 8]])).then_inc(ldg, 16)

    vm_tr = []
    with nc.psum_tensor("tr_ps", [128, D], BF16) as tr_ps:
        with nc.Block() as blk:
            @blk.vector
            def _(e):
                for j in range(NT):
                    VE.wait(str_t, j + 1)
                    vm_tr.append(VE.op(
                        "tensor_copy",
                        out=bass.AP(xnT, j * 128, [[NDC * TC, 128], [TC, NDC], [1, 128]]),
                        in_=tr_ps[:, :]))

            @blk.gpsimd
            def _(e):
                for j in range(NT):
                    if j >= 1:
                        e.wait_ge(str_t, j)
                    e.dma_start(out=xstage[:, :],
                                in_=_ap3_tokmaj(xn_d, 1, D, offset=j * 128 * D)).then_inc(ltr, 16)

            @blk.tensor
            def _(e):
                for j in range(NT):
                    e.wait_ge(ltr, 16 * (j + 1))
                    if j >= 1:
                        e.wait_ge(VE.sem, vm_tr[j - 1])
                    last = None
                    for k in range(NDC):
                        last = e.transpose(tr_ps[:, k * 128:(k + 1) * 128],
                                           xstage[:, k * 128:(k + 1) * 128], ident_bf[:, :])
                    last.then_inc(str_t, 1)

    NITER = 42
    vm_bis = {}
    with (nc.psum_tensor("row_ps", [1, 128], F32) as row_ps,
          nc.psum_tensor("bb_ps", [128, 1], F32) as bb_ps,
          nc.psum_tensor("sc_ps", [1, 1], F32) as sc_ps):
        with nc.Block() as blk:
            @blk.vector
            def _(e):
                e.wait_ge(ld, 32)
                vm_bis["r1"] = VE.op("reduce_max", out=red_col[:, 0:1], in_=rw_sb[:, :], axis=AX.X)
                VE.wait(spb, 1)
                VE.op("tensor_copy", out=red_row_sb[:, :], in_=row_ps[:, :])
                VE.op("reduce_max", out=hi[0:1, 0:1], in_=red_row_sb[0:1, :], axis=AX.X)
                VE.op("tensor_scalar", out=neg_sb[:, :], in0=rw_sb[:, :], scalar1=-1.0,
                      scalar2=None, op0=ALU.mult)
                vm_bis["r2"] = VE.op("reduce_max", out=red_col[:, 0:1], in_=neg_sb[:, :], axis=AX.X)
                VE.wait(spb, 2)
                VE.op("tensor_copy", out=red_row_sb[:, :], in_=row_ps[:, :])
                VE.op("reduce_max", out=lo[0:1, 0:1], in_=red_row_sb[0:1, :], axis=AX.X)
                VE.op("tensor_scalar", out=lo[0:1, 0:1], in0=lo[0:1, 0:1], scalar1=-1.0, scalar2=-1.0,
                      op0=ALU.mult, op1=ALU.add)
                VE.op("tensor_tensor", out=t1x[0:1, 0:1], in0=lo[0:1, 0:1], in1=hi[0:1, 0:1], op=ALU.add)
                vm_bis["mid", -1] = VE.op("tensor_scalar", out=mid[0:1, 0:1], in0=t1x[0:1, 0:1],
                                          scalar1=0.5, scalar2=None, op0=ALU.mult)
                for i in range(NITER):
                    VE.wait(spb, 3 + i)
                    VE.op("tensor_copy", out=midb[:, :], in_=bb_ps[:, :])
                    VE.op("tensor_scalar", out=cmp_sb[:, :], in0=rw_sb[:, :], scalar1=midb[:, 0:1],
                          scalar2=None, op0=ALU.is_gt)
                    vm_bis["cnt", i] = VE.op("reduce_sum", out=cnt_col[:, 0:1], in_=cmp_sb[:, :], axis=AX.X)
                    VE.wait(spc, i + 1)
                    VE.op("tensor_copy", out=t2x[0:1, 0:1], in_=sc_ps[0:1, 0:1])
                    VE.op("tensor_scalar", out=pred[0:1, 0:1], in0=t2x[0:1, 0:1], scalar1=1024.0,
                          scalar2=None, op0=ALU.is_ge)
                    VE.op("tensor_tensor", out=t1x[0:1, 0:1], in0=mid[0:1, 0:1], in1=lo[0:1, 0:1], op=ALU.subtract)
                    VE.op("tensor_tensor", out=t1x[0:1, 0:1], in0=pred[0:1, 0:1], in1=t1x[0:1, 0:1], op=ALU.mult)
                    VE.op("tensor_tensor", out=lo[0:1, 0:1], in0=lo[0:1, 0:1], in1=t1x[0:1, 0:1], op=ALU.add)
                    VE.op("tensor_tensor", out=t1x[0:1, 0:1], in0=hi[0:1, 0:1], in1=mid[0:1, 0:1], op=ALU.subtract)
                    VE.op("tensor_scalar", out=t2x[0:1, 0:1], in0=pred[0:1, 0:1], scalar1=1.0,
                          scalar2=None, op0=ALU.subtract)
                    VE.op("tensor_tensor", out=t1x[0:1, 0:1], in0=t2x[0:1, 0:1], in1=t1x[0:1, 0:1], op=ALU.mult)
                    VE.op("tensor_tensor", out=hi[0:1, 0:1], in0=hi[0:1, 0:1], in1=t1x[0:1, 0:1], op=ALU.add)
                    VE.op("tensor_tensor", out=t1x[0:1, 0:1], in0=lo[0:1, 0:1], in1=hi[0:1, 0:1], op=ALU.add)
                    vm_bis["mid", i] = VE.op("tensor_scalar", out=mid[0:1, 0:1], in0=t1x[0:1, 0:1],
                                             scalar1=0.5, scalar2=None, op0=ALU.mult)
                VE.wait(spb, 3 + NITER)
                VE.op("tensor_copy", out=midb[:, :], in_=bb_ps[:, :])
                VE.op("tensor_scalar", out=mask_sb[:, :], in0=rwo_sb[:, :], scalar1=midb[:, 0:1],
                      scalar2=None, op0=ALU.is_gt)

            @blk.tensor
            def _(e):
                e.wait_ge(VE.sem, vm_bis["r1"])
                e.matmul(row_ps[0:1, :], red_col[:, 0:1], ident_f[:, :],
                         start=True, stop=True).then_inc(spb, 1)
                e.wait_ge(VE.sem, vm_bis["r2"])
                e.matmul(row_ps[0:1, :], red_col[:, 0:1], ident_f[:, :],
                         start=True, stop=True).then_inc(spb, 1)
                for i in range(NITER):
                    e.wait_ge(VE.sem, vm_bis["mid", i - 1])
                    e.matmul(bb_ps[:, 0:1], ones_row[0:1, 0:128], mid[0:1, 0:1],
                             start=True, stop=True).then_inc(spb, 1)
                    e.wait_ge(VE.sem, vm_bis["cnt", i])
                    e.matmul(sc_ps[0:1, 0:1], cnt_col[:, 0:1], ones_col[:, 0:1],
                             start=True, stop=True).then_inc(spc, 1)
                e.wait_ge(VE.sem, vm_bis["mid", NITER - 1])
                e.matmul(bb_ps[:, 0:1], ones_row[0:1, 0:128], lo[0:1, 0:1],
                         start=True, stop=True).then_inc(spb, 1)

    vm_lg = []
    with (nc.psum_tensor("lg_ps0", [128, 8], F32) as lg0,
          nc.psum_tensor("lg_ps1", [128, 8], F32) as lg1):
        lg_ps = [lg0, lg1]
        with nc.Block() as blk:
            @blk.vector
            def _(e):
                for j in range(NT):
                    pb = j % 2
                    VE.wait(slg, j + 1)
                    vm_lg.append(VE.op("tensor_copy", out=logits_sb[:, j * 8:(j + 1) * 8],
                                       in_=lg_ps[pb][:, :]))

            @blk.tensor
            def _(e):
                e.wait_ge(ldg, 16)
                for j in range(NT):
                    pb = j % 2
                    if j >= 2:
                        e.wait_ge(VE.sem, vm_lg[j - 2])
                    last = None
                    for k in range(NDC):
                        last = e.matmul(lg_ps[pb][:, :],
                                        xnT[:, k * TC + j * 128:k * TC + (j + 1) * 128],
                                        wg_sb[:, k * 8:(k + 1) * 8],
                                        start=(k == 0), stop=(k == NDC - 1))
                    last.then_inc(slg, 1)

    with nc.Block() as blk:
        @blk.vector
        def _(e):
            for j in range(NT):
                VE.op("max", out=mx_sb[:, j * 8:(j + 1) * 8], in_=logits_sb[:, j * 8:(j + 1) * 8])
            for j in range(NT):
                VE.op("tensor_tensor", out=d_sb[:, j:j + 1], in0=mx_sb[:, j * 8:j * 8 + 1],
                      in1=mx_sb[:, j * 8 + 1:j * 8 + 2], op=ALU.subtract)

    with nc.Block() as blk:
        @blk.scalar
        def _(e):
            SC.op("activation", out=p1_sb[:, :], in_=d_sb[:, :], func=AF.Sigmoid)
            SC.op("activation", out=p2_sb[:, :], in_=d_sb[:, :], func=AF.Sigmoid, scale=-1.0)

    with nc.Block() as blk:
        @blk.vector
        def _(e):
            for j in range(NT):
                lj = logits_sb[:, j * 8:(j + 1) * 8]
                VE.op("tensor_scalar", out=wsel_sb[:, j * 8:(j + 1) * 8], in0=lj,
                      scalar1=mx_sb[:, j * 8:j * 8 + 1], scalar2=p1_sb[:, j:j + 1],
                      op0=ALU.is_equal, op1=ALU.mult)
                VE.op("tensor_scalar", out=wtmp_sb[:, :], in0=lj,
                      scalar1=mx_sb[:, j * 8 + 1:j * 8 + 2], scalar2=p2_sb[:, j:j + 1],
                      op0=ALU.is_equal, op1=ALU.mult)
                VE.op("tensor_tensor", out=wsel_sb[:, j * 8:(j + 1) * 8],
                      in0=wsel_sb[:, j * 8:(j + 1) * 8], in1=wtmp_sb[:, :], op=ALU.add)
            VE.op("memset", g_acc[:, :], 0.0)

    with (nc.psum_tensor("h1_ps0", [128, 512], F32) as h1a,
          nc.psum_tensor("h1_ps1", [128, 512], F32) as h1b,
          nc.psum_tensor("h3_ps0", [128, 512], F32) as h3a,
          nc.psum_tensor("h3_ps1", [128, 512], F32) as h3b,
          nc.psum_tensor("y_ps", [128, D], F32) as y_ps):
        h1_ps = [h1a, h1b]
        h3_ps = [h3a, h3b]
        vm_h = {}
        sm_sil = {}
        vm_y = {}
        NGRP = NHC * NG
        for eix in range(8):
            sc_base = SC.n
            ve_base = VE.n
            for q in range(NGRP):
                seqg = eix * NGRP + q
                sm_sil[seqg] = sc_base + q + 1
                vm_h[seqg] = ve_base + q + 1
            with nc.Block() as blk:
                @blk.scalar
                def _(e, eix=eix):
                    for hc in range(NHC):
                        for g in range(NG):
                            seqg = (eix * NHC + hc) * NG + g
                            p = seqg % 2
                            SC.wait(spe, seqg + 1)
                            if seqg >= 2:
                                SC.wait(VE.sem, vm_h[seqg - 2])
                            mk = SC.op("activation", out=sil[p][:, :],
                                       in_=h1_ps[p][:, :], func=AF.Silu)
                            assert mk == sm_sil[seqg]

                @blk.vector
                def _(e, eix=eix):
                    for hc in range(NHC):
                        for g in range(NG):
                            seqg = (eix * NHC + hc) * NG + g
                            p = seqg % 2
                            VE.wait(SC.sem, sm_sil[seqg])
                            mk = VE.op(
                                "tensor_tensor",
                                out=h_sb[:, hc * TC + g * 512:hc * TC + (g + 1) * 512],
                                in0=sil[p][:, :], in1=h3_ps[p][:, :], op=ALU.mult)
                            assert mk == vm_h[seqg]

                @blk.gpsimd
                def _(e, eix=eix):
                    for hc in range(NHC):
                        p = hc % 2
                        seq = eix * NHC + hc
                        if seq >= 2:
                            e.wait_ge(spe, (seq - 1) * NG)
                        off = eix * D * HH + hc * 128
                        e.dma_start(out=w1c[p][:, :],
                                    in_=bass.AP(w1_d, off, [[HH, 128], [128 * HH, NDC], [1, 128]])
                                    ).then_inc(lw[p], 16)
                        e.dma_start(out=w3c[p][:, :],
                                    in_=bass.AP(w3_d, off, [[HH, 128], [128 * HH, NDC], [1, 128]])
                                    ).then_inc(lw[p], 16)

                @blk.tensor
                def _(e, eix=eix):
                    for hc in range(NHC):
                        pw = hc % 2
                        seq = eix * NHC + hc
                        e.wait_ge(lw[pw], 32 * (seq // 2 + 1))
                        for g in range(NG):
                            seqg = seq * NG + g
                            p = seqg % 2
                            if seqg >= 2:
                                e.wait_ge(VE.sem, vm_h[seqg - 2])
                            for k in range(NDC):
                                e.matmul(h1_ps[p][:, :], w1c[pw][:, k * 128:(k + 1) * 128],
                                         xnT[:, k * TC + g * 512:k * TC + (g + 1) * 512],
                                         start=(k == 0), stop=(k == NDC - 1))
                            last = None
                            for k in range(NDC):
                                last = e.matmul(h3_ps[p][:, :], w3c[pw][:, k * 128:(k + 1) * 128],
                                                xnT[:, k * TC + g * 512:k * TC + (g + 1) * 512],
                                                start=(k == 0), stop=(k == NDC - 1))
                            last.then_inc(spe, 1)

            with nc.Block() as blk:
                @blk.vector
                def _(e, eix=eix):
                    for j in range(NT):
                        seq = eix * NT + j
                        VE.wait(sty, seq + 1)
                        for half in range(2):
                            mk = VE.op(
                                "scalar_tensor_tensor",
                                out=g_acc[:, j * D + half * 512:j * D + (half + 1) * 512],
                                in0=y_ps[:, half * 512:(half + 1) * 512],
                                scalar=wsel_sb[:, j * 8 + eix:j * 8 + eix + 1],
                                in1=g_acc[:, j * D + half * 512:j * D + (half + 1) * 512],
                                op0=ALU.mult, op1=ALU.add)
                        vm_y[seq] = mk

                @blk.gpsimd
                def _(e, eix=eix):
                    if eix >= 1:
                        e.wait_ge(sty, eix * NT)
                    e.dma_start(out=w2_sb[:, :],
                                in_=bass.AP(w2_d, eix * HH * D, [[D, 128], [128 * D, NHC], [1, D]])
                                ).then_inc(lw2, 16)

                @blk.tensor
                def _(e, eix=eix):
                    e.wait_ge(lw2, 16 * (eix + 1))
                    for j in range(NT):
                        seq = eix * NT + j
                        if seq >= 1:
                            e.wait_ge(VE.sem, vm_y[seq - 1])
                        last = None
                        for hc in range(NHC):
                            hj = h_sb[:, hc * TC + j * 128:hc * TC + (j + 1) * 128]
                            e.matmul(y_ps[:, 0:512], hj, w2_sb[:, hc * D:hc * D + 512],
                                     start=(hc == 0), stop=(hc == NHC - 1))
                            last = e.matmul(y_ps[:, 512:1024], hj, w2_sb[:, hc * D + 512:(hc + 1) * D],
                                            start=(hc == 0), stop=(hc == NHC - 1))
                        last.then_inc(sty, 1)

    sc_base_f = SC.n
    ve_base_f = VE.n
    sm_f = {j: sc_base_f + j + 1 for j in range(NT)}
    vm_f = {j: ve_base_f + 3 * (j + 1) for j in range(NT)}
    with nc.Block() as blk:
        @blk.scalar
        def _(e):
            for j in range(NT):
                p = j % 2
                if j >= 2:
                    SC.wait(VE.sem, vm_f[j - 2])
                mk = SC.op("activation", out=gs[p][:, :], in_=g_acc[:, j * D:(j + 1) * D],
                           func=AF.Silu)
                assert mk == sm_f[j]

        @blk.vector
        def _(e):
            for j in range(NT):
                p = j % 2
                VE.wait(lx[j % 2], 32 * (j // 2 + 1))
                VE.wait(SC.sem, sm_f[j])
                if j >= 2:
                    VE.wait(st[j % 2], 16 * (j // 2))
                VE.op("tensor_scalar", out=xr[p][:, :], in0=xr[p][:, :], scalar1=mask_sb[:, j:j + 1],
                      scalar2=None, op0=ALU.mult)
                VE.op("tensor_tensor", out=xr[p][:, :], in0=xr[p][:, :], in1=gs[p][:, :], op=ALU.mult)
                mk = VE.op("tensor_tensor", out=xf[p][:, :], in0=xf[p][:, :], in1=xr[p][:, :],
                           op=ALU.add)
                assert mk == vm_f[j]

        @blk.sync
        def _(e):
            for j in range(NT):
                p = j % 2
                if j >= 2:
                    e.wait_ge(VE.sem, vm_f[j - 2])
                    e.wait_ge(st[j % 2], 16 * (j // 2))
                e.dma_start(out=xf[p][:, :], in_=_ap3_tokmaj(x_d, 1, D, offset=j * 128 * D)).then_inc(lx[j % 2], 16)
                e.dma_start(out=xr[p][:, :], in_=_ap3_tokmaj(xr_d, 1, D, offset=j * 128 * D)).then_inc(lx[j % 2], 16)

        @blk.gpsimd
        def _(e):
            for j in range(NT):
                e.wait_ge(VE.sem, vm_f[j])
                p = j % 2
                e.dma_start(out=_ap3_tokmaj(x1_d, 1, D, offset=j * 128 * D),
                            in_=xf[p][:, :]).then_inc(st[p], 16)
            e.wait_ge(st[0], 16 * (NT // 2))
            e.wait_ge(st[1], 16 * (NT // 2))

    es.close()
    _BUILD_CACHE["l0b"] = nc
    return nc





def build_l1a():
    if "l1a" in _BUILD_CACHE:
        return _BUILD_CACHE["l1a"]
    NDC = D // 128
    NG = TC // 512
    nc = _nc()
    x_d = nc.dram_tensor("x", [TC, D], F32, kind="ExternalInput")
    wnames = ["wq", "wk", "wv", "wgk", "wgc", "wproj"]
    w_ds = [nc.dram_tensor(nm, [D, D], F32, kind="ExternalInput") for nm in wnames]
    bgk_d = nc.dram_tensor("bgk2d", [128, 8], F32, kind="ExternalInput")
    bgc_d = nc.dram_tensor("bgc2d", [128, 8], F32, kind="ExternalInput")
    bpr_d = nc.dram_tensor("bproj2d", [128, 8], F32, kind="ExternalInput")
    wgate_d = nc.dram_tensor("wgate", [D, 8], F32, kind="ExternalInput")
    onames = ["qT", "kT", "vT", "aKT", "aCT", "xpT"]
    out_ds = [nc.dram_tensor(nm, [D, TC], F32, kind="ExternalOutput") for nm in onames]
    lg_d = nc.dram_tensor("logits2d", [128, NT * 8], F32, kind="ExternalOutput")

    es, sb, sem, VE, SC, GP = _mk_ctx(nc)
    x_sb = sb("x_sb", [128, NT * D])
    lns = _ln_scratch(sb, "a", D)
    xnT = sb("xnT", [128, NDC * TC], BF16)
    xn_st = sb("xn_st", [128, D])
    xn_bf = sb("xn_bf", [128, D], BF16)
    ident_bf = sb("ident_bf", [128, 128], BF16)
    w_sb = [sb(f"w_sb{p}", [128, NDC * D], BF16) for p in range(2)]
    stage = [sb(f"stage{p}", [128, 512]) for p in range(2)]
    bias_sb = sb("bias_sb", [128, 3 * 8])
    zeros_c = sb("zeros_c", [128, 1])
    wg_sb = sb("wg_sb", [128, NDC * 8], BF16)
    lg_sb = sb("lg_sb", [128, NT * 8])
    xp_bf = sb("xp_bf", [128, NDC * TC], BF16)

    ld = sem("ld")
    ldg = sem("ldg")
    ltr = sem("ltr")
    lw = [sem("lw0"), sem("lw1")]
    spj = sem("spj")
    slg = sem("slg")
    lxp = sem("lxp")
    st = [sem("st0"), sem("st1")]
    stl = sem("stl")

    with nc.Block() as blk:
        @blk.sync
        def _(e):
            e.dma_start(out=x_sb[:, :], in_=_ap3_tokmaj(x_d, NT, D)).then_inc(ld, 16)
            e.dma_start(out=bias_sb[:, 0:8], in_=bgk_d.ap()).then_inc(ld, 16)
            e.dma_start(out=bias_sb[:, 8:16], in_=bgc_d.ap()).then_inc(ld, 16)
            e.dma_start(out=bias_sb[:, 16:24], in_=bpr_d.ap()).then_inc(ld, 16)

        @blk.gpsimd
        def _(e):
            _build_ident(GP, ident_bf)
            GP.op("memset", lns["eps"][:, :], 1e-5)
            GP.op("memset", zeros_c[:, :], 0.0)
            e.dma_start(out=wg_sb[:, :],
                        in_=bass.AP(wgate_d, 0, [[8, 128], [8 * 128, NDC], [1, 8]])).then_inc(ldg, 16)

        @blk.vector
        def _(e):
            e.wait_ge(ld, 64)

    _emit_ln(nc, VE, SC, x_sb, None, lns, NT, D, normalize=False)

    with nc.psum_tensor("tr_ps", [128, D], BF16) as tr_ps:
        ve_base = VE.n
        with nc.Block() as blk:
            @blk.vector
            def _(e):
                for j in range(NT):
                    VE.op("tensor_scalar", out=xn_st[:, :], in0=x_sb[:, j * D:(j + 1) * D],
                          scalar1=lns["m"][:, j:j + 1], scalar2=lns["rs"][:, j:j + 1],
                          op0=ALU.subtract, op1=ALU.mult)
                    VE.op("tensor_copy", out=xn_bf[:, :], in_=xn_st[:, :])
                    VE.wait(ltr, j + 1)
                    VE.op("tensor_copy",
                          out=bass.AP(xnT, j * 128, [[NDC * TC, 128], [TC, NDC], [1, 128]]),
                          in_=tr_ps[:, :])

            @blk.tensor
            def _(e):
                for j in range(NT):
                    e.wait_ge(VE.sem, ve_base + 3 * j + 2)
                    last = None
                    for k in range(NDC):
                        last = e.transpose(tr_ps[:, k * 128:(k + 1) * 128],
                                           xn_bf[:, k * 128:(k + 1) * 128], ident_bf[:, :])
                    last.then_inc(ltr, 1)

    ACTS = [(AF.Identity, None), (AF.Identity, None), (AF.Identity, None),
            (AF.Sigmoid, 0), (AF.Sigmoid, 8), (AF.Identity, 16)]
    NSEQ = 6 * 8 * NG
    sc_base = SC.n
    sm_stage = {q: sc_base + q + 1 for q in range(NSEQ)}
    with (nc.psum_tensor("pj_ps0", [128, 512], F32) as pj0,
          nc.psum_tensor("pj_ps1", [128, 512], F32) as pj1):
        pj_ps = [pj0, pj1]
        with nc.Block() as blk:
            @blk.gpsimd
            def _(e):
                for wi in range(6):
                    p = wi % 2
                    if wi >= 2:
                        e.wait_ge(spj, (wi - 1) * 32)
                    e.dma_start(out=w_sb[p][:, :],
                                in_=bass.AP(w_ds[wi], 0, [[D, 128], [128 * D, NDC], [1, D]])
                                ).then_inc(lw[p], 16)

            @blk.tensor
            def _(e):
                nseq = 0
                for wi in range(6):
                    p = wi % 2
                    e.wait_ge(lw[p], 16 * (wi // 2 + 1))
                    for fc in range(8):
                        for g in range(NG):
                            pp = nseq % 2
                            if nseq >= 2:
                                e.wait_ge(SC.sem, sm_stage[nseq - 2])
                            last = None
                            for k in range(NDC):
                                last = e.matmul(pj_ps[pp][:, :],
                                                w_sb[p][:, k * D + fc * 128:k * D + (fc + 1) * 128],
                                                xnT[:, k * TC + g * 512:k * TC + (g + 1) * 512],
                                                start=(k == 0), stop=(k == NDC - 1))
                            last.then_inc(spj, 1)
                            nseq += 1

            @blk.scalar
            def _(e):
                nseq = 0
                for wi in range(6):
                    af, boff = ACTS[wi]
                    for fc in range(8):
                        for g in range(NG):
                            pp = nseq % 2
                            SC.wait(spj, nseq + 1)
                            if nseq >= 2:
                                SC.wait(st[pp], 16 * (nseq // 2))
                            bias_ap = zeros_c[:, 0:1] if boff is None else bias_sb[:, boff + fc:boff + fc + 1]
                            mk = SC.op("activation", out=stage[pp][:, :],
                                       in_=pj_ps[pp][:, :], func=af, bias=bias_ap)
                            assert mk == sm_stage[nseq]
                            nseq += 1

            @blk.sync
            def _(e):
                nseq = 0
                for wi in range(6):
                    for fc in range(8):
                        for g in range(NG):
                            pp = nseq % 2
                            e.wait_ge(SC.sem, sm_stage[nseq])
                            e.dma_start(out=bass.AP(out_ds[wi], fc * 128 * TC + g * 512,
                                                    [[TC, 128], [1, 512]]),
                                        in_=stage[pp][:, :]).then_inc(st[pp], 16)
                            nseq += 1

    with (nc.psum_tensor("lg_ps0", [128, 8], F32) as lgp0,
          nc.psum_tensor("lg_ps1", [128, 8], F32) as lgp1):
        lg_ps = [lgp0, lgp1]
        ve_base2 = VE.n
        with nc.Block() as blk:
            @blk.gpsimd
            def _(e):
                e.wait_ge(st[0], 16 * 96)
                e.wait_ge(st[1], 16 * 96)
                e.dma_start(out=xp_bf[:, :],
                            in_=bass.AP(out_ds[5], 0, [[TC, 128], [128 * TC, NDC], [1, TC]])
                            ).then_inc(lxp, 16)

            @blk.tensor
            def _(e):
                e.wait_ge(lxp, 16)
                e.wait_ge(ldg, 16)
                for j in range(NT):
                    pp = j % 2
                    if j >= 2:
                        e.wait_ge(VE.sem, ve_base2 + j - 1)
                    last = None
                    for k in range(NDC):
                        last = e.matmul(lg_ps[pp][:, :],
                                        xp_bf[:, k * TC + j * 128:k * TC + (j + 1) * 128],
                                        wg_sb[:, k * 8:(k + 1) * 8],
                                        start=(k == 0), stop=(k == NDC - 1))
                    last.then_inc(slg, 1)

            @blk.vector
            def _(e):
                for j in range(NT):
                    pp = j % 2
                    VE.wait(slg, j + 1)
                    VE.op("tensor_copy", out=lg_sb[:, j * 8:(j + 1) * 8], in_=lg_ps[pp][:, :])

    with nc.Block() as blk:
        @blk.sync
        def _(e):
            e.dma_start(out=lg_d.ap(), in_=lg_sb[:, :]).then_inc(stl, 16)
            e.wait_ge(stl, 16)

    es.close()
    _BUILD_CACHE["l1a"] = nc
    return nc




def build_l1b(NBH=4, NCHK=None, trunc=None):
    key = ("l1b", NBH, NCHK, trunc)
    if key in _BUILD_CACHE:
        return _BUILD_CACHE[key]
    DK = 64
    CH = 128
    if NCHK is None:
        NCHK = L // CH
    WCH = 8
    WLEN = WCH * CH
    NWIN = NCHK // WCH
    nc = _nc()
    qk_d = nc.dram_tensor("qkvab", [NBH, 5, DK, L], F32, kind="ExternalInput")
    o_d = nc.dram_tensor("oT", [NBH, L, DK], F32, kind="ExternalOutput")

    es, sb, sem, VE, SC, GP = _mk_ctx(nc)
    win = sb("win", [64, 5 * WLEN])
    lnK = sb("lnK", [64, CH]); lnC = sb("lnC", [64, CH])
    LK = sb("LK", [64, CH]); LC = sb("LC", [64, CH])
    eKp = sb("eKp", [64, CH]); eKm = sb("eKm", [64, CH])
    eCp = sb("eCp", [64, CH]); eCm = sb("eCm", [64, CH])
    QK1 = sb("QK1", [64, CH], BF16); KK1 = sb("KK1", [64, CH], BF16)
    KC1 = sb("KC1", [64, CH], BF16); QC1 = sb("QC1", [64, CH], BF16)
    Vbf = sb("Vbf", [64, CH], BF16)
    K2 = sb("K2", [64, CH], BF16); Q2 = sb("Q2", [64, CH], BF16)
    VT_sb = sb("VT_sb", [128, 64], BF16)
    KK1T_sb = sb("KK1T_sb", [128, 64], BF16)
    K2T_sb = sb("K2T_sb", [128, 64], BF16)
    Q2T_sb = sb("Q2T_sb", [128, 64], BF16)
    PmKT = sb("PmKT", [128, CH], BF16)
    A2Tm = sb("A2Tm", [128, CH], BF16)
    Pm3T = sb("Pm3T", [128, CH], BF16)
    M_sb = sb("M_sb", [128, 64], BF16)
    W1bf = sb("W1bf", [128, 64], BF16)
    U_sb = sb("U_sb", [64, CH])
    UBbf = sb("UBbf", [64, CH], BF16)
    o1_sb = sb("o1_sb", [128, 64])
    o_sb = [sb(f"o_sb{p}", [128, 64]) for p in range(2)]
    S0_sb = sb("S0_sb", [64, 64]); C0_sb = sb("C0_sb", [64, 64]); G0_sb = sb("G0_sb", [64, 64])
    S0_bf = sb("S0_bf", [64, 64], BF16); C0_bf = sb("C0_bf", [64, 64], BF16)
    G0_bf = sb("G0_bf", [64, 64], BF16)
    Stmp = sb("Stmp", [64, 64]); Ctmp = sb("Ctmp", [64, 64]); Gtmp = sb("Gtmp", [64, 64])
    eKcRow = sb("eKcRow", [1, 64])
    tri_i = sb("tri_i", [128, CH])
    tri_s = sb("tri_s", [128, CH])
    ones64 = sb("ones64", [64, CH])
    ident_bf = sb("ident_bf", [128, 128], BF16)
    ident_f = sb("ident_f", [128, 128], F32)

    lwin = sem("lwin")
    spp = sem("spp")
    sto = [sem("sto0"), sem("sto1")]
    pe_count = [0]
    dma_counts = {}

    with nc.Block() as blk:
        @blk.gpsimd
        def _(e):
            _build_ident(GP, ident_bf, ident_f)
            GP.op("memset", ones64[:, :], 1.0)
            GP.op("memset", tri_i[:, :], 1.0)
            GP.op("affine_select", out=tri_i[:, :], in_=tri_i[:, :], pattern=[[1, CH]],
                  compare_op=ALU.is_ge, fill=0.0, base=0, channel_multiplier=-1)
            GP.op("memset", tri_s[:, :], 1.0)
            GP.op("affine_select", out=tri_s[:, :], in_=tri_s[:, :], pattern=[[1, CH]],
                  compare_op=ALU.is_gt, fill=0.0, base=0, channel_multiplier=-1)


    with (nc.psum_tensor("psA", [128, CH], F32) as psA,
          nc.psum_tensor("psB", [128, 64], F32) as psB,
          nc.psum_tensor("psC", [64, 64], F32) as psC,
          nc.psum_tensor("trA_ps", [128, 128], BF16) as trA_ps,
          nc.psum_tensor("trB_ps", [128, 128], BF16) as trB_ps,
          nc.psum_tensor("psR", [1, 64], F32) as psR):
        for bh in range(NBH):
            for ci in range(NCHK):
                cg = bh * NCHK + ci
                co = (ci % WCH) * CH
                T0 = ci * CH
                newwin = (ci % WCH == 0)
                w = ci // WCH
                gw = bh * NWIN + w

                qs = win[:, 0 * WLEN + co:0 * WLEN + co + CH]
                ks_ = win[:, 1 * WLEN + co:1 * WLEN + co + CH]
                vs = win[:, 2 * WLEN + co:2 * WLEN + co + CH]
                aks = win[:, 3 * WLEN + co:3 * WLEN + co + CH]
                acs = win[:, 4 * WLEN + co:4 * WLEN + co + CH]
                eKc = eKp[:, CH - 1:CH]
                eCc = eCp[:, CH - 1:CH]

                with nc.Block() as blk:
                    q = Seq(nc, VE, SC, GP, spp, pe_count, dma_counts)
                    zdep = []
                    if ci == 0:
                        for buf in (S0_sb, C0_sb, G0_sb, S0_bf, C0_bf, G0_bf):
                            zz = q.step("V", [], (lambda e, buf=buf: e.op("memset", buf[:, :], 0.0)))
                        zdep = [zz]
                    dmas = []
                    if newwin:
                        for i5 in range(5):
                            dmas.append(q.step("GD", [], (lambda e, i5=i5, bh=bh, w=w: e.dma_start(
                                out=win[:, i5 * WLEN:(i5 + 1) * WLEN],
                                in_=bass.AP(qk_d, (bh * 5 + i5) * DK * L + w * WLEN,
                                            [[L, 64], [1, WLEN]]))), dma_sem=lwin))
                    windep = [(lwin, 80 * (gw + 1))]
                    s_lnK = q.step("S", windep, lambda e: e.op("activation", out=lnK[:, :], in_=aks, func=AF.Ln))
                    s_lnC = q.step("S", windep, lambda e: e.op("activation", out=lnC[:, :], in_=acs, func=AF.Ln))
                    v_LK = q.step("V", [s_lnK], lambda e: e.op("tensor_tensor_scan", out=LK[:, :],
                                  data0=ones64[:, :], data1=lnK[:, :], initial=0.0,
                                  op0=ALU.mult, op1=ALU.add))
                    v_LC = q.step("V", [s_lnC], lambda e: e.op("tensor_tensor_scan", out=LC[:, :],
                                  data0=ones64[:, :], data1=lnC[:, :], initial=0.0,
                                  op0=ALU.mult, op1=ALU.add))
                    s_eKp = q.step("S", [v_LK], lambda e: e.op("activation", out=eKp[:, :], in_=LK[:, :], func=AF.Exp))
                    s_eKm = q.step("S", [v_LK], lambda e: e.op("activation", out=eKm[:, :], in_=LK[:, :], func=AF.Exp, scale=-1.0))
                    s_eCp = q.step("S", [v_LC], lambda e: e.op("activation", out=eCp[:, :], in_=LC[:, :], func=AF.Exp))
                    s_eCm = q.step("S", [v_LC], lambda e: e.op("activation", out=eCm[:, :], in_=LC[:, :], func=AF.Exp, scale=-1.0))
                    v_QK1 = q.step("V", [s_eKp, windep[0]], lambda e: e.op("tensor_tensor", out=QK1[:, :], in0=qs, in1=eKp[:, :], op=ALU.mult))
                    v_KK1 = q.step("V", [s_eKm], lambda e: e.op("tensor_tensor", out=KK1[:, :], in0=ks_, in1=eKm[:, :], op=ALU.mult))
                    v_KC1 = q.step("V", [s_eCp], lambda e: e.op("tensor_tensor", out=KC1[:, :], in0=ks_, in1=eCp[:, :], op=ALU.mult))
                    v_QC1 = q.step("V", [s_eCm], lambda e: e.op("tensor_tensor", out=QC1[:, :], in0=qs, in1=eCm[:, :], op=ALU.mult))
                    v_Vbf = q.step("V", windep, lambda e: e.op("tensor_copy", out=Vbf[:, :], in_=vs))
                    v_K2 = q.step("V", [v_KK1, s_eKp], lambda e: e.op("tensor_scalar", out=K2[:, :], in0=KK1[:, :], scalar1=eKc, scalar2=None, op0=ALU.mult))
                    v_Q2 = q.step("V", [v_QC1, s_eCp], lambda e: e.op("tensor_scalar", out=Q2[:, :], in0=QC1[:, :], scalar1=eCc, scalar2=None, op0=ALU.mult))
                    t_VT = q.step("T", [v_Vbf], lambda e: e.transpose(trA_ps[:, 0:64], Vbf[:, :], ident_bf[0:64, 0:64]))
                    v_VTc = q.step("V", [t_VT], lambda e: e.op("tensor_copy", out=VT_sb[:, :], in_=trA_ps[:, 0:64]))
                    t_K1T = q.step("T", [v_KK1], lambda e: e.transpose(trB_ps[:, 0:64], KK1[:, :], ident_bf[0:64, 0:64]))
                    v_K1Tc = q.step("V", [t_K1T], lambda e: e.op("tensor_copy", out=KK1T_sb[:, :], in_=trB_ps[:, 0:64]))
                    t_K2T = q.step("T", [v_K2, v_VTc], lambda e: e.transpose(trA_ps[:, 0:64], K2[:, :], ident_bf[0:64, 0:64]))
                    v_K2Tc = q.step("V", [t_K2T], lambda e: e.op("tensor_copy", out=K2T_sb[:, :], in_=trA_ps[:, 0:64]))
                    t_Q2T = q.step("T", [v_Q2, v_K1Tc], lambda e: e.transpose(trB_ps[:, 0:64], Q2[:, :], ident_bf[0:64, 0:64]))
                    v_Q2Tc = q.step("V", [t_Q2T], lambda e: e.op("tensor_copy", out=Q2T_sb[:, :], in_=trB_ps[:, 0:64]))
                    t_A = q.step("T", [v_QK1, v_KK1], lambda e: e.matmul(psA[:, :], KK1[:, :], QK1[:, :], start=True, stop=True))
                    v_mA = q.step("V", [t_A], lambda e: e.op("tensor_tensor", out=PmKT[:, :], in0=psA[:, :], in1=tri_i[:, :], op=ALU.mult))
                    t_B = q.step("T", [v_QC1, v_KC1, v_mA], lambda e: e.matmul(psA[:, :], QC1[:, :], KC1[:, :], start=True, stop=True))
                    v_mB = q.step("V", [t_B], lambda e: e.op("tensor_tensor", out=A2Tm[:, :], in0=psA[:, :], in1=tri_s[:, :], op=ALU.mult))

                    def mk_M(e):
                        e.matmul(psB[:, :], KC1[:, :], C0_bf[:, :], start=True, stop=False)
                        return e.matmul(psB[:, :], A2Tm[:, :], VT_sb[:, :], start=False, stop=True)
                    t_M = q.step("T", [v_mB, v_VTc] + zdep, mk_M)
                    v_Mc = q.step("V", [t_M], lambda e: e.op("tensor_copy", out=M_sb[:, :], in_=psB[:, :]))

                    def mk_W1(e):
                        e.matmul(psB[:, :], QK1[:, :], S0_bf[:, :], start=True, stop=False)
                        return e.matmul(psB[:, :], PmKT[:, :], KK1T_sb[:, :], start=False, stop=True)
                    t_W1 = q.step("T", [v_Mc, v_K1Tc, v_mA] + zdep, mk_W1)
                    v_W1c = q.step("V", [t_W1], lambda e: e.op("tensor_copy", out=W1bf[:, :], in_=psB[:, :]))
                    t_W1T = q.step("T", [v_W1c, v_K2Tc], lambda e: e.transpose(trA_ps[0:64, 0:128], W1bf[:, :], ident_bf[:, :]))
                    v_U = q.step("V", [t_W1T, s_eKp], lambda e: e.op("tensor_tensor", out=U_sb[:, :], in0=trA_ps[0:64, 0:128], in1=eKp[:, :], op=ALU.mult))
                    v_UB = q.step("V", [v_U, s_eCp], lambda e: e.op("tensor_tensor", out=UBbf[:, :], in0=U_sb[:, :], in1=eCp[:, :], op=ALU.mult))
                    t_A3 = q.step("T", [v_UB, v_mB], lambda e: e.matmul(psA[:, :], QC1[:, :], UBbf[:, :], start=True, stop=True))
                    v_m3 = q.step("V", [t_A3], lambda e: e.op("tensor_tensor", out=Pm3T[:, :], in0=psA[:, :], in1=tri_i[:, :], op=ALU.mult))

                    def mk_o1(e):
                        e.matmul(psB[:, :], UBbf[:, :], C0_bf[:, :], start=True, stop=False)
                        return e.matmul(psB[:, :], Pm3T[:, :], VT_sb[:, :], start=False, stop=True)
                    t_o1 = q.step("T", [v_m3, v_W1c] + zdep, mk_o1)
                    v_o1c = q.step("V", [t_o1], lambda e: e.op("tensor_copy", out=o1_sb[:, :], in_=psB[:, :]))

                    def mk_o2(e):
                        e.matmul(psB[:, :], QK1[:, :], G0_bf[:, :], start=True, stop=False)
                        return e.matmul(psB[:, :], PmKT[:, :], M_sb[:, :], start=False, stop=True)
                    t_o2 = q.step("T", [v_o1c] + zdep, mk_o2)
                    o_deps = [t_o2]
                    if cg >= 2:
                        o_deps.append((sto[cg % 2], 16 * (cg // 2)))
                    v_osub = q.step("V", o_deps, lambda e, cg=cg: e.op("tensor_tensor", out=o_sb[cg % 2][:, :], in0=o1_sb[:, :], in1=psB[:, :], op=ALU.subtract))
                    q.step("GD", [v_osub], (lambda e, cg=cg, bh=bh, T0=T0: e.dma_start(
                        out=bass.AP(o_d, bh * L * DK + T0 * DK, [[DK, 128], [1, DK]]),
                        in_=o_sb[cg % 2][:, :])), dma_sem=sto[cg % 2])
                    t_eKcT = q.step("T", [s_eKp], lambda e: e.transpose(psR[0:1, 0:64], eKc, ident_f[0:64, 0:64]))
                    v_eKcc = q.step("V", [t_eKcT], lambda e: e.op("tensor_copy", out=eKcRow[:, :], in_=psR[0:1, 0:64]))
                    t_out = q.step("T", [v_eKcc], lambda e: e.matmul(psC[:, :], eKcRow[0:1, 0:64], eKcRow[0:1, 0:64], start=True, stop=True))
                    v_Ssc = q.step("V", [t_out], lambda e: e.op("tensor_tensor", out=Stmp[:, :], in0=psC[:, :], in1=S0_sb[:, :], op=ALU.mult))
                    t_SK = q.step("T", [v_Ssc, v_K2Tc, t_W1], lambda e: e.matmul(psC[:, :], K2T_sb[:, :], K2T_sb[:, :], start=True, stop=True))
                    v_Sadd = q.step("V", [t_SK], lambda e: e.op("tensor_tensor", out=S0_sb[:, :], in0=Stmp[:, :], in1=psC[:, :], op=ALU.add))
                    v_Sbf = q.step("V", [v_Sadd], lambda e: e.op("tensor_copy", out=S0_bf[:, :], in_=S0_sb[:, :]))
                    v_Csc = q.step("V", [s_eCp, t_o1], lambda e: e.op("tensor_scalar", out=Ctmp[:, :], in0=C0_sb[:, :], scalar1=eCc, scalar2=None, op0=ALU.mult))
                    t_CQ = q.step("T", [v_Sadd, v_Q2Tc], lambda e: e.matmul(psC[:, :], Q2T_sb[:, :], VT_sb[:, :], start=True, stop=True))
                    v_Cadd = q.step("V", [t_CQ, v_Csc], lambda e: e.op("tensor_tensor", out=C0_sb[:, :], in0=Ctmp[:, :], in1=psC[:, :], op=ALU.add))
                    v_Cbf = q.step("V", [v_Cadd], lambda e: e.op("tensor_copy", out=C0_bf[:, :], in_=C0_sb[:, :]))
                    v_Gsc = q.step("V", [s_eKp, t_o2], lambda e: e.op("tensor_scalar", out=Gtmp[:, :], in0=G0_sb[:, :], scalar1=eKc, scalar2=None, op0=ALU.mult))
                    t_GK = q.step("T", [v_Cadd, v_Mc], lambda e: e.matmul(psC[:, :], K2T_sb[:, :], M_sb[:, :], start=True, stop=True))
                    v_Gadd = q.step("V", [t_GK, v_Gsc], lambda e: e.op("tensor_tensor", out=G0_sb[:, :], in0=Gtmp[:, :], in1=psC[:, :], op=ALU.add))
                    q.step("V", [v_Gadd], lambda e: e.op("tensor_copy", out=G0_bf[:, :], in_=G0_sb[:, :]))
                    if trunc is not None:
                        q.steps = q.steps[:trunc]
                    q.emit(blk)

    with nc.Block() as blk:
        @blk.gpsimd
        def _(e):
            for s_ in sto:
                if id(s_) in dma_counts:
                    e.wait_ge(s_, dma_counts[id(s_)])

    es.close()
    _BUILD_CACHE[key] = nc
    return nc



def _bcast_row_ap(handle, n, offset=0):
    return bass.AP(handle, offset, [[0, 128], [1, n]])


def _emit_top1_routing(nc, q, VE, SC, GP, spp, tiles, cap,
                       lgall, lgown, mcpre8, ohT, ohTown, postokT, pm8,
                       mxg, mxo, prefix8, keeprow, keeprow_bf, keepcol, ohk,
                       ohtile_bf, ones64r, psT8pair, psS, psKpair, ident_bf, lddep):
    psT8a, psT8b = psT8pair
    psKa, psKb = psKpair
    """Top-1 expert routing with global capacity in token order.

    lgall:  [128, 128*8] all-token logits token-major (replicated input)
    lgown:  [128, NT*8] own-token logits token-major
    mcpre8: [8, T] f32 prefix mask (1 for tokens before this core's range)
    Produces keepcol [128, NT] (keep per own token) and ohk [128, NT*8]
    (= onehot * keep per own token).
    """
    NGT = 128
    vmx = q.step("V", lddep, lambda e: e.op(
        "tensor_reduce", out=mxg[:, :],
        in_=bass.AP(lgall.tensor if hasattr(lgall, "tensor") else lgall, 0,
                    [[128 * 8, 128], [8, NGT], [1, 8]]),
        axis=AX.X, op=ALU.max))
    tprev = []
    vcp = {}
    for jj in range(NGT):
        voh = q.step("V", [vmx], (lambda e, jj=jj: e.op(
            "tensor_scalar", out=ohtile_bf[:, :], in0=lgall[:, jj * 8:(jj + 1) * 8],
            scalar1=mxg[:, jj:jj + 1], scalar2=None, op0=ALU.is_equal)))
        pst = psT8a if jj % 2 == 0 else psT8b
        tt = q.step("T", [voh] + tprev, (lambda e, pst=pst: e.transpose(
            pst[0:8, 0:128], ohtile_bf[:, :], ident_bf[:, :])))
        vv = q.step("V", [tt], (lambda e, jj=jj, pst=pst: e.op(
            "tensor_copy", out=ohT[0:8, jj * 128:(jj + 1) * 128],
            in_=pst[0:8, 0:128])))
        tprev = [vv]
        vcp[jj] = vv
    vz = q.step("V", [], lambda e: e.op("memset", prefix8[:, :], 0.0))
    prev = vz
    for ck in range(T // 512):
        deps = [prev, vcp[min(NGT - 1, (ck + 1) * 4 - 1)]]
        v1 = q.step("V", deps, (lambda e, ck=ck: e.op(
            "tensor_tensor", out=pm8[:, :], in0=ohT[0:8, ck * 512:(ck + 1) * 512],
            in1=mcpre8[0:8, ck * 512:(ck + 1) * 512], op=ALU.mult)))
        v2 = q.step("V", [v1], (lambda e: e.op(
            "tensor_reduce", out=pm8[:, 512:513] if False else prefix8[:, 1:2],
            in_=pm8[:, :], axis=AX.X, op=ALU.add)))
        prev = q.step("V", [v2], (lambda e: e.op(
            "tensor_tensor", out=prefix8[:, 0:1], in0=prefix8[:, 0:1],
            in1=prefix8[:, 1:2], op=ALU.add)))
    vmxo = q.step("V", lddep, lambda e: e.op(
        "tensor_reduce", out=mxo[:, :],
        in_=bass.AP(lgown.tensor if hasattr(lgown, "tensor") else lgown, 0,
                    [[NT * 8, 128], [8, NT], [1, 8]]),
        axis=AX.X, op=ALU.max))
    tprev2 = []
    vown = {}
    for j in range(NT):
        voh = q.step("V", [vmxo], (lambda e, j=j: e.op(
            "tensor_scalar", out=ohtile_bf[:, :], in0=lgown[:, j * 8:(j + 1) * 8],
            scalar1=mxo[:, j:j + 1], scalar2=None, op0=ALU.is_equal)))
        pst = psT8a if j % 2 == 0 else psT8b
        tt = q.step("T", [voh] + tprev2, (lambda e, pst=pst: e.transpose(
            pst[0:8, 0:128], ohtile_bf[:, :], ident_bf[:, :])))
        vv = q.step("V", [tt], (lambda e, j=j, pst=pst: e.op(
            "tensor_copy", out=ohTown[0:8, j * 128:(j + 1) * 128],
            in_=pst[0:8, 0:128])))
        tprev2 = [vv]
        vown[j] = vv
    vscan = q.step("V", [prev, vown[NT - 1]], lambda e: e.op(
        "tensor_tensor_scan", out=postokT[0:8, :], data0=ones64r[0:8, 0:TC],
        data1=ohTown[0:8, :], initial=prefix8[:, 0:1], op0=ALU.mult, op1=ALU.add))
    vkr = {}
    for ck in range(TC // 512):
        v1 = q.step("V", [vscan], (lambda e, ck=ck: e.op(
            "tensor_tensor", out=pm8[:, :], in0=postokT[0:8, ck * 512:(ck + 1) * 512],
            in1=ohTown[0:8, ck * 512:(ck + 1) * 512], op=ALU.mult)))
        t1 = q.step("T", [v1], (lambda e: e.matmul(
            psS[0:1, :], ones64r[0:8, 0:1], pm8[0:8, :], start=True, stop=True)))
        v2 = q.step("V", [t1], (lambda e, ck=ck: e.op(
            "tensor_scalar", out=keeprow[0:1, ck * 512:(ck + 1) * 512], in0=psS[0:1, :],
            scalar1=float(cap) + 0.5, scalar2=None, op0=ALU.is_le)))
        vkr[ck] = v2
    vkb = q.step("V", [vkr[k] for k in vkr], lambda e: e.op(
        "tensor_copy", out=keeprow_bf[0:1, :], in_=keeprow[0:1, :]))
    tprev3 = []
    vkc = {}
    for j in range(NT):
        psk = psKa if j % 2 == 0 else psKb
        tt = q.step("T", [vkb] + tprev3, (lambda e, j=j, psk=psk: e.transpose(
            psk[:, 0:1], keeprow_bf[0:1, j * 128:(j + 1) * 128],
            ident_bf[0:1, 0:1])))
        vv = q.step("V", [tt], (lambda e, j=j, psk=psk: e.op(
            "tensor_copy", out=keepcol[:, j:j + 1], in_=psk[:, 0:1])))
        tprev3 = [vv]
        vkc[j] = vv
    vohk = {}
    for j in range(NT):
        vohk[j] = q.step("V", [vkc[j], vmxo], (lambda e, j=j: e.op(
            "tensor_scalar", out=ohk[:, j * 8:(j + 1) * 8], in0=lgown[:, j * 8:(j + 1) * 8],
            scalar1=mxo[:, j:j + 1], scalar2=keepcol[:, j:j + 1],
            op0=ALU.is_equal, op1=ALU.mult)))
    return vohk



def build_l1c():
    if "l1c" in _BUILD_CACHE:
        return _BUILD_CACHE["l1c"]
    NDC = D // 128
    FF = 4096
    NHG = FF // 512
    NHC = FF // 128
    nc = _nc()
    x_d = nc.dram_tensor("x", [TC, D], F32, kind="ExternalInput")
    oT_d = nc.dram_tensor("oTall", [D, TC], F32, kind="ExternalInput")
    wo_d = nc.dram_tensor("wo", [D, D], F32, kind="ExternalInput")
    xpT_d = nc.dram_tensor("xpT", [D, TC], F32, kind="ExternalInput")
    lgall_d = nc.dram_tensor("lgall2d", [128, 128 * 8], F32, kind="ExternalInput")
    lgown_d = nc.dram_tensor("lgown2d", [128, NT * 8], F32, kind="ExternalInput")
    mcpre_d = nc.dram_tensor("mcpre", [T], F32, kind="ExternalInput")
    w1_d = nc.dram_tensor("w1", [8, D, FF], F32, kind="ExternalInput")
    b1_d = nc.dram_tensor("b1", [8, FF], F32, kind="ExternalInput")
    g1_d = nc.dram_tensor("g1", [8, FF], F32, kind="ExternalInput")
    bb1_d = nc.dram_tensor("bb1", [8, FF], F32, kind="ExternalInput")
    w2_d = nc.dram_tensor("w2", [8, FF, D], F32, kind="ExternalInput")
    b2_d = nc.dram_tensor("b2", [8, D], F32, kind="ExternalInput")
    lng_d = nc.dram_tensor("lng", [D], F32, kind="ExternalInput")
    lnb_d = nc.dram_tensor("lnb", [D], F32, kind="ExternalInput")
    x2_d = nc.dram_tensor("x2", [TC, D], F32, kind="ExternalOutput")

    es, sb, sem, VE, SC, GP = _mk_ctx(nc)
    xp_bf = sb("xp_bf", [128, NDC * TC], BF16)
    acc = sb("acc", [128, NT * D], BF16)
    lgall = sb("lgall", [128, 128 * 8])
    lgown = sb("lgown", [128, NT * 8])
    mxg = sb("mxg", [128, 128])
    mxo = sb("mxo", [128, NT])
    keepcol = sb("keepcol", [128, NT])
    ohk = sb("ohk", [128, NT * 8])
    ohtile_bf = sb("ohtile_bf", [128, 8], BF16)
    ident_bf = sb("ident_bf", [128, 128], BF16)
    eps_t = sb("eps_t", [128, 1])
    prefix8 = sb("prefix8", [8, 2])
    pm8 = sb("pm8", [8, 512])

    ld = sem("ld")
    ldg = sem("ldg")
    lwe = sem("lwe")
    lbc = sem("lbc")
    lht = sem("lht")
    lhtb = sem("lhtb")
    lto = sem("lto")
    lx = [sem("lx0"), sem("lx1")]
    sst = [sem("sst0"), sem("sst1")]
    spp = sem("spp")
    pe_count = [0]
    dma_counts = {}

    with nc.Block() as blk:
        @blk.sync
        def _(e):
            e.dma_start(out=lgall[:, :], in_=lgall_d.ap()).then_inc(ld, 16)
            e.dma_start(out=lgown[:, :], in_=lgown_d.ap()).then_inc(ld, 16)

        @blk.gpsimd
        def _(e):
            _build_ident(GP, ident_bf)
            GP.op("memset", eps_t[:, :], 1e-5)
            GP.op("memset", acc[:, :], 0.0)
            e.dma_start(out=xp_bf[:, :],
                        in_=bass.AP(xpT_d, 0, [[TC, 128], [128 * TC, NDC], [1, TC]])
                        ).then_inc(ldg, 16)

    with (nc.sbuf_tensor("ohT", [8, T], BF16) as ohT,
          nc.sbuf_tensor("ohTown", [8, TC], BF16) as ohTown,
          nc.sbuf_tensor("postokT", [8, TC], F32) as postokT,
          nc.sbuf_tensor("mcpre8", [8, T], F32) as mcpre8,
          nc.sbuf_tensor("keeprow", [1, TC], F32) as keeprow,
          nc.sbuf_tensor("keeprow_bf", [1, TC], BF16) as keeprow_bf,
          nc.sbuf_tensor("ones64r", [8, TC], F32) as ones64r,
          nc.psum_tensor("psT8a", [8, 128], BF16) as psT8a,
          nc.psum_tensor("psT8b", [8, 128], BF16) as psT8b,
          nc.psum_tensor("psS", [1, 512], F32) as psS,
          nc.psum_tensor("psKa", [128, 1], BF16) as psKa,
          nc.psum_tensor("psKb", [128, 1], BF16) as psKb):
        with nc.Block() as blk:
            @blk.sync
            def _(e):
                e.dma_start(out=mcpre8[0:8, :],
                            in_=bass.AP(mcpre_d, 0, [[0, 8], [1, T]])).then_inc(ld, 16)

            @blk.gpsimd
            def _(e):
                GP.op("memset", ones64r[:, :], 1.0)

        with nc.Block() as blk:
            q = Seq(nc, VE, SC, GP, spp, pe_count, dma_counts)
            _emit_top1_routing(nc, q, VE, SC, GP, spp, NT, 4096.0,
                               lgall, lgown, mcpre8, ohT, ohTown, postokT, pm8,
                               mxg, mxo, prefix8, keeprow, keeprow_bf, keepcol, ohk,
                               ohtile_bf, ones64r, (psT8a, psT8b), psS, (psKa, psKb), ident_bf,
                               [(ld, 48)])
            q.emit(blk)

    hT_scr = nc.dram_tensor("hT_scr", [NT, NHC * 128 * 128], BF16, kind="Internal")
    with (nc.sbuf_tensor("wbuf", [128, NDC * FF], BF16) as wbuf,
          nc.sbuf_tensor("b1b", [128, FF], BF16) as b1b,
          nc.sbuf_tensor("g1b", [128, FF], BF16) as g1b,
          nc.sbuf_tensor("bb1b", [128, FF], BF16) as bb1b,
          nc.sbuf_tensor("b2b", [128, D], BF16) as b2b,
          nc.sbuf_tensor("h_bf", [128, FF], BF16) as h_bf,
          nc.sbuf_tensor("hT_j", [128, NHC * 128], BF16) as hT_j,
          nc.sbuf_tensor("sqc", [128, 512], F32) as sqc,
          nc.sbuf_tensor("lnsc", [128, 16], F32) as lnsc,
          nc.psum_tensor("h_ps0", [128, 512], F32) as hps0,
          nc.psum_tensor("h_ps1", [128, 512], F32) as hps1,
          nc.psum_tensor("trhA", [128, 128], BF16) as trhA,
          nc.psum_tensor("trhB", [128, 128], BF16) as trhB,
          nc.psum_tensor("y_ps", [128, D], F32) as y_ps):
        h_ps = [hps0, hps1]
        assert NDC * FF == NHC * D
        for eix in range(8):
            with nc.Block() as blk:
                @blk.gpsimd
                def _(e, eix=eix):
                    e.dma_start(out=wbuf[:, :],
                                in_=bass.AP(w1_d, eix * D * FF, [[FF, 128], [128 * FF, NDC], [1, FF]])
                                ).then_inc(lwe, 16)
                    e.dma_start(out=b1b[:, :], in_=_bcast_row_ap(b1_d, FF, eix * FF)).then_inc(lbc, 16)
                    e.dma_start(out=g1b[:, :], in_=_bcast_row_ap(g1_d, FF, eix * FF)).then_inc(lbc, 16)
                    e.dma_start(out=bb1b[:, :], in_=_bcast_row_ap(bb1_d, FF, eix * FF)).then_inc(lbc, 16)

            for j in range(NT):
                with nc.Block() as blk:
                    q = Seq(nc, VE, SC, GP, spp, pe_count, dma_counts)
                    wdep = [(lwe, 32 * eix + 16), (lbc, (64 * eix) + 48), (ldg, 16)]
                    vh = {}
                    for hg in range(NHG):
                        pp = hg % 2
                        deps = list(wdep)
                        if hg >= 2:
                            deps.append(vh[hg - 2])

                        def mk_h(e, hg=hg, pp=pp, j=j):
                            last = None
                            for dc in range(NDC):
                                last = e.matmul(h_ps[pp][:, :],
                                                xp_bf[:, dc * TC + j * 128:dc * TC + (j + 1) * 128],
                                                wbuf[:, dc * FF + hg * 512:dc * FF + (hg + 1) * 512],
                                                start=(dc == 0), stop=(dc == NDC - 1))
                            return last
                        th = q.step("T", deps, mk_h)
                        vh[hg] = q.step("V", [th], (lambda e, hg=hg, pp=pp: e.op(
                            "tensor_tensor", out=h_bf[:, hg * 512:(hg + 1) * 512],
                            in0=h_ps[pp][:, :], in1=b1b[:, hg * 512:(hg + 1) * 512], op=ALU.add)))
                    hdeps = [vh[hg] for hg in range(NHG)]
                    v_m = q.step("V", hdeps, lambda e: e.op(
                        "reduce_sum", out=lnsc[:, 0:1], in_=h_bf[:, :], axis=AX.X))
                    s_m = q.step("S", [v_m], lambda e: e.op("mul", lnsc[:, 0:1], lnsc[:, 0:1], 1.0 / FF))
                    vv = []
                    prev = s_m
                    for hg in range(NHG):
                        v1 = q.step("V", [prev], (lambda e, hg=hg: e.op(
                            "tensor_scalar", out=sqc[:, :], in0=h_bf[:, hg * 512:(hg + 1) * 512],
                            scalar1=lnsc[:, 0:1], scalar2=None, op0=ALU.subtract)))
                        v2 = q.step("V", [v1], lambda e: e.op(
                            "tensor_tensor", out=sqc[:, :], in0=sqc[:, :], in1=sqc[:, :], op=ALU.mult))
                        v3 = q.step("V", [v2], (lambda e, hg=hg: e.op(
                            "reduce_sum", out=lnsc[:, 4 + hg:5 + hg], in_=sqc[:, :], axis=AX.X)))
                        prev = v3
                        vv.append(v3)
                    v_v = q.step("V", vv, lambda e: e.op(
                        "reduce_sum", out=lnsc[:, 1:2], in_=lnsc[:, 4:12], axis=AX.X))
                    s_sd = q.step("S", [v_v], lambda e: e.op(
                        "activation", out=lnsc[:, 2:3], in_=lnsc[:, 1:2], func=AF.Sqrt,
                        bias=eps_t[:, 0:1], scale=1.0 / FF))
                    v_rs = q.step("V", [s_sd], lambda e: e.op(
                        "reciprocal", out=lnsc[:, 3:4], in_=lnsc[:, 2:3]))
                    v_n1 = q.step("V", [v_rs], lambda e: e.op(
                        "tensor_scalar", out=h_bf[:, :], in0=h_bf[:, :],
                        scalar1=lnsc[:, 0:1], scalar2=lnsc[:, 3:4],
                        op0=ALU.subtract, op1=ALU.mult))
                    v_n2 = q.step("V", [v_n1], lambda e: e.op(
                        "tensor_tensor", out=h_bf[:, :], in0=h_bf[:, :], in1=g1b[:, :], op=ALU.mult))
                    v_n3 = q.step("V", [v_n2], lambda e: e.op(
                        "tensor_tensor", out=h_bf[:, :], in0=h_bf[:, :], in1=bb1b[:, :], op=ALU.add))
                    s_gel = q.step("S", [v_n3], lambda e: e.op(
                        "activation", out=h_bf[:, :], in_=h_bf[:, :], func=AF.Gelu))
                    vt = {}
                    nstores = eix * NT + j
                    stdep = [(lht, 16 * nstores)] if nstores >= 1 else []
                    for hc in range(NHC):
                        trh = trhA if hc % 2 == 0 else trhB
                        deps = [s_gel]
                        if hc >= 2:
                            deps.append(vt[hc - 2])
                        tt = q.step("T", deps, (lambda e, hc=hc, trh=trh: e.transpose(
                            trh[:, 0:128], h_bf[:, hc * 128:(hc + 1) * 128],
                            ident_bf[:, :])))
                        vt[hc] = q.step("V", [tt] + (stdep if hc < 2 else []),
                                        (lambda e, hc=hc, trh=trh: e.op(
                            "tensor_copy", out=hT_j[:, hc * 128:(hc + 1) * 128],
                            in_=trh[:, 0:128])))
                    q.step("GD", [vt[hc] for hc in range(NHC)],
                           (lambda e, j=j: e.dma_start(
                               out=bass.AP(hT_scr, j * NHC * 128 * 128,
                                           [[NHC * 128, 128], [1, NHC * 128]]),
                               in_=hT_j[:, :])), dma_sem=lht)
                    q.emit(blk)

            with nc.Block() as blk:
                @blk.gpsimd
                def _(e, eix=eix):
                    e.wait_ge(lht, dma_counts[id(lht)])
                    e.dma_start(out=wbuf[:, :],
                                in_=bass.AP(w2_d, eix * FF * D, [[D, 128], [128 * D, NHC], [1, D]])
                                ).then_inc(lwe, 16)
                    e.dma_start(out=b2b[:, :], in_=_bcast_row_ap(b2_d, D, eix * D)).then_inc(lbc, 16)

            for j in range(NT):
                with nc.Block() as blk:
                    q = Seq(nc, VE, SC, GP, spp, pe_count, dma_counts)
                    dh = q.step("GD", [], (lambda e, j=j: e.dma_start(
                        out=hT_j[:, :],
                        in_=bass.AP(hT_scr, j * NHC * 128 * 128,
                                    [[NHC * 128, 128], [1, NHC * 128]]))), dma_sem=lhtb)
                    wdep = [(lwe, 32 * (eix + 1)), (lbc, 64 * eix + 64), dh]

                    def mk_y(e, j=j):
                        last = None
                        for hc in range(NHC):
                            e.matmul(y_ps[:, 0:512], hT_j[:, hc * 128:(hc + 1) * 128],
                                     wbuf[:, hc * D:hc * D + 512],
                                     start=(hc == 0), stop=(hc == NHC - 1))
                            last = e.matmul(y_ps[:, 512:1024], hT_j[:, hc * 128:(hc + 1) * 128],
                                            wbuf[:, hc * D + 512:(hc + 1) * D],
                                            start=(hc == 0), stop=(hc == NHC - 1))
                        return last
                    t_y = q.step("T", wdep, mk_y)
                    v_a1 = q.step("V", [t_y], (lambda e, j=j, eix=eix: e.op(
                        "scalar_tensor_tensor", out=acc[:, j * D:(j + 1) * D],
                        in0=y_ps[:, :], scalar=ohk[:, j * 8 + eix:j * 8 + eix + 1],
                        in1=acc[:, j * D:(j + 1) * D], op0=ALU.mult, op1=ALU.add)))
                    q.step("V", [v_a1], (lambda e, j=j, eix=eix: e.op(
                        "scalar_tensor_tensor", out=acc[:, j * D:(j + 1) * D],
                        in0=b2b[:, :], scalar=ohk[:, j * 8 + eix:j * 8 + eix + 1],
                        in1=acc[:, j * D:(j + 1) * D], op0=ALU.mult, op1=ALU.add)))
                    q.emit(blk)

    with (nc.sbuf_tensor("oT_bf", [128, NDC * TC], BF16) as oT_bf,
          nc.sbuf_tensor("wo_sb", [128, NDC * D], BF16) as wo_sb,
          nc.sbuf_tensor("lngb", [128, D], F32) as lngb,
          nc.sbuf_tensor("lnbb", [128, D], F32) as lnbb,
          nc.sbuf_tensor("fin", [128, D], F32) as fin,
          nc.sbuf_tensor("gsil", [128, D], F32) as gsil,
          nc.sbuf_tensor("sq2", [128, D], F32) as sq2,
          nc.sbuf_tensor("x1t0", [128, D], F32) as x1t0,
          nc.sbuf_tensor("x1t1", [128, D], F32) as x1t1,
          nc.sbuf_tensor("lns2", [128, 8], F32) as lns2,
          nc.psum_tensor("s_ps", [128, D], F32) as s_ps):
        x1t = [x1t0, x1t1]
        with nc.Block() as blk:
            @blk.gpsimd
            def _(e):
                e.dma_start(out=oT_bf[:, :],
                            in_=bass.AP(oT_d, 0, [[TC, 128], [128 * TC, NDC], [1, TC]])
                            ).then_inc(lto, 16)
                e.dma_start(out=wo_sb[:, :],
                            in_=bass.AP(wo_d, 0, [[D, 128], [128 * D, NDC], [1, D]])
                            ).then_inc(lto, 16)
                e.dma_start(out=lngb[:, :], in_=_bcast_row_ap(lng_d, D)).then_inc(lto, 16)
                e.dma_start(out=lnbb[:, :], in_=_bcast_row_ap(lnb_d, D)).then_inc(lto, 16)

        with nc.Block() as blk:
            q = Seq(nc, VE, SC, GP, spp, pe_count, dma_counts)
            ldp = [(lto, 64)]
            prev_out = {}
            st_steps = {}
            for j in range(NT):
                pp = j % 2
                xdeps = []
                if j >= 2:
                    xdeps = [prev_out[j - 2], st_steps[j - 2]]
                dx = q.step("SD", xdeps, (lambda e, j=j, pp=pp: e.dma_start(
                    out=x1t[pp][:, :], in_=_ap3_tokmaj(x_d, 1, D, offset=j * 128 * D))),
                    dma_sem=lx[pp])
                v_m = q.step("V", [], (lambda e, j=j: e.op(
                    "reduce_sum", out=lns2[:, 0:1], in_=acc[:, j * D:(j + 1) * D], axis=AX.X)))
                s_m = q.step("S", [v_m], lambda e: e.op("mul", lns2[:, 0:1], lns2[:, 0:1], 1.0 / D))
                v_sq = q.step("V", [s_m], (lambda e, j=j: e.op(
                    "tensor_scalar", out=sq2[:, :], in0=acc[:, j * D:(j + 1) * D],
                    scalar1=lns2[:, 0:1], scalar2=None, op0=ALU.subtract)))
                v_sq2 = q.step("V", [v_sq], lambda e: e.op(
                    "tensor_tensor", out=sq2[:, :], in0=sq2[:, :], in1=sq2[:, :], op=ALU.mult))
                v_v = q.step("V", [v_sq2], lambda e: e.op(
                    "reduce_sum", out=lns2[:, 1:2], in_=sq2[:, :], axis=AX.X))
                s_sd = q.step("S", [v_v], lambda e: e.op(
                    "activation", out=lns2[:, 2:3], in_=lns2[:, 1:2], func=AF.Sqrt,
                    bias=eps_t[:, 0:1], scale=1.0 / D))
                v_rs = q.step("V", [s_sd], lambda e: e.op(
                    "reciprocal", out=lns2[:, 3:4], in_=lns2[:, 2:3]))
                v_f1 = q.step("V", [v_rs], (lambda e, j=j: e.op(
                    "tensor_scalar", out=fin[:, :], in0=acc[:, j * D:(j + 1) * D],
                    scalar1=lns2[:, 0:1], scalar2=lns2[:, 3:4],
                    op0=ALU.subtract, op1=ALU.mult)))
                v_f2 = q.step("V", [v_f1] + ldp, lambda e: e.op(
                    "tensor_tensor", out=fin[:, :], in0=fin[:, :], in1=lngb[:, :], op=ALU.mult))
                v_f3 = q.step("V", [v_f2], lambda e: e.op(
                    "tensor_tensor", out=fin[:, :], in0=fin[:, :], in1=lnbb[:, :], op=ALU.add))
                s_sil = q.step("S", [v_f3], lambda e: e.op(
                    "activation", out=gsil[:, :], in_=fin[:, :], func=AF.Silu))

                def mk_s(e, j=j):
                    last = None
                    for hv in range(NDC):
                        e.matmul(s_ps[:, 0:512], oT_bf[:, hv * TC + j * 128:hv * TC + (j + 1) * 128],
                                 wo_sb[:, hv * D:hv * D + 512],
                                 start=(hv == 0), stop=(hv == NDC - 1))
                        last = e.matmul(s_ps[:, 512:1024],
                                        oT_bf[:, hv * TC + j * 128:hv * TC + (j + 1) * 128],
                                        wo_sb[:, hv * D + 512:(hv + 1) * D],
                                        start=(hv == 0), stop=(hv == NDC - 1))
                    return last
                sdeps = ldp + ([prev_out[j - 1]] if j >= 1 else [])
                t_s = q.step("T", sdeps, mk_s)
                v_sg = q.step("V", [t_s, s_sil], lambda e: e.op(
                    "tensor_tensor", out=gsil[:, :], in0=gsil[:, :], in1=s_ps[:, :], op=ALU.mult))
                v_o = q.step("V", [v_sg, (lx[pp], dma_counts.get(id(lx[pp]), 0) + 16) if False else dx],
                             (lambda e, pp=pp: e.op(
                                 "tensor_tensor", out=x1t[pp][:, :], in0=x1t[pp][:, :],
                                 in1=gsil[:, :], op=ALU.add)))
                prev_out[j] = v_o
                st_steps[j] = q.step("GD", [v_o], (lambda e, j=j, pp=pp: e.dma_start(
                    out=_ap3_tokmaj(x2_d, 1, D, offset=j * 128 * D), in_=x1t[pp][:, :])),
                    dma_sem=sst[pp])
            q.emit(blk)

    with nc.Block() as blk:
        @blk.gpsimd
        def _(e):
            e.wait_ge(sst[0], dma_counts[id(sst[0])])
            e.wait_ge(sst[1], dma_counts[id(sst[1])])

    es.close()
    _BUILD_CACHE["l1c"] = nc
    return nc






def _subview3(ap_or_h, nblk):
    t = ap_or_h.tensor if isinstance(ap_or_h, bass.AP) else ap_or_h
    off = ap_or_h.offset if isinstance(ap_or_h, bass.AP) else 0
    pstep = t.shape[1]
    return bass.AP(t, off, [[pstep, 128], [16, nblk], [1, 16]])


def _subbc(ap_or_h, rel_off, nblk):
    t = ap_or_h.tensor if isinstance(ap_or_h, bass.AP) else ap_or_h
    off = (ap_or_h.offset if isinstance(ap_or_h, bass.AP) else 0) + rel_off
    pstep = t.shape[1]
    return bass.AP(t, off, [[pstep, 128], [1, nblk], [0, 16]])


def _emit_fp4_quant(q, src, dst, scl, sgn, accq, cqt, nblk, dwid, deps, accq_w=None, scl_w=None):
    if accq_w is None:
        accq_w = accq.shape[1]
    if scl_w is None:
        scl_w = scl.shape[1]
    GRIDW = [0.5, 0.5, 0.5, 0.5, 1.0, 1.0, 2.0]
    MID = [0.25, 0.75, 1.25, 1.75, 2.5, 3.5, 5.0]
    v_ab = q.step("S", deps, lambda e: e.op("activation", out=accq[:, 0:dwid], in_=src, func=AF.Abs))
    v_sg = q.step("S", deps, lambda e: e.op("activation", out=sgn[:, 0:dwid], in_=src, func=AF.Sign))
    v_mx = q.step("V", [v_ab], (lambda e: e.op(
        "tensor_reduce", out=scl[:, 0:nblk],
        in_=_subview3(accq, nblk),
        axis=AX.X, op=ALU.max)))
    v_sc = q.step("V", [v_mx], (lambda e: e.op(
        "tensor_scalar", out=scl[:, 0:nblk], in0=scl[:, 0:nblk], scalar1=1.0 / 6.0,
        scalar2=1e-12, op0=ALU.mult, op1=ALU.max)))
    vacc = v_sc
    for ji in range(7):
        v_sm = q.step("V", [vacc], (lambda e, ji=ji: e.op(
            "tensor_scalar", out=scl[:, nblk:2 * nblk], in0=scl[:, 0:nblk],
            scalar1=MID[ji], scalar2=None, op0=ALU.mult)))
        scl_bc = _subbc(scl, nblk, nblk)
        v_c = q.step("V", [v_sm], (lambda e, scl_bc=scl_bc: e.op(
            "tensor_tensor", out=cqt[:, 0:dwid], in0=accq[:, 0:dwid], in1=scl_bc, op=ALU.is_gt)))
        if ji == 0:
            vacc = q.step("V", [v_c], (lambda e, ji=ji: e.op(
                "tensor_scalar", out=dst, in0=cqt[:, 0:dwid], scalar1=GRIDW[ji],
                scalar2=None, op0=ALU.mult)))
        else:
            v_c2 = q.step("V", [v_c], (lambda e, ji=ji: e.op(
                "tensor_scalar", out=cqt[:, 0:dwid], in0=cqt[:, 0:dwid], scalar1=GRIDW[ji],
                scalar2=None, op0=ALU.mult)))
            vacc = q.step("V", [v_c2], (lambda e: e.op(
                "tensor_tensor", out=dst, in0=dst, in1=cqt[:, 0:dwid], op=ALU.add)))
    scl_bc0 = _subbc(scl, 0, nblk)
    v1 = q.step("V", [vacc], (lambda e, scl_bc0=scl_bc0: e.op(
        "tensor_tensor", out=dst, in0=dst, in1=scl_bc0, op=ALU.mult)))
    v2 = q.step("V", [v1, v_sg], lambda e: e.op(
        "tensor_tensor", out=dst, in0=dst, in1=sgn[:, 0:dwid], op=ALU.mult))
    return v2


def build_l2a():
    if "l2a" in _BUILD_CACHE:
        return _BUILD_CACHE["l2a"]
    NTH = NT + 1
    TCH = NTH * 128
    DI = 2048
    nc = _nc()
    x_d = nc.dram_tensor("xh", [TCH, D], F32, kind="ExternalInput")
    nw_d = nc.dram_tensor("nw", [D], F32, kind="ExternalInput")
    nb_d = nc.dram_tensor("nb", [D], F32, kind="ExternalInput")
    ipT_d = nc.dram_tensor("ipT", [D, 4096], F32, kind="ExternalInput")
    cw_d = nc.dram_tensor("cw2d", [128, 16 * 4], F32, kind="ExternalInput")
    cb_d = nc.dram_tensor("cb2d", [128, 16], F32, kind="ExternalInput")
    xpw_d = nc.dram_tensor("xpwT", [DI, 33], F32, kind="ExternalInput")
    hm_d = nc.dram_tensor("halom", [1], F32, kind="ExternalInput")
    wgs_d = nc.dram_tensor("wgs", [384, D], F32, kind="ExternalInput")
    wus_d = nc.dram_tensor("wus", [384, D], F32, kind="ExternalInput")
    wds_d = nc.dram_tensor("wds", [128, 2816], F32, kind="ExternalInput")
    z_d = nc.dram_tensor("zT", [DI, TC], BF16, kind="ExternalOutput")
    xc_d = nc.dram_tensor("xcT", [DI, TC], BF16, kind="ExternalOutput")
    xp_d = nc.dram_tensor("xp33", [33, TC], F32, kind="ExternalOutput")
    wgq_d = nc.dram_tensor("wgq", [384, D], BF16, kind="ExternalOutput")
    wuq_d = nc.dram_tensor("wuq", [384, D], BF16, kind="ExternalOutput")
    wdq_d = nc.dram_tensor("wdq", [128, 2816], BF16, kind="ExternalOutput")

    es, sb, sem, VE, SC, GP = _mk_ctx(nc)
    x_sb = sb("x_sb", [128, NTH * D])
    xn2T = sb("xn2T", [128, 8 * TCH], BF16)
    xiF = sb("xiF", [128, 16 * TCH], BF16)
    xt1 = sb("xt1", [128, D])
    xt2 = sb("xt2", [128, D], BF16)
    sqa = sb("sqa", [128, D])
    lnt = sb("lnt", [128, 8])
    nwb = sb("nwb", [128, D])
    nbb = sb("nbb", [128, D])
    wch = [sb(f"wch{p}", [128, 8 * 256], BF16) for p in range(2)]
    zstage = sb("zstage", [128, 512], BF16)
    cw_sb = sb("cw_sb", [128, 64])
    cb_sb = sb("cb_sb", [128, 16])
    hm_sb = sb("hm_sb", [128, 1])
    xpw_sb = sb("xpw_sb", [128, 16 * 33], BF16)
    xcw = sb("xcw", [128, TC])
    eps_t = sb("eps_t", [128, 1])
    ident_bf = sb("ident_bf", [128, 128], BF16)
    qdst = None
    qsrc = x_sb[:, 0:2816]
    qscl = x_sb[:, 2816:2816 + 352]
    qsgn = x_sb[:, 4096:4096 + 2816]
    qabs = x_sb[:, 8192:8192 + 2816]
    cqt = x_sb[:, 12288:12288 + 2816]
    qdst = xiF[:, 0:2816]

    ld = sem("ld")
    ldg = sem("ldg")
    lw = [sem("lw0"), sem("lw1")]
    st = [sem("st0"), sem("st1")]
    stq = sem("stq")
    lq = sem("lq")
    spp = sem("spp")
    pe_count = [0]
    dma_counts = {}

    with nc.Block() as blk:
        @blk.sync
        def _(e):
            e.dma_start(out=x_sb[:, :], in_=_ap3_tokmaj(x_d, NTH, D)).then_inc(ld, 16)
            e.dma_start(out=nwb[:, :], in_=_bcast_row_ap(nw_d, D)).then_inc(ld, 16)
            e.dma_start(out=nbb[:, :], in_=_bcast_row_ap(nb_d, D)).then_inc(ld, 16)
            e.dma_start(out=cw_sb[:, :], in_=cw_d.ap()).then_inc(ld, 16)
            e.dma_start(out=cb_sb[:, :], in_=cb_d.ap()).then_inc(ld, 16)
            e.dma_start(out=hm_sb[:, :], in_=_bcast_row_ap(hm_d, 1)).then_inc(ld, 16)

        @blk.gpsimd
        def _(e):
            _build_ident(GP, ident_bf)
            GP.op("memset", eps_t[:, :], 1e-5)
            e.dma_start(out=xpw_sb[:, :],
                        in_=bass.AP(xpw_d, 0, [[33, 128], [128 * 33, 16], [1, 33]])).then_inc(ldg, 16)

    with nc.psum_tensor("tr_ps", [128, D], BF16) as tr_ps:
        with nc.Block() as blk:
            q = Seq(nc, VE, SC, GP, spp, pe_count, dma_counts)
            lddep = [(ld, 96)]
            tprev = []
            for j in range(NTH):
                xt = x_sb[:, j * D:(j + 1) * D]
                v1 = q.step("V", lddep, (lambda e, xt=xt: e.op(
                    "reduce_sum", out=lnt[:, 0:1], in_=xt, axis=AX.X)))
                s1 = q.step("S", [v1], lambda e: e.op("mul", lnt[:, 0:1], lnt[:, 0:1], 1.0 / D))
                v2 = q.step("V", [s1], (lambda e, xt=xt: e.op(
                    "tensor_scalar", out=sqa[:, :], in0=xt, scalar1=lnt[:, 0:1],
                    scalar2=None, op0=ALU.subtract)))
                v3 = q.step("V", [v2], lambda e: e.op(
                    "tensor_tensor", out=sqa[:, :], in0=sqa[:, :], in1=sqa[:, :], op=ALU.mult))
                v4 = q.step("V", [v3], lambda e: e.op(
                    "reduce_sum", out=lnt[:, 1:2], in_=sqa[:, :], axis=AX.X))
                s2 = q.step("S", [v4], lambda e: e.op(
                    "activation", out=lnt[:, 2:3], in_=lnt[:, 1:2], func=AF.Sqrt,
                    bias=eps_t[:, 0:1], scale=1.0 / D))
                v5 = q.step("V", [s2], lambda e: e.op(
                    "reciprocal", out=lnt[:, 3:4], in_=lnt[:, 2:3]))
                v6 = q.step("V", [v5], (lambda e, xt=xt: e.op(
                    "tensor_scalar", out=xt1[:, :], in0=xt, scalar1=lnt[:, 0:1],
                    scalar2=lnt[:, 3:4], op0=ALU.subtract, op1=ALU.mult)))
                v7 = q.step("V", [v6], lambda e: e.op(
                    "reduce_sum", out=lnt[:, 4:5], in_=xt1[:, :], axis=AX.X))
                s3 = q.step("S", [v7], lambda e: e.op("mul", lnt[:, 4:5], lnt[:, 4:5], 1.0 / D))
                v8 = q.step("V", [s3], lambda e: e.op(
                    "tensor_scalar", out=sqa[:, :], in0=xt1[:, :], scalar1=lnt[:, 4:5],
                    scalar2=None, op0=ALU.subtract))
                v9 = q.step("V", [v8], lambda e: e.op(
                    "tensor_tensor", out=sqa[:, :], in0=sqa[:, :], in1=sqa[:, :], op=ALU.mult))
                v10 = q.step("V", [v9], lambda e: e.op(
                    "reduce_sum", out=lnt[:, 5:6], in_=sqa[:, :], axis=AX.X))
                s4 = q.step("S", [v10], lambda e: e.op(
                    "activation", out=lnt[:, 6:7], in_=lnt[:, 5:6], func=AF.Sqrt,
                    bias=eps_t[:, 0:1], scale=1.0 / D))
                v11 = q.step("V", [s4], lambda e: e.op(
                    "reciprocal", out=lnt[:, 7:8], in_=lnt[:, 6:7]))
                v12 = q.step("V", [v11], lambda e: e.op(
                    "tensor_scalar", out=xt1[:, :], in0=xt1[:, :], scalar1=lnt[:, 4:5],
                    scalar2=lnt[:, 7:8], op0=ALU.subtract, op1=ALU.mult))
                v13 = q.step("V", [v12], lambda e: e.op(
                    "tensor_tensor", out=xt1[:, :], in0=xt1[:, :], in1=nwb[:, :], op=ALU.mult))
                v14 = q.step("V", [v13], lambda e: e.op(
                    "tensor_tensor", out=xt2[:, :], in0=xt1[:, :], in1=nbb[:, :], op=ALU.add))

                def mktr(e):
                    last = None
                    for k in range(8):
                        last = e.transpose(tr_ps[:, k * 128:(k + 1) * 128],
                                           xt2[:, k * 128:(k + 1) * 128], ident_bf[:, :])
                    return last
                tt = q.step("T", [v14] + tprev, mktr)
                vc = q.step("V", [tt], (lambda e, j=j: e.op(
                    "tensor_copy",
                    out=bass.AP(xn2T, j * 128, [[8 * TCH, 128], [TCH, 8], [1, 128]]),
                    in_=tr_ps[:, :])))
                tprev = [vc]
            q.emit(blk)

    tgl = [(0, 512), (512, 512), (1024, 512), (1536, 512), (2048, 128)]
    with (nc.psum_tensor("pj0", [128, 512], F32) as pj0,
          nc.psum_tensor("pj1", [128, 512], F32) as pj1):
        pj = [pj0, pj1]
        with nc.Block() as blk:
            q = Seq(nc, VE, SC, GP, spp, pe_count, dma_counts)
            nseq = 0
            vstg = {}
            lastv_of_cg = {}
            last_zst = []
            for cg in range(16):
                p = cg % 2
                deps = [lastv_of_cg[cg - 2]] if cg >= 2 else []
                dwl = q.step("GD", deps, (lambda e, cg=cg, p=p: e.dma_start(
                    out=wch[p][:, :],
                    in_=bass.AP(ipT_d, cg * 256, [[4096, 128], [128 * 4096, 8], [1, 256]]))),
                    dma_sem=lw[p])
                for sg in range(2):
                    pc = cg * 2 + sg
                    is_z = pc >= 16
                    for (t0, tl) in tgl:
                        pp = nseq % 2
                        deps2 = [dwl]
                        if nseq >= 2:
                            deps2.append(vstg[nseq - 2])

                        def mk(e, sg=sg, t0=t0, tl=tl, pp=pp, p=p):
                            last = None
                            for k in range(8):
                                last = e.matmul(pj[pp][:, 0:tl],
                                                wch[p][:, k * 256 + sg * 128:k * 256 + (sg + 1) * 128],
                                                xn2T[:, k * TCH + t0:k * TCH + t0 + tl],
                                                start=(k == 0), stop=(k == 7))
                            return last
                        t_m = q.step("T", deps2, mk)
                        if is_z:
                            v_c = q.step("V", [t_m] + last_zst, (lambda e, tl=tl, pp=pp: e.op(
                                "tensor_copy", out=zstage[:, 0:tl], in_=pj[pp][:, 0:tl])))
                            lo = max(t0, 128)
                            ln_ = t0 + tl - lo
                            zst = q.step("GD", [v_c], (lambda e, pc=pc, t0=t0, lo=lo, ln_=ln_: e.dma_start(
                                out=bass.AP(z_d, (pc - 16) * 128 * TC + (lo - 128),
                                            [[TC, 128], [1, ln_]]),
                                in_=zstage[:, lo - t0:lo - t0 + ln_])), dma_sem=st[0])
                            last_zst = [zst]
                        else:
                            v_c = q.step("V", [t_m], (lambda e, pc=pc, t0=t0, tl=tl, pp=pp: e.op(
                                "tensor_copy",
                                out=xiF[:, pc * TCH + t0:pc * TCH + t0 + tl],
                                in_=pj[pp][:, 0:tl])))
                        vstg[nseq] = v_c
                        nseq += 1
                lastv_of_cg[cg] = vstg[nseq - 1]
            q.emit(blk)

    vconv = {}
    xp_ps_ctx = nc.psum_tensor("xp_ps", [33, 512], F32)
    xp_ps = xp_ps_ctx.__enter__()
    with nc.Block() as blk:
        q = Seq(nc, VE, SC, GP, spp, pe_count, dma_counts)
        for ch in range(16):
            v0 = q.step("V", [], (lambda e, ch=ch: e.op(
                "tensor_scalar", out=xiF[:, ch * TCH:ch * TCH + 128],
                in0=xiF[:, ch * TCH:ch * TCH + 128],
                scalar1=hm_sb[:, 0:1], scalar2=None, op0=ALU.mult)))
            v1 = q.step("V", [v0], (lambda e, ch=ch: e.op(
                "tensor_scalar", out=xcw[:, :],
                in0=xiF[:, ch * TCH + 125:ch * TCH + 125 + TC],
                scalar1=cw_sb[:, ch * 4:ch * 4 + 1], scalar2=None, op0=ALU.mult)))
            for jj in range(1, 4):
                v1 = q.step("V", [v1], (lambda e, ch=ch, jj=jj: e.op(
                    "scalar_tensor_tensor", out=xcw[:, :],
                    in0=xiF[:, ch * TCH + 125 + jj:ch * TCH + 125 + jj + TC],
                    scalar=cw_sb[:, ch * 4 + jj:ch * 4 + jj + 1],
                    in1=xcw[:, :], op0=ALU.mult, op1=ALU.add)))
            s1 = q.step("S", [v1], (lambda e, ch=ch: e.op(
                "activation", out=xcw[:, :], in_=xcw[:, :], func=AF.Silu,
                bias=cb_sb[:, ch:ch + 1])))
            v2 = q.step("V", [s1], (lambda e, ch=ch: e.op(
                "tensor_copy", out=xiF[:, ch * TCH:ch * TCH + TC], in_=xcw[:, :])))
            vconv[ch] = v2
            q.step("GD", [v2], (lambda e, ch=ch: e.dma_start(
                out=bass.AP(xc_d, ch * 128 * TC, [[TC, 128], [1, TC]]),
                in_=xiF[:, ch * TCH:ch * TCH + TC])), dma_sem=st[1])
        last_ve = []
        for g in range(4):
            prevt = None
            for ch in range(16):
                deps = [vconv[15], (ldg, 16)] + (last_ve if ch == 0 else [])
                if prevt is not None:
                    deps.append(prevt)

                def mkp(e, ch=ch, g=g):
                    return e.matmul(xp_ps[0:33, 0:512],
                                    xpw_sb[:, ch * 33:(ch + 1) * 33],
                                    xiF[:, ch * TCH + g * 512:ch * TCH + (g + 1) * 512],
                                    start=(ch == 0), stop=(ch == 15))
                prevt = q.step("T", deps, mkp)
            v_e = q.step("V", [prevt], (lambda e, g=g: e.op(
                "tensor_copy",
                out=bass.AP(x_sb, 15104 + g * 512, [[17408, 33], [1, 512]]),
                in_=xp_ps[0:33, 0:512])))
            vconv[("xp", g)] = v_e
            last_ve = [v_e]
        q.step("GD", [vconv[("xp", g)] for g in range(4)],
               lambda e: e.dma_start(out=xp_d.ap(),
                                     in_=bass.AP(x_sb, 15104, [[17408, 33], [1, 2048]])),
               dma_sem=st[1])
        q.emit(blk)
    xp_ps_ctx.__exit__(None, None, None)

    xc_done = (st[1], dma_counts.get(id(st[1]), 0))
    with nc.Block() as blk:
        q = Seq(nc, VE, SC, GP, spp, pe_count, dma_counts)
        prev_q = []
        for (src_d, dst_d, rows, dwid) in [(wgs_d, wgq_d, 384, D), (wus_d, wuq_d, 384, D),
                                           (wds_d, wdq_d, 128, 2816)]:
            nt_ = rows // 128
            nblk = dwid // 16
            for t in range(nt_):
                dl = q.step("GD", list(prev_q), (lambda e, src_d=src_d, t=t, dwid=dwid: e.dma_start(
                    out=qsrc[:, 0:dwid],
                    in_=bass.AP(src_d, t * 128 * dwid, [[dwid, 128], [1, dwid]]))), dma_sem=lq)
                vq = _emit_fp4_quant(q, qsrc[:, 0:dwid], qdst[:, 0:dwid], qscl, qsgn, qabs,
                                     cqt, nblk, dwid, [dl, xc_done])
                stp = q.step("GD", [vq], (lambda e, dst_d=dst_d, t=t, dwid=dwid: e.dma_start(
                    out=bass.AP(dst_d, t * 128 * dwid, [[dwid, 128], [1, dwid]]),
                    in_=qdst[:, 0:dwid])), dma_sem=stq)
                prev_q = [vq, stp]
        q.emit(blk)

    with nc.Block() as blk:
        @blk.gpsimd
        def _(e):
            for s_ in (st[0], st[1], stq):
                if id(s_) in dma_counts:
                    e.wait_ge(s_, dma_counts[id(s_)])

    es.close()
    _BUILD_CACHE["l2a"] = nc
    return nc


def dma_counts_probe(dc, sem):
    return dc.get(id(sem), 0)




def build_l2b():
    if "l2b" in _BUILD_CACHE:
        return _BUILD_CACHE["l2b"]
    DSN = 16
    WL = 2048
    NW = L // WL
    nc = _nc()
    xc_d = nc.dram_tensor("xcg", [4, 128, L], BF16, kind="ExternalInput")
    xp0_d = nc.dram_tensor("xp0", [B, L], F32, kind="ExternalInput")
    bp_d = nc.dram_tensor("bpT", [B, DSN, L], F32, kind="ExternalInput")
    cp_d = nc.dram_tensor("cpT", [B, DSN, L], F32, kind="ExternalInput")
    dtw_d = nc.dram_tensor("dtw2d", [128, 2], F32, kind="ExternalInput")
    dtb_d = nc.dram_tensor("dtb2d", [128, 2], F32, kind="ExternalInput")
    A_d = nc.dram_tensor("A2d", [128, 2 * DSN], F32, kind="ExternalInput")
    dsk_d = nc.dram_tensor("dsk2d", [128, 2], F32, kind="ExternalInput")
    y_d = nc.dram_tensor("yT", [4, 128, L], F32, kind="ExternalOutput")

    es, sb, sem, GPV = None, None, None, None
    es, sb, sem, VE, SC, GP = _mk_ctx(nc)
    xcw_bf = sb("xcw_bf", [128, WL], BF16)
    xcw_bf_view = [None]
    xp0b = sb("xp0b", [128, WL])
    bpb = sb("bpb", [128, WL])
    cpb = sb("cpb", [128, WL])
    dl_ = sb("dl_", [128, WL])
    dx_ = sb("dx_", [128, WL])
    dA_ = sb("dA_", [128, WL])
    dB_ = sb("dB_", [128, WL])
    hn_ = sb("hn_", [128, WL])
    yac = [sb(f"yac{p}", [128, WL]) for p in range(2)]
    tq = sb("tq", [128, WL])
    hst = sb("hst", [128, DSN])
    dtw_sb = sb("dtw_sb", [128, 2])
    dtb_sb = sb("dtb_sb", [128, 2])
    A_sb = sb("A_sb", [128, 2 * DSN])
    dsk_sb = sb("dsk_sb", [128, 2])
    one_t = sb("one_t", [128, 1])
    ones_w = sb("ones_w", [128, WL])

    ld = sem("ld")
    lxc = sem("lxc")
    lx0 = sem("lx0")
    lbp = sem("lbp")
    lcp = sem("lcp")
    sty = [sem("sty0"), sem("sty1")]
    spp = sem("spp")
    pe_count = [0]
    dma_counts = {}

    with nc.Block() as blk:
        @blk.sync
        def _(e):
            e.dma_start(out=dtw_sb[:, :], in_=dtw_d.ap()).then_inc(ld, 16)
            e.dma_start(out=dtb_sb[:, :], in_=dtb_d.ap()).then_inc(ld, 16)
            e.dma_start(out=A_sb[:, :], in_=A_d.ap()).then_inc(ld, 16)
            e.dma_start(out=dsk_sb[:, :], in_=dsk_d.ap()).then_inc(ld, 16)

        @blk.gpsimd
        def _(e):
            GP.op("memset", one_t[:, :], 1.0)
            GP.op("memset", ones_w[:, :], 1.0)

    xcw_bf_view[0] = xcw_bf[:, :]

    prev_xc_rd = []
    prev_x0_rd = []
    for g in range(4):
        bb = g // 2
        cg = g % 2
        with nc.Block() as blk:
            @blk.vector
            def _(e):
                VE.op("memset", hst[:, :], 0.0)

        for w in range(NW):
            gw = g * NW + w
            with nc.Block() as blk:
                q = Seq(nc, VE, SC, GP, spp, pe_count, dma_counts)
                dxc = q.step("GD", prev_xc_rd, (lambda e, g=g, w=w: e.dma_start(
                    out=xcw_bf_view[0],
                    in_=bass.AP(xc_d, (g * 128) * L + w * WL, [[L, 128], [1, WL]]))),
                    dma_sem=lxc)
                dx0 = q.step("GD", prev_x0_rd, (lambda e, bb=bb, w=w: e.dma_start(
                    out=xp0b[:, :], in_=bass.AP(xp0_d, bb * L + w * WL, [[0, 128], [1, WL]]))),
                    dma_sem=lx0)
                v1 = q.step("V", [dx0] + ([(ld, 64)] if gw == 0 else []), (lambda e, cg=cg: e.op(
                    "tensor_scalar", out=dl_[:, :], in0=xp0b[:, :],
                    scalar1=dtw_sb[:, cg:cg + 1], scalar2=dtb_sb[:, cg:cg + 1],
                    op0=ALU.mult, op1=ALU.add)))
                s1 = q.step("S", [v1], lambda e: e.op(
                    "activation", out=dx_[:, :], in_=dl_[:, :], func=AF.Abs))
                s2 = q.step("S", [s1], lambda e: e.op(
                    "activation", out=dx_[:, :], in_=dx_[:, :], func=AF.Exp, scale=-1.0))
                s3 = q.step("S", [s2], lambda e: e.op(
                    "activation", out=dx_[:, :], in_=dx_[:, :], func=AF.Ln, bias=one_t[:, 0:1]))
                s4 = q.step("S", [s3], lambda e: e.op(
                    "activation", out=dl_[:, :], in_=dl_[:, :], func=AF.Relu))
                v2 = q.step("V", [s4], lambda e: e.op(
                    "tensor_tensor", out=dl_[:, :], in0=dl_[:, :], in1=dx_[:, :], op=ALU.add))
                v3 = q.step("V", [v2, dxc], (lambda e: e.op(
                    "tensor_tensor", out=dx_[:, :], in0=dl_[:, :], in1=xcw_bf_view[0], op=ALU.mult)))
                pw = gw % 2
                ydeps = []
                if gw >= 2:
                    ydeps = [(sty[pw], 16 * (gw // 2))]
                vy0 = q.step("V", ydeps, (lambda e, pw=pw: e.op("memset", yac[pw][:, :], 0.0)))
                vprev_h = None
                prev_vB = []
                prev_vM = []
                for n in range(DSN):
                    dbp = q.step("GD", prev_vB, (lambda e, bb=bb, n=n, w=w: e.dma_start(
                        out=bpb[:, :],
                        in_=bass.AP(bp_d, (bb * DSN + n) * L + w * WL, [[0, 128], [1, WL]]))),
                        dma_sem=lbp)
                    dcp = q.step("GD", prev_vM, (lambda e, bb=bb, n=n, w=w: e.dma_start(
                        out=cpb[:, :],
                        in_=bass.AP(cp_d, (bb * DSN + n) * L + w * WL, [[0, 128], [1, WL]]))),
                        dma_sem=lcp)
                    sA = q.step("S", [v2] + ([vprev_h] if vprev_h else [])
                                + ([(ld, 64)] if gw == 0 and n == 0 else []),
                                (lambda e, cg=cg, n=n: e.op(
                        "activation", out=dA_[:, :], in_=dl_[:, :], func=AF.Exp,
                        scale=A_sb[:, cg * DSN + n:cg * DSN + n + 1])))
                    vB = q.step("V", [dbp, v3], (lambda e: e.op(
                        "tensor_tensor", out=dB_[:, :], in0=dx_[:, :], in1=bpb[:, :], op=ALU.mult)))
                    vS = q.step("V", [sA, vB], (lambda e, n=n: e.op(
                        "tensor_tensor_scan", out=hn_[:, :], data0=dA_[:, :], data1=dB_[:, :],
                        initial=hst[:, n:n + 1], op0=ALU.mult, op1=ALU.add)))
                    vH = q.step("V", [vS], (lambda e, n=n: e.op(
                        "tensor_copy", out=hst[:, n:n + 1], in_=hn_[:, WL - 1:WL])))
                    vM = q.step("V", [vH, dcp], (lambda e: e.op(
                        "tensor_tensor", out=tq[:, :], in0=hn_[:, :], in1=cpb[:, :], op=ALU.mult)))
                    vprev_h = q.step("V", [vM, vy0], (lambda e, pw=pw: e.op(
                        "tensor_tensor", out=yac[pw][:, :], in0=yac[pw][:, :], in1=tq[:, :],
                        op=ALU.add)))
                    prev_vB = [vB]
                    prev_vM = [vM]
                vD = q.step("V", [vprev_h], (lambda e, cg=cg, pw=pw: e.op(
                    "scalar_tensor_tensor", out=yac[pw][:, :], in0=xcw_bf_view[0],
                    scalar=dsk_sb[:, cg:cg + 1], in1=yac[pw][:, :], op0=ALU.mult, op1=ALU.add)))
                q.step("GD", [vD], (lambda e, g=g, w=w, pw=pw: e.dma_start(
                    out=bass.AP(y_d, g * 128 * L + w * WL, [[L, 128], [1, WL]]),
                    in_=yac[pw][:, :])), dma_sem=sty[pw])
                mks = q.emit(blk)
                prev_xc_rd = [mks[vD]]
                prev_x0_rd = [mks[v1]]

    with nc.Block() as blk:
        @blk.gpsimd
        def _(e):
            for s_ in sty:
                if id(s_) in dma_counts:
                    e.wait_ge(s_, dma_counts[id(s_)])

    es.close()
    _BUILD_CACHE["l2b"] = nc
    return nc




def build_l2c():
    if "l2c" in _BUILD_CACHE:
        return _BUILD_CACHE["l2c"]
    FH = 2816
    NFH = 22
    QT = 512
    nc = _nc()
    x2_d = nc.dram_tensor("x", [TC, D], F32, kind="ExternalInput")
    yT_d = nc.dram_tensor("yTg", [2048, TC], F32, kind="ExternalInput")
    zT_d = nc.dram_tensor("zTg", [2048, TC], BF16, kind="ExternalInput")
    opT_d = nc.dram_tensor("opT", [2048, D], F32, kind="ExternalInput")
    wgT_d = nc.dram_tensor("wgqT", [D, FH], BF16, kind="ExternalInput")
    wuT_d = nc.dram_tensor("wuqT", [D, FH], BF16, kind="ExternalInput")
    wdT_d = nc.dram_tensor("wdqT", [FH, D], BF16, kind="ExternalInput")
    x3_d = nc.dram_tensor("x3", [TC, D], F32, kind="ExternalOutput")

    es, sb, sem, VE, SC, GP = _mk_ctx(nc)
    xn2bf = sb("xn2bf", [128, NT * D], BF16)
    s_tok = sb("s_tok", [128, NT * D], BF16)
    xqT = sb("xqT", [128, 8 * TC], BF16)
    fdown = sb("fdown", [128, 8 * TC], BF16)
    x2t = [sb(f"x2t{p}", [128, D]) for p in range(2)]
    lnt = sb("lnt2c", [128, 8])
    eps_t = sb("eps2c", [128, 1])
    ident_bf = sb("ident2c", [128, 128], BF16)
    qabs = sb("qabs2c", [128, D])
    qsgn = sb("qsgn2c", [128, D])
    qcqt = sb("qcqt2c", [128, D])
    qscl = sb("qscl2c", [128, 128])

    lx2 = [sem("lx2a"), sem("lx2b")]
    lyy = [sem("lyya"), sem("lyyb")]
    lzz = [sem("lzza"), sem("lzzb")]
    lop = [sem("lopa"), sem("lopb")]
    lwg = [sem("lwga"), sem("lwgb")]
    lwu = [sem("lwua"), sem("lwub")]
    lwd = [sem("lwda"), sem("lwdb")]
    sst = [sem("ssta"), sem("sstb")]
    spp = sem("spp")
    pe_count = [0]
    dma_counts = {}

    with nc.Block() as blk:
        @blk.gpsimd
        def _(e):
            _build_ident(GP, ident_bf)
            GP.op("memset", eps_t[:, :], 1e-5)

    def mk_trh(q, psA, psB):
        state = {"cnt": 0, "last": [None, None]}

        def tr(src_ap, dst_ap, deps=()):
            b = state["cnt"] % 2
            state["cnt"] += 1
            ps = (psA, psB)[b]
            d = list(deps)
            if state["last"][b] is not None:
                d.append(state["last"][b])
            t = q.step("T", d, (lambda e, ps=ps, src_ap=src_ap: e.transpose(
                ps[0:128, 0:128], src_ap, ident_bf[:, :])))
            v = q.step("V", [t], (lambda e, ps=ps, dst_ap=dst_ap: e.op(
                "tensor_copy", out=dst_ap, in_=ps[0:128, 0:128])))
            state["last"][b] = v
            return t, v
        return tr

    with (nc.psum_tensor("trPa", [128, 128], BF16) as trPa,
          nc.psum_tensor("trPb", [128, 128], BF16) as trPb,
          nc.sbuf_tensor("xn_t", [128, D], F32) as xn_t,
          nc.sbuf_tensor("xq_t", [128, D], BF16) as xq_t):
        with nc.Block() as blk:
            q = Seq(nc, VE, SC, GP, spp, pe_count, dma_counts)
            tr = mk_trh(q, trPa, trPb)
            last_x2_rd = {}
            last_tr_t = []
            for j in range(NT):
                p = j % 2
                deps = [last_x2_rd[j - 2]] if j >= 2 else []
                dl = q.step("SD", deps, (lambda e, j=j, p=p: e.dma_start(
                    out=x2t[p][:, :], in_=bass.AP(x2_d, j * 128 * D, [[D, 128], [1, D]]))),
                    dma_sem=lx2[p])
                v1 = q.step("V", [dl], (lambda e, p=p: e.op(
                    "reduce_sum", out=lnt[:, 0:1], in_=x2t[p][:, :], axis=AX.X)))
                s1 = q.step("S", [v1], lambda e: e.op("mul", lnt[:, 0:1], lnt[:, 0:1], 1.0 / D))
                v2 = q.step("V", [s1], (lambda e, p=p: e.op(
                    "tensor_scalar", out=xn_t[:, :], in0=x2t[p][:, :], scalar1=lnt[:, 0:1],
                    scalar2=None, op0=ALU.subtract)))
                last_x2_rd[j] = v2
                v3 = q.step("V", [v2], lambda e: e.op(
                    "tensor_tensor", out=qcqt[:, :], in0=xn_t[:, :], in1=xn_t[:, :], op=ALU.mult))
                v4 = q.step("V", [v3], lambda e: e.op(
                    "reduce_sum", out=lnt[:, 1:2], in_=qcqt[:, :], axis=AX.X))
                s2 = q.step("S", [v4], lambda e: e.op(
                    "activation", out=lnt[:, 2:3], in_=lnt[:, 1:2], func=AF.Sqrt,
                    bias=eps_t[:, 0:1], scale=1.0 / D))
                v5 = q.step("V", [s2], lambda e: e.op(
                    "reciprocal", out=lnt[:, 3:4], in_=lnt[:, 2:3]))
                v6 = q.step("V", [v5], lambda e: e.op(
                    "tensor_scalar", out=xn_t[:, :], in0=xn_t[:, :], scalar1=lnt[:, 3:4],
                    scalar2=None, op0=ALU.mult))
                v7 = q.step("V", [v6], (lambda e, j=j: e.op(
                    "tensor_copy", out=xn2bf[:, j * D:(j + 1) * D], in_=xn_t[:, :])))
                vq = _emit_fp4_quant(q, xn_t[:, 0:D], xq_t[:, 0:D], qscl, qsgn, qabs,
                                     qcqt, D // 16, D, [v7] + last_tr_t)
                last_tr_t = []
                for kd in range(8):
                    t_, v_ = tr(xq_t[:, kd * 128:(kd + 1) * 128],
                                xqT[:, kd * TC + j * 128:kd * TC + (j + 1) * 128], [vq])
                    last_tr_t.append(t_)
                last_tr_t = [last_tr_t[-1]]
            q.emit(blk)

    with (nc.sbuf_tensor("yz2c", [128, 16 * 1024], BF16) as yz,
          nc.sbuf_tensor("wop0", [128, 16 * 128], BF16) as wop0,
          nc.sbuf_tensor("wop1", [128, 16 * 128], BF16) as wop1,
          nc.sbuf_tensor("yt2c0", [128, 1024], F32) as yt0,
          nc.sbuf_tensor("yt2c1", [128, 1024], F32) as yt1,
          nc.sbuf_tensor("zt2c0", [128, 1024], BF16) as zt0,
          nc.sbuf_tensor("zt2c1", [128, 1024], BF16) as zt1,
          nc.sbuf_tensor("szt2c", [128, 1024], F32) as szt,
          nc.sbuf_tensor("spf2c", [128, 512], BF16) as spf,
          nc.psum_tensor("psPa", [128, 512], F32) as psPa,
          nc.psum_tensor("psPb", [128, 512], F32) as psPb,
          nc.psum_tensor("trQa", [128, 128], BF16) as trQa,
          nc.psum_tensor("trQb", [128, 128], BF16) as trQb):
        wop = [wop0, wop1]
        yt = [yt0, yt1]
        zt = [zt0, zt1]
        psP = [psPa, psPb]
        with nc.Block() as blk:
            q = Seq(nc, VE, SC, GP, spp, pe_count, dma_counts)
            tr = mk_trh(q, trQa, trQb)
            gidx = 0
            last_cp = {}
            last_mm_of = {}
            prev_half_tr = []
            for hh in range(2):
                last_rd = {}
                vlast = None
                for i in range(16):
                    p = i % 2
                    deps = [last_rd[i - 2]] if i >= 2 else list(prev_half_tr)
                    dly = q.step("SD", deps, (lambda e, i=i, p=p, hh=hh: e.dma_start(
                        out=yt[p][:, :],
                        in_=bass.AP(yT_d, i * 128 * TC + hh * 1024, [[TC, 128], [1, 1024]]))),
                        dma_sem=lyy[p])
                    dlz = q.step("SD", deps, (lambda e, i=i, p=p, hh=hh: e.dma_start(
                        out=zt[p][:, :],
                        in_=bass.AP(zT_d, i * 128 * TC + hh * 1024, [[TC, 128], [1, 1024]]))),
                        dma_sem=lzz[p])
                    sA = q.step("S", [dlz] + ([vlast] if vlast else []), (lambda e, p=p: e.op(
                        "activation", out=szt[:, :], in_=zt[p][:, :], func=AF.Silu)))
                    vlast = q.step("V", [sA, dly], (lambda e, i=i, p=p: e.op(
                        "tensor_tensor", out=yz[:, i * 1024:(i + 1) * 1024], in0=yt[p][:, :],
                        in1=szt[:, :], op=ALU.mult)))
                    last_rd[i] = vlast
                for j in range(8):
                    p = j % 2
                    deps = [vlast]
                    if (hh, j - 2) in last_mm_of:
                        deps.append(last_mm_of[(hh, j - 2)])
                    if hh == 1:
                        deps.append(last_mm_of[(0, j)] if (0, j) in last_mm_of else vlast)
                    dlw = q.step("GD", deps, (lambda e, j=j, p=p: e.dma_start(
                        out=wop[p][:, :],
                        in_=bass.AP(opT_d, j * 128, [[1024, 128], [128 * 1024, 16], [1, 128]]))),
                        dma_sem=lop[p])
                    for cgt in range(2):
                        pb = gidx % 2
                        deps2 = [dlw]
                        if gidx >= 2:
                            deps2.append(last_cp[gidx - 2])

                        def mk(e, p=p, cgt=cgt, pb=pb):
                            last = None
                            for k in range(16):
                                last = e.matmul(psP[pb][:, 0:512],
                                                wop[p][:, k * 128:(k + 1) * 128],
                                                yz[:, k * 1024 + cgt * 512:k * 1024 + cgt * 512 + 512],
                                                start=(k == 0), stop=(k == 15))
                            return last
                        t_m = q.step("T", deps2, mk)
                        prev_tr = last_mm_of.get(("tr", gidx - 1))
                        v_c = q.step("V", [t_m] + ([prev_tr] if prev_tr else []),
                                     (lambda e, pb=pb: e.op(
                            "tensor_copy", out=spf[:, :], in_=psP[pb][:, 0:512])))
                        last_cp[gidx] = v_c
                        trts = []
                        for ss in range(4):
                            jt = hh * 8 + cgt * 4 + ss
                            t_, v_ = tr(spf[:, ss * 128:(ss + 1) * 128],
                                        s_tok[:, jt * D + j * 128:jt * D + (j + 1) * 128],
                                        [v_c])
                            trts.append(t_)
                        last_mm_of[("tr", gidx)] = trts[-1]
                        last_mm_of[(hh, j)] = t_m
                        gidx += 1
                prev_half_tr = [last_mm_of[("tr", gidx - 1)]]
            q.emit(blk)

    es2 = contextlib.ExitStack()
    _e = es2.enter_context
    h_fm = _e(nc.sbuf_tensor("hfm2c", [128, NFH * QT], BF16))
    wg = [_e(nc.sbuf_tensor(f"wgsb{i}", [128, 8 * 128], BF16)) for i in range(2)]
    wu = [_e(nc.sbuf_tensor(f"wusb{i}", [128, 8 * 128], BF16)) for i in range(2)]
    wd0 = _e(nc.sbuf_tensor("wdsb0", [128, NFH * 128], BF16))
    wd = [wd0, wd0]
    h_tok = _e(nc.sbuf_tensor("htok2c", [128, FH], F32))
    hq_tok = _e(nc.sbuf_tensor("hqtok2c", [128, FH], BF16))
    gsil = _e(nc.sbuf_tensor("gsil2c", [128, 512], F32))
    psG = [_e(nc.psum_tensor(f"psG{i}", [128, 512], F32)) for i in range(2)]
    psU = [_e(nc.psum_tensor(f"psU{i}", [128, 512], F32)) for i in range(2)]
    psD = [_e(nc.psum_tensor(f"psD{i}", [128, 512], F32)) for i in range(2)]
    trRa = _e(nc.psum_tensor("trRa", [128, 128], BF16))
    trRb = _e(nc.psum_tensor("trRb", [128, 128], BF16))
    if True:
        with nc.Block() as blk:
            q = Seq(nc, VE, SC, GP, spp, pe_count, dma_counts)
            state_f = {"last": None}
            state_b = {"last": None}

            def trf(src_ap, dst_ap, deps=()):
                d = list(deps)
                if state_f["last"] is not None:
                    d.append(state_f["last"])
                t = q.step("T", d, (lambda e, src_ap=src_ap: e.transpose(
                    trRa[0:128, 0:128], src_ap, ident_bf[:, :])))
                v = q.step("V", [t], (lambda e, dst_ap=dst_ap: e.op(
                    "tensor_copy", out=dst_ap, in_=trRa[0:128, 0:128])))
                state_f["last"] = v
                return t, v

            def trb(src_ap, dst_ap, deps=()):
                d = list(deps)
                if state_b["last"] is not None:
                    d.append(state_b["last"])
                t = q.step("T", d, (lambda e, src_ap=src_ap: e.transpose(
                    trRb[0:128, 0:128], src_ap, ident_bf[:, :])))
                v = q.step("V", [t], (lambda e, dst_ap=dst_ap: e.op(
                    "tensor_copy", out=dst_ap, in_=trRb[0:128, 0:128])))
                state_b["last"] = v
                return t, v

            last_dn_t = []
            lw_rd = {}
            lu_rd = {}
            ld_rd = {}
            kq = 0
            for qq in range(4):
                vmul_prev = None
                last_hfm_wr = []
                for kh in range(NFH):
                    p = kh % 2
                    depw = [lw_rd[(qq, kh - 2)]] if kh >= 2 else (
                        [lw_rd[(qq - 1, NFH - 2 + (kh % 2))]] if qq >= 1 else [])
                    dlg = q.step("SD", depw, (lambda e, kh=kh, p=p: e.dma_start(
                        out=wg[p][:, :],
                        in_=bass.AP(wgT_d, kh * 128, [[FH, 128], [128 * FH, 8], [1, 128]]))),
                        dma_sem=lwg[p])
                    depu = [lu_rd[(qq, kh - 2)]] if kh >= 2 else (
                        [lu_rd[(qq - 1, NFH - 2 + (kh % 2))]] if qq >= 1 else [])
                    dlu = q.step("SD", depu, (lambda e, kh=kh, p=p: e.dma_start(
                        out=wu[p][:, :],
                        in_=bass.AP(wuT_d, kh * 128, [[FH, 128], [128 * FH, 8], [1, 128]]))),
                        dma_sem=lwu[p])

                    def mkg(e, p=p, qq=qq):
                        last = None
                        for k in range(8):
                            last = e.matmul(psG[p][:, 0:512], wg[p][:, k * 128:(k + 1) * 128],
                                            xqT[:, k * TC + qq * QT:k * TC + qq * QT + QT],
                                            start=(k == 0), stop=(k == 7))
                        return last
                    depg = [dlg] + last_dn_t
                    if kh >= 2:
                        depg.append(lw_rd[(qq, kh - 2, "gs")])
                    t_g = q.step("T", depg, mkg)
                    lw_rd[(qq, kh)] = t_g
                    s_g = q.step("S", [t_g] + ([vmul_prev] if vmul_prev else []),
                                 (lambda e, p=p: e.op(
                        "activation", out=gsil[:, :], in_=psG[p][:, 0:512], func=AF.Silu)))
                    lw_rd[(qq, kh, "gs")] = s_g

                    def mku(e, p=p, qq=qq):
                        last = None
                        for k in range(8):
                            last = e.matmul(psU[p][:, 0:512], wu[p][:, k * 128:(k + 1) * 128],
                                            xqT[:, k * TC + qq * QT:k * TC + qq * QT + QT],
                                            start=(k == 0), stop=(k == 7))
                        return last
                    depu2 = [dlu] + last_dn_t
                    if kh >= 2:
                        depu2.append(lw_rd[(qq, kh - 2, "um")])
                    t_u = q.step("T", depu2, mku)
                    lu_rd[(qq, kh)] = t_u
                    vmul = q.step("V", [s_g, t_u] + last_dn_t, (lambda e, kh=kh, p=p: e.op(
                        "tensor_tensor", out=h_fm[:, kh * QT:(kh + 1) * QT], in0=gsil[:, :],
                        in1=psU[p][:, 0:512], op=ALU.mult)))
                    lw_rd[(qq, kh, "um")] = vmul
                    vmul_prev = vmul
                    last_hfm_wr.append(vmul)
                last_dn_t = []
                prev_wb_t = []
                for jt in range(4):
                    rts = []
                    for kf in range(NFH):
                        t_, v_ = trf(h_fm[:, kf * QT + jt * 128:kf * QT + (jt + 1) * 128],
                                     h_tok[:, kf * 128:(kf + 1) * 128],
                                     [last_hfm_wr[kf]])
                        rts.append(v_)
                    vqs = []
                    for (off, wid) in ((0, 1024), (1024, 1024), (2048, 768)):
                        vq = _emit_fp4_quant(q, h_tok[:, off:off + wid],
                                             hq_tok[:, off:off + wid], qscl, qsgn, qabs,
                                             qcqt, wid // 16, wid,
                                             [rts[-1]] + prev_wb_t
                                             + ([vqs[-1]] if vqs else []),
                                             accq_w=D, scl_w=128)
                        vqs.append(vq)
                    prev_wb_t = []
                    wb_vs = []
                    for kf in range(NFH):
                        t_, v_ = trb(hq_tok[:, kf * 128:(kf + 1) * 128],
                                     h_fm[:, kf * QT + jt * 128:kf * QT + (jt + 1) * 128],
                                     [vqs[min(kf * 128 // 1024, 2)], rts[kf]])
                        prev_wb_t.append(t_)
                        wb_vs.append(v_)
                    prev_wb_t = [prev_wb_t[-1]]
                wb_done = [wb_vs[-1]]
                for j in range(8):
                    p = j % 2
                    depw = [ld_rd[(qq, j - 1)]] if j >= 1 else (
                        [ld_rd[(qq - 1, 7)]] if qq >= 1 else [])
                    dld = q.step("SD", depw, (lambda e, j=j, p=p: e.dma_start(
                        out=wd[p][:, :],
                        in_=bass.AP(wdT_d, j * 128, [[1024, 128], [128 * 1024, NFH], [1, 128]]))),
                        dma_sem=lwd[p])

                    def mkd(e, p=p):
                        last = None
                        for k in range(NFH):
                            last = e.matmul(psD[p][:, 0:512], wd[p][:, k * 128:(k + 1) * 128],
                                            h_fm[:, k * QT:k * QT + QT],
                                            start=(k == 0), stop=(k == NFH - 1))
                        return last
                    depd = [dld] + wb_done
                    if j >= 2:
                        depd.append(ld_rd[(qq, j - 2, "cp")])
                    t_d = q.step("T", depd, mkd)
                    ld_rd[(qq, j)] = t_d
                    v_d = q.step("V", [t_d], (lambda e, j=j, qq=qq, p=p: e.op(
                        "tensor_copy", out=fdown[:, j * TC + qq * QT:j * TC + qq * QT + QT],
                        in_=psD[p][:, 0:512])))
                    ld_rd[(qq, j, "cp")] = v_d
                    last_dn_t.append(t_d)
                last_dn_t = [last_dn_t[-1]]
            q.emit(blk)
    es2.close()

    with (nc.sbuf_tensor("ffntok", [128, D], F32) as ffn_tok,
          nc.sbuf_tensor("g2t2c", [128, D], F32) as g2t,
          nc.sbuf_tensor("s2t2c", [128, D], F32) as s2t,
          nc.sbuf_tensor("x3t0", [128, D], F32) as x3t0,
          nc.sbuf_tensor("x3t1", [128, D], F32) as x3t1,
          nc.psum_tensor("trSa", [128, 128], BF16) as trSa,
          nc.psum_tensor("trSb", [128, 128], BF16) as trSb):
        x3t = [x3t0, x3t1]
        with nc.Block() as blk:
            q = Seq(nc, VE, SC, GP, spp, pe_count, dma_counts)
            tr = mk_trh(q, trSa, trSb)
            last_st = {}
            last_x2_rd = {}
            for j in range(NT):
                p = j % 2
                trvs = []
                for kd in range(8):
                    t_, v_ = tr(fdown[:, kd * TC + j * 128:kd * TC + (j + 1) * 128],
                                ffn_tok[:, kd * 128:(kd + 1) * 128], [])
                    trvs.append(v_)
                s_g = q.step("S", [trvs[-1]], lambda e: e.op(
                    "activation", out=g2t[:, :], in_=ffn_tok[:, :], func=AF.Silu))
                v_s2 = q.step("V", [s_g], (lambda e, j=j: e.op(
                    "tensor_tensor", out=s2t[:, :], in0=xn2bf[:, j * D:(j + 1) * D],
                    in1=s_tok[:, j * D:(j + 1) * D], op=ALU.add)))
                deps = [last_x2_rd[j - 2]] if j >= 2 else []
                dl = q.step("SD", deps, (lambda e, j=j, p=p: e.dma_start(
                    out=x2t[p][:, :], in_=bass.AP(x2_d, j * 128 * D, [[D, 128], [1, D]]))),
                    dma_sem=lx2[p])
                v_m = q.step("V", [v_s2, s_g], lambda e: e.op(
                    "tensor_tensor", out=s2t[:, :], in0=s2t[:, :], in1=g2t[:, :], op=ALU.mult))
                depst = [last_st[j - 2]] if j >= 2 else []
                v_x3 = q.step("V", [v_m, dl] + depst, (lambda e, p=p: e.op(
                    "tensor_tensor", out=x3t[p][:, :], in0=x2t[p][:, :], in1=s2t[:, :],
                    op=ALU.add)))
                last_x2_rd[j] = v_x3
                stp = q.step("GD", [v_x3], (lambda e, j=j, p=p: e.dma_start(
                    out=bass.AP(x3_d, j * 128 * D, [[D, 128], [1, D]]), in_=x3t[p][:, :])),
                    dma_sem=sst[p])
                last_st[j] = stp
            q.emit(blk)

    with nc.Block() as blk:
        @blk.gpsimd
        def _(e):
            for s_ in (sst[0], sst[1]):
                if id(s_) in dma_counts:
                    e.wait_ge(s_, dma_counts[id(s_)])

    es.close()
    _BUILD_CACHE["l2c"] = nc
    return nc




def build_l3a():
    if "l3a" in _BUILD_CACHE:
        return _BUILD_CACHE["l3a"]
    nc = _nc()
    x_d = nc.dram_tensor("x", [TC, D], F32, kind="ExternalInput")
    wq_d = nc.dram_tensor("wqkvT", [D, 3072], F32, kind="ExternalInput")
    bq_d = nc.dram_tensor("bqkv", [3072], F32, kind="ExternalInput")
    wsw_d = nc.dram_tensor("wsw", [D, 8], F32, kind="ExternalInput")
    bsw_d = nc.dram_tensor("bsw", [8], F32, kind="ExternalInput")
    cos_d = nc.dram_tensor("cosr", [TC, 32], F32, kind="ExternalInput")
    sin_d = nc.dram_tensor("sinr", [TC, 32], F32, kind="ExternalInput")
    qg_d = nc.dram_tensor("qgam", [D], F32, kind="ExternalInput")
    kg_d = nc.dram_tensor("kgam", [D], F32, kind="ExternalInput")
    qr_d = nc.dram_tensor("qr", [TC, D], F32, kind="ExternalOutput")
    kr_d = nc.dram_tensor("kr", [TC, D], F32, kind="ExternalOutput")
    vr_d = nc.dram_tensor("vr", [TC, D], F32, kind="ExternalOutput")
    lg_d = nc.dram_tensor("logits2d", [128, NT * 8], F32, kind="ExternalOutput")

    es, sb, sem, VE, SC, GP = _mk_ctx(nc)
    wq_all = sb("wq_all", [128, 8 * 3072], BF16)
    bqb = sb("bqb", [128, 3072])
    wsw_sb = sb("wsw_sb", [128, 8 * 8], BF16)
    bswb = sb("bswb", [128, 8])
    qgb = sb("qgb", [128, D])
    kgb = sb("kgb", [128, D])
    cosb = [sb(f"cosb{p}", [128, 32]) for p in range(2)]
    sinb = [sb(f"sinb{p}", [128, 32]) for p in range(2)]
    x2t = [sb(f"x3t{p}", [128, D]) for p in range(2)]
    xn_t = sb("xn_t3", [128, D])
    xnc = sb("xnc3", [128, D], BF16)
    xnT_t = sb("xnT_t3", [128, 8 * 128], BF16)
    qkv_t = sb("qkv_t3", [128, 3072])
    rt = sb("rt3", [128, D])
    tmp = sb("tmp3", [128, D])
    sqr = sb("sqr3", [128, D])
    rs16 = sb("rs16", [128, 16])
    rr16 = sb("rr16", [128, 16])
    qob = [sb(f"qob{p}", [128, D]) for p in range(2)]
    kob = [sb(f"kob{p}", [128, D]) for p in range(2)]
    vob = [sb(f"vob{p}", [128, D]) for p in range(2)]
    lg2d = sb("lg2d3", [128, NT * 8])
    lnt = sb("lnt3", [128, 8])
    eps_t = sb("eps3", [128, 1])
    eps6 = sb("eps6", [128, 1])
    ident_bf = sb("ident3", [128, 128], BF16)

    ld = sem("ld")
    ldg = sem("ldg")
    ldg2 = sem("ldg2")
    lx = [sem("lxa"), sem("lxb")]
    lco = [sem("lcoa"), sem("lcob")]
    lsi = [sem("lsia"), sem("lsib")]
    sq_ = [sem("sqa"), sem("sqb")]
    sk_ = [sem("ska"), sem("skb")]
    sv_ = [sem("sva"), sem("svb")]
    slg = sem("slg")
    spp = sem("spp")
    pe_count = [0]
    dma_counts = {}

    with nc.Block() as blk:
        @blk.sync
        def _(e):
            e.dma_start(out=bqb[:, :], in_=_bcast_row_ap(bq_d, 3072)).then_inc(ld, 16)
            e.dma_start(out=bswb[:, :], in_=_bcast_row_ap(bsw_d, 8)).then_inc(ld, 16)
            e.dma_start(out=qgb[:, :], in_=_bcast_row_ap(qg_d, D)).then_inc(ld, 16)
            e.dma_start(out=kgb[:, :], in_=_bcast_row_ap(kg_d, D)).then_inc(ld, 16)

        @blk.gpsimd
        def _(e):
            _build_ident(GP, ident_bf)
            GP.op("memset", eps_t[:, :], 1e-5)
            GP.op("memset", eps6[:, :], 1e-6)
            e.dma_start(out=wq_all[:, :],
                        in_=bass.AP(wq_d, 0, [[3072, 128], [128 * 3072, 8], [1, 3072]])
                        ).then_inc(ldg, 16)
            e.dma_start(out=wsw_sb[:, :],
                        in_=bass.AP(wsw_d, 0, [[8, 128], [128 * 8, 8], [1, 8]])
                        ).then_inc(ldg2, 16)

    def v3d(handle_or_ap, pstep, off, step2_off):
        t = handle_or_ap
        return bass.AP(t, off + step2_off, [[pstep, 128], [64, 16], [2, 32]])

    def vcs(tb):
        return bass.AP(tb, 0, [[32, 128], [0, 16], [1, 32]])

    with (nc.psum_tensor("trXa", [128, 128], BF16) as trXa,
          nc.psum_tensor("trXb", [128, 128], BF16) as trXb,
          nc.psum_tensor("psQ0", [128, 512], F32) as psQ0,
          nc.psum_tensor("psQ1", [128, 512], F32) as psQ1,
          nc.psum_tensor("psL3", [128, 8], F32) as psL3):
        psQ = [psQ0, psQ1]
        with nc.Block() as blk:
            q = Seq(nc, VE, SC, GP, spp, pe_count, dma_counts)
            trcnt = [0]
            last_tr = [None, None]
            last_x_rd = {}
            last_cs_rd = {}
            last_mmL = None
            last_st = {}
            for j in range(NT):
                p = j % 2
                xdeps = [last_x_rd[j - 2]] if j >= 2 else []
                dx = q.step("SD", xdeps, (lambda e, j=j, p=p: e.dma_start(
                    out=x2t[p][:, :], in_=bass.AP(x_d, j * 128 * D, [[D, 128], [1, D]]))),
                    dma_sem=lx[p])
                csdeps = [last_cs_rd[j - 2]] if j >= 2 else []
                dco = q.step("SD", csdeps, (lambda e, j=j, p=p: e.dma_start(
                    out=cosb[p][:, :], in_=bass.AP(cos_d, j * 128 * 32, [[32, 128], [1, 32]]))),
                    dma_sem=lco[p])
                dsi = q.step("SD", csdeps, (lambda e, j=j, p=p: e.dma_start(
                    out=sinb[p][:, :], in_=bass.AP(sin_d, j * 128 * 32, [[32, 128], [1, 32]]))),
                    dma_sem=lsi[p])
                v1 = q.step("V", [dx], (lambda e, p=p: e.op(
                    "reduce_sum", out=lnt[:, 0:1], in_=x2t[p][:, :], axis=AX.X)))
                s1 = q.step("S", [v1], lambda e: e.op("mul", lnt[:, 0:1], lnt[:, 0:1], 1.0 / D))
                v2 = q.step("V", [s1], (lambda e, p=p: e.op(
                    "tensor_scalar", out=xn_t[:, :], in0=x2t[p][:, :], scalar1=lnt[:, 0:1],
                    scalar2=None, op0=ALU.subtract)))
                last_x_rd[j] = v2
                v3 = q.step("V", [v2], lambda e: e.op(
                    "tensor_tensor", out=sqr[:, :], in0=xn_t[:, :], in1=xn_t[:, :], op=ALU.mult))
                v4 = q.step("V", [v3], lambda e: e.op(
                    "reduce_sum", out=lnt[:, 1:2], in_=sqr[:, :], axis=AX.X))
                s2 = q.step("S", [v4], lambda e: e.op(
                    "activation", out=lnt[:, 2:3], in_=lnt[:, 1:2], func=AF.Sqrt,
                    bias=eps_t[:, 0:1], scale=1.0 / D))
                v5 = q.step("V", [s2], lambda e: e.op(
                    "reciprocal", out=lnt[:, 3:4], in_=lnt[:, 2:3]))
                v6 = q.step("V", [v5], lambda e: e.op(
                    "tensor_scalar", out=xn_t[:, :], in0=xn_t[:, :], scalar1=lnt[:, 3:4],
                    scalar2=None, op0=ALU.mult))
                v7 = q.step("V", [v6], lambda e: e.op(
                    "tensor_copy", out=xnc[:, :], in_=xn_t[:, :]))
                trvs = []
                for kd in range(8):
                    b = trcnt[0] % 2
                    trcnt[0] += 1
                    trp = (trXa, trXb)[b]
                    d = [v7]
                    if last_tr[b] is not None:
                        d.append(last_tr[b])
                    tt = q.step("T", d, (lambda e, kd=kd, trp=trp: e.transpose(
                        trp[:, 0:128], xnc[:, kd * 128:(kd + 1) * 128], ident_bf[:, :])))
                    dd = [tt] + ([last_mmL] if last_mmL is not None and kd == 0 else [])
                    vv = q.step("V", dd, (lambda e, kd=kd, trp=trp: e.op(
                        "tensor_copy", out=xnT_t[:, kd * 128:(kd + 1) * 128],
                        in_=trp[:, 0:128])))
                    last_tr[b] = vv
                    trvs.append(vv)
                vadds = []
                for fg in range(6):
                    pb = fg % 2

                    def mkq(e, fg=fg, pb=pb):
                        last = None
                        for kd in range(8):
                            last = e.matmul(psQ[pb][:, 0:512],
                                            xnT_t[:, kd * 128:(kd + 1) * 128],
                                            wq_all[:, kd * 3072 + fg * 512:kd * 3072 + (fg + 1) * 512],
                                            start=(kd == 0), stop=(kd == 7))
                        return last
                    deps = [trvs[-1], (ldg, 16), (ld, 64)]
                    if fg >= 2:
                        deps.append(vadds[fg - 2])
                    t_m = q.step("T", deps, mkq)
                    v_a = q.step("V", [t_m], (lambda e, fg=fg, pb=pb: e.op(
                        "tensor_tensor", out=qkv_t[:, fg * 512:(fg + 1) * 512],
                        in0=psQ[pb][:, 0:512], in1=bqb[:, fg * 512:(fg + 1) * 512], op=ALU.add)))
                    vadds.append(v_a)

                def mkl(e):
                    last = None
                    for kd in range(8):
                        last = e.matmul(psL3[:, 0:8], xnT_t[:, kd * 128:(kd + 1) * 128],
                                        wsw_sb[:, kd * 8:(kd + 1) * 8],
                                        start=(kd == 0), stop=(kd == 7))
                    return last
                depl = [trvs[-1], (ldg2, 16)] + ([last_mmL] if last_mmL is not None else [])
                t_l = q.step("T", depl, mkl)
                last_mmL = t_l
                last_lgv = q.step("V", [t_l, (ld, 64)], (lambda e, j=j: e.op(
                    "tensor_tensor", out=lg2d[:, j * 8:(j + 1) * 8], in0=psL3[:, 0:8],
                    in1=bswb[:, :], op=ALU.add)))

                for (off, gam, outb, osem, od) in ((0, qgb, qob, sq_, qr_d),
                                                   (1024, kgb, kob, sk_, kr_d)):
                    qa_e = v3d(qkv_t, 3072, off, 0)
                    qa_o = v3d(qkv_t, 3072, off, 1)
                    rt_e = v3d(rt, 1024, 0, 0)
                    rt_o = v3d(rt, 1024, 0, 1)
                    tp_e = v3d(tmp, 1024, 0, 0)
                    cosv = vcs(cosb[p])
                    sinv = vcs(sinb[p])
                    ve1 = q.step("V", [vadds[-1], dco], (lambda e, qa_e=qa_e, rt_e=rt_e, cosv=cosv: e.op(
                        "tensor_tensor", out=rt_e, in0=qa_e, in1=cosv, op=ALU.mult)))
                    ve2 = q.step("V", [ve1, dsi], (lambda e, qa_o=qa_o, tp_e=tp_e, sinv=sinv: e.op(
                        "tensor_tensor", out=tp_e, in0=qa_o, in1=sinv, op=ALU.mult)))
                    ve3 = q.step("V", [ve2], (lambda e, rt_e=rt_e, tp_e=tp_e: e.op(
                        "tensor_tensor", out=rt_e, in0=rt_e, in1=tp_e, op=ALU.subtract)))
                    vo1 = q.step("V", [ve3], (lambda e, qa_e=qa_e, rt_o=rt_o, sinv=sinv: e.op(
                        "tensor_tensor", out=rt_o, in0=qa_e, in1=sinv, op=ALU.mult)))
                    vo2 = q.step("V", [vo1], (lambda e, qa_o=qa_o, tp_e=tp_e, cosv=cosv: e.op(
                        "tensor_tensor", out=tp_e, in0=qa_o, in1=cosv, op=ALU.mult)))
                    vo3 = q.step("V", [vo2], (lambda e, rt_o=rt_o, tp_e=tp_e: e.op(
                        "tensor_tensor", out=rt_o, in0=rt_o, in1=tp_e, op=ALU.add)))
                    last_cs_rd[j] = vo2
                    vs1 = q.step("V", [vo3], lambda e: e.op(
                        "tensor_tensor", out=sqr[:, :], in0=rt[:, :], in1=rt[:, :], op=ALU.mult))
                    vs2 = q.step("V", [vs1], lambda e: e.op(
                        "tensor_reduce", out=rs16[:, 0:16],
                        in_=bass.AP(sqr, 0, [[1024, 128], [64, 16], [1, 64]]),
                        axis=AX.X, op=ALU.add))
                    ss1 = q.step("S", [vs2], lambda e: e.op(
                        "activation", out=rs16[:, 0:16], in_=rs16[:, 0:16], func=AF.Sqrt,
                        bias=eps6[:, 0:1], scale=1.0 / 64))
                    vr1 = q.step("V", [ss1], lambda e: e.op(
                        "reciprocal", out=rr16[:, 0:16], in_=rs16[:, 0:16]))
                    vg1 = q.step("V", [vr1], lambda e: e.op(
                        "tensor_tensor", out=rt[:, :], in0=rt[:, :],
                        in1=bass.AP(rr16, 0, [[16, 128], [1, 16], [0, 64]]), op=ALU.mult))
                    stdep = [last_st[(off, j - 2)]] if (off, j - 2) in last_st else []
                    vg2 = q.step("V", [vg1, (ld, 64)] + stdep, (lambda e, gam=gam, outb=outb, p=p: e.op(
                        "tensor_tensor", out=outb[p][:, :], in0=rt[:, :], in1=gam[:, :],
                        op=ALU.mult)))
                    stp = q.step("GD", [vg2], (lambda e, j=j, p=p, outb=outb, od=od: e.dma_start(
                        out=bass.AP(od, j * 128 * D, [[D, 128], [1, D]]), in_=outb[p][:, :])),
                        dma_sem=osem[p])
                    last_st[(off, j)] = stp
                vdep = [last_st[("v", j - 2)]] if ("v", j - 2) in last_st else []
                vcv = q.step("V", [vadds[-1]] + vdep, (lambda e, p=p: e.op(
                    "tensor_copy", out=vob[p][:, :], in_=qkv_t[:, 2048:3072])))
                stv = q.step("GD", [vcv], (lambda e, j=j, p=p: e.dma_start(
                    out=bass.AP(vr_d, j * 128 * D, [[D, 128], [1, D]]), in_=vob[p][:, :])),
                    dma_sem=sv_[p])
                last_st[("v", j)] = stv
            q.step("GD", [last_lgv],
                   lambda e: e.dma_start(out=lg_d.ap(), in_=lg2d[:, :]), dma_sem=slg)
            q.emit(blk)

    with nc.Block() as blk:
        @blk.gpsimd
        def _(e):
            for s_ in (sq_[0], sq_[1], sk_[0], sk_[1], sv_[0], sv_[1], slg):
                if id(s_) in dma_counts:
                    e.wait_ge(s_, dma_counts[id(s_)])

    es.close()
    _BUILD_CACHE["l3a"] = nc
    return nc




def build_l3b():
    if "l3b" in _BUILD_CACHE:
        return _BUILD_CACHE["l3b"]
    H = 16
    DK = 64
    NKT = L // 128
    nc = _nc()
    qT_d = nc.dram_tensor("qTs", [D, TC], F32, kind="ExternalInput")
    kT_d = nc.dram_tensor("kTb", [D, L], F32, kind="ExternalInput")
    vx_d = nc.dram_tensor("vextb", [L, H * 65], F32, kind="ExternalInput")
    oT_d = nc.dram_tensor("oTs", [D, TC], F32, kind="ExternalOutput")

    es, sb, sem, VE, SC, GP = _mk_ctx(nc)
    kb = sb("kb3", [64, L], BF16)
    k2b = sb("k2b3", [64, L], BF16)
    qb = sb("qb3", [64, TC], BF16)
    vb = sb("vb3", [128, NKT * 65], BF16)
    expS = [sb(f"expS{p}", [128, 512], BF16) for p in range(2)]
    dsb = sb("dsb3", [128, 512])
    dbb = sb("dbb3", [128, 512])
    ob = [sb(f"ob3{p}", [64, 512]) for p in range(2)]
    nmx = sb("nmx3", [1, 16])
    kqm = sb("kqm3", [1, 8])
    nbb = sb("nbb3", [128, 1])
    ones_sb = sb("ones3", [128, 128])
    ones_bf = sb("ones3bf", [128, 1], BF16)

    lk = sem("lk")
    lq = sem("lq")
    lv = sem("lv")
    so = [sem("soa"), sem("sob")]
    spp = sem("spp")
    pe_count = [0]
    dma_counts = {}

    with nc.Block() as blk:
        @blk.gpsimd
        def _(e):
            GP.op("memset", ones_sb[:, :], 1.0)
            GP.op("memset", ones_bf[:, :], 1.0)

    with (nc.psum_tensor("psSa", [128, 512], F32) as psSa,
          nc.psum_tensor("psSb", [128, 512], F32) as psSb,
          nc.psum_tensor("psO3", [128, 512], F32) as psO,
          nc.psum_tensor("psB3", [128, 512], F32) as psB2,
          nc.psum_tensor("psNa", [128, 512], F32) as psNa,
          nc.psum_tensor("psNb", [128, 512], F32) as psNb,
          nc.psum_tensor("psBB", [128, 1], F32) as psBB):
        psS = [psSa, psSb]
        psN = [psNa, psNb]
        with nc.Block() as blk:
            q = Seq(nc, VE, SC, GP, spp, pe_count, dma_counts)
            last_k_rd = []
            last_q_rd = []
            last_v_rd = []
            last_exp = []
            last_st = {}
            for h in range(H):
                dk = q.step("GD", last_k_rd, (lambda e, h=h: e.dma_start(
                    out=kb[:, :], in_=bass.AP(kT_d, h * 64 * L, [[L, 64], [1, L]]))),
                    dma_sem=lk)
                dq = q.step("GD", last_q_rd, (lambda e, h=h: e.dma_start(
                    out=qb[:, :], in_=bass.AP(qT_d, h * 64 * TC, [[TC, 64], [1, TC]]))),
                    dma_sem=lq)
                dv = q.step("GD", last_v_rd, (lambda e, h=h: e.dma_start(
                    out=vb[:, :],
                    in_=bass.AP(vx_d, h * 65, [[H * 65, 128], [128 * H * 65, NKT], [1, 65]]))),
                    dma_sem=lv)
                vk2 = q.step("V", [dk], lambda e: e.op(
                    "tensor_tensor", out=k2b[:, :], in0=kb[:, :], in1=kb[:, :], op=ALU.mult))
                nred = []
                for ch in range(16):
                    pb = ch % 2
                    d = [vk2]
                    if ch >= 2:
                        d.append(nred[ch - 2])
                    t_n = q.step("T", d, (lambda e, ch=ch, pb=pb: e.matmul(
                        psN[pb][0:1, 0:512], ones_bf[0:64, 0:1],
                        k2b[:, ch * 512:(ch + 1) * 512], start=True, stop=True)))
                    v_n = q.step("V", [t_n], (lambda e, ch=ch, pb=pb: e.op(
                        "tensor_reduce", out=nmx[0:1, ch:ch + 1], in_=psN[pb][0:1, 0:512],
                        axis=AX.X, op=ALU.max)))
                    nred.append(v_n)
                v_km = q.step("V", [nred[-1]], lambda e: e.op(
                    "tensor_reduce", out=kqm[0:1, 0:1], in_=nmx[0:1, 0:16],
                    axis=AX.X, op=ALU.max))
                vq2 = q.step("V", [dq, v_km], lambda e: e.op(
                    "tensor_tensor", out=k2b[:, 0:TC], in0=qb[:, :], in1=qb[:, :], op=ALU.mult))
                qred = []
                for ch in range(4):
                    pb = ch % 2
                    d = [vq2]
                    if ch >= 2:
                        d.append(qred[ch - 2])
                    t_n = q.step("T", d, (lambda e, ch=ch, pb=pb: e.matmul(
                        psN[pb][0:1, 0:512], ones_bf[0:64, 0:1],
                        k2b[:, ch * 512:(ch + 1) * 512], start=True, stop=True)))
                    v_n = q.step("V", [t_n], (lambda e, ch=ch, pb=pb: e.op(
                        "tensor_reduce", out=nmx[0:1, ch:ch + 1], in_=psN[pb][0:1, 0:512],
                        axis=AX.X, op=ALU.max)))
                    qred.append(v_n)
                v_qm = q.step("V", [qred[-1]], lambda e: e.op(
                    "tensor_reduce", out=kqm[0:1, 1:2], in_=nmx[0:1, 0:4],
                    axis=AX.X, op=ALU.max))
                v_pr = q.step("V", [v_qm], lambda e: e.op(
                    "tensor_tensor", out=kqm[0:1, 2:3], in0=kqm[0:1, 0:1],
                    in1=kqm[0:1, 1:2], op=ALU.mult))
                s_sq = q.step("S", [v_pr], lambda e: e.op(
                    "activation", out=kqm[0:1, 3:4], in_=kqm[0:1, 2:3], func=AF.Sqrt))
                s_ng = q.step("S", [s_sq], lambda e: e.op(
                    "mul", kqm[0:1, 4:5], kqm[0:1, 3:4], -0.125))
                t_bb = q.step("T", [s_ng] + last_exp, lambda e: e.matmul(
                    psBB[0:128, 0:1], ones_sb[0:1, 0:128], kqm[0:1, 4:5],
                    start=True, stop=True))
                v_bb = q.step("V", [t_bb], lambda e: e.op(
                    "tensor_copy", out=nbb[:, :], in_=psBB[0:128, 0:1]))
                last_av = None
                last_psO_rd = []
                last_dbb = []
                for qg in range(4):
                    sexp = {}
                    avs = {}
                    for kt in range(NKT):
                        pb = kt % 2
                        d = [v_bb]
                        if kt >= 2:
                            d.append(sexp[kt - 2])
                        t_s = q.step("T", d + (last_psO_rd if kt == 0 else []),
                                     (lambda e, kt=kt, qg=qg, pb=pb: e.matmul(
                            psS[pb][:, 0:512], kb[:, kt * 128:(kt + 1) * 128],
                            qb[:, qg * 512:(qg + 1) * 512], start=True, stop=True)))
                        d2 = [t_s]
                        if kt >= 2:
                            d2.append(avs[kt - 2])
                        s_e = q.step("S", d2, (lambda e, kt=kt, pb=pb: e.op(
                            "activation", out=expS[pb][:, :], in_=psS[pb][:, 0:512],
                            func=AF.Exp, scale=0.125, bias=nbb[:, 0:1])))
                        sexp[kt] = s_e
                        d3 = [s_e]
                        if kt == 0:
                            d3 += last_psO_rd
                            if qg == 0:
                                d3.append(dv)
                        t_a = q.step("T", d3, (lambda e, kt=kt, pb=pb: e.matmul(
                            psO[0:65, 0:512], vb[:, kt * 65:(kt + 1) * 65], expS[pb][:, :],
                            start=(kt == 0), stop=(kt == NKT - 1))))
                        avs[kt] = t_a
                        last_av = t_a
                    v_rc = q.step("V", [last_av], lambda e: e.op(
                        "reciprocal", out=dsb[64:65, 0:512], in_=psO[64:65, 0:512]))
                    t_b2 = q.step("T", [v_rc] + last_dbb, lambda e: e.matmul(
                        psB2[0:64, 0:512], ones_sb[64:65, 0:64], dsb[64:65, 0:512],
                        start=True, stop=True))
                    v_db = q.step("V", [t_b2], lambda e: e.op(
                        "tensor_copy", out=dbb[0:64, 0:512], in_=psB2[0:64, 0:512]))
                    last_dbb = [v_db]
                    p = qg % 2
                    stdep = []
                    if ("st", h, qg - 2) in dma_counts:
                        pass
                    prev_st = getattr(build_l3b, "_dummy", None)
                    v_ml = q.step("V", [v_db] + ([last_st[(h, qg - 2)]] if (h, qg - 2) in last_st else ([last_st[(h - 1, qg + 2)]] if (h - 1, qg + 2) in last_st else [])),
                                  (lambda e, p=p: e.op(
                        "tensor_tensor", out=ob[p][:, :], in0=psO[0:64, 0:512],
                        in1=dbb[0:64, 0:512], op=ALU.mult)))
                    last_psO_rd = [v_ml]
                    stp = q.step("GD", [v_ml], (lambda e, h=h, qg=qg, p=p: e.dma_start(
                        out=bass.AP(oT_d, h * 64 * TC + qg * 512, [[TC, 64], [1, 512]]),
                        in_=ob[p][:, :])), dma_sem=so[p])
                    last_st[(h, qg)] = stp
                last_k_rd = [sexp[NKT - 1], avs[NKT - 1]]
                last_k_rd = [avs[NKT - 1]]
                last_q_rd = [avs[NKT - 1]]
                last_v_rd = [avs[NKT - 1]]
                last_exp = [sexp[NKT - 1]]
            q.emit(blk)

    with nc.Block() as blk:
        @blk.gpsimd
        def _(e):
            for s_ in (so[0], so[1]):
                if id(s_) in dma_counts:
                    e.wait_ge(s_, dma_counts[id(s_)])

    es.close()
    _BUILD_CACHE["l3b"] = nc
    return nc




def build_l3c():
    if "l3c" in _BUILD_CACHE:
        return _BUILD_CACHE["l3c"]
    FF = 4096
    NDC = D // 128
    NHG = FF // 512
    NHC = FF // 128
    nc = _nc()
    x_d = nc.dram_tensor("x", [TC, D], F32, kind="ExternalInput")
    oT_d = nc.dram_tensor("oTall", [D, TC], F32, kind="ExternalInput")
    wo_d = nc.dram_tensor("wo", [D, D], F32, kind="ExternalInput")
    bo_d = nc.dram_tensor("bout", [D], F32, kind="ExternalInput")
    lgall_d = nc.dram_tensor("lgall2d", [128, 128 * 8], F32, kind="ExternalInput")
    lgown_d = nc.dram_tensor("lgown2d", [128, NT * 8], F32, kind="ExternalInput")
    mcpre_d = nc.dram_tensor("mcpre", [T], F32, kind="ExternalInput")
    w1_d = nc.dram_tensor("w1", [8, D, FF], F32, kind="ExternalInput")
    b1_d = nc.dram_tensor("b1", [8, FF], F32, kind="ExternalInput")
    w2_d = nc.dram_tensor("w2", [8, FF, D], F32, kind="ExternalInput")
    b2_d = nc.dram_tensor("b2", [8, D], F32, kind="ExternalInput")
    x4_d = nc.dram_tensor("x4", [TC, D], F32, kind="ExternalOutput")

    es, sb, sem, VE, SC, GP = _mk_ctx(nc)
    xn_bf = sb("xn_bf", [128, NDC * TC], BF16)
    xn_tok = sb("xn_tok", [128, NT * D], BF16)
    acc = sb("acc", [128, NT * D], BF16)
    lgall = sb("lgall", [128, 128 * 8])
    lgown = sb("lgown", [128, NT * 8])
    mxg = sb("mxg", [128, 128])
    mxo = sb("mxo", [128, NT])
    keepcol = sb("keepcol", [128, NT])
    invk = sb("invk", [128, NT])
    ohk = sb("ohk", [128, NT * 8])
    ohtile_bf = sb("ohtile_bf", [128, 8], BF16)
    ident_bf = sb("ident_bf", [128, 128], BF16)
    eps_t = sb("eps_t", [128, 1])
    prefix8 = sb("prefix8", [8, 2])
    pm8 = sb("pm8", [8, 512])

    ld = sem("ld")
    ldm = sem("ldm")
    lwe = sem("lwe")
    lbc = sem("lbc")
    lht = sem("lht")
    lhtb = sem("lhtb")
    lto = sem("lto")
    lx = [sem("lx0"), sem("lx1")]
    sst = [sem("sst0"), sem("sst1")]
    spp = sem("spp")
    pe_count = [0]
    dma_counts = {}

    with nc.Block() as blk:
        @blk.sync
        def _(e):
            e.dma_start(out=lgall[:, :], in_=lgall_d.ap()).then_inc(ld, 16)
            e.dma_start(out=lgown[:, :], in_=lgown_d.ap()).then_inc(ld, 16)

        @blk.gpsimd
        def _(e):
            _build_ident(GP, ident_bf)
            GP.op("memset", eps_t[:, :], 1e-5)
            GP.op("memset", acc[:, :], 0.0)

    with (nc.sbuf_tensor("xnt3c", [128, D], F32) as xn_t,
          nc.sbuf_tensor("xnc3c", [128, D], BF16) as xnc,
          nc.sbuf_tensor("sqr3c", [128, D], F32) as sqr,
          nc.sbuf_tensor("lnt3c", [128, 8], F32) as lnt,
          nc.sbuf_tensor("x3t0c", [128, D], F32) as x3t0,
          nc.sbuf_tensor("x3t1c", [128, D], F32) as x3t1,
          nc.psum_tensor("trYa", [128, 128], BF16) as trYa,
          nc.psum_tensor("trYb", [128, 128], BF16) as trYb):
        x3t = [x3t0, x3t1]
        with nc.Block() as blk:
            q = Seq(nc, VE, SC, GP, spp, pe_count, dma_counts)
            last_x_rd = {}
            last_tr = [None, None]
            trcnt = [0]
            for j in range(NT):
                p = j % 2
                xdeps = [last_x_rd[j - 2]] if j >= 2 else []
                dx = q.step("SD", xdeps, (lambda e, j=j, p=p: e.dma_start(
                    out=x3t[p][:, :], in_=bass.AP(x_d, j * 128 * D, [[D, 128], [1, D]]))),
                    dma_sem=lx[p])
                v1 = q.step("V", [dx], (lambda e, p=p: e.op(
                    "reduce_sum", out=lnt[:, 0:1], in_=x3t[p][:, :], axis=AX.X)))
                s1 = q.step("S", [v1], lambda e: e.op("mul", lnt[:, 0:1], lnt[:, 0:1], 1.0 / D))
                v2 = q.step("V", [s1], (lambda e, p=p: e.op(
                    "tensor_scalar", out=xn_t[:, :], in0=x3t[p][:, :], scalar1=lnt[:, 0:1],
                    scalar2=None, op0=ALU.subtract)))
                last_x_rd[j] = v2
                v3 = q.step("V", [v2], lambda e: e.op(
                    "tensor_tensor", out=sqr[:, :], in0=xn_t[:, :], in1=xn_t[:, :], op=ALU.mult))
                v4 = q.step("V", [v3], lambda e: e.op(
                    "reduce_sum", out=lnt[:, 1:2], in_=sqr[:, :], axis=AX.X))
                s2 = q.step("S", [v4], lambda e: e.op(
                    "activation", out=lnt[:, 2:3], in_=lnt[:, 1:2], func=AF.Sqrt,
                    bias=eps_t[:, 0:1], scale=1.0 / D))
                v5 = q.step("V", [s2], lambda e: e.op(
                    "reciprocal", out=lnt[:, 3:4], in_=lnt[:, 2:3]))
                v6 = q.step("V", [v5], lambda e: e.op(
                    "tensor_scalar", out=xn_t[:, :], in0=xn_t[:, :], scalar1=lnt[:, 3:4],
                    scalar2=None, op0=ALU.mult))
                v7 = q.step("V", [v6], (lambda e, j=j: e.op(
                    "tensor_copy", out=xn_tok[:, j * D:(j + 1) * D], in_=xn_t[:, :])))
                v8 = q.step("V", [v7], lambda e: e.op(
                    "tensor_copy", out=xnc[:, :], in_=xn_t[:, :]))
                for kd in range(NDC):
                    b = trcnt[0] % 2
                    trcnt[0] += 1
                    trp = (trYa, trYb)[b]
                    d = [v8]
                    if last_tr[b] is not None:
                        d.append(last_tr[b])
                    tt = q.step("T", d, (lambda e, kd=kd, trp=trp: e.transpose(
                        trp[:, 0:128], xnc[:, kd * 128:(kd + 1) * 128], ident_bf[:, :])))
                    vv = q.step("V", [tt], (lambda e, kd=kd, j=j, trp=trp: e.op(
                        "tensor_copy", out=xn_bf[:, kd * TC + j * 128:kd * TC + (j + 1) * 128],
                        in_=trp[:, 0:128])))
                    last_tr[b] = vv
            q.emit(blk)

    with (nc.sbuf_tensor("ohT", [8, T], BF16) as ohT,
          nc.sbuf_tensor("ohTown", [8, TC], BF16) as ohTown,
          nc.sbuf_tensor("postokT", [8, TC], F32) as postokT,
          nc.sbuf_tensor("mcpre8", [8, T], BF16) as mcpre8,
          nc.sbuf_tensor("keeprow", [1, TC], F32) as keeprow,
          nc.sbuf_tensor("keeprow_bf", [1, TC], BF16) as keeprow_bf,
          nc.sbuf_tensor("ones64r", [8, TC], F32) as ones64r,
          nc.psum_tensor("psT8a", [8, 128], BF16) as psT8a,
          nc.psum_tensor("psT8b", [8, 128], BF16) as psT8b,
          nc.psum_tensor("psS", [1, 512], F32) as psS,
          nc.psum_tensor("psKa", [128, 1], BF16) as psKa,
          nc.psum_tensor("psKb", [128, 1], BF16) as psKb):
        with nc.Block() as blk:
            @blk.gpsimd
            def _(e):
                GP.op("memset", ones64r[:, :], 1.0)
                e.dma_start(out=mcpre8[0:8, :],
                            in_=bass.AP(mcpre_d, 0, [[0, 8], [1, T]])).then_inc(ldm, 16)

        with nc.Block() as blk:
            q = Seq(nc, VE, SC, GP, spp, pe_count, dma_counts)
            _emit_top1_routing(nc, q, VE, SC, GP, spp, NT, 2048.0,
                               lgall, lgown, mcpre8, ohT, ohTown, postokT, pm8,
                               mxg, mxo, prefix8, keeprow, keeprow_bf, keepcol, ohk,
                               ohtile_bf, ones64r, (psT8a, psT8b), psS, (psKa, psKb), ident_bf,
                               [(ld, 32), (ldm, 16)])
            q.emit(blk)

    with nc.Block() as blk:
        @blk.vector
        def _(e):
            VE.op("tensor_scalar", out=invk[:, :], in0=keepcol[:, :],
                  scalar1=-1.0, scalar2=1.0, op0=ALU.mult, op1=ALU.add)

    hT_scr = nc.dram_tensor("hT_scr", [NT, NHC * 128 * 128], BF16, kind="Internal")
    with (nc.sbuf_tensor("wbuf", [128, NDC * FF], BF16) as wbuf,
          nc.sbuf_tensor("b1b", [128, FF], BF16) as b1b,
          nc.sbuf_tensor("b2b", [128, D], BF16) as b2b,
          nc.sbuf_tensor("h_bf", [128, FF], BF16) as h_bf,
          nc.sbuf_tensor("hT_j", [128, NHC * 128], BF16) as hT_j,
          nc.psum_tensor("h_ps0", [128, 512], F32) as hps0,
          nc.psum_tensor("h_ps1", [128, 512], F32) as hps1,
          nc.psum_tensor("trhA", [128, 128], BF16) as trhA,
          nc.psum_tensor("trhB", [128, 128], BF16) as trhB,
          nc.psum_tensor("y_ps", [128, D], F32) as y_ps):
        h_ps = [hps0, hps1]
        assert NDC * FF == NHC * D
        for eix in range(8):
            with nc.Block() as blk:
                @blk.gpsimd
                def _(e, eix=eix):
                    e.dma_start(out=wbuf[:, :],
                                in_=bass.AP(w1_d, eix * D * FF, [[FF, 128], [128 * FF, NDC], [1, FF]])
                                ).then_inc(lwe, 16)
                    e.dma_start(out=b1b[:, :], in_=_bcast_row_ap(b1_d, FF, eix * FF)).then_inc(lbc, 16)

            for j in range(NT):
                with nc.Block() as blk:
                    q = Seq(nc, VE, SC, GP, spp, pe_count, dma_counts)
                    wdep = [(lwe, 32 * eix + 16), (lbc, 32 * eix + 16)]
                    vh = {}
                    for hg in range(NHG):
                        pp = hg % 2
                        deps = list(wdep)
                        if hg >= 2:
                            deps.append(vh[hg - 2])

                        def mk_h(e, hg=hg, pp=pp, j=j):
                            last = None
                            for dc in range(NDC):
                                last = e.matmul(h_ps[pp][:, :],
                                                xn_bf[:, dc * TC + j * 128:dc * TC + (j + 1) * 128],
                                                wbuf[:, dc * FF + hg * 512:dc * FF + (hg + 1) * 512],
                                                start=(dc == 0), stop=(dc == NDC - 1))
                            return last
                        th = q.step("T", deps, mk_h)
                        vh[hg] = q.step("V", [th], (lambda e, hg=hg, pp=pp: e.op(
                            "tensor_tensor", out=h_bf[:, hg * 512:(hg + 1) * 512],
                            in0=h_ps[pp][:, :], in1=b1b[:, hg * 512:(hg + 1) * 512], op=ALU.add)))
                    s_gel = q.step("S", [vh[hg] for hg in range(NHG)], lambda e: e.op(
                        "activation", out=h_bf[:, :], in_=h_bf[:, :], func=AF.Gelu))
                    vt = {}
                    nstores = eix * NT + j
                    stdep = [(lht, 16 * nstores)] if nstores >= 1 else []
                    for hc in range(NHC):
                        trh = trhA if hc % 2 == 0 else trhB
                        deps = [s_gel]
                        if hc >= 2:
                            deps.append(vt[hc - 2])
                        tt = q.step("T", deps, (lambda e, hc=hc, trh=trh: e.transpose(
                            trh[:, 0:128], h_bf[:, hc * 128:(hc + 1) * 128],
                            ident_bf[:, :])))
                        vt[hc] = q.step("V", [tt] + (stdep if hc < 2 else []),
                                        (lambda e, hc=hc, trh=trh: e.op(
                            "tensor_copy", out=hT_j[:, hc * 128:(hc + 1) * 128],
                            in_=trh[:, 0:128])))
                    q.step("GD", [vt[hc] for hc in range(NHC)],
                           (lambda e, j=j: e.dma_start(
                               out=bass.AP(hT_scr, j * NHC * 128 * 128,
                                           [[NHC * 128, 128], [1, NHC * 128]]),
                               in_=hT_j[:, :])), dma_sem=lht)
                    q.emit(blk)

            with nc.Block() as blk:
                @blk.gpsimd
                def _(e, eix=eix):
                    e.wait_ge(lht, dma_counts[id(lht)])
                    e.dma_start(out=wbuf[:, :],
                                in_=bass.AP(w2_d, eix * FF * D, [[D, 128], [128 * D, NHC], [1, D]])
                                ).then_inc(lwe, 16)
                    e.dma_start(out=b2b[:, :], in_=_bcast_row_ap(b2_d, D, eix * D)).then_inc(lbc, 16)

            for j in range(NT):
                with nc.Block() as blk:
                    q = Seq(nc, VE, SC, GP, spp, pe_count, dma_counts)
                    dh = q.step("GD", [], (lambda e, j=j: e.dma_start(
                        out=hT_j[:, :],
                        in_=bass.AP(hT_scr, j * NHC * 128 * 128,
                                    [[NHC * 128, 128], [1, NHC * 128]]))), dma_sem=lhtb)
                    wdep = [(lwe, 32 * (eix + 1)), (lbc, 32 * eix + 32), dh]

                    def mk_y(e, j=j):
                        last = None
                        for hc in range(NHC):
                            e.matmul(y_ps[:, 0:512], hT_j[:, hc * 128:(hc + 1) * 128],
                                     wbuf[:, hc * D:hc * D + 512],
                                     start=(hc == 0), stop=(hc == NHC - 1))
                            last = e.matmul(y_ps[:, 512:1024], hT_j[:, hc * 128:(hc + 1) * 128],
                                            wbuf[:, hc * D + 512:(hc + 1) * D],
                                            start=(hc == 0), stop=(hc == NHC - 1))
                        return last
                    t_y = q.step("T", wdep, mk_y)
                    v_a1 = q.step("V", [t_y], (lambda e, j=j, eix=eix: e.op(
                        "scalar_tensor_tensor", out=acc[:, j * D:(j + 1) * D],
                        in0=y_ps[:, :], scalar=ohk[:, j * 8 + eix:j * 8 + eix + 1],
                        in1=acc[:, j * D:(j + 1) * D], op0=ALU.mult, op1=ALU.add)))
                    q.step("V", [v_a1], (lambda e, j=j, eix=eix: e.op(
                        "scalar_tensor_tensor", out=acc[:, j * D:(j + 1) * D],
                        in0=b2b[:, :], scalar=ohk[:, j * 8 + eix:j * 8 + eix + 1],
                        in1=acc[:, j * D:(j + 1) * D], op0=ALU.mult, op1=ALU.add)))
                    q.emit(blk)

    with (nc.sbuf_tensor("oT_bf", [128, NDC * TC], BF16) as oT_bf,
          nc.sbuf_tensor("wo_sb", [128, NDC * D], BF16) as wo_sb,
          nc.sbuf_tensor("bob", [128, D], F32) as bob,
          nc.sbuf_tensor("fin", [128, D], F32) as fin,
          nc.sbuf_tensor("gsil", [128, D], F32) as gsil,
          nc.sbuf_tensor("sres", [128, D], F32) as sres,
          nc.sbuf_tensor("x1t0", [128, D], F32) as x1t0,
          nc.sbuf_tensor("x1t1", [128, D], F32) as x1t1,
          nc.psum_tensor("s_ps", [128, D], F32) as s_ps):
        x1t = [x1t0, x1t1]
        with nc.Block() as blk:
            @blk.gpsimd
            def _(e):
                e.dma_start(out=oT_bf[:, :],
                            in_=bass.AP(oT_d, 0, [[TC, 128], [128 * TC, NDC], [1, TC]])
                            ).then_inc(lto, 16)
                e.dma_start(out=wo_sb[:, :],
                            in_=bass.AP(wo_d, 0, [[D, 128], [128 * D, NDC], [1, D]])
                            ).then_inc(lto, 16)

            @blk.sync
            def _(e):
                e.dma_start(out=bob[:, :], in_=_bcast_row_ap(bo_d, D)).then_inc(ld, 16)

        with nc.Block() as blk:
            q = Seq(nc, VE, SC, GP, spp, pe_count, dma_counts)
            ldp = [(lto, 32), (ld, 48)]
            prev_out = {}
            st_steps = {}
            for j in range(NT):
                pp = j % 2
                xdeps = []
                if j >= 2:
                    xdeps = [prev_out[j - 2], st_steps[j - 2]]
                dx = q.step("SD", xdeps, (lambda e, j=j, pp=pp: e.dma_start(
                    out=x1t[pp][:, :], in_=_ap3_tokmaj(x_d, 1, D, offset=j * 128 * D))),
                    dma_sem=lx[pp])
                v_f = q.step("V", [], (lambda e, j=j: e.op(
                    "scalar_tensor_tensor", out=fin[:, :],
                    in0=xn_tok[:, j * D:(j + 1) * D], scalar=invk[:, j:j + 1],
                    in1=acc[:, j * D:(j + 1) * D], op0=ALU.mult, op1=ALU.add)))
                s_sil = q.step("S", [v_f], lambda e: e.op(
                    "activation", out=gsil[:, :], in_=fin[:, :], func=AF.Silu))

                def mk_s(e, j=j):
                    last = None
                    for hv in range(NDC):
                        e.matmul(s_ps[:, 0:512], oT_bf[:, hv * TC + j * 128:hv * TC + (j + 1) * 128],
                                 wo_sb[:, hv * D:hv * D + 512],
                                 start=(hv == 0), stop=(hv == NDC - 1))
                        last = e.matmul(s_ps[:, 512:1024],
                                        oT_bf[:, hv * TC + j * 128:hv * TC + (j + 1) * 128],
                                        wo_sb[:, hv * D + 512:(hv + 1) * D],
                                        start=(hv == 0), stop=(hv == NDC - 1))
                    return last
                sdeps = ldp + ([prev_out[j - 1]] if j >= 1 else [])
                t_s = q.step("T", sdeps, mk_s)
                v_b = q.step("V", [t_s], lambda e: e.op(
                    "tensor_tensor", out=sres[:, :], in0=s_ps[:, :], in1=bob[:, :], op=ALU.add))
                prev_out[j] = v_b
                v_sg = q.step("V", [v_b, s_sil], lambda e: e.op(
                    "tensor_tensor", out=gsil[:, :], in0=gsil[:, :], in1=sres[:, :], op=ALU.mult))
                v_o = q.step("V", [v_sg, dx], (lambda e, pp=pp: e.op(
                    "tensor_tensor", out=x1t[pp][:, :], in0=x1t[pp][:, :],
                    in1=gsil[:, :], op=ALU.add)))
                st_steps[j] = q.step("GD", [v_o], (lambda e, j=j, pp=pp: e.dma_start(
                    out=_ap3_tokmaj(x4_d, 1, D, offset=j * 128 * D), in_=x1t[pp][:, :])),
                    dma_sem=sst[pp])
            q.emit(blk)

    with nc.Block() as blk:
        @blk.gpsimd
        def _(e):
            e.wait_ge(sst[0], dma_counts[id(sst[0])])
            e.wait_ge(sst[1], dma_counts[id(sst[1])])

    es.close()
    _BUILD_CACHE["l3c"] = nc
    return nc



def _run(nc, in_maps):
    return run_bass_kernel_spmd(nc, in_maps, core_ids=CORE_IDS)


def run_l0(ins):
    x = np.ascontiguousarray(ins["L0_attn_x"].reshape(T, D))
    nc = build_l0a()
    im = [{"x": x[c * TC:(c + 1) * TC],
           "rwv": np.ascontiguousarray(ins["L0_attn_router_w"][0]),
           "rbv": np.ascontiguousarray(ins["L0_attn_router_b"])} for c in range(NC)]
    res = _run(nc, im)
    xn = np.concatenate([r["xn"] for r in res.results], axis=0)
    rw = np.concatenate([r["rw2d"].T.reshape(-1) for r in res.results])

    xn3 = xn.reshape(B, L, D)
    xnrev = np.ascontiguousarray(xn3[:, ::-1, :].reshape(T, D))
    rwb = rw.reshape(B, L)
    nc2 = build_l0b()
    im2 = []
    for c in range(NC):
        bsel = c // (NC // B)
        im2.append({
            "x": x[c * TC:(c + 1) * TC],
            "xn": xn[c * TC:(c + 1) * TC],
            "xnrev": xnrev[c * TC:(c + 1) * TC],
            "rwb": np.ascontiguousarray(rwb[bsel]),
            "rwown2d": np.ascontiguousarray(rw[c * TC:(c + 1) * TC].reshape(NT, 128).T),
            "wg": np.ascontiguousarray(ins["L0_ffn_w_gate"]),
            "w1": np.ascontiguousarray(ins["L0_ffn_w1"]),
            "w3": np.ascontiguousarray(ins["L0_ffn_w3"]),
            "w2": np.ascontiguousarray(ins["L0_ffn_w2"]),
        })
    res2 = _run(nc2, im2)
    x1 = np.concatenate([r["x1"] for r in res2.results], axis=0)
    return x1, xn, rw



def run_l1(ins, x1):
    H, DK = 16, 64
    nc = build_l1a()
    im = []
    for c in range(NC):
        im.append({
            "x": np.ascontiguousarray(x1[c * TC:(c + 1) * TC]),
            "wq": np.ascontiguousarray(ins["L1_attn_W_q"]),
            "wk": np.ascontiguousarray(ins["L1_attn_W_k"]),
            "wv": np.ascontiguousarray(ins["L1_attn_W_v"]),
            "wgk": np.ascontiguousarray(ins["L1_attn_Wg_K"]),
            "wgc": np.ascontiguousarray(ins["L1_attn_Wg_C"]),
            "wproj": np.ascontiguousarray(ins["L1_ffn_W_proj"]),
            "bgk2d": np.ascontiguousarray(ins["L1_attn_bg_K"].reshape(8, 128).T),
            "bgc2d": np.ascontiguousarray(ins["L1_attn_bg_C"].reshape(8, 128).T),
            "bproj2d": np.ascontiguousarray(ins["L1_ffn_b_proj"].reshape(8, 128).T),
            "wgate": np.ascontiguousarray(ins["L1_ffn_W_gate"]),
        })
    res = _run(nc, im)
    full = {}
    for nm in ["qT", "kT", "vT", "aKT", "aCT", "xpT"]:
        full[nm] = np.concatenate([res.results[c][nm] for c in range(NC)], axis=1)
    lg = np.concatenate([res.results[c]["logits2d"].reshape(128, NT, 8).transpose(1, 0, 2).reshape(TC, 8)
                         for c in range(NC)], axis=0)

    nc2 = build_l1b()
    im2 = []
    for c in range(NC):
        packs = []
        for i in range(4):
            bhg = c * 4 + i
            b, h = bhg // H, bhg % H
            rows = slice(h * DK, (h + 1) * DK)
            cols = slice(b * L, (b + 1) * L)
            arrs = [full[nm][rows, cols] for nm in ["qT", "kT", "vT", "aKT", "aCT"]]
            packs.append(np.stack([np.ascontiguousarray(a) for a in arrs], 0))
        im2.append({"qkvab": np.ascontiguousarray(np.stack(packs, 0))})
    res2 = _run(nc2, im2)
    oT = np.zeros((D, T), np.float32)
    for c in range(NC):
        o4 = res2.results[c]["oT"].reshape(4, L, DK)
        for i in range(4):
            bhg = c * 4 + i
            b, h = bhg // H, bhg % H
            oT[h * DK:(h + 1) * DK, b * L:(b + 1) * L] = o4[i].T

    nc3 = build_l1c()
    lgall2d = np.ascontiguousarray(lg.reshape(128, 128, 8).transpose(1, 0, 2).reshape(128, 128 * 8))
    im3 = []
    for c in range(NC):
        sl = slice(c * TC, (c + 1) * TC)
        lgown = lg[sl]
        mcpre = np.zeros(T, np.float32)
        mcpre[:c * TC] = 1.0
        im3.append({
            "x": np.ascontiguousarray(x1[sl]),
            "oTall": np.ascontiguousarray(oT[:, sl]),
            "wo": np.ascontiguousarray(ins["L1_attn_W_o"]),
            "xpT": np.ascontiguousarray(full["xpT"][:, sl]),
            "lgall2d": lgall2d,
            "lgown2d": np.ascontiguousarray(lgown.reshape(NT, 128, 8).transpose(1, 0, 2).reshape(128, NT * 8)),
            "mcpre": mcpre,
            "w1": ins["L1_ffn_W1"], "b1": ins["L1_ffn_b1"],
            "g1": ins["L1_ffn_ln1_g"], "bb1": ins["L1_ffn_ln1_b"],
            "w2": ins["L1_ffn_W2"], "b2": ins["L1_ffn_b2"],
            "lng": ins["L1_ffn_ln_g"], "lnb": ins["L1_ffn_ln_b"],
        })
    res3 = _run(nc3, im3)
    x2 = np.concatenate([res3.results[c]["x2"] for c in range(NC)], axis=0)
    return x2


def run_l2(ins, x2):
    DI = 2048
    nc = build_l2a()
    cw = ins["L2_attn_conv_w"][:, 0, :]
    cw2d = np.zeros((128, 64), np.float32)
    cb2d = np.zeros((128, 16), np.float32)
    for ch in range(16):
        for j in range(4):
            cw2d[:, ch * 4 + j] = cw[ch * 128:(ch + 1) * 128, j]
        cb2d[:, ch] = ins["L2_attn_conv_b"][ch * 128:(ch + 1) * 128]
    ipT = np.ascontiguousarray(ins["L2_attn_in_proj_w"].T)
    xpwT = np.ascontiguousarray(ins["L2_attn_x_proj_w"].T)
    wg = ins["L2_ffn_w_gate"]; wu = ins["L2_ffn_w_up"]; wd = ins["L2_ffn_w_down"]
    im = []
    for c in range(NC):
        xh = np.zeros((TC + 128, D), np.float32)
        xh[128:] = x2[c * TC:(c + 1) * TC]
        if c % 4:
            xh[:128] = x2[c * TC - 128:c * TC]
        wgs = np.zeros((384, D), np.float32); wgs[:352] = wg[c * 352:(c + 1) * 352]
        wus = np.zeros((384, D), np.float32); wus[:352] = wu[c * 352:(c + 1) * 352]
        im.append({
            "xh": xh, "nw": ins["L2_attn_norm_w"], "nb": ins["L2_attn_norm_b"],
            "ipT": ipT, "cw2d": cw2d, "cb2d": cb2d, "xpwT": xpwT,
            "halom": np.array([0.0 if c % 4 == 0 else 1.0], np.float32),
            "wgs": wgs, "wus": wus,
            "wds": np.ascontiguousarray(wd[c * 128:(c + 1) * 128]).astype(np.float32),
        })
    res = _run(nc, im)
    zT = [res.results[c]["zT"] for c in range(NC)]
    xcT = [res.results[c]["xcT"] for c in range(NC)]
    xp_full = np.concatenate([np.asarray(res.results[c]["xp33"], np.float32)
                              for c in range(NC)], axis=1)
    wgq = np.concatenate([np.asarray(res.results[c]["wgq"], np.float32)[:352]
                          for c in range(NC)], axis=0)
    wuq = np.concatenate([np.asarray(res.results[c]["wuq"], np.float32)[:352]
                          for c in range(NC)], axis=0)
    wdq = np.concatenate([np.asarray(res.results[c]["wdq"], np.float32)
                          for c in range(NC)], axis=0)

    nc2 = build_l2b()
    xc_fm = np.concatenate([np.asarray(x, np.float32) for x in xcT], axis=1)
    A = -np.exp(ins["L2_attn_A_log"])
    xp0 = np.ascontiguousarray(xp_full[0].reshape(B, L))
    bpT = np.ascontiguousarray(xp_full[1:17].reshape(16, B, L).transpose(1, 0, 2))
    cpT = np.ascontiguousarray(xp_full[17:33].reshape(16, B, L).transpose(1, 0, 2))
    im2 = []
    for c in range(NC):
        xcg = np.zeros((4, 128, L), np.float32)
        dtw2d = np.zeros((128, 2), np.float32); dtb2d = np.zeros((128, 2), np.float32)
        A2d = np.zeros((128, 32), np.float32); dsk2d = np.zeros((128, 2), np.float32)
        for g in range(4):
            bb, cg = g // 2, g % 2
            xcg[g] = xc_fm[c * 256 + cg * 128:c * 256 + (cg + 1) * 128, bb * L:(bb + 1) * L]
        for cg in range(2):
            sl = slice(c * 256 + cg * 128, c * 256 + (cg + 1) * 128)
            dtw2d[:, cg] = ins["L2_attn_dt_w"][sl, 0]
            dtb2d[:, cg] = ins["L2_attn_dt_b"][sl]
            A2d[:, cg * 16:(cg + 1) * 16] = A[sl]
            dsk2d[:, cg] = ins["L2_attn_D_skip"][sl]
        im2.append({"xcg": xcg.astype(ml_dtypes.bfloat16), "xp0": xp0, "bpT": bpT, "cpT": cpT,
                    "dtw2d": dtw2d, "dtb2d": dtb2d, "A2d": A2d, "dsk2d": dsk2d})
    res2 = _run(nc2, im2)

    yall = np.zeros((B, 2048, L), np.float32)
    for c in range(NC):
        y4 = np.asarray(res2.results[c]["yT"], np.float32)
        for g in range(4):
            bb, cg = g // 2, g % 2
            yall[bb, c * 256 + cg * 128:c * 256 + (cg + 1) * 128, :] = y4[g]

    nc3 = build_l2c()
    opT = np.ascontiguousarray(ins["L2_attn_out_proj_w"].T)
    wgqT = np.ascontiguousarray(wgq.T).astype(ml_dtypes.bfloat16)
    wuqT = np.ascontiguousarray(wuq.T).astype(ml_dtypes.bfloat16)
    wdqT = np.ascontiguousarray(wdq.T).astype(ml_dtypes.bfloat16)
    im3 = []
    for c in range(NC):
        b = c // 4
        t0 = (c % 4) * TC
        im3.append({
            "x": np.ascontiguousarray(x2[c * TC:(c + 1) * TC]),
            "yTg": np.ascontiguousarray(yall[b, :, t0:t0 + TC]),
            "zTg": np.ascontiguousarray(np.asarray(zT[c], ml_dtypes.bfloat16)),
            "opT": opT, "wgqT": wgqT, "wuqT": wuqT, "wdqT": wdqT,
        })
    res3 = _run(nc3, im3)
    x3 = np.concatenate([res3.results[c]["x3"] for c in range(NC)], axis=0)
    return x3


def run_l3(ins, x3):
    nc = build_l3a()
    half = 32
    inv_freq = 1.0 / 10000.0 ** (np.arange(half) / half)
    ang = np.arange(L)[:, None] * inv_freq[None, :]
    cost = np.cos(ang).astype(np.float32)
    sint = np.sin(ang).astype(np.float32)
    wqkvT = np.ascontiguousarray(ins["L3_attn_w_qkv"].T)
    im = []
    for c in range(NC):
        l0 = (c % 4) * TC
        im.append({
            "x": np.ascontiguousarray(x3[c * TC:(c + 1) * TC]),
            "wqkvT": wqkvT,
            "bqkv": ins["L3_attn_b_qkv"],
            "wsw": np.ascontiguousarray(ins["L3_ffn_W_sw"]),
            "bsw": ins["L3_ffn_b_sw"],
            "cosr": np.ascontiguousarray(cost[l0:l0 + TC]),
            "sinr": np.ascontiguousarray(sint[l0:l0 + TC]),
            "qgam": np.ascontiguousarray(ins["L3_attn_q_gamma"].reshape(-1)),
            "kgam": np.ascontiguousarray(ins["L3_attn_k_gamma"].reshape(-1)),
        })
    res = _run(nc, im)
    qf = np.concatenate([res.results[c]["qr"] for c in range(NC)], axis=0)
    kf = np.concatenate([res.results[c]["kr"] for c in range(NC)], axis=0)
    vf = np.concatenate([res.results[c]["vr"] for c in range(NC)], axis=0)
    lg = np.concatenate([np.asarray(res.results[c]["logits2d"], np.float32)
                         .reshape(128, NT, 8).transpose(1, 0, 2).reshape(TC, 8)
                         for c in range(NC)], axis=0)

    nc2 = build_l3b()
    kTb = [np.ascontiguousarray(kf[b * L:(b + 1) * L].T) for b in range(B)]
    veb = []
    for b in range(B):
        ve = np.ones((L, 16 * 65), np.float32)
        for h in range(16):
            ve[:, h * 65:h * 65 + 64] = vf[b * L:(b + 1) * L, h * 64:(h + 1) * 64]
        veb.append(ve)
    im2 = []
    for c in range(NC):
        b = c // 4
        im2.append({
            "qTs": np.ascontiguousarray(qf[c * TC:(c + 1) * TC].T),
            "kTb": kTb[b],
            "vextb": veb[b],
        })
    res2 = _run(nc2, im2)

    nc3 = build_l3c()
    lgall2d = np.ascontiguousarray(lg.reshape(128, 128, 8).transpose(1, 0, 2).reshape(128, 128 * 8))
    woT = np.ascontiguousarray(ins["L3_attn_w_out"].T)
    im3 = []
    for c in range(NC):
        sl = slice(c * TC, (c + 1) * TC)
        lgown = lg[sl]
        mcpre = np.zeros(T, np.float32)
        mcpre[:c * TC] = 1.0
        im3.append({
            "x": np.ascontiguousarray(x3[sl]),
            "oTall": np.asarray(res2.results[c]["oTs"], np.float32),
            "wo": woT,
            "bout": ins["L3_attn_b_out"],
            "lgall2d": lgall2d,
            "lgown2d": np.ascontiguousarray(lgown.reshape(NT, 128, 8).transpose(1, 0, 2).reshape(128, NT * 8)),
            "mcpre": mcpre,
            "w1": ins["L3_ffn_W1"], "b1": ins["L3_ffn_b1"],
            "w2": ins["L3_ffn_W2"], "b2": ins["L3_ffn_b2"],
        })
    res3 = _run(nc3, im3)
    x4 = np.concatenate([res3.results[c]["x4"] for c in range(NC)], axis=0)
    return x4


def kernel(**ins):
    x1, xn0, rw0 = run_l0(ins)
    x2 = run_l1(ins, x1)
    x3 = run_l2(ins, x2)
    x4 = run_l3(ins, x3)
    return np.ascontiguousarray(x4.reshape(B, L, D)).astype(np.float32)


# version 03 (final)
# speedup vs baseline: 1.0566x; 1.0566x over previous
import contextlib
import hashlib
import os
import shutil
import numpy as np
import ml_dtypes
import concourse.bass as bass
import concourse.mybir as mybir
from concourse.bass_utils import run_bass_kernel_spmd

import concourse.bass2jax as _b2j

if not getattr(_b2j, "_neff_cache_installed", False):
    _ORIG_COMPILE = _b2j.compile_bir_kernel
    _NEFF_CACHE_DIR = os.environ.get("BASS_NEFF_CACHE", "/root/.bass_neff_cache")

    def _cached_compile_bir_kernel(ant_bir_str, compile_dir_path, neff_name="model.neff"):
        cpath = None
        try:
            os.makedirs(_NEFF_CACHE_DIR, exist_ok=True)
            key = hashlib.sha256(ant_bir_str).hexdigest()
            cpath = os.path.join(_NEFF_CACHE_DIR, key + ".neff")
            if os.path.exists(cpath):
                out = os.path.join(compile_dir_path, neff_name)
                shutil.copyfile(cpath, out)
                return out
        except Exception:
            cpath = None
        out = _ORIG_COMPILE(ant_bir_str, compile_dir_path, neff_name=neff_name)
        if cpath is not None:
            try:
                shutil.copyfile(out, cpath + ".tmp")
                os.replace(cpath + ".tmp", cpath)
            except Exception:
                pass
        return out

    _b2j.compile_bir_kernel = _cached_compile_bir_kernel
    _b2j._neff_cache_installed = True

F32 = mybir.dt.float32
BF16 = mybir.dt.bfloat16
U8 = mybir.dt.uint8
AF = mybir.ActivationFunctionType
ALU = mybir.AluOpType
AX = mybir.AxisListType

NC = 8
B = 2
L = 8192
D = 1024
T = B * L
TC = T // NC
NT = TC // 128
CORE_IDS = list(range(NC))

_BUILD_CACHE = {}


def _nc():
    return bass.Bass("TRN2", target_bir_lowering=False, debug=False)


class W:

    def __init__(self, nc, name, sem):
        self._nc = nc
        self._name = name
        self.sem = sem
        self.n = 0

    @property
    def e(self):
        return getattr(self._nc, self._name)

    def wait(self, sem, val):
        self.e.wait_ge(sem, val)

    def op(self, name, *a, **k):
        if self.n > 0:
            self.e.wait_ge(self.sem, self.n)
        inst = getattr(self.e, name)(*a, **k)
        inst.then_inc(self.sem, 1)
        self.n += 1
        return self.n


def _ap3_tokmaj(handle, ntile, d, offset=0):
    return bass.AP(handle, offset, [[d, 128], [128 * d, ntile], [1, d]])


def _row_ap(handle, n, offset=0):
    return bass.AP(handle, offset, [[n, 1], [1, n]])


def _build_ident(gp, ident_bf, ident_f32=None):
    gp.op("memset", ident_bf[:, :], 1.0)
    gp.op("affine_select", out=ident_bf[:, :], in_=ident_bf[:, :], pattern=[[-1, 128]],
          compare_op=ALU.is_equal, fill=0.0, base=0, channel_multiplier=1)
    if ident_f32 is not None:
        gp.op("memset", ident_f32[:, :], 1.0)
        gp.op("affine_select", out=ident_f32[:, :], in_=ident_f32[:, :], pattern=[[-1, 128]],
              compare_op=ALU.is_equal, fill=0.0, base=0, channel_multiplier=1)


def _emit_ln(nc, VE, SC, x_sb, xn_sb, lns, ntile, d, normalize=True):
    sq_a, sq_b = lns["sq_a"], lns["sq_b"]
    m_sb, v_sb, sd_sb, rs_sb = lns["m"], lns["v"], lns["sd"], lns["rs"]
    with nc.Block() as blk:
        @blk.vector
        def _(e):
            for j in range(ntile):
                VE.op("reduce_sum", out=m_sb[:, j:j + 1], in_=x_sb[:, j * d:(j + 1) * d], axis=AX.X)
    with nc.Block() as blk:
        @blk.scalar
        def _(e):
            SC.op("mul", m_sb[:, 0:ntile], m_sb[:, 0:ntile], 1.0 / d)
    with nc.Block() as blk:
        @blk.vector
        def _(e):
            for j in range(ntile):
                VE.op("tensor_scalar", out=sq_a[:, :], in0=x_sb[:, j * d:(j + 1) * d],
                      scalar1=m_sb[:, j:j + 1], scalar2=None, op0=ALU.subtract)
                VE.op("tensor_tensor", out=sq_b[:, :], in0=sq_a[:, :], in1=sq_a[:, :], op=ALU.mult)
                VE.op("reduce_sum", out=v_sb[:, j:j + 1], in_=sq_b[:, :], axis=AX.X)
    with nc.Block() as blk:
        @blk.scalar
        def _(e):
            SC.op("activation", out=sd_sb[:, 0:ntile], in_=v_sb[:, 0:ntile], func=AF.Sqrt,
                  bias=lns["eps"][:, 0:1], scale=1.0 / d)
    with nc.Block() as blk:
        @blk.vector
        def _(e):
            VE.op("reciprocal", out=rs_sb[:, 0:ntile], in_=sd_sb[:, 0:ntile])
            if normalize:
                for j in range(ntile):
                    VE.op("tensor_scalar", out=xn_sb[:, j * d:(j + 1) * d],
                          in0=x_sb[:, j * d:(j + 1) * d],
                          scalar1=m_sb[:, j:j + 1], scalar2=rs_sb[:, j:j + 1],
                          op0=ALU.subtract, op1=ALU.mult)


def _ln_scratch(sb, tag, d, ntile=NT):
    return {"sq_a": sb(f"lnsqa{tag}", [128, d]), "sq_b": sb(f"lnsqb{tag}", [128, d]),
            "m": sb(f"lnm{tag}", [128, ntile]), "v": sb(f"lnv{tag}", [128, ntile]),
            "sd": sb(f"lnsd{tag}", [128, ntile]), "rs": sb(f"lnrs{tag}", [128, ntile]),
            "eps": sb(f"lneps{tag}", [128, 1])}



class Seq:

    def __init__(self, nc, VE, SC, GP, pe_sem, pe_count, dma_counts=None):
        self.nc = nc
        self.W = {"V": VE, "S": SC, "G": GP}
        self.pe_sem = pe_sem
        self.pe_count = pe_count
        self.dma_counts = dma_counts if dma_counts is not None else {}
        self.steps = []

    def step(self, eng, deps, fn, dma_sem=None):
        h = len(self.steps)
        self.steps.append([eng, fn, list(deps), dma_sem, None])
        return h

    def emit(self, blk):
        base = {k: w.n for k, w in self.W.items()}
        cnt = dict(base)
        pe = self.pe_count[0]
        dma_cnt = self.dma_counts
        for st_ in self.steps:
            eng, fn, deps, dsem, _ = st_
            if eng in self.W:
                cnt[eng] += 1
                st_[4] = ("W", self.W[eng].sem, cnt[eng])
            elif eng == "T":
                pe += 1
                st_[4] = ("T", self.pe_sem, pe)
            else:
                key = id(dsem)
                dma_cnt.setdefault(key, 0)
                dma_cnt[key] += 16
                st_[4] = ("D", dsem, dma_cnt[key])
        emitted_pe = [self.pe_count[0]]

        def run_engine(engine_names, e):
            for st_ in self.steps:
                eng, fn, deps, dsem, mark = st_
                if eng not in engine_names:
                    continue
                for d in deps:
                    dm = self.steps[d][4] if isinstance(d, int) else d
                    kind = dm[0]
                    if kind in ("W", "T", "D"):
                        sem_, val_ = dm[1], dm[2]
                    else:
                        sem_, val_ = dm
                    if eng in self.W and kind == "W" and sem_ is self.W[eng].sem:
                        continue
                    if eng == "T" and kind == "T":
                        continue
                    if eng in self.W:
                        self.W[eng].wait(sem_, val_)
                    else:
                        e.wait_ge(sem_, val_)
                if eng in self.W:
                    got = fn(self.W[eng])
                    assert got is None or got == st_[4][2], (got, st_[4])
                elif eng == "T":
                    inst = fn(e)
                    inst.then_inc(self.pe_sem, 1)
                    emitted_pe[0] += 1
                else:
                    inst = fn(e)
                    inst.then_inc(st_[4][1], 16)

        @blk.vector
        def _(e):
            run_engine(("V",), e)

        @blk.scalar
        def _(e):
            run_engine(("S",), e)

        @blk.tensor
        def _(e):
            run_engine(("T",), e)

        @blk.gpsimd
        def _(e):
            run_engine(("G", "GD"), e)

        @blk.sync
        def _(e):
            run_engine(("SD",), e)

        self.pe_count[0] = emitted_pe[0]
        for k, w in self.W.items():
            pass
        return {i: st_[4] for i, st_ in enumerate(self.steps)}


def _mk_ctx(nc):
    es = contextlib.ExitStack()
    sbf = lambda name, shape, dt=F32: es.enter_context(nc.sbuf_tensor(name, shape, dt))
    semf = lambda name: es.enter_context(nc.semaphore(name))
    VE = W(nc, "vector", semf("_sv"))
    SC = W(nc, "scalar", semf("_ss"))
    GP = W(nc, "gpsimd", semf("_sg"))
    return es, sbf, semf, VE, SC, GP



def build_l0a():
    if "l0a" in _BUILD_CACHE:
        return _BUILD_CACHE["l0a"]
    nc = _nc()
    x_d = nc.dram_tensor("x", [TC, D], F32, kind="ExternalInput")
    rwv_d = nc.dram_tensor("rwv", [D], F32, kind="ExternalInput")
    rbv_d = nc.dram_tensor("rbv", [1], F32, kind="ExternalInput")
    xn_d = nc.dram_tensor("xn", [TC, D], F32, kind="ExternalOutput")
    rw_d = nc.dram_tensor("rw2d", [128, NT], F32, kind="ExternalOutput")

    es, sb, sem, VE, SC, GP = _mk_ctx(nc)

    x_sb = sb("x_sb", [128, NT * D])
    xn_sb = sb("xn_sb", [128, NT * D])
    lns = _ln_scratch(sb, "a", D)
    rw_sb = sb("rw_sb", [128, NT])
    wrow = sb("wrow", [1, D])
    rb1 = sb("rb1", [1, 1])
    ones_row = sb("ones_row", [1, 128])
    wb_sb = sb("wb_sb", [128, D])
    rbb = sb("rbb", [128, 1])
    wb_ps = es.enter_context(nc.psum_tensor("wb_ps", [128, D], F32))
    rb_ps = es.enter_context(nc.psum_tensor("rb_ps", [128, 1], F32))
    ld = sem("ld")
    ldg = sem("ldg")
    st = sem("st")

    with nc.Block() as blk:
        @blk.sync
        def _(e):
            e.dma_start(out=x_sb[:, :], in_=_ap3_tokmaj(x_d, NT, D)).then_inc(ld, 16)

        @blk.gpsimd
        def _(e):
            e.dma_start(out=wrow[0:1, :], in_=_row_ap(rwv_d, D)).then_inc(ldg, 16)
            e.dma_start(out=rb1[0:1, :], in_=_row_ap(rbv_d, 1)).then_inc(ldg, 16)
            GP.op("memset", ones_row[:, :], 1.0)
            GP.op("memset", lns["eps"][:, :], 1e-5)

        @blk.vector
        def _(e):
            e.wait_ge(ld, 16)

    with nc.Block() as blk:
        @blk.tensor
        def _(e):
            e.wait_ge(ldg, 32)
            e.wait_ge(GP.sem, 2)
            e.matmul(wb_ps[:, 0:512], ones_row[0:1, 0:128], wrow[0:1, 0:512], start=True, stop=True)
            e.matmul(wb_ps[:, 512:1024], ones_row[0:1, 0:128], wrow[0:1, 512:1024], start=True, stop=True)
            e.matmul(rb_ps[:, 0:1], ones_row[0:1, 0:128], rb1[0:1, 0:1], start=True, stop=True)

    _emit_ln(nc, VE, SC, x_sb, xn_sb, lns, NT, D)

    with nc.Block() as blk:
        @blk.vector
        def _(e):
            VE.op("tensor_copy", out=wb_sb[:, :], in_=wb_ps[:, :])
            VE.op("tensor_copy", out=rbb[:, :], in_=rb_ps[:, :])
            for j in range(NT):
                VE.op("scalar_tensor_tensor", out=lns["sq_a"][:, :],
                      in0=xn_sb[:, j * D:(j + 1) * D],
                      scalar=1.0, in1=wb_sb[:, :], op0=ALU.bypass,
                      op1=ALU.mult, accum_out=rw_sb[:, j:j + 1])
            VE.op("tensor_scalar", out=rw_sb[:, :], in0=rw_sb[:, :], scalar1=rbb[:, 0:1],
                  scalar2=None, op0=ALU.add)

    with nc.Block() as blk:
        @blk.sync
        def _(e):
            e.dma_start(out=_ap3_tokmaj(xn_d, NT, D), in_=xn_sb[:, :]).then_inc(st, 16)
            e.dma_start(out=rw_d.ap(), in_=rw_sb[:, :]).then_inc(st, 16)
            e.wait_ge(st, 32)

    es.close()
    _BUILD_CACHE["l0a"] = nc
    return nc



def build_l0b():
    if "l0b" in _BUILD_CACHE:
        return _BUILD_CACHE["l0b"]
    HH = 2048
    NHC = HH // 128
    NDC = D // 128
    NG = 4
    nc = _nc()
    x_d = nc.dram_tensor("x", [TC, D], F32, kind="ExternalInput")
    xn_d = nc.dram_tensor("xn", [TC, D], F32, kind="ExternalInput")
    xr_d = nc.dram_tensor("xnrev", [TC, D], F32, kind="ExternalInput")
    rwb_d = nc.dram_tensor("rwb", [L], F32, kind="ExternalInput")
    rwo_d = nc.dram_tensor("rwown2d", [128, NT], F32, kind="ExternalInput")
    wg_d = nc.dram_tensor("wg", [D, 8], F32, kind="ExternalInput")
    w1_d = nc.dram_tensor("w1", [8, D, HH], F32, kind="ExternalInput")
    w3_d = nc.dram_tensor("w3", [8, D, HH], F32, kind="ExternalInput")
    w2_d = nc.dram_tensor("w2", [8, HH, D], F32, kind="ExternalInput")
    x1_d = nc.dram_tensor("x1", [TC, D], F32, kind="ExternalOutput")

    es, sb, sem, VE, SC, GP = _mk_ctx(nc)

    xnT = sb("xnT", [128, NDC * TC], BF16)
    h_sb = sb("h_sb", [128, NHC * TC], BF16)
    g_acc = sb("g_acc", [128, NT * D], BF16)
    w1c = [sb(f"w1c{p}", [128, NDC * 128], BF16) for p in range(2)]
    w3c = [sb(f"w3c{p}", [128, NDC * 128], BF16) for p in range(2)]
    w2_sb = sb("w2_sb", [128, NHC * D], BF16)
    wg_sb = sb("wg_sb", [128, NDC * 8], BF16)
    sil = [sb(f"sil{p}", [128, 512]) for p in range(2)]
    xstage = sb("xstage", [128, D], BF16)
    ident_bf = sb("ident_bf", [128, 128], BF16)
    ident_f = sb("ident_f", [128, 128], F32)
    ones_row = sb("ones_row", [1, 128])
    ones_col = sb("ones_col", [128, 1])
    rw_sb = sb("rw_sb", [128, 64])
    rwo_sb = sb("rwo_sb", [128, NT])
    cmp_sb = sb("cmp_sb", [128, 64])
    cnt_col = sb("cnt_col", [128, 1])
    lo = sb("lo", [1, 1])
    hi = sb("hi", [1, 1])
    mid = sb("mid", [1, 1])
    t1x = sb("t1x", [1, 1])
    t2x = sb("t2x", [1, 1])
    pred = sb("pred", [1, 1])
    midb = sb("midb", [128, 1])
    mask_sb = sb("mask_sb", [128, NT])
    neg_sb = sb("neg_sb", [128, 64])
    red_col = sb("red_col", [128, 1])
    red_row_sb = sb("red_row_sb", [1, 128])
    mx_sb = sb("mx_sb", [128, NT * 8])
    logits_sb = sb("logits_sb", [128, NT * 8])
    d_sb = sb("d_sb", [128, NT])
    p1_sb = sb("p1_sb", [128, NT])
    p2_sb = sb("p2_sb", [128, NT])
    wsel_sb = sb("wsel_sb", [128, NT * 8])
    wtmp_sb = sb("wtmp_sb", [128, 8])
    xf = [sb(f"xf{p}", [128, D]) for p in range(2)]
    xr = [sb(f"xr{p}", [128, D]) for p in range(2)]
    gs = [sb(f"gs{p}", [128, D]) for p in range(2)]

    ld = sem("ld")
    ldg = sem("ldg")
    ltr = sem("ltr")
    str_t = sem("str_t")
    spb = sem("spb")
    spc = sem("spc")
    lw = [sem("lw0"), sem("lw1")]
    spe = sem("spe")
    lw2 = sem("lw2")
    sty = sem("sty")
    slg = sem("slg")
    lx = [sem("lx0"), sem("lx1")]
    st = [sem("st0"), sem("st1")]

    with nc.Block() as blk:
        @blk.sync
        def _(e):
            e.dma_start(out=rw_sb[:, :], in_=bass.AP(rwb_d, 0, [[64, 128], [1, 64]])).then_inc(ld, 16)
            e.dma_start(out=rwo_sb[:, :], in_=rwo_d.ap()).then_inc(ld, 16)

        @blk.gpsimd
        def _(e):
            _build_ident(GP, ident_bf, ident_f)
            GP.op("memset", ones_row[:, :], 1.0)
            GP.op("memset", ones_col[:, :], 1.0)
            e.dma_start(out=wg_sb[:, :],
                        in_=bass.AP(wg_d, 0, [[8, 128], [8 * 128, NDC], [1, 8]])).then_inc(ldg, 16)

    vm_tr = []
    with nc.psum_tensor("tr_ps", [128, D], BF16) as tr_ps:
        with nc.Block() as blk:
            @blk.vector
            def _(e):
                for j in range(NT):
                    VE.wait(str_t, j + 1)
                    vm_tr.append(VE.op(
                        "tensor_copy",
                        out=bass.AP(xnT, j * 128, [[NDC * TC, 128], [TC, NDC], [1, 128]]),
                        in_=tr_ps[:, :]))

            @blk.gpsimd
            def _(e):
                for j in range(NT):
                    if j >= 1:
                        e.wait_ge(str_t, j)
                    e.dma_start(out=xstage[:, :],
                                in_=_ap3_tokmaj(xn_d, 1, D, offset=j * 128 * D)).then_inc(ltr, 16)

            @blk.tensor
            def _(e):
                for j in range(NT):
                    e.wait_ge(ltr, 16 * (j + 1))
                    if j >= 1:
                        e.wait_ge(VE.sem, vm_tr[j - 1])
                    last = None
                    for k in range(NDC):
                        last = e.transpose(tr_ps[:, k * 128:(k + 1) * 128],
                                           xstage[:, k * 128:(k + 1) * 128], ident_bf[:, :])
                    last.then_inc(str_t, 1)

    NITER = 42
    vm_bis = {}
    with (nc.psum_tensor("row_ps", [1, 128], F32) as row_ps,
          nc.psum_tensor("bb_ps", [128, 1], F32) as bb_ps,
          nc.psum_tensor("sc_ps", [1, 1], F32) as sc_ps):
        with nc.Block() as blk:
            @blk.vector
            def _(e):
                e.wait_ge(ld, 32)
                vm_bis["r1"] = VE.op("reduce_max", out=red_col[:, 0:1], in_=rw_sb[:, :], axis=AX.X)
                VE.wait(spb, 1)
                VE.op("tensor_copy", out=red_row_sb[:, :], in_=row_ps[:, :])
                VE.op("reduce_max", out=hi[0:1, 0:1], in_=red_row_sb[0:1, :], axis=AX.X)
                VE.op("tensor_scalar", out=neg_sb[:, :], in0=rw_sb[:, :], scalar1=-1.0,
                      scalar2=None, op0=ALU.mult)
                vm_bis["r2"] = VE.op("reduce_max", out=red_col[:, 0:1], in_=neg_sb[:, :], axis=AX.X)
                VE.wait(spb, 2)
                VE.op("tensor_copy", out=red_row_sb[:, :], in_=row_ps[:, :])
                VE.op("reduce_max", out=lo[0:1, 0:1], in_=red_row_sb[0:1, :], axis=AX.X)
                VE.op("tensor_scalar", out=lo[0:1, 0:1], in0=lo[0:1, 0:1], scalar1=-1.0, scalar2=-1.0,
                      op0=ALU.mult, op1=ALU.add)
                VE.op("tensor_tensor", out=t1x[0:1, 0:1], in0=lo[0:1, 0:1], in1=hi[0:1, 0:1], op=ALU.add)
                vm_bis["mid", -1] = VE.op("tensor_scalar", out=mid[0:1, 0:1], in0=t1x[0:1, 0:1],
                                          scalar1=0.5, scalar2=None, op0=ALU.mult)
                for i in range(NITER):
                    VE.wait(spb, 3 + i)
                    VE.op("tensor_copy", out=midb[:, :], in_=bb_ps[:, :])
                    VE.op("tensor_scalar", out=cmp_sb[:, :], in0=rw_sb[:, :], scalar1=midb[:, 0:1],
                          scalar2=None, op0=ALU.is_gt)
                    vm_bis["cnt", i] = VE.op("reduce_sum", out=cnt_col[:, 0:1], in_=cmp_sb[:, :], axis=AX.X)
                    VE.wait(spc, i + 1)
                    VE.op("tensor_copy", out=t2x[0:1, 0:1], in_=sc_ps[0:1, 0:1])
                    VE.op("tensor_scalar", out=pred[0:1, 0:1], in0=t2x[0:1, 0:1], scalar1=1024.0,
                          scalar2=None, op0=ALU.is_ge)
                    VE.op("tensor_tensor", out=t1x[0:1, 0:1], in0=mid[0:1, 0:1], in1=lo[0:1, 0:1], op=ALU.subtract)
                    VE.op("tensor_tensor", out=t1x[0:1, 0:1], in0=pred[0:1, 0:1], in1=t1x[0:1, 0:1], op=ALU.mult)
                    VE.op("tensor_tensor", out=lo[0:1, 0:1], in0=lo[0:1, 0:1], in1=t1x[0:1, 0:1], op=ALU.add)
                    VE.op("tensor_tensor", out=t1x[0:1, 0:1], in0=hi[0:1, 0:1], in1=mid[0:1, 0:1], op=ALU.subtract)
                    VE.op("tensor_scalar", out=t2x[0:1, 0:1], in0=pred[0:1, 0:1], scalar1=1.0,
                          scalar2=None, op0=ALU.subtract)
                    VE.op("tensor_tensor", out=t1x[0:1, 0:1], in0=t2x[0:1, 0:1], in1=t1x[0:1, 0:1], op=ALU.mult)
                    VE.op("tensor_tensor", out=hi[0:1, 0:1], in0=hi[0:1, 0:1], in1=t1x[0:1, 0:1], op=ALU.add)
                    VE.op("tensor_tensor", out=t1x[0:1, 0:1], in0=lo[0:1, 0:1], in1=hi[0:1, 0:1], op=ALU.add)
                    vm_bis["mid", i] = VE.op("tensor_scalar", out=mid[0:1, 0:1], in0=t1x[0:1, 0:1],
                                             scalar1=0.5, scalar2=None, op0=ALU.mult)
                VE.wait(spb, 3 + NITER)
                VE.op("tensor_copy", out=midb[:, :], in_=bb_ps[:, :])
                VE.op("tensor_scalar", out=mask_sb[:, :], in0=rwo_sb[:, :], scalar1=midb[:, 0:1],
                      scalar2=None, op0=ALU.is_gt)

            @blk.tensor
            def _(e):
                e.wait_ge(VE.sem, vm_bis["r1"])
                e.matmul(row_ps[0:1, :], red_col[:, 0:1], ident_f[:, :],
                         start=True, stop=True).then_inc(spb, 1)
                e.wait_ge(VE.sem, vm_bis["r2"])
                e.matmul(row_ps[0:1, :], red_col[:, 0:1], ident_f[:, :],
                         start=True, stop=True).then_inc(spb, 1)
                for i in range(NITER):
                    e.wait_ge(VE.sem, vm_bis["mid", i - 1])
                    e.matmul(bb_ps[:, 0:1], ones_row[0:1, 0:128], mid[0:1, 0:1],
                             start=True, stop=True).then_inc(spb, 1)
                    e.wait_ge(VE.sem, vm_bis["cnt", i])
                    e.matmul(sc_ps[0:1, 0:1], cnt_col[:, 0:1], ones_col[:, 0:1],
                             start=True, stop=True).then_inc(spc, 1)
                e.wait_ge(VE.sem, vm_bis["mid", NITER - 1])
                e.matmul(bb_ps[:, 0:1], ones_row[0:1, 0:128], lo[0:1, 0:1],
                         start=True, stop=True).then_inc(spb, 1)

    vm_lg = []
    with (nc.psum_tensor("lg_ps0", [128, 8], F32) as lg0,
          nc.psum_tensor("lg_ps1", [128, 8], F32) as lg1):
        lg_ps = [lg0, lg1]
        with nc.Block() as blk:
            @blk.vector
            def _(e):
                for j in range(NT):
                    pb = j % 2
                    VE.wait(slg, j + 1)
                    vm_lg.append(VE.op("tensor_copy", out=logits_sb[:, j * 8:(j + 1) * 8],
                                       in_=lg_ps[pb][:, :]))

            @blk.tensor
            def _(e):
                e.wait_ge(ldg, 16)
                for j in range(NT):
                    pb = j % 2
                    if j >= 2:
                        e.wait_ge(VE.sem, vm_lg[j - 2])
                    last = None
                    for k in range(NDC):
                        last = e.matmul(lg_ps[pb][:, :],
                                        xnT[:, k * TC + j * 128:k * TC + (j + 1) * 128],
                                        wg_sb[:, k * 8:(k + 1) * 8],
                                        start=(k == 0), stop=(k == NDC - 1))
                    last.then_inc(slg, 1)

    with nc.Block() as blk:
        @blk.vector
        def _(e):
            for j in range(NT):
                VE.op("max", out=mx_sb[:, j * 8:(j + 1) * 8], in_=logits_sb[:, j * 8:(j + 1) * 8])
            for j in range(NT):
                VE.op("tensor_tensor", out=d_sb[:, j:j + 1], in0=mx_sb[:, j * 8:j * 8 + 1],
                      in1=mx_sb[:, j * 8 + 1:j * 8 + 2], op=ALU.subtract)

    with nc.Block() as blk:
        @blk.scalar
        def _(e):
            SC.op("activation", out=p1_sb[:, :], in_=d_sb[:, :], func=AF.Sigmoid)
            SC.op("activation", out=p2_sb[:, :], in_=d_sb[:, :], func=AF.Sigmoid, scale=-1.0)

    with nc.Block() as blk:
        @blk.vector
        def _(e):
            for j in range(NT):
                lj = logits_sb[:, j * 8:(j + 1) * 8]
                VE.op("tensor_scalar", out=wsel_sb[:, j * 8:(j + 1) * 8], in0=lj,
                      scalar1=mx_sb[:, j * 8:j * 8 + 1], scalar2=p1_sb[:, j:j + 1],
                      op0=ALU.is_equal, op1=ALU.mult)
                VE.op("tensor_scalar", out=wtmp_sb[:, :], in0=lj,
                      scalar1=mx_sb[:, j * 8 + 1:j * 8 + 2], scalar2=p2_sb[:, j:j + 1],
                      op0=ALU.is_equal, op1=ALU.mult)
                VE.op("tensor_tensor", out=wsel_sb[:, j * 8:(j + 1) * 8],
                      in0=wsel_sb[:, j * 8:(j + 1) * 8], in1=wtmp_sb[:, :], op=ALU.add)
            VE.op("memset", g_acc[:, :], 0.0)

    with (nc.psum_tensor("h1_ps0", [128, 512], F32) as h1a,
          nc.psum_tensor("h1_ps1", [128, 512], F32) as h1b,
          nc.psum_tensor("h3_ps0", [128, 512], F32) as h3a,
          nc.psum_tensor("h3_ps1", [128, 512], F32) as h3b,
          nc.psum_tensor("y_ps", [128, D], F32) as y_ps):
        h1_ps = [h1a, h1b]
        h3_ps = [h3a, h3b]
        vm_h = {}
        sm_sil = {}
        vm_y = {}
        NGRP = NHC * NG
        for eix in range(8):
            sc_base = SC.n
            ve_base = VE.n
            for q in range(NGRP):
                seqg = eix * NGRP + q
                sm_sil[seqg] = sc_base + q + 1
                vm_h[seqg] = ve_base + q + 1
            with nc.Block() as blk:
                @blk.scalar
                def _(e, eix=eix):
                    for hc in range(NHC):
                        for g in range(NG):
                            seqg = (eix * NHC + hc) * NG + g
                            p = seqg % 2
                            SC.wait(spe, seqg + 1)
                            if seqg >= 2:
                                SC.wait(VE.sem, vm_h[seqg - 2])
                            mk = SC.op("activation", out=sil[p][:, :],
                                       in_=h1_ps[p][:, :], func=AF.Silu)
                            assert mk == sm_sil[seqg]

                @blk.vector
                def _(e, eix=eix):
                    for hc in range(NHC):
                        for g in range(NG):
                            seqg = (eix * NHC + hc) * NG + g
                            p = seqg % 2
                            VE.wait(SC.sem, sm_sil[seqg])
                            mk = VE.op(
                                "tensor_tensor",
                                out=h_sb[:, hc * TC + g * 512:hc * TC + (g + 1) * 512],
                                in0=sil[p][:, :], in1=h3_ps[p][:, :], op=ALU.mult)
                            assert mk == vm_h[seqg]

                @blk.gpsimd
                def _(e, eix=eix):
                    for hc in range(NHC):
                        p = hc % 2
                        seq = eix * NHC + hc
                        if seq >= 2:
                            e.wait_ge(spe, (seq - 1) * NG)
                        off = eix * D * HH + hc * 128
                        e.dma_start(out=w1c[p][:, :],
                                    in_=bass.AP(w1_d, off, [[HH, 128], [128 * HH, NDC], [1, 128]])
                                    ).then_inc(lw[p], 16)
                        e.dma_start(out=w3c[p][:, :],
                                    in_=bass.AP(w3_d, off, [[HH, 128], [128 * HH, NDC], [1, 128]])
                                    ).then_inc(lw[p], 16)

                @blk.tensor
                def _(e, eix=eix):
                    for hc in range(NHC):
                        pw = hc % 2
                        seq = eix * NHC + hc
                        e.wait_ge(lw[pw], 32 * (seq // 2 + 1))
                        for g in range(NG):
                            seqg = seq * NG + g
                            p = seqg % 2
                            if seqg >= 2:
                                e.wait_ge(VE.sem, vm_h[seqg - 2])
                            for k in range(NDC):
                                e.matmul(h1_ps[p][:, :], w1c[pw][:, k * 128:(k + 1) * 128],
                                         xnT[:, k * TC + g * 512:k * TC + (g + 1) * 512],
                                         start=(k == 0), stop=(k == NDC - 1))
                            last = None
                            for k in range(NDC):
                                last = e.matmul(h3_ps[p][:, :], w3c[pw][:, k * 128:(k + 1) * 128],
                                                xnT[:, k * TC + g * 512:k * TC + (g + 1) * 512],
                                                start=(k == 0), stop=(k == NDC - 1))
                            last.then_inc(spe, 1)

            with nc.Block() as blk:
                @blk.vector
                def _(e, eix=eix):
                    for j in range(NT):
                        seq = eix * NT + j
                        VE.wait(sty, seq + 1)
                        for half in range(2):
                            mk = VE.op(
                                "scalar_tensor_tensor",
                                out=g_acc[:, j * D + half * 512:j * D + (half + 1) * 512],
                                in0=y_ps[:, half * 512:(half + 1) * 512],
                                scalar=wsel_sb[:, j * 8 + eix:j * 8 + eix + 1],
                                in1=g_acc[:, j * D + half * 512:j * D + (half + 1) * 512],
                                op0=ALU.mult, op1=ALU.add)
                        vm_y[seq] = mk

                @blk.gpsimd
                def _(e, eix=eix):
                    if eix >= 1:
                        e.wait_ge(sty, eix * NT)
                    e.dma_start(out=w2_sb[:, :],
                                in_=bass.AP(w2_d, eix * HH * D, [[D, 128], [128 * D, NHC], [1, D]])
                                ).then_inc(lw2, 16)

                @blk.tensor
                def _(e, eix=eix):
                    e.wait_ge(lw2, 16 * (eix + 1))
                    for j in range(NT):
                        seq = eix * NT + j
                        if seq >= 1:
                            e.wait_ge(VE.sem, vm_y[seq - 1])
                        last = None
                        for hc in range(NHC):
                            hj = h_sb[:, hc * TC + j * 128:hc * TC + (j + 1) * 128]
                            e.matmul(y_ps[:, 0:512], hj, w2_sb[:, hc * D:hc * D + 512],
                                     start=(hc == 0), stop=(hc == NHC - 1))
                            last = e.matmul(y_ps[:, 512:1024], hj, w2_sb[:, hc * D + 512:(hc + 1) * D],
                                            start=(hc == 0), stop=(hc == NHC - 1))
                        last.then_inc(sty, 1)

    sc_base_f = SC.n
    ve_base_f = VE.n
    sm_f = {j: sc_base_f + j + 1 for j in range(NT)}
    vm_f = {j: ve_base_f + 3 * (j + 1) for j in range(NT)}
    with nc.Block() as blk:
        @blk.scalar
        def _(e):
            for j in range(NT):
                p = j % 2
                if j >= 2:
                    SC.wait(VE.sem, vm_f[j - 2])
                mk = SC.op("activation", out=gs[p][:, :], in_=g_acc[:, j * D:(j + 1) * D],
                           func=AF.Silu)
                assert mk == sm_f[j]

        @blk.vector
        def _(e):
            for j in range(NT):
                p = j % 2
                VE.wait(lx[j % 2], 32 * (j // 2 + 1))
                VE.wait(SC.sem, sm_f[j])
                if j >= 2:
                    VE.wait(st[j % 2], 16 * (j // 2))
                VE.op("tensor_scalar", out=xr[p][:, :], in0=xr[p][:, :], scalar1=mask_sb[:, j:j + 1],
                      scalar2=None, op0=ALU.mult)
                VE.op("tensor_tensor", out=xr[p][:, :], in0=xr[p][:, :], in1=gs[p][:, :], op=ALU.mult)
                mk = VE.op("tensor_tensor", out=xf[p][:, :], in0=xf[p][:, :], in1=xr[p][:, :],
                           op=ALU.add)
                assert mk == vm_f[j]

        @blk.sync
        def _(e):
            for j in range(NT):
                p = j % 2
                if j >= 2:
                    e.wait_ge(VE.sem, vm_f[j - 2])
                    e.wait_ge(st[j % 2], 16 * (j // 2))
                e.dma_start(out=xf[p][:, :], in_=_ap3_tokmaj(x_d, 1, D, offset=j * 128 * D)).then_inc(lx[j % 2], 16)
                e.dma_start(out=xr[p][:, :], in_=_ap3_tokmaj(xr_d, 1, D, offset=j * 128 * D)).then_inc(lx[j % 2], 16)

        @blk.gpsimd
        def _(e):
            for j in range(NT):
                e.wait_ge(VE.sem, vm_f[j])
                p = j % 2
                e.dma_start(out=_ap3_tokmaj(x1_d, 1, D, offset=j * 128 * D),
                            in_=xf[p][:, :]).then_inc(st[p], 16)
            e.wait_ge(st[0], 16 * (NT // 2))
            e.wait_ge(st[1], 16 * (NT // 2))

    es.close()
    _BUILD_CACHE["l0b"] = nc
    return nc





def build_l1a():
    if "l1a" in _BUILD_CACHE:
        return _BUILD_CACHE["l1a"]
    NDC = D // 128
    NG = TC // 512
    nc = _nc()
    x_d = nc.dram_tensor("x", [TC, D], F32, kind="ExternalInput")
    wnames = ["wq", "wk", "wv", "wgk", "wgc", "wproj"]
    w_ds = [nc.dram_tensor(nm, [D, D], F32, kind="ExternalInput") for nm in wnames]
    bgk_d = nc.dram_tensor("bgk2d", [128, 8], F32, kind="ExternalInput")
    bgc_d = nc.dram_tensor("bgc2d", [128, 8], F32, kind="ExternalInput")
    bpr_d = nc.dram_tensor("bproj2d", [128, 8], F32, kind="ExternalInput")
    wgate_d = nc.dram_tensor("wgate", [D, 8], F32, kind="ExternalInput")
    onames = ["qT", "kT", "vT", "aKT", "aCT", "xpT"]
    out_ds = [nc.dram_tensor(nm, [D, TC], F32, kind="ExternalOutput") for nm in onames]
    lg_d = nc.dram_tensor("logits2d", [128, NT * 8], F32, kind="ExternalOutput")

    es, sb, sem, VE, SC, GP = _mk_ctx(nc)
    x_sb = sb("x_sb", [128, NT * D])
    lns = _ln_scratch(sb, "a", D)
    xnT = sb("xnT", [128, NDC * TC], BF16)
    xn_st = sb("xn_st", [128, D])
    xn_bf = sb("xn_bf", [128, D], BF16)
    ident_bf = sb("ident_bf", [128, 128], BF16)
    w_sb = [sb(f"w_sb{p}", [128, NDC * D], BF16) for p in range(2)]
    stage = [sb(f"stage{p}", [128, 512]) for p in range(2)]
    bias_sb = sb("bias_sb", [128, 3 * 8])
    zeros_c = sb("zeros_c", [128, 1])
    wg_sb = sb("wg_sb", [128, NDC * 8], BF16)
    lg_sb = sb("lg_sb", [128, NT * 8])
    xp_bf = sb("xp_bf", [128, NDC * TC], BF16)

    ld = sem("ld")
    ldg = sem("ldg")
    ltr = sem("ltr")
    lw = [sem("lw0"), sem("lw1")]
    spj = sem("spj")
    slg = sem("slg")
    lxp = sem("lxp")
    st = [sem("st0"), sem("st1")]
    stl = sem("stl")

    with nc.Block() as blk:
        @blk.sync
        def _(e):
            e.dma_start(out=x_sb[:, :], in_=_ap3_tokmaj(x_d, NT, D)).then_inc(ld, 16)
            e.dma_start(out=bias_sb[:, 0:8], in_=bgk_d.ap()).then_inc(ld, 16)
            e.dma_start(out=bias_sb[:, 8:16], in_=bgc_d.ap()).then_inc(ld, 16)
            e.dma_start(out=bias_sb[:, 16:24], in_=bpr_d.ap()).then_inc(ld, 16)

        @blk.gpsimd
        def _(e):
            _build_ident(GP, ident_bf)
            GP.op("memset", lns["eps"][:, :], 1e-5)
            GP.op("memset", zeros_c[:, :], 0.0)
            e.dma_start(out=wg_sb[:, :],
                        in_=bass.AP(wgate_d, 0, [[8, 128], [8 * 128, NDC], [1, 8]])).then_inc(ldg, 16)

        @blk.vector
        def _(e):
            e.wait_ge(ld, 64)

    _emit_ln(nc, VE, SC, x_sb, None, lns, NT, D, normalize=False)

    with nc.psum_tensor("tr_ps", [128, D], BF16) as tr_ps:
        ve_base = VE.n
        with nc.Block() as blk:
            @blk.vector
            def _(e):
                for j in range(NT):
                    VE.op("tensor_scalar", out=xn_st[:, :], in0=x_sb[:, j * D:(j + 1) * D],
                          scalar1=lns["m"][:, j:j + 1], scalar2=lns["rs"][:, j:j + 1],
                          op0=ALU.subtract, op1=ALU.mult)
                    VE.op("tensor_copy", out=xn_bf[:, :], in_=xn_st[:, :])
                    VE.wait(ltr, j + 1)
                    VE.op("tensor_copy",
                          out=bass.AP(xnT, j * 128, [[NDC * TC, 128], [TC, NDC], [1, 128]]),
                          in_=tr_ps[:, :])

            @blk.tensor
            def _(e):
                for j in range(NT):
                    e.wait_ge(VE.sem, ve_base + 3 * j + 2)
                    last = None
                    for k in range(NDC):
                        last = e.transpose(tr_ps[:, k * 128:(k + 1) * 128],
                                           xn_bf[:, k * 128:(k + 1) * 128], ident_bf[:, :])
                    last.then_inc(ltr, 1)

    ACTS = [(AF.Identity, None), (AF.Identity, None), (AF.Identity, None),
            (AF.Sigmoid, 0), (AF.Sigmoid, 8), (AF.Identity, 16)]
    NSEQ = 6 * 8 * NG
    sc_base = SC.n
    sm_stage = {q: sc_base + q + 1 for q in range(NSEQ)}
    with (nc.psum_tensor("pj_ps0", [128, 512], F32) as pj0,
          nc.psum_tensor("pj_ps1", [128, 512], F32) as pj1):
        pj_ps = [pj0, pj1]
        with nc.Block() as blk:
            @blk.gpsimd
            def _(e):
                for wi in range(6):
                    p = wi % 2
                    if wi >= 2:
                        e.wait_ge(spj, (wi - 1) * 32)
                    e.dma_start(out=w_sb[p][:, :],
                                in_=bass.AP(w_ds[wi], 0, [[D, 128], [128 * D, NDC], [1, D]])
                                ).then_inc(lw[p], 16)

            @blk.tensor
            def _(e):
                nseq = 0
                for wi in range(6):
                    p = wi % 2
                    e.wait_ge(lw[p], 16 * (wi // 2 + 1))
                    for fc in range(8):
                        for g in range(NG):
                            pp = nseq % 2
                            if nseq >= 2:
                                e.wait_ge(SC.sem, sm_stage[nseq - 2])
                            last = None
                            for k in range(NDC):
                                last = e.matmul(pj_ps[pp][:, :],
                                                w_sb[p][:, k * D + fc * 128:k * D + (fc + 1) * 128],
                                                xnT[:, k * TC + g * 512:k * TC + (g + 1) * 512],
                                                start=(k == 0), stop=(k == NDC - 1))
                            last.then_inc(spj, 1)
                            nseq += 1

            @blk.scalar
            def _(e):
                nseq = 0
                for wi in range(6):
                    af, boff = ACTS[wi]
                    for fc in range(8):
                        for g in range(NG):
                            pp = nseq % 2
                            SC.wait(spj, nseq + 1)
                            if nseq >= 2:
                                SC.wait(st[pp], 16 * (nseq // 2))
                            bias_ap = zeros_c[:, 0:1] if boff is None else bias_sb[:, boff + fc:boff + fc + 1]
                            mk = SC.op("activation", out=stage[pp][:, :],
                                       in_=pj_ps[pp][:, :], func=af, bias=bias_ap)
                            assert mk == sm_stage[nseq]
                            nseq += 1

            @blk.sync
            def _(e):
                nseq = 0
                for wi in range(6):
                    for fc in range(8):
                        for g in range(NG):
                            pp = nseq % 2
                            e.wait_ge(SC.sem, sm_stage[nseq])
                            e.dma_start(out=bass.AP(out_ds[wi], fc * 128 * TC + g * 512,
                                                    [[TC, 128], [1, 512]]),
                                        in_=stage[pp][:, :]).then_inc(st[pp], 16)
                            nseq += 1

    with (nc.psum_tensor("lg_ps0", [128, 8], F32) as lgp0,
          nc.psum_tensor("lg_ps1", [128, 8], F32) as lgp1):
        lg_ps = [lgp0, lgp1]
        ve_base2 = VE.n
        with nc.Block() as blk:
            @blk.gpsimd
            def _(e):
                e.wait_ge(st[0], 16 * 96)
                e.wait_ge(st[1], 16 * 96)
                e.dma_start(out=xp_bf[:, :],
                            in_=bass.AP(out_ds[5], 0, [[TC, 128], [128 * TC, NDC], [1, TC]])
                            ).then_inc(lxp, 16)

            @blk.tensor
            def _(e):
                e.wait_ge(lxp, 16)
                e.wait_ge(ldg, 16)
                for j in range(NT):
                    pp = j % 2
                    if j >= 2:
                        e.wait_ge(VE.sem, ve_base2 + j - 1)
                    last = None
                    for k in range(NDC):
                        last = e.matmul(lg_ps[pp][:, :],
                                        xp_bf[:, k * TC + j * 128:k * TC + (j + 1) * 128],
                                        wg_sb[:, k * 8:(k + 1) * 8],
                                        start=(k == 0), stop=(k == NDC - 1))
                    last.then_inc(slg, 1)

            @blk.vector
            def _(e):
                for j in range(NT):
                    pp = j % 2
                    VE.wait(slg, j + 1)
                    VE.op("tensor_copy", out=lg_sb[:, j * 8:(j + 1) * 8], in_=lg_ps[pp][:, :])

    with nc.Block() as blk:
        @blk.sync
        def _(e):
            e.dma_start(out=lg_d.ap(), in_=lg_sb[:, :]).then_inc(stl, 16)
            e.wait_ge(stl, 16)

    es.close()
    _BUILD_CACHE["l1a"] = nc
    return nc




def build_l1b(NBH=4, NCHK=None, trunc=None):
    key = ("l1b", NBH, NCHK, trunc)
    if key in _BUILD_CACHE:
        return _BUILD_CACHE[key]
    DK = 64
    CH = 128
    if NCHK is None:
        NCHK = L // CH
    WCH = 8
    WLEN = WCH * CH
    NWIN = NCHK // WCH
    nc = _nc()
    qk_d = nc.dram_tensor("qkvab", [NBH, 5, DK, L], F32, kind="ExternalInput")
    o_d = nc.dram_tensor("oT", [NBH, L, DK], F32, kind="ExternalOutput")

    es, sb, sem, VE, SC, GP = _mk_ctx(nc)
    win = sb("win", [64, 5 * WLEN])
    lnK = sb("lnK", [64, CH]); lnC = sb("lnC", [64, CH])
    LK = sb("LK", [64, CH]); LC = sb("LC", [64, CH])
    eKp = sb("eKp", [64, CH]); eKm = sb("eKm", [64, CH])
    eCp = sb("eCp", [64, CH]); eCm = sb("eCm", [64, CH])
    QK1 = sb("QK1", [64, CH], BF16); KK1 = sb("KK1", [64, CH], BF16)
    KC1 = sb("KC1", [64, CH], BF16); QC1 = sb("QC1", [64, CH], BF16)
    Vbf = sb("Vbf", [64, CH], BF16)
    K2 = sb("K2", [64, CH], BF16); Q2 = sb("Q2", [64, CH], BF16)
    VT_sb = sb("VT_sb", [128, 64], BF16)
    KK1T_sb = sb("KK1T_sb", [128, 64], BF16)
    K2T_sb = sb("K2T_sb", [128, 64], BF16)
    Q2T_sb = sb("Q2T_sb", [128, 64], BF16)
    PmKT = sb("PmKT", [128, CH], BF16)
    A2Tm = sb("A2Tm", [128, CH], BF16)
    Pm3T = sb("Pm3T", [128, CH], BF16)
    M_sb = sb("M_sb", [128, 64], BF16)
    W1bf = sb("W1bf", [128, 64], BF16)
    U_sb = sb("U_sb", [64, CH])
    UBbf = sb("UBbf", [64, CH], BF16)
    o1_sb = sb("o1_sb", [128, 64])
    o_sb = [sb(f"o_sb{p}", [128, 64]) for p in range(2)]
    S0_sb = sb("S0_sb", [64, 64]); C0_sb = sb("C0_sb", [64, 64]); G0_sb = sb("G0_sb", [64, 64])
    S0_bf = sb("S0_bf", [64, 64], BF16); C0_bf = sb("C0_bf", [64, 64], BF16)
    G0_bf = sb("G0_bf", [64, 64], BF16)
    Stmp = sb("Stmp", [64, 64]); Ctmp = sb("Ctmp", [64, 64]); Gtmp = sb("Gtmp", [64, 64])
    eKcRow = sb("eKcRow", [1, 64])
    tri_i = sb("tri_i", [128, CH])
    tri_s = sb("tri_s", [128, CH])
    ones64 = sb("ones64", [64, CH])
    ident_bf = sb("ident_bf", [128, 128], BF16)
    ident_f = sb("ident_f", [128, 128], F32)

    lwin = sem("lwin")
    spp = sem("spp")
    sto = [sem("sto0"), sem("sto1")]
    pe_count = [0]
    dma_counts = {}

    with nc.Block() as blk:
        @blk.gpsimd
        def _(e):
            _build_ident(GP, ident_bf, ident_f)
            GP.op("memset", ones64[:, :], 1.0)
            GP.op("memset", tri_i[:, :], 1.0)
            GP.op("affine_select", out=tri_i[:, :], in_=tri_i[:, :], pattern=[[1, CH]],
                  compare_op=ALU.is_ge, fill=0.0, base=0, channel_multiplier=-1)
            GP.op("memset", tri_s[:, :], 1.0)
            GP.op("affine_select", out=tri_s[:, :], in_=tri_s[:, :], pattern=[[1, CH]],
                  compare_op=ALU.is_gt, fill=0.0, base=0, channel_multiplier=-1)


    with (nc.psum_tensor("psA", [128, CH], F32) as psA,
          nc.psum_tensor("psB", [128, 64], F32) as psB,
          nc.psum_tensor("psC", [64, 64], F32) as psC,
          nc.psum_tensor("trA_ps", [128, 128], BF16) as trA_ps,
          nc.psum_tensor("trB_ps", [128, 128], BF16) as trB_ps,
          nc.psum_tensor("psR", [1, 64], F32) as psR):
        for bh in range(NBH):
            for ci in range(NCHK):
                cg = bh * NCHK + ci
                co = (ci % WCH) * CH
                T0 = ci * CH
                newwin = (ci % WCH == 0)
                w = ci // WCH
                gw = bh * NWIN + w

                qs = win[:, 0 * WLEN + co:0 * WLEN + co + CH]
                ks_ = win[:, 1 * WLEN + co:1 * WLEN + co + CH]
                vs = win[:, 2 * WLEN + co:2 * WLEN + co + CH]
                aks = win[:, 3 * WLEN + co:3 * WLEN + co + CH]
                acs = win[:, 4 * WLEN + co:4 * WLEN + co + CH]
                eKc = eKp[:, CH - 1:CH]
                eCc = eCp[:, CH - 1:CH]

                with nc.Block() as blk:
                    q = Seq(nc, VE, SC, GP, spp, pe_count, dma_counts)
                    zdep = []
                    if ci == 0:
                        for buf in (S0_sb, C0_sb, G0_sb, S0_bf, C0_bf, G0_bf):
                            zz = q.step("V", [], (lambda e, buf=buf: e.op("memset", buf[:, :], 0.0)))
                        zdep = [zz]
                    dmas = []
                    if newwin:
                        for i5 in range(5):
                            dmas.append(q.step("GD", [], (lambda e, i5=i5, bh=bh, w=w: e.dma_start(
                                out=win[:, i5 * WLEN:(i5 + 1) * WLEN],
                                in_=bass.AP(qk_d, (bh * 5 + i5) * DK * L + w * WLEN,
                                            [[L, 64], [1, WLEN]]))), dma_sem=lwin))
                    windep = [(lwin, 80 * (gw + 1))]
                    s_lnK = q.step("S", windep, lambda e: e.op("activation", out=lnK[:, :], in_=aks, func=AF.Ln))
                    s_lnC = q.step("S", windep, lambda e: e.op("activation", out=lnC[:, :], in_=acs, func=AF.Ln))
                    v_LK = q.step("V", [s_lnK], lambda e: e.op("tensor_tensor_scan", out=LK[:, :],
                                  data0=ones64[:, :], data1=lnK[:, :], initial=0.0,
                                  op0=ALU.mult, op1=ALU.add))
                    v_LC = q.step("V", [s_lnC], lambda e: e.op("tensor_tensor_scan", out=LC[:, :],
                                  data0=ones64[:, :], data1=lnC[:, :], initial=0.0,
                                  op0=ALU.mult, op1=ALU.add))
                    s_eKp = q.step("S", [v_LK], lambda e: e.op("activation", out=eKp[:, :], in_=LK[:, :], func=AF.Exp))
                    s_eKm = q.step("S", [v_LK], lambda e: e.op("activation", out=eKm[:, :], in_=LK[:, :], func=AF.Exp, scale=-1.0))
                    s_eCp = q.step("S", [v_LC], lambda e: e.op("activation", out=eCp[:, :], in_=LC[:, :], func=AF.Exp))
                    s_eCm = q.step("S", [v_LC], lambda e: e.op("activation", out=eCm[:, :], in_=LC[:, :], func=AF.Exp, scale=-1.0))
                    v_QK1 = q.step("V", [s_eKp, windep[0]], lambda e: e.op("tensor_tensor", out=QK1[:, :], in0=qs, in1=eKp[:, :], op=ALU.mult))
                    v_KK1 = q.step("V", [s_eKm], lambda e: e.op("tensor_tensor", out=KK1[:, :], in0=ks_, in1=eKm[:, :], op=ALU.mult))
                    v_KC1 = q.step("V", [s_eCp], lambda e: e.op("tensor_tensor", out=KC1[:, :], in0=ks_, in1=eCp[:, :], op=ALU.mult))
                    v_QC1 = q.step("V", [s_eCm], lambda e: e.op("tensor_tensor", out=QC1[:, :], in0=qs, in1=eCm[:, :], op=ALU.mult))
                    v_Vbf = q.step("V", windep, lambda e: e.op("tensor_copy", out=Vbf[:, :], in_=vs))
                    v_K2 = q.step("V", [v_KK1, s_eKp], lambda e: e.op("tensor_scalar", out=K2[:, :], in0=KK1[:, :], scalar1=eKc, scalar2=None, op0=ALU.mult))
                    v_Q2 = q.step("V", [v_QC1, s_eCp], lambda e: e.op("tensor_scalar", out=Q2[:, :], in0=QC1[:, :], scalar1=eCc, scalar2=None, op0=ALU.mult))
                    t_VT = q.step("T", [v_Vbf], lambda e: e.transpose(trA_ps[:, 0:64], Vbf[:, :], ident_bf[0:64, 0:64]))
                    v_VTc = q.step("V", [t_VT], lambda e: e.op("tensor_copy", out=VT_sb[:, :], in_=trA_ps[:, 0:64]))
                    t_K1T = q.step("T", [v_KK1], lambda e: e.transpose(trB_ps[:, 0:64], KK1[:, :], ident_bf[0:64, 0:64]))
                    v_K1Tc = q.step("V", [t_K1T], lambda e: e.op("tensor_copy", out=KK1T_sb[:, :], in_=trB_ps[:, 0:64]))
                    t_K2T = q.step("T", [v_K2, v_VTc], lambda e: e.transpose(trA_ps[:, 0:64], K2[:, :], ident_bf[0:64, 0:64]))
                    v_K2Tc = q.step("V", [t_K2T], lambda e: e.op("tensor_copy", out=K2T_sb[:, :], in_=trA_ps[:, 0:64]))
                    t_Q2T = q.step("T", [v_Q2, v_K1Tc], lambda e: e.transpose(trB_ps[:, 0:64], Q2[:, :], ident_bf[0:64, 0:64]))
                    v_Q2Tc = q.step("V", [t_Q2T], lambda e: e.op("tensor_copy", out=Q2T_sb[:, :], in_=trB_ps[:, 0:64]))
                    t_A = q.step("T", [v_QK1, v_KK1], lambda e: e.matmul(psA[:, :], KK1[:, :], QK1[:, :], start=True, stop=True))
                    v_mA = q.step("V", [t_A], lambda e: e.op("tensor_tensor", out=PmKT[:, :], in0=psA[:, :], in1=tri_i[:, :], op=ALU.mult))
                    t_B = q.step("T", [v_QC1, v_KC1, v_mA], lambda e: e.matmul(psA[:, :], QC1[:, :], KC1[:, :], start=True, stop=True))
                    v_mB = q.step("V", [t_B], lambda e: e.op("tensor_tensor", out=A2Tm[:, :], in0=psA[:, :], in1=tri_s[:, :], op=ALU.mult))

                    def mk_M(e):
                        e.matmul(psB[:, :], KC1[:, :], C0_bf[:, :], start=True, stop=False)
                        return e.matmul(psB[:, :], A2Tm[:, :], VT_sb[:, :], start=False, stop=True)
                    t_M = q.step("T", [v_mB, v_VTc] + zdep, mk_M)
                    v_Mc = q.step("V", [t_M], lambda e: e.op("tensor_copy", out=M_sb[:, :], in_=psB[:, :]))

                    def mk_W1(e):
                        e.matmul(psB[:, :], QK1[:, :], S0_bf[:, :], start=True, stop=False)
                        return e.matmul(psB[:, :], PmKT[:, :], KK1T_sb[:, :], start=False, stop=True)
                    t_W1 = q.step("T", [v_Mc, v_K1Tc, v_mA] + zdep, mk_W1)
                    v_W1c = q.step("V", [t_W1], lambda e: e.op("tensor_copy", out=W1bf[:, :], in_=psB[:, :]))
                    t_W1T = q.step("T", [v_W1c, v_K2Tc], lambda e: e.transpose(trA_ps[0:64, 0:128], W1bf[:, :], ident_bf[:, :]))
                    v_U = q.step("V", [t_W1T, s_eKp], lambda e: e.op("tensor_tensor", out=U_sb[:, :], in0=trA_ps[0:64, 0:128], in1=eKp[:, :], op=ALU.mult))
                    v_UB = q.step("V", [v_U, s_eCp], lambda e: e.op("tensor_tensor", out=UBbf[:, :], in0=U_sb[:, :], in1=eCp[:, :], op=ALU.mult))
                    t_A3 = q.step("T", [v_UB, v_mB], lambda e: e.matmul(psA[:, :], QC1[:, :], UBbf[:, :], start=True, stop=True))
                    v_m3 = q.step("V", [t_A3], lambda e: e.op("tensor_tensor", out=Pm3T[:, :], in0=psA[:, :], in1=tri_i[:, :], op=ALU.mult))

                    def mk_o1(e):
                        e.matmul(psB[:, :], UBbf[:, :], C0_bf[:, :], start=True, stop=False)
                        return e.matmul(psB[:, :], Pm3T[:, :], VT_sb[:, :], start=False, stop=True)
                    t_o1 = q.step("T", [v_m3, v_W1c] + zdep, mk_o1)
                    v_o1c = q.step("V", [t_o1], lambda e: e.op("tensor_copy", out=o1_sb[:, :], in_=psB[:, :]))

                    def mk_o2(e):
                        e.matmul(psB[:, :], QK1[:, :], G0_bf[:, :], start=True, stop=False)
                        return e.matmul(psB[:, :], PmKT[:, :], M_sb[:, :], start=False, stop=True)
                    t_o2 = q.step("T", [v_o1c] + zdep, mk_o2)
                    o_deps = [t_o2]
                    if cg >= 2:
                        o_deps.append((sto[cg % 2], 16 * (cg // 2)))
                    v_osub = q.step("V", o_deps, lambda e, cg=cg: e.op("tensor_tensor", out=o_sb[cg % 2][:, :], in0=o1_sb[:, :], in1=psB[:, :], op=ALU.subtract))
                    q.step("GD", [v_osub], (lambda e, cg=cg, bh=bh, T0=T0: e.dma_start(
                        out=bass.AP(o_d, bh * L * DK + T0 * DK, [[DK, 128], [1, DK]]),
                        in_=o_sb[cg % 2][:, :])), dma_sem=sto[cg % 2])
                    t_eKcT = q.step("T", [s_eKp], lambda e: e.transpose(psR[0:1, 0:64], eKc, ident_f[0:64, 0:64]))
                    v_eKcc = q.step("V", [t_eKcT], lambda e: e.op("tensor_copy", out=eKcRow[:, :], in_=psR[0:1, 0:64]))
                    t_out = q.step("T", [v_eKcc], lambda e: e.matmul(psC[:, :], eKcRow[0:1, 0:64], eKcRow[0:1, 0:64], start=True, stop=True))
                    v_Ssc = q.step("V", [t_out], lambda e: e.op("tensor_tensor", out=Stmp[:, :], in0=psC[:, :], in1=S0_sb[:, :], op=ALU.mult))
                    t_SK = q.step("T", [v_Ssc, v_K2Tc, t_W1], lambda e: e.matmul(psC[:, :], K2T_sb[:, :], K2T_sb[:, :], start=True, stop=True))
                    v_Sadd = q.step("V", [t_SK], lambda e: e.op("tensor_tensor", out=S0_sb[:, :], in0=Stmp[:, :], in1=psC[:, :], op=ALU.add))
                    v_Sbf = q.step("V", [v_Sadd], lambda e: e.op("tensor_copy", out=S0_bf[:, :], in_=S0_sb[:, :]))
                    v_Csc = q.step("V", [s_eCp, t_o1], lambda e: e.op("tensor_scalar", out=Ctmp[:, :], in0=C0_sb[:, :], scalar1=eCc, scalar2=None, op0=ALU.mult))
                    t_CQ = q.step("T", [v_Sadd, v_Q2Tc], lambda e: e.matmul(psC[:, :], Q2T_sb[:, :], VT_sb[:, :], start=True, stop=True))
                    v_Cadd = q.step("V", [t_CQ, v_Csc], lambda e: e.op("tensor_tensor", out=C0_sb[:, :], in0=Ctmp[:, :], in1=psC[:, :], op=ALU.add))
                    v_Cbf = q.step("V", [v_Cadd], lambda e: e.op("tensor_copy", out=C0_bf[:, :], in_=C0_sb[:, :]))
                    v_Gsc = q.step("V", [s_eKp, t_o2], lambda e: e.op("tensor_scalar", out=Gtmp[:, :], in0=G0_sb[:, :], scalar1=eKc, scalar2=None, op0=ALU.mult))
                    t_GK = q.step("T", [v_Cadd, v_Mc], lambda e: e.matmul(psC[:, :], K2T_sb[:, :], M_sb[:, :], start=True, stop=True))
                    v_Gadd = q.step("V", [t_GK, v_Gsc], lambda e: e.op("tensor_tensor", out=G0_sb[:, :], in0=Gtmp[:, :], in1=psC[:, :], op=ALU.add))
                    q.step("V", [v_Gadd], lambda e: e.op("tensor_copy", out=G0_bf[:, :], in_=G0_sb[:, :]))
                    if trunc is not None:
                        q.steps = q.steps[:trunc]
                    q.emit(blk)

    with nc.Block() as blk:
        @blk.gpsimd
        def _(e):
            for s_ in sto:
                if id(s_) in dma_counts:
                    e.wait_ge(s_, dma_counts[id(s_)])

    es.close()
    _BUILD_CACHE[key] = nc
    return nc



def _bcast_row_ap(handle, n, offset=0):
    return bass.AP(handle, offset, [[0, 128], [1, n]])


def _emit_top1_routing(nc, q, VE, SC, GP, spp, tiles, cap,
                       lgall, lgown, mcpre8, ohT, ohTown, postokT, pm8,
                       mxg, mxo, prefix8, keeprow, keeprow_bf, keepcol, ohk,
                       ohtile_bf, ones64r, psT8pair, psS, psKpair, ident_bf, lddep):
    psT8a, psT8b = psT8pair
    psKa, psKb = psKpair
    """Top-1 expert routing with global capacity in token order.

    lgall:  [128, 128*8] all-token logits token-major (replicated input)
    lgown:  [128, NT*8] own-token logits token-major
    mcpre8: [8, T] f32 prefix mask (1 for tokens before this core's range)
    Produces keepcol [128, NT] (keep per own token) and ohk [128, NT*8]
    (= onehot * keep per own token).
    """
    NGT = 128
    vmx = q.step("V", lddep, lambda e: e.op(
        "tensor_reduce", out=mxg[:, :],
        in_=bass.AP(lgall.tensor if hasattr(lgall, "tensor") else lgall, 0,
                    [[128 * 8, 128], [8, NGT], [1, 8]]),
        axis=AX.X, op=ALU.max))
    tprev = []
    vcp = {}
    for jj in range(NGT):
        voh = q.step("V", [vmx], (lambda e, jj=jj: e.op(
            "tensor_scalar", out=ohtile_bf[:, :], in0=lgall[:, jj * 8:(jj + 1) * 8],
            scalar1=mxg[:, jj:jj + 1], scalar2=None, op0=ALU.is_equal)))
        pst = psT8a if jj % 2 == 0 else psT8b
        tt = q.step("T", [voh] + tprev, (lambda e, pst=pst: e.transpose(
            pst[0:8, 0:128], ohtile_bf[:, :], ident_bf[:, :])))
        vv = q.step("V", [tt], (lambda e, jj=jj, pst=pst: e.op(
            "tensor_copy", out=ohT[0:8, jj * 128:(jj + 1) * 128],
            in_=pst[0:8, 0:128])))
        tprev = [vv]
        vcp[jj] = vv
    vz = q.step("V", [], lambda e: e.op("memset", prefix8[:, :], 0.0))
    prev = vz
    for ck in range(T // 512):
        deps = [prev, vcp[min(NGT - 1, (ck + 1) * 4 - 1)]]
        v1 = q.step("V", deps, (lambda e, ck=ck: e.op(
            "tensor_tensor", out=pm8[:, :], in0=ohT[0:8, ck * 512:(ck + 1) * 512],
            in1=mcpre8[0:8, ck * 512:(ck + 1) * 512], op=ALU.mult)))
        v2 = q.step("V", [v1], (lambda e: e.op(
            "tensor_reduce", out=pm8[:, 512:513] if False else prefix8[:, 1:2],
            in_=pm8[:, :], axis=AX.X, op=ALU.add)))
        prev = q.step("V", [v2], (lambda e: e.op(
            "tensor_tensor", out=prefix8[:, 0:1], in0=prefix8[:, 0:1],
            in1=prefix8[:, 1:2], op=ALU.add)))
    vmxo = q.step("V", lddep, lambda e: e.op(
        "tensor_reduce", out=mxo[:, :],
        in_=bass.AP(lgown.tensor if hasattr(lgown, "tensor") else lgown, 0,
                    [[NT * 8, 128], [8, NT], [1, 8]]),
        axis=AX.X, op=ALU.max))
    tprev2 = []
    vown = {}
    for j in range(NT):
        voh = q.step("V", [vmxo], (lambda e, j=j: e.op(
            "tensor_scalar", out=ohtile_bf[:, :], in0=lgown[:, j * 8:(j + 1) * 8],
            scalar1=mxo[:, j:j + 1], scalar2=None, op0=ALU.is_equal)))
        pst = psT8a if j % 2 == 0 else psT8b
        tt = q.step("T", [voh] + tprev2, (lambda e, pst=pst: e.transpose(
            pst[0:8, 0:128], ohtile_bf[:, :], ident_bf[:, :])))
        vv = q.step("V", [tt], (lambda e, j=j, pst=pst: e.op(
            "tensor_copy", out=ohTown[0:8, j * 128:(j + 1) * 128],
            in_=pst[0:8, 0:128])))
        tprev2 = [vv]
        vown[j] = vv
    vscan = q.step("V", [prev, vown[NT - 1]], lambda e: e.op(
        "tensor_tensor_scan", out=postokT[0:8, :], data0=ones64r[0:8, 0:TC],
        data1=ohTown[0:8, :], initial=prefix8[:, 0:1], op0=ALU.mult, op1=ALU.add))
    vkr = {}
    for ck in range(TC // 512):
        v1 = q.step("V", [vscan], (lambda e, ck=ck: e.op(
            "tensor_tensor", out=pm8[:, :], in0=postokT[0:8, ck * 512:(ck + 1) * 512],
            in1=ohTown[0:8, ck * 512:(ck + 1) * 512], op=ALU.mult)))
        t1 = q.step("T", [v1], (lambda e: e.matmul(
            psS[0:1, :], ones64r[0:8, 0:1], pm8[0:8, :], start=True, stop=True)))
        v2 = q.step("V", [t1], (lambda e, ck=ck: e.op(
            "tensor_scalar", out=keeprow[0:1, ck * 512:(ck + 1) * 512], in0=psS[0:1, :],
            scalar1=float(cap) + 0.5, scalar2=None, op0=ALU.is_le)))
        vkr[ck] = v2
    vkb = q.step("V", [vkr[k] for k in vkr], lambda e: e.op(
        "tensor_copy", out=keeprow_bf[0:1, :], in_=keeprow[0:1, :]))
    tprev3 = []
    vkc = {}
    for j in range(NT):
        psk = psKa if j % 2 == 0 else psKb
        tt = q.step("T", [vkb] + tprev3, (lambda e, j=j, psk=psk: e.transpose(
            psk[:, 0:1], keeprow_bf[0:1, j * 128:(j + 1) * 128],
            ident_bf[0:1, 0:1])))
        vv = q.step("V", [tt], (lambda e, j=j, psk=psk: e.op(
            "tensor_copy", out=keepcol[:, j:j + 1], in_=psk[:, 0:1])))
        tprev3 = [vv]
        vkc[j] = vv
    vohk = {}
    for j in range(NT):
        vohk[j] = q.step("V", [vkc[j], vmxo], (lambda e, j=j: e.op(
            "tensor_scalar", out=ohk[:, j * 8:(j + 1) * 8], in0=lgown[:, j * 8:(j + 1) * 8],
            scalar1=mxo[:, j:j + 1], scalar2=keepcol[:, j:j + 1],
            op0=ALU.is_equal, op1=ALU.mult)))
    return vohk



def build_l1c():
    if "l1c" in _BUILD_CACHE:
        return _BUILD_CACHE["l1c"]
    NDC = D // 128
    FF = 4096
    NHG = FF // 512
    NHC = FF // 128
    nc = _nc()
    x_d = nc.dram_tensor("x", [TC, D], F32, kind="ExternalInput")
    oT_d = nc.dram_tensor("oTall", [D, TC], F32, kind="ExternalInput")
    wo_d = nc.dram_tensor("wo", [D, D], F32, kind="ExternalInput")
    xpT_d = nc.dram_tensor("xpT", [D, TC], F32, kind="ExternalInput")
    lgall_d = nc.dram_tensor("lgall2d", [128, 128 * 8], F32, kind="ExternalInput")
    lgown_d = nc.dram_tensor("lgown2d", [128, NT * 8], F32, kind="ExternalInput")
    mcpre_d = nc.dram_tensor("mcpre", [T], F32, kind="ExternalInput")
    w1_d = nc.dram_tensor("w1", [8, D, FF], F32, kind="ExternalInput")
    b1_d = nc.dram_tensor("b1", [8, FF], F32, kind="ExternalInput")
    g1_d = nc.dram_tensor("g1", [8, FF], F32, kind="ExternalInput")
    bb1_d = nc.dram_tensor("bb1", [8, FF], F32, kind="ExternalInput")
    w2_d = nc.dram_tensor("w2", [8, FF, D], F32, kind="ExternalInput")
    b2_d = nc.dram_tensor("b2", [8, D], F32, kind="ExternalInput")
    lng_d = nc.dram_tensor("lng", [D], F32, kind="ExternalInput")
    lnb_d = nc.dram_tensor("lnb", [D], F32, kind="ExternalInput")
    x2_d = nc.dram_tensor("x2", [TC, D], F32, kind="ExternalOutput")

    es, sb, sem, VE, SC, GP = _mk_ctx(nc)
    xp_bf = sb("xp_bf", [128, NDC * TC], BF16)
    acc = sb("acc", [128, NT * D], BF16)
    lgall = sb("lgall", [128, 128 * 8])
    lgown = sb("lgown", [128, NT * 8])
    mxg = sb("mxg", [128, 128])
    mxo = sb("mxo", [128, NT])
    keepcol = sb("keepcol", [128, NT])
    ohk = sb("ohk", [128, NT * 8])
    ohtile_bf = sb("ohtile_bf", [128, 8], BF16)
    ident_bf = sb("ident_bf", [128, 128], BF16)
    eps_t = sb("eps_t", [128, 1])
    prefix8 = sb("prefix8", [8, 2])
    pm8 = sb("pm8", [8, 512])

    ld = sem("ld")
    ldg = sem("ldg")
    lwe = sem("lwe")
    lbc = sem("lbc")
    lht = sem("lht")
    lhtb = sem("lhtb")
    lto = sem("lto")
    lx = [sem("lx0"), sem("lx1")]
    sst = [sem("sst0"), sem("sst1")]
    spp = sem("spp")
    pe_count = [0]
    dma_counts = {}

    with nc.Block() as blk:
        @blk.sync
        def _(e):
            e.dma_start(out=lgall[:, :], in_=lgall_d.ap()).then_inc(ld, 16)
            e.dma_start(out=lgown[:, :], in_=lgown_d.ap()).then_inc(ld, 16)

        @blk.gpsimd
        def _(e):
            _build_ident(GP, ident_bf)
            GP.op("memset", eps_t[:, :], 1e-5)
            GP.op("memset", acc[:, :], 0.0)
            e.dma_start(out=xp_bf[:, :],
                        in_=bass.AP(xpT_d, 0, [[TC, 128], [128 * TC, NDC], [1, TC]])
                        ).then_inc(ldg, 16)

    with (nc.sbuf_tensor("ohT", [8, T], BF16) as ohT,
          nc.sbuf_tensor("ohTown", [8, TC], BF16) as ohTown,
          nc.sbuf_tensor("postokT", [8, TC], F32) as postokT,
          nc.sbuf_tensor("mcpre8", [8, T], F32) as mcpre8,
          nc.sbuf_tensor("keeprow", [1, TC], F32) as keeprow,
          nc.sbuf_tensor("keeprow_bf", [1, TC], BF16) as keeprow_bf,
          nc.sbuf_tensor("ones64r", [8, TC], F32) as ones64r,
          nc.psum_tensor("psT8a", [8, 128], BF16) as psT8a,
          nc.psum_tensor("psT8b", [8, 128], BF16) as psT8b,
          nc.psum_tensor("psS", [1, 512], F32) as psS,
          nc.psum_tensor("psKa", [128, 1], BF16) as psKa,
          nc.psum_tensor("psKb", [128, 1], BF16) as psKb):
        with nc.Block() as blk:
            @blk.sync
            def _(e):
                e.dma_start(out=mcpre8[0:8, :],
                            in_=bass.AP(mcpre_d, 0, [[0, 8], [1, T]])).then_inc(ld, 16)

            @blk.gpsimd
            def _(e):
                GP.op("memset", ones64r[:, :], 1.0)

        with nc.Block() as blk:
            q = Seq(nc, VE, SC, GP, spp, pe_count, dma_counts)
            _emit_top1_routing(nc, q, VE, SC, GP, spp, NT, 4096.0,
                               lgall, lgown, mcpre8, ohT, ohTown, postokT, pm8,
                               mxg, mxo, prefix8, keeprow, keeprow_bf, keepcol, ohk,
                               ohtile_bf, ones64r, (psT8a, psT8b), psS, (psKa, psKb), ident_bf,
                               [(ld, 48)])
            q.emit(blk)

    hT_scr = nc.dram_tensor("hT_scr", [NT, NHC * 128 * 128], BF16, kind="Internal")
    with (nc.sbuf_tensor("wbuf", [128, NDC * FF], BF16) as wbuf,
          nc.sbuf_tensor("b1b", [128, FF], BF16) as b1b,
          nc.sbuf_tensor("g1b", [128, FF], BF16) as g1b,
          nc.sbuf_tensor("bb1b", [128, FF], BF16) as bb1b,
          nc.sbuf_tensor("b2b", [128, D], BF16) as b2b,
          nc.sbuf_tensor("h_bf", [128, FF], BF16) as h_bf,
          nc.sbuf_tensor("hT_j", [128, NHC * 128], BF16) as hT_j,
          nc.sbuf_tensor("sqc", [128, 512], F32) as sqc,
          nc.sbuf_tensor("lnsc", [128, 16], F32) as lnsc,
          nc.psum_tensor("h_ps0", [128, 512], F32) as hps0,
          nc.psum_tensor("h_ps1", [128, 512], F32) as hps1,
          nc.psum_tensor("trhA", [128, 128], BF16) as trhA,
          nc.psum_tensor("trhB", [128, 128], BF16) as trhB,
          nc.psum_tensor("y_ps", [128, D], F32) as y_ps):
        h_ps = [hps0, hps1]
        assert NDC * FF == NHC * D
        for eix in range(8):
            with nc.Block() as blk:
                @blk.gpsimd
                def _(e, eix=eix):
                    e.dma_start(out=wbuf[:, :],
                                in_=bass.AP(w1_d, eix * D * FF, [[FF, 128], [128 * FF, NDC], [1, FF]])
                                ).then_inc(lwe, 16)
                    e.dma_start(out=b1b[:, :], in_=_bcast_row_ap(b1_d, FF, eix * FF)).then_inc(lbc, 16)
                    e.dma_start(out=g1b[:, :], in_=_bcast_row_ap(g1_d, FF, eix * FF)).then_inc(lbc, 16)
                    e.dma_start(out=bb1b[:, :], in_=_bcast_row_ap(bb1_d, FF, eix * FF)).then_inc(lbc, 16)

            with nc.Block() as blk:
                q = Seq(nc, VE, SC, GP, spp, pe_count, dma_counts)
                vh_prev = None
                vt_prev = None
                for j in range(NT):
                    wdep = [(lwe, 32 * eix + 16), (lbc, (64 * eix) + 48), (ldg, 16)]
                    vh = {}
                    for hg in range(NHG):
                        pp = hg % 2
                        deps = list(wdep)
                        if hg >= 2:
                            deps.append(vh[hg - 2])
                        elif vh_prev is not None:
                            deps.append(vh_prev[NHG - 2 + hg])

                        def mk_h(e, hg=hg, pp=pp, j=j):
                            last = None
                            for dc in range(NDC):
                                last = e.matmul(h_ps[pp][:, :],
                                                xp_bf[:, dc * TC + j * 128:dc * TC + (j + 1) * 128],
                                                wbuf[:, dc * FF + hg * 512:dc * FF + (hg + 1) * 512],
                                                start=(dc == 0), stop=(dc == NDC - 1))
                            return last
                        th = q.step("T", deps, mk_h)
                        vh[hg] = q.step("V", [th], (lambda e, hg=hg, pp=pp: e.op(
                            "tensor_tensor", out=h_bf[:, hg * 512:(hg + 1) * 512],
                            in0=h_ps[pp][:, :], in1=b1b[:, hg * 512:(hg + 1) * 512], op=ALU.add)))
                    hdeps = [vh[hg] for hg in range(NHG)]
                    v_m = q.step("V", hdeps, lambda e: e.op(
                        "reduce_sum", out=lnsc[:, 0:1], in_=h_bf[:, :], axis=AX.X))
                    s_m = q.step("S", [v_m], lambda e: e.op("mul", lnsc[:, 0:1], lnsc[:, 0:1], 1.0 / FF))
                    vv = []
                    prev = s_m
                    for hg in range(NHG):
                        v1 = q.step("V", [prev], (lambda e, hg=hg: e.op(
                            "tensor_scalar", out=sqc[:, :], in0=h_bf[:, hg * 512:(hg + 1) * 512],
                            scalar1=lnsc[:, 0:1], scalar2=None, op0=ALU.subtract)))
                        v2 = q.step("V", [v1], lambda e: e.op(
                            "tensor_tensor", out=sqc[:, :], in0=sqc[:, :], in1=sqc[:, :], op=ALU.mult))
                        v3 = q.step("V", [v2], (lambda e, hg=hg: e.op(
                            "reduce_sum", out=lnsc[:, 4 + hg:5 + hg], in_=sqc[:, :], axis=AX.X)))
                        prev = v3
                        vv.append(v3)
                    v_v = q.step("V", vv, lambda e: e.op(
                        "reduce_sum", out=lnsc[:, 1:2], in_=lnsc[:, 4:12], axis=AX.X))
                    s_sd = q.step("S", [v_v], lambda e: e.op(
                        "activation", out=lnsc[:, 2:3], in_=lnsc[:, 1:2], func=AF.Sqrt,
                        bias=eps_t[:, 0:1], scale=1.0 / FF))
                    v_rs = q.step("V", [s_sd], lambda e: e.op(
                        "reciprocal", out=lnsc[:, 3:4], in_=lnsc[:, 2:3]))
                    v_n1 = q.step("V", [v_rs], lambda e: e.op(
                        "tensor_scalar", out=h_bf[:, :], in0=h_bf[:, :],
                        scalar1=lnsc[:, 0:1], scalar2=lnsc[:, 3:4],
                        op0=ALU.subtract, op1=ALU.mult))
                    v_n2 = q.step("V", [v_n1], lambda e: e.op(
                        "tensor_tensor", out=h_bf[:, :], in0=h_bf[:, :], in1=g1b[:, :], op=ALU.mult))
                    v_n3 = q.step("V", [v_n2], lambda e: e.op(
                        "tensor_tensor", out=h_bf[:, :], in0=h_bf[:, :], in1=bb1b[:, :], op=ALU.add))
                    s_gel = q.step("S", [v_n3], lambda e: e.op(
                        "activation", out=h_bf[:, :], in_=h_bf[:, :], func=AF.Gelu))
                    vt = {}
                    nstores = eix * NT + j
                    stdep = [(lht, 16 * nstores)] if nstores >= 1 else []
                    for hc in range(NHC):
                        trh = trhA if hc % 2 == 0 else trhB
                        deps = [s_gel]
                        if hc >= 2:
                            deps.append(vt[hc - 2])
                        elif vt_prev is not None:
                            deps.append(vt_prev[NHC - 2 + hc])
                        tt = q.step("T", deps, (lambda e, hc=hc, trh=trh: e.transpose(
                            trh[:, 0:128], h_bf[:, hc * 128:(hc + 1) * 128],
                            ident_bf[:, :])))
                        vt[hc] = q.step("V", [tt] + (stdep if hc < 2 else []),
                                        (lambda e, hc=hc, trh=trh: e.op(
                            "tensor_copy", out=hT_j[:, hc * 128:(hc + 1) * 128],
                            in_=trh[:, 0:128])))
                    q.step("GD", [vt[hc] for hc in range(NHC)],
                           (lambda e, j=j: e.dma_start(
                               out=bass.AP(hT_scr, j * NHC * 128 * 128,
                                           [[NHC * 128, 128], [1, NHC * 128]]),
                               in_=hT_j[:, :])), dma_sem=lht)
                    vh_prev = vh
                    vt_prev = vt
                q.emit(blk)

            with nc.Block() as blk:
                @blk.gpsimd
                def _(e, eix=eix):
                    e.wait_ge(lht, dma_counts[id(lht)])
                    e.dma_start(out=wbuf[:, :],
                                in_=bass.AP(w2_d, eix * FF * D, [[D, 128], [128 * D, NHC], [1, D]])
                                ).then_inc(lwe, 16)
                    e.dma_start(out=b2b[:, :], in_=_bcast_row_ap(b2_d, D, eix * D)).then_inc(lbc, 16)

            with nc.Block() as blk:
                q = Seq(nc, VE, SC, GP, spp, pe_count, dma_counts)
                prev_ty = None
                prev_va = None
                for j in range(NT):
                    dh = q.step("GD", ([prev_ty] if prev_ty is not None else []),
                                (lambda e, j=j: e.dma_start(
                        out=hT_j[:, :],
                        in_=bass.AP(hT_scr, j * NHC * 128 * 128,
                                    [[NHC * 128, 128], [1, NHC * 128]]))), dma_sem=lhtb)
                    wdep = [(lwe, 32 * (eix + 1)), (lbc, 64 * eix + 64), dh]
                    if prev_va is not None:
                        wdep.append(prev_va)

                    def mk_y(e, j=j):
                        last = None
                        for hc in range(NHC):
                            e.matmul(y_ps[:, 0:512], hT_j[:, hc * 128:(hc + 1) * 128],
                                     wbuf[:, hc * D:hc * D + 512],
                                     start=(hc == 0), stop=(hc == NHC - 1))
                            last = e.matmul(y_ps[:, 512:1024], hT_j[:, hc * 128:(hc + 1) * 128],
                                            wbuf[:, hc * D + 512:(hc + 1) * D],
                                            start=(hc == 0), stop=(hc == NHC - 1))
                        return last
                    t_y = q.step("T", wdep, mk_y)
                    v_a1 = q.step("V", [t_y], (lambda e, j=j, eix=eix: e.op(
                        "scalar_tensor_tensor", out=acc[:, j * D:(j + 1) * D],
                        in0=y_ps[:, :], scalar=ohk[:, j * 8 + eix:j * 8 + eix + 1],
                        in1=acc[:, j * D:(j + 1) * D], op0=ALU.mult, op1=ALU.add)))
                    q.step("V", [v_a1], (lambda e, j=j, eix=eix: e.op(
                        "scalar_tensor_tensor", out=acc[:, j * D:(j + 1) * D],
                        in0=b2b[:, :], scalar=ohk[:, j * 8 + eix:j * 8 + eix + 1],
                        in1=acc[:, j * D:(j + 1) * D], op0=ALU.mult, op1=ALU.add)))
                    prev_ty = t_y
                    prev_va = v_a1
                q.emit(blk)

    with (nc.sbuf_tensor("oT_bf", [128, NDC * TC], BF16) as oT_bf,
          nc.sbuf_tensor("wo_sb", [128, NDC * D], BF16) as wo_sb,
          nc.sbuf_tensor("lngb", [128, D], F32) as lngb,
          nc.sbuf_tensor("lnbb", [128, D], F32) as lnbb,
          nc.sbuf_tensor("fin", [128, D], F32) as fin,
          nc.sbuf_tensor("gsil", [128, D], F32) as gsil,
          nc.sbuf_tensor("sq2", [128, D], F32) as sq2,
          nc.sbuf_tensor("x1t0", [128, D], F32) as x1t0,
          nc.sbuf_tensor("x1t1", [128, D], F32) as x1t1,
          nc.sbuf_tensor("lns2", [128, 8], F32) as lns2,
          nc.psum_tensor("s_ps", [128, D], F32) as s_ps):
        x1t = [x1t0, x1t1]
        with nc.Block() as blk:
            @blk.gpsimd
            def _(e):
                e.dma_start(out=oT_bf[:, :],
                            in_=bass.AP(oT_d, 0, [[TC, 128], [128 * TC, NDC], [1, TC]])
                            ).then_inc(lto, 16)
                e.dma_start(out=wo_sb[:, :],
                            in_=bass.AP(wo_d, 0, [[D, 128], [128 * D, NDC], [1, D]])
                            ).then_inc(lto, 16)
                e.dma_start(out=lngb[:, :], in_=_bcast_row_ap(lng_d, D)).then_inc(lto, 16)
                e.dma_start(out=lnbb[:, :], in_=_bcast_row_ap(lnb_d, D)).then_inc(lto, 16)

        with nc.Block() as blk:
            q = Seq(nc, VE, SC, GP, spp, pe_count, dma_counts)
            ldp = [(lto, 64)]
            prev_out = {}
            st_steps = {}
            for j in range(NT):
                pp = j % 2
                xdeps = []
                if j >= 2:
                    xdeps = [prev_out[j - 2], st_steps[j - 2]]
                dx = q.step("SD", xdeps, (lambda e, j=j, pp=pp: e.dma_start(
                    out=x1t[pp][:, :], in_=_ap3_tokmaj(x_d, 1, D, offset=j * 128 * D))),
                    dma_sem=lx[pp])
                v_m = q.step("V", [], (lambda e, j=j: e.op(
                    "reduce_sum", out=lns2[:, 0:1], in_=acc[:, j * D:(j + 1) * D], axis=AX.X)))
                s_m = q.step("S", [v_m], lambda e: e.op("mul", lns2[:, 0:1], lns2[:, 0:1], 1.0 / D))
                v_sq = q.step("V", [s_m], (lambda e, j=j: e.op(
                    "tensor_scalar", out=sq2[:, :], in0=acc[:, j * D:(j + 1) * D],
                    scalar1=lns2[:, 0:1], scalar2=None, op0=ALU.subtract)))
                v_sq2 = q.step("V", [v_sq], lambda e: e.op(
                    "tensor_tensor", out=sq2[:, :], in0=sq2[:, :], in1=sq2[:, :], op=ALU.mult))
                v_v = q.step("V", [v_sq2], lambda e: e.op(
                    "reduce_sum", out=lns2[:, 1:2], in_=sq2[:, :], axis=AX.X))
                s_sd = q.step("S", [v_v], lambda e: e.op(
                    "activation", out=lns2[:, 2:3], in_=lns2[:, 1:2], func=AF.Sqrt,
                    bias=eps_t[:, 0:1], scale=1.0 / D))
                v_rs = q.step("V", [s_sd], lambda e: e.op(
                    "reciprocal", out=lns2[:, 3:4], in_=lns2[:, 2:3]))
                v_f1 = q.step("V", [v_rs], (lambda e, j=j: e.op(
                    "tensor_scalar", out=fin[:, :], in0=acc[:, j * D:(j + 1) * D],
                    scalar1=lns2[:, 0:1], scalar2=lns2[:, 3:4],
                    op0=ALU.subtract, op1=ALU.mult)))
                v_f2 = q.step("V", [v_f1] + ldp, lambda e: e.op(
                    "tensor_tensor", out=fin[:, :], in0=fin[:, :], in1=lngb[:, :], op=ALU.mult))
                v_f3 = q.step("V", [v_f2], lambda e: e.op(
                    "tensor_tensor", out=fin[:, :], in0=fin[:, :], in1=lnbb[:, :], op=ALU.add))
                s_sil = q.step("S", [v_f3], lambda e: e.op(
                    "activation", out=gsil[:, :], in_=fin[:, :], func=AF.Silu))

                def mk_s(e, j=j):
                    last = None
                    for hv in range(NDC):
                        e.matmul(s_ps[:, 0:512], oT_bf[:, hv * TC + j * 128:hv * TC + (j + 1) * 128],
                                 wo_sb[:, hv * D:hv * D + 512],
                                 start=(hv == 0), stop=(hv == NDC - 1))
                        last = e.matmul(s_ps[:, 512:1024],
                                        oT_bf[:, hv * TC + j * 128:hv * TC + (j + 1) * 128],
                                        wo_sb[:, hv * D + 512:(hv + 1) * D],
                                        start=(hv == 0), stop=(hv == NDC - 1))
                    return last
                sdeps = ldp + ([prev_out[j - 1]] if j >= 1 else [])
                t_s = q.step("T", sdeps, mk_s)
                v_sg = q.step("V", [t_s, s_sil], lambda e: e.op(
                    "tensor_tensor", out=gsil[:, :], in0=gsil[:, :], in1=s_ps[:, :], op=ALU.mult))
                v_o = q.step("V", [v_sg, (lx[pp], dma_counts.get(id(lx[pp]), 0) + 16) if False else dx],
                             (lambda e, pp=pp: e.op(
                                 "tensor_tensor", out=x1t[pp][:, :], in0=x1t[pp][:, :],
                                 in1=gsil[:, :], op=ALU.add)))
                prev_out[j] = v_o
                st_steps[j] = q.step("GD", [v_o], (lambda e, j=j, pp=pp: e.dma_start(
                    out=_ap3_tokmaj(x2_d, 1, D, offset=j * 128 * D), in_=x1t[pp][:, :])),
                    dma_sem=sst[pp])
            q.emit(blk)

    with nc.Block() as blk:
        @blk.gpsimd
        def _(e):
            e.wait_ge(sst[0], dma_counts[id(sst[0])])
            e.wait_ge(sst[1], dma_counts[id(sst[1])])

    es.close()
    _BUILD_CACHE["l1c"] = nc
    return nc






def _subview3(ap_or_h, nblk):
    t = ap_or_h.tensor if isinstance(ap_or_h, bass.AP) else ap_or_h
    off = ap_or_h.offset if isinstance(ap_or_h, bass.AP) else 0
    pstep = t.shape[1]
    return bass.AP(t, off, [[pstep, 128], [16, nblk], [1, 16]])


def _subbc(ap_or_h, rel_off, nblk):
    t = ap_or_h.tensor if isinstance(ap_or_h, bass.AP) else ap_or_h
    off = (ap_or_h.offset if isinstance(ap_or_h, bass.AP) else 0) + rel_off
    pstep = t.shape[1]
    return bass.AP(t, off, [[pstep, 128], [1, nblk], [0, 16]])


def _emit_fp4_quant(q, src, dst, scl, sgn, accq, cqt, nblk, dwid, deps, accq_w=None, scl_w=None):
    if accq_w is None:
        accq_w = accq.shape[1]
    if scl_w is None:
        scl_w = scl.shape[1]
    GRIDW = [0.5, 0.5, 0.5, 0.5, 1.0, 1.0, 2.0]
    MID = [0.25, 0.75, 1.25, 1.75, 2.5, 3.5, 5.0]
    v_ab = q.step("S", deps, lambda e: e.op("activation", out=accq[:, 0:dwid], in_=src, func=AF.Abs))
    v_sg = q.step("S", deps, lambda e: e.op("activation", out=sgn[:, 0:dwid], in_=src, func=AF.Sign))
    v_mx = q.step("V", [v_ab], (lambda e: e.op(
        "tensor_reduce", out=scl[:, 0:nblk],
        in_=_subview3(accq, nblk),
        axis=AX.X, op=ALU.max)))
    v_sc = q.step("V", [v_mx], (lambda e: e.op(
        "tensor_scalar", out=scl[:, 0:nblk], in0=scl[:, 0:nblk], scalar1=1.0 / 6.0,
        scalar2=1e-12, op0=ALU.mult, op1=ALU.max)))
    vacc = v_sc
    for ji in range(7):
        v_sm = q.step("V", [vacc], (lambda e, ji=ji: e.op(
            "tensor_scalar", out=scl[:, nblk:2 * nblk], in0=scl[:, 0:nblk],
            scalar1=MID[ji], scalar2=None, op0=ALU.mult)))
        scl_bc = _subbc(scl, nblk, nblk)
        v_c = q.step("V", [v_sm], (lambda e, scl_bc=scl_bc: e.op(
            "tensor_tensor", out=cqt[:, 0:dwid], in0=accq[:, 0:dwid], in1=scl_bc, op=ALU.is_gt)))
        if ji == 0:
            vacc = q.step("V", [v_c], (lambda e, ji=ji: e.op(
                "tensor_scalar", out=dst, in0=cqt[:, 0:dwid], scalar1=GRIDW[ji],
                scalar2=None, op0=ALU.mult)))
        else:
            v_c2 = q.step("V", [v_c], (lambda e, ji=ji: e.op(
                "tensor_scalar", out=cqt[:, 0:dwid], in0=cqt[:, 0:dwid], scalar1=GRIDW[ji],
                scalar2=None, op0=ALU.mult)))
            vacc = q.step("V", [v_c2], (lambda e: e.op(
                "tensor_tensor", out=dst, in0=dst, in1=cqt[:, 0:dwid], op=ALU.add)))
    scl_bc0 = _subbc(scl, 0, nblk)
    v1 = q.step("V", [vacc], (lambda e, scl_bc0=scl_bc0: e.op(
        "tensor_tensor", out=dst, in0=dst, in1=scl_bc0, op=ALU.mult)))
    v2 = q.step("V", [v1, v_sg], lambda e: e.op(
        "tensor_tensor", out=dst, in0=dst, in1=sgn[:, 0:dwid], op=ALU.mult))
    return v2


def build_l2a():
    if "l2a" in _BUILD_CACHE:
        return _BUILD_CACHE["l2a"]
    NTH = NT + 1
    TCH = NTH * 128
    DI = 2048
    nc = _nc()
    x_d = nc.dram_tensor("xh", [TCH, D], F32, kind="ExternalInput")
    nw_d = nc.dram_tensor("nw", [D], F32, kind="ExternalInput")
    nb_d = nc.dram_tensor("nb", [D], F32, kind="ExternalInput")
    ipT_d = nc.dram_tensor("ipT", [D, 4096], F32, kind="ExternalInput")
    cw_d = nc.dram_tensor("cw2d", [128, 16 * 4], F32, kind="ExternalInput")
    cb_d = nc.dram_tensor("cb2d", [128, 16], F32, kind="ExternalInput")
    xpw_d = nc.dram_tensor("xpwT", [DI, 33], F32, kind="ExternalInput")
    hm_d = nc.dram_tensor("halom", [1], F32, kind="ExternalInput")
    wgs_d = nc.dram_tensor("wgs", [384, D], F32, kind="ExternalInput")
    wus_d = nc.dram_tensor("wus", [384, D], F32, kind="ExternalInput")
    wds_d = nc.dram_tensor("wds", [128, 2816], F32, kind="ExternalInput")
    z_d = nc.dram_tensor("zT", [DI, TC], BF16, kind="ExternalOutput")
    xc_d = nc.dram_tensor("xcT", [DI, TC], BF16, kind="ExternalOutput")
    xp_d = nc.dram_tensor("xp33", [33, TC], F32, kind="ExternalOutput")
    wgq_d = nc.dram_tensor("wgq", [384, D], BF16, kind="ExternalOutput")
    wuq_d = nc.dram_tensor("wuq", [384, D], BF16, kind="ExternalOutput")
    wdq_d = nc.dram_tensor("wdq", [128, 2816], BF16, kind="ExternalOutput")

    es, sb, sem, VE, SC, GP = _mk_ctx(nc)
    x_sb = sb("x_sb", [128, NTH * D])
    xn2T = sb("xn2T", [128, 8 * TCH], BF16)
    xiF = sb("xiF", [128, 16 * TCH], BF16)
    xt1 = sb("xt1", [128, D])
    xt2 = sb("xt2", [128, D], BF16)
    sqa = sb("sqa", [128, D])
    lnt = sb("lnt", [128, 8])
    nwb = sb("nwb", [128, D])
    nbb = sb("nbb", [128, D])
    wch = [sb(f"wch{p}", [128, 8 * 256], BF16) for p in range(2)]
    zstage = sb("zstage", [128, 512], BF16)
    cw_sb = sb("cw_sb", [128, 64])
    cb_sb = sb("cb_sb", [128, 16])
    hm_sb = sb("hm_sb", [128, 1])
    xpw_sb = sb("xpw_sb", [128, 16 * 33], BF16)
    xcw = sb("xcw", [128, TC])
    eps_t = sb("eps_t", [128, 1])
    ident_bf = sb("ident_bf", [128, 128], BF16)
    qdst = None
    qsrc = x_sb[:, 0:2816]
    qscl = x_sb[:, 2816:2816 + 352]
    qsgn = x_sb[:, 4096:4096 + 2816]
    qabs = x_sb[:, 8192:8192 + 2816]
    cqt = x_sb[:, 12288:12288 + 2816]
    qdst = xiF[:, 0:2816]

    ld = sem("ld")
    ldg = sem("ldg")
    lw = [sem("lw0"), sem("lw1")]
    st = [sem("st0"), sem("st1")]
    stq = sem("stq")
    lq = sem("lq")
    spp = sem("spp")
    pe_count = [0]
    dma_counts = {}

    with nc.Block() as blk:
        @blk.sync
        def _(e):
            e.dma_start(out=x_sb[:, :], in_=_ap3_tokmaj(x_d, NTH, D)).then_inc(ld, 16)
            e.dma_start(out=nwb[:, :], in_=_bcast_row_ap(nw_d, D)).then_inc(ld, 16)
            e.dma_start(out=nbb[:, :], in_=_bcast_row_ap(nb_d, D)).then_inc(ld, 16)
            e.dma_start(out=cw_sb[:, :], in_=cw_d.ap()).then_inc(ld, 16)
            e.dma_start(out=cb_sb[:, :], in_=cb_d.ap()).then_inc(ld, 16)
            e.dma_start(out=hm_sb[:, :], in_=_bcast_row_ap(hm_d, 1)).then_inc(ld, 16)

        @blk.gpsimd
        def _(e):
            _build_ident(GP, ident_bf)
            GP.op("memset", eps_t[:, :], 1e-5)
            e.dma_start(out=xpw_sb[:, :],
                        in_=bass.AP(xpw_d, 0, [[33, 128], [128 * 33, 16], [1, 33]])).then_inc(ldg, 16)

    with nc.psum_tensor("tr_ps", [128, D], BF16) as tr_ps:
        with nc.Block() as blk:
            q = Seq(nc, VE, SC, GP, spp, pe_count, dma_counts)
            lddep = [(ld, 96)]
            tprev = []
            for j in range(NTH):
                xt = x_sb[:, j * D:(j + 1) * D]
                v1 = q.step("V", lddep, (lambda e, xt=xt: e.op(
                    "reduce_sum", out=lnt[:, 0:1], in_=xt, axis=AX.X)))
                s1 = q.step("S", [v1], lambda e: e.op("mul", lnt[:, 0:1], lnt[:, 0:1], 1.0 / D))
                v2 = q.step("V", [s1], (lambda e, xt=xt: e.op(
                    "tensor_scalar", out=sqa[:, :], in0=xt, scalar1=lnt[:, 0:1],
                    scalar2=None, op0=ALU.subtract)))
                v3 = q.step("V", [v2], lambda e: e.op(
                    "tensor_tensor", out=sqa[:, :], in0=sqa[:, :], in1=sqa[:, :], op=ALU.mult))
                v4 = q.step("V", [v3], lambda e: e.op(
                    "reduce_sum", out=lnt[:, 1:2], in_=sqa[:, :], axis=AX.X))
                s2 = q.step("S", [v4], lambda e: e.op(
                    "activation", out=lnt[:, 2:3], in_=lnt[:, 1:2], func=AF.Sqrt,
                    bias=eps_t[:, 0:1], scale=1.0 / D))
                v5 = q.step("V", [s2], lambda e: e.op(
                    "reciprocal", out=lnt[:, 3:4], in_=lnt[:, 2:3]))
                v6 = q.step("V", [v5], (lambda e, xt=xt: e.op(
                    "tensor_scalar", out=xt1[:, :], in0=xt, scalar1=lnt[:, 0:1],
                    scalar2=lnt[:, 3:4], op0=ALU.subtract, op1=ALU.mult)))
                v7 = q.step("V", [v6], lambda e: e.op(
                    "reduce_sum", out=lnt[:, 4:5], in_=xt1[:, :], axis=AX.X))
                s3 = q.step("S", [v7], lambda e: e.op("mul", lnt[:, 4:5], lnt[:, 4:5], 1.0 / D))
                v8 = q.step("V", [s3], lambda e: e.op(
                    "tensor_scalar", out=sqa[:, :], in0=xt1[:, :], scalar1=lnt[:, 4:5],
                    scalar2=None, op0=ALU.subtract))
                v9 = q.step("V", [v8], lambda e: e.op(
                    "tensor_tensor", out=sqa[:, :], in0=sqa[:, :], in1=sqa[:, :], op=ALU.mult))
                v10 = q.step("V", [v9], lambda e: e.op(
                    "reduce_sum", out=lnt[:, 5:6], in_=sqa[:, :], axis=AX.X))
                s4 = q.step("S", [v10], lambda e: e.op(
                    "activation", out=lnt[:, 6:7], in_=lnt[:, 5:6], func=AF.Sqrt,
                    bias=eps_t[:, 0:1], scale=1.0 / D))
                v11 = q.step("V", [s4], lambda e: e.op(
                    "reciprocal", out=lnt[:, 7:8], in_=lnt[:, 6:7]))
                v12 = q.step("V", [v11], lambda e: e.op(
                    "tensor_scalar", out=xt1[:, :], in0=xt1[:, :], scalar1=lnt[:, 4:5],
                    scalar2=lnt[:, 7:8], op0=ALU.subtract, op1=ALU.mult))
                v13 = q.step("V", [v12], lambda e: e.op(
                    "tensor_tensor", out=xt1[:, :], in0=xt1[:, :], in1=nwb[:, :], op=ALU.mult))
                v14 = q.step("V", [v13], lambda e: e.op(
                    "tensor_tensor", out=xt2[:, :], in0=xt1[:, :], in1=nbb[:, :], op=ALU.add))

                def mktr(e):
                    last = None
                    for k in range(8):
                        last = e.transpose(tr_ps[:, k * 128:(k + 1) * 128],
                                           xt2[:, k * 128:(k + 1) * 128], ident_bf[:, :])
                    return last
                tt = q.step("T", [v14] + tprev, mktr)
                vc = q.step("V", [tt], (lambda e, j=j: e.op(
                    "tensor_copy",
                    out=bass.AP(xn2T, j * 128, [[8 * TCH, 128], [TCH, 8], [1, 128]]),
                    in_=tr_ps[:, :])))
                tprev = [vc]
            q.emit(blk)

    tgl = [(0, 512), (512, 512), (1024, 512), (1536, 512), (2048, 128)]
    with (nc.psum_tensor("pj0", [128, 512], F32) as pj0,
          nc.psum_tensor("pj1", [128, 512], F32) as pj1):
        pj = [pj0, pj1]
        with nc.Block() as blk:
            q = Seq(nc, VE, SC, GP, spp, pe_count, dma_counts)
            nseq = 0
            vstg = {}
            lastv_of_cg = {}
            last_zst = []
            for cg in range(16):
                p = cg % 2
                deps = [lastv_of_cg[cg - 2]] if cg >= 2 else []
                dwl = q.step("GD", deps, (lambda e, cg=cg, p=p: e.dma_start(
                    out=wch[p][:, :],
                    in_=bass.AP(ipT_d, cg * 256, [[4096, 128], [128 * 4096, 8], [1, 256]]))),
                    dma_sem=lw[p])
                for sg in range(2):
                    pc = cg * 2 + sg
                    is_z = pc >= 16
                    for (t0, tl) in tgl:
                        pp = nseq % 2
                        deps2 = [dwl]
                        if nseq >= 2:
                            deps2.append(vstg[nseq - 2])

                        def mk(e, sg=sg, t0=t0, tl=tl, pp=pp, p=p):
                            last = None
                            for k in range(8):
                                last = e.matmul(pj[pp][:, 0:tl],
                                                wch[p][:, k * 256 + sg * 128:k * 256 + (sg + 1) * 128],
                                                xn2T[:, k * TCH + t0:k * TCH + t0 + tl],
                                                start=(k == 0), stop=(k == 7))
                            return last
                        t_m = q.step("T", deps2, mk)
                        if is_z:
                            v_c = q.step("V", [t_m] + last_zst, (lambda e, tl=tl, pp=pp: e.op(
                                "tensor_copy", out=zstage[:, 0:tl], in_=pj[pp][:, 0:tl])))
                            lo = max(t0, 128)
                            ln_ = t0 + tl - lo
                            zst = q.step("GD", [v_c], (lambda e, pc=pc, t0=t0, lo=lo, ln_=ln_: e.dma_start(
                                out=bass.AP(z_d, (pc - 16) * 128 * TC + (lo - 128),
                                            [[TC, 128], [1, ln_]]),
                                in_=zstage[:, lo - t0:lo - t0 + ln_])), dma_sem=st[0])
                            last_zst = [zst]
                        else:
                            v_c = q.step("V", [t_m], (lambda e, pc=pc, t0=t0, tl=tl, pp=pp: e.op(
                                "tensor_copy",
                                out=xiF[:, pc * TCH + t0:pc * TCH + t0 + tl],
                                in_=pj[pp][:, 0:tl])))
                        vstg[nseq] = v_c
                        nseq += 1
                lastv_of_cg[cg] = vstg[nseq - 1]
            q.emit(blk)

    vconv = {}
    xp_ps_ctx = nc.psum_tensor("xp_ps", [33, 512], F32)
    xp_ps = xp_ps_ctx.__enter__()
    with nc.Block() as blk:
        q = Seq(nc, VE, SC, GP, spp, pe_count, dma_counts)
        for ch in range(16):
            v0 = q.step("V", [], (lambda e, ch=ch: e.op(
                "tensor_scalar", out=xiF[:, ch * TCH:ch * TCH + 128],
                in0=xiF[:, ch * TCH:ch * TCH + 128],
                scalar1=hm_sb[:, 0:1], scalar2=None, op0=ALU.mult)))
            v1 = q.step("V", [v0], (lambda e, ch=ch: e.op(
                "tensor_scalar", out=xcw[:, :],
                in0=xiF[:, ch * TCH + 125:ch * TCH + 125 + TC],
                scalar1=cw_sb[:, ch * 4:ch * 4 + 1], scalar2=None, op0=ALU.mult)))
            for jj in range(1, 4):
                v1 = q.step("V", [v1], (lambda e, ch=ch, jj=jj: e.op(
                    "scalar_tensor_tensor", out=xcw[:, :],
                    in0=xiF[:, ch * TCH + 125 + jj:ch * TCH + 125 + jj + TC],
                    scalar=cw_sb[:, ch * 4 + jj:ch * 4 + jj + 1],
                    in1=xcw[:, :], op0=ALU.mult, op1=ALU.add)))
            s1 = q.step("S", [v1], (lambda e, ch=ch: e.op(
                "activation", out=xcw[:, :], in_=xcw[:, :], func=AF.Silu,
                bias=cb_sb[:, ch:ch + 1])))
            v2 = q.step("V", [s1], (lambda e, ch=ch: e.op(
                "tensor_copy", out=xiF[:, ch * TCH:ch * TCH + TC], in_=xcw[:, :])))
            vconv[ch] = v2
            q.step("GD", [v2], (lambda e, ch=ch: e.dma_start(
                out=bass.AP(xc_d, ch * 128 * TC, [[TC, 128], [1, TC]]),
                in_=xiF[:, ch * TCH:ch * TCH + TC])), dma_sem=st[1])
        last_ve = []
        for g in range(4):
            prevt = None
            for ch in range(16):
                deps = [vconv[15], (ldg, 16)] + (last_ve if ch == 0 else [])
                if prevt is not None:
                    deps.append(prevt)

                def mkp(e, ch=ch, g=g):
                    return e.matmul(xp_ps[0:33, 0:512],
                                    xpw_sb[:, ch * 33:(ch + 1) * 33],
                                    xiF[:, ch * TCH + g * 512:ch * TCH + (g + 1) * 512],
                                    start=(ch == 0), stop=(ch == 15))
                prevt = q.step("T", deps, mkp)
            v_e = q.step("V", [prevt], (lambda e, g=g: e.op(
                "tensor_copy",
                out=bass.AP(x_sb, 15104 + g * 512, [[17408, 33], [1, 512]]),
                in_=xp_ps[0:33, 0:512])))
            vconv[("xp", g)] = v_e
            last_ve = [v_e]
        q.step("GD", [vconv[("xp", g)] for g in range(4)],
               lambda e: e.dma_start(out=xp_d.ap(),
                                     in_=bass.AP(x_sb, 15104, [[17408, 33], [1, 2048]])),
               dma_sem=st[1])
        q.emit(blk)
    xp_ps_ctx.__exit__(None, None, None)

    xc_done = (st[1], dma_counts.get(id(st[1]), 0))
    with nc.Block() as blk:
        q = Seq(nc, VE, SC, GP, spp, pe_count, dma_counts)
        prev_q = []
        for (src_d, dst_d, rows, dwid) in [(wgs_d, wgq_d, 384, D), (wus_d, wuq_d, 384, D),
                                           (wds_d, wdq_d, 128, 2816)]:
            nt_ = rows // 128
            nblk = dwid // 16
            for t in range(nt_):
                dl = q.step("GD", list(prev_q), (lambda e, src_d=src_d, t=t, dwid=dwid: e.dma_start(
                    out=qsrc[:, 0:dwid],
                    in_=bass.AP(src_d, t * 128 * dwid, [[dwid, 128], [1, dwid]]))), dma_sem=lq)
                vq = _emit_fp4_quant(q, qsrc[:, 0:dwid], qdst[:, 0:dwid], qscl, qsgn, qabs,
                                     cqt, nblk, dwid, [dl, xc_done])
                stp = q.step("GD", [vq], (lambda e, dst_d=dst_d, t=t, dwid=dwid: e.dma_start(
                    out=bass.AP(dst_d, t * 128 * dwid, [[dwid, 128], [1, dwid]]),
                    in_=qdst[:, 0:dwid])), dma_sem=stq)
                prev_q = [vq, stp]
        q.emit(blk)

    with nc.Block() as blk:
        @blk.gpsimd
        def _(e):
            for s_ in (st[0], st[1], stq):
                if id(s_) in dma_counts:
                    e.wait_ge(s_, dma_counts[id(s_)])

    es.close()
    _BUILD_CACHE["l2a"] = nc
    return nc


def dma_counts_probe(dc, sem):
    return dc.get(id(sem), 0)




def build_l2b():
    if "l2b" in _BUILD_CACHE:
        return _BUILD_CACHE["l2b"]
    DSN = 16
    WL = 2048
    NW = L // WL
    nc = _nc()
    xc_d = nc.dram_tensor("xcg", [4, 128, L], BF16, kind="ExternalInput")
    xp0_d = nc.dram_tensor("xp0", [B, L], F32, kind="ExternalInput")
    bp_d = nc.dram_tensor("bpT", [B, DSN, L], F32, kind="ExternalInput")
    cp_d = nc.dram_tensor("cpT", [B, DSN, L], F32, kind="ExternalInput")
    dtw_d = nc.dram_tensor("dtw2d", [128, 2], F32, kind="ExternalInput")
    dtb_d = nc.dram_tensor("dtb2d", [128, 2], F32, kind="ExternalInput")
    A_d = nc.dram_tensor("A2d", [128, 2 * DSN], F32, kind="ExternalInput")
    dsk_d = nc.dram_tensor("dsk2d", [128, 2], F32, kind="ExternalInput")
    y_d = nc.dram_tensor("yT", [4, 128, L], F32, kind="ExternalOutput")

    es, sb, sem, GPV = None, None, None, None
    es, sb, sem, VE, SC, GP = _mk_ctx(nc)
    xcw_bf = sb("xcw_bf", [128, WL], BF16)
    xcw_bf_view = [None]
    xp0b = sb("xp0b", [128, WL])
    bpb = sb("bpb", [128, WL])
    cpb = sb("cpb", [128, WL])
    dl_ = sb("dl_", [128, WL])
    dx_ = sb("dx_", [128, WL])
    dA_ = sb("dA_", [128, WL])
    dB_ = sb("dB_", [128, WL])
    hn_ = sb("hn_", [128, WL])
    yac = [sb(f"yac{p}", [128, WL]) for p in range(2)]
    tq = sb("tq", [128, WL])
    hst = sb("hst", [128, DSN])
    dtw_sb = sb("dtw_sb", [128, 2])
    dtb_sb = sb("dtb_sb", [128, 2])
    A_sb = sb("A_sb", [128, 2 * DSN])
    dsk_sb = sb("dsk_sb", [128, 2])
    one_t = sb("one_t", [128, 1])
    ones_w = sb("ones_w", [128, WL])

    ld = sem("ld")
    lxc = sem("lxc")
    lx0 = sem("lx0")
    lbp = sem("lbp")
    lcp = sem("lcp")
    sty = [sem("sty0"), sem("sty1")]
    spp = sem("spp")
    pe_count = [0]
    dma_counts = {}

    with nc.Block() as blk:
        @blk.sync
        def _(e):
            e.dma_start(out=dtw_sb[:, :], in_=dtw_d.ap()).then_inc(ld, 16)
            e.dma_start(out=dtb_sb[:, :], in_=dtb_d.ap()).then_inc(ld, 16)
            e.dma_start(out=A_sb[:, :], in_=A_d.ap()).then_inc(ld, 16)
            e.dma_start(out=dsk_sb[:, :], in_=dsk_d.ap()).then_inc(ld, 16)

        @blk.gpsimd
        def _(e):
            GP.op("memset", one_t[:, :], 1.0)
            GP.op("memset", ones_w[:, :], 1.0)

    xcw_bf_view[0] = xcw_bf[:, :]

    prev_xc_rd = []
    prev_x0_rd = []
    for g in range(4):
        bb = g // 2
        cg = g % 2
        with nc.Block() as blk:
            @blk.vector
            def _(e):
                VE.op("memset", hst[:, :], 0.0)

        for w in range(NW):
            gw = g * NW + w
            with nc.Block() as blk:
                q = Seq(nc, VE, SC, GP, spp, pe_count, dma_counts)
                dxc = q.step("GD", prev_xc_rd, (lambda e, g=g, w=w: e.dma_start(
                    out=xcw_bf_view[0],
                    in_=bass.AP(xc_d, (g * 128) * L + w * WL, [[L, 128], [1, WL]]))),
                    dma_sem=lxc)
                dx0 = q.step("GD", prev_x0_rd, (lambda e, bb=bb, w=w: e.dma_start(
                    out=xp0b[:, :], in_=bass.AP(xp0_d, bb * L + w * WL, [[0, 128], [1, WL]]))),
                    dma_sem=lx0)
                v1 = q.step("V", [dx0] + ([(ld, 64)] if gw == 0 else []), (lambda e, cg=cg: e.op(
                    "tensor_scalar", out=dl_[:, :], in0=xp0b[:, :],
                    scalar1=dtw_sb[:, cg:cg + 1], scalar2=dtb_sb[:, cg:cg + 1],
                    op0=ALU.mult, op1=ALU.add)))
                s1 = q.step("S", [v1], lambda e: e.op(
                    "activation", out=dx_[:, :], in_=dl_[:, :], func=AF.Abs))
                s2 = q.step("S", [s1], lambda e: e.op(
                    "activation", out=dx_[:, :], in_=dx_[:, :], func=AF.Exp, scale=-1.0))
                s3 = q.step("S", [s2], lambda e: e.op(
                    "activation", out=dx_[:, :], in_=dx_[:, :], func=AF.Ln, bias=one_t[:, 0:1]))
                s4 = q.step("S", [s3], lambda e: e.op(
                    "activation", out=dl_[:, :], in_=dl_[:, :], func=AF.Relu))
                v2 = q.step("V", [s4], lambda e: e.op(
                    "tensor_tensor", out=dl_[:, :], in0=dl_[:, :], in1=dx_[:, :], op=ALU.add))
                v3 = q.step("V", [v2, dxc], (lambda e: e.op(
                    "tensor_tensor", out=dx_[:, :], in0=dl_[:, :], in1=xcw_bf_view[0], op=ALU.mult)))
                pw = gw % 2
                ydeps = []
                if gw >= 2:
                    ydeps = [(sty[pw], 16 * (gw // 2))]
                vy0 = q.step("V", ydeps, (lambda e, pw=pw: e.op("memset", yac[pw][:, :], 0.0)))
                vprev_h = None
                prev_vB = []
                prev_vM = []
                for n in range(DSN):
                    dbp = q.step("GD", prev_vB, (lambda e, bb=bb, n=n, w=w: e.dma_start(
                        out=bpb[:, :],
                        in_=bass.AP(bp_d, (bb * DSN + n) * L + w * WL, [[0, 128], [1, WL]]))),
                        dma_sem=lbp)
                    dcp = q.step("GD", prev_vM, (lambda e, bb=bb, n=n, w=w: e.dma_start(
                        out=cpb[:, :],
                        in_=bass.AP(cp_d, (bb * DSN + n) * L + w * WL, [[0, 128], [1, WL]]))),
                        dma_sem=lcp)
                    sA = q.step("S", [v2] + ([vprev_h] if vprev_h else [])
                                + ([(ld, 64)] if gw == 0 and n == 0 else []),
                                (lambda e, cg=cg, n=n: e.op(
                        "activation", out=dA_[:, :], in_=dl_[:, :], func=AF.Exp,
                        scale=A_sb[:, cg * DSN + n:cg * DSN + n + 1])))
                    vB = q.step("V", [dbp, v3], (lambda e: e.op(
                        "tensor_tensor", out=dB_[:, :], in0=dx_[:, :], in1=bpb[:, :], op=ALU.mult)))
                    vS = q.step("V", [sA, vB], (lambda e, n=n: e.op(
                        "tensor_tensor_scan", out=hn_[:, :], data0=dA_[:, :], data1=dB_[:, :],
                        initial=hst[:, n:n + 1], op0=ALU.mult, op1=ALU.add)))
                    vH = q.step("V", [vS], (lambda e, n=n: e.op(
                        "tensor_copy", out=hst[:, n:n + 1], in_=hn_[:, WL - 1:WL])))
                    vM = q.step("V", [vH, dcp], (lambda e: e.op(
                        "tensor_tensor", out=tq[:, :], in0=hn_[:, :], in1=cpb[:, :], op=ALU.mult)))
                    vprev_h = q.step("V", [vM, vy0], (lambda e, pw=pw: e.op(
                        "tensor_tensor", out=yac[pw][:, :], in0=yac[pw][:, :], in1=tq[:, :],
                        op=ALU.add)))
                    prev_vB = [vB]
                    prev_vM = [vM]
                vD = q.step("V", [vprev_h], (lambda e, cg=cg, pw=pw: e.op(
                    "scalar_tensor_tensor", out=yac[pw][:, :], in0=xcw_bf_view[0],
                    scalar=dsk_sb[:, cg:cg + 1], in1=yac[pw][:, :], op0=ALU.mult, op1=ALU.add)))
                q.step("GD", [vD], (lambda e, g=g, w=w, pw=pw: e.dma_start(
                    out=bass.AP(y_d, g * 128 * L + w * WL, [[L, 128], [1, WL]]),
                    in_=yac[pw][:, :])), dma_sem=sty[pw])
                mks = q.emit(blk)
                prev_xc_rd = [mks[vD]]
                prev_x0_rd = [mks[v1]]

    with nc.Block() as blk:
        @blk.gpsimd
        def _(e):
            for s_ in sty:
                if id(s_) in dma_counts:
                    e.wait_ge(s_, dma_counts[id(s_)])

    es.close()
    _BUILD_CACHE["l2b"] = nc
    return nc




def build_l2c():
    if "l2c" in _BUILD_CACHE:
        return _BUILD_CACHE["l2c"]
    FH = 2816
    NFH = 22
    QT = 512
    nc = _nc()
    x2_d = nc.dram_tensor("x", [TC, D], F32, kind="ExternalInput")
    yT_d = nc.dram_tensor("yTg", [2048, TC], F32, kind="ExternalInput")
    zT_d = nc.dram_tensor("zTg", [2048, TC], BF16, kind="ExternalInput")
    opT_d = nc.dram_tensor("opT", [2048, D], F32, kind="ExternalInput")
    wgT_d = nc.dram_tensor("wgqT", [D, FH], BF16, kind="ExternalInput")
    wuT_d = nc.dram_tensor("wuqT", [D, FH], BF16, kind="ExternalInput")
    wdT_d = nc.dram_tensor("wdqT", [FH, D], BF16, kind="ExternalInput")
    x3_d = nc.dram_tensor("x3", [TC, D], F32, kind="ExternalOutput")

    es, sb, sem, VE, SC, GP = _mk_ctx(nc)
    xn2bf = sb("xn2bf", [128, NT * D], BF16)
    s_tok = sb("s_tok", [128, NT * D], BF16)
    xqT = sb("xqT", [128, 8 * TC], BF16)
    fdown = sb("fdown", [128, 8 * TC], BF16)
    x2t = [sb(f"x2t{p}", [128, D]) for p in range(2)]
    lnt = sb("lnt2c", [128, 8])
    eps_t = sb("eps2c", [128, 1])
    ident_bf = sb("ident2c", [128, 128], BF16)
    qabs = sb("qabs2c", [128, D])
    qsgn = sb("qsgn2c", [128, D])
    qcqt = sb("qcqt2c", [128, D])
    qscl = sb("qscl2c", [128, 128])

    lx2 = [sem("lx2a"), sem("lx2b")]
    lyy = [sem("lyya"), sem("lyyb")]
    lzz = [sem("lzza"), sem("lzzb")]
    lop = [sem("lopa"), sem("lopb")]
    lwg = [sem("lwga"), sem("lwgb")]
    lwu = [sem("lwua"), sem("lwub")]
    lwd = [sem("lwda"), sem("lwdb")]
    sst = [sem("ssta"), sem("sstb")]
    spp = sem("spp")
    pe_count = [0]
    dma_counts = {}

    with nc.Block() as blk:
        @blk.gpsimd
        def _(e):
            _build_ident(GP, ident_bf)
            GP.op("memset", eps_t[:, :], 1e-5)

    def mk_trh(q, psA, psB):
        state = {"cnt": 0, "last": [None, None]}

        def tr(src_ap, dst_ap, deps=()):
            b = state["cnt"] % 2
            state["cnt"] += 1
            ps = (psA, psB)[b]
            d = list(deps)
            if state["last"][b] is not None:
                d.append(state["last"][b])
            t = q.step("T", d, (lambda e, ps=ps, src_ap=src_ap: e.transpose(
                ps[0:128, 0:128], src_ap, ident_bf[:, :])))
            v = q.step("V", [t], (lambda e, ps=ps, dst_ap=dst_ap: e.op(
                "tensor_copy", out=dst_ap, in_=ps[0:128, 0:128])))
            state["last"][b] = v
            return t, v
        return tr

    with (nc.psum_tensor("trPa", [128, 128], BF16) as trPa,
          nc.psum_tensor("trPb", [128, 128], BF16) as trPb,
          nc.sbuf_tensor("xn_t", [128, D], F32) as xn_t,
          nc.sbuf_tensor("xq_t", [128, D], BF16) as xq_t):
        with nc.Block() as blk:
            q = Seq(nc, VE, SC, GP, spp, pe_count, dma_counts)
            tr = mk_trh(q, trPa, trPb)
            last_x2_rd = {}
            last_tr_t = []
            for j in range(NT):
                p = j % 2
                deps = [last_x2_rd[j - 2]] if j >= 2 else []
                dl = q.step("SD", deps, (lambda e, j=j, p=p: e.dma_start(
                    out=x2t[p][:, :], in_=bass.AP(x2_d, j * 128 * D, [[D, 128], [1, D]]))),
                    dma_sem=lx2[p])
                v1 = q.step("V", [dl], (lambda e, p=p: e.op(
                    "reduce_sum", out=lnt[:, 0:1], in_=x2t[p][:, :], axis=AX.X)))
                s1 = q.step("S", [v1], lambda e: e.op("mul", lnt[:, 0:1], lnt[:, 0:1], 1.0 / D))
                v2 = q.step("V", [s1], (lambda e, p=p: e.op(
                    "tensor_scalar", out=xn_t[:, :], in0=x2t[p][:, :], scalar1=lnt[:, 0:1],
                    scalar2=None, op0=ALU.subtract)))
                last_x2_rd[j] = v2
                v3 = q.step("V", [v2], lambda e: e.op(
                    "tensor_tensor", out=qcqt[:, :], in0=xn_t[:, :], in1=xn_t[:, :], op=ALU.mult))
                v4 = q.step("V", [v3], lambda e: e.op(
                    "reduce_sum", out=lnt[:, 1:2], in_=qcqt[:, :], axis=AX.X))
                s2 = q.step("S", [v4], lambda e: e.op(
                    "activation", out=lnt[:, 2:3], in_=lnt[:, 1:2], func=AF.Sqrt,
                    bias=eps_t[:, 0:1], scale=1.0 / D))
                v5 = q.step("V", [s2], lambda e: e.op(
                    "reciprocal", out=lnt[:, 3:4], in_=lnt[:, 2:3]))
                v6 = q.step("V", [v5], lambda e: e.op(
                    "tensor_scalar", out=xn_t[:, :], in0=xn_t[:, :], scalar1=lnt[:, 3:4],
                    scalar2=None, op0=ALU.mult))
                v7 = q.step("V", [v6], (lambda e, j=j: e.op(
                    "tensor_copy", out=xn2bf[:, j * D:(j + 1) * D], in_=xn_t[:, :])))
                vq = _emit_fp4_quant(q, xn_t[:, 0:D], xq_t[:, 0:D], qscl, qsgn, qabs,
                                     qcqt, D // 16, D, [v7] + last_tr_t)
                last_tr_t = []
                for kd in range(8):
                    t_, v_ = tr(xq_t[:, kd * 128:(kd + 1) * 128],
                                xqT[:, kd * TC + j * 128:kd * TC + (j + 1) * 128], [vq])
                    last_tr_t.append(t_)
                last_tr_t = [last_tr_t[-1]]
            q.emit(blk)

    with (nc.sbuf_tensor("yz2c", [128, 16 * 1024], BF16) as yz,
          nc.sbuf_tensor("wop0", [128, 16 * 128], BF16) as wop0,
          nc.sbuf_tensor("wop1", [128, 16 * 128], BF16) as wop1,
          nc.sbuf_tensor("yt2c0", [128, 1024], F32) as yt0,
          nc.sbuf_tensor("yt2c1", [128, 1024], F32) as yt1,
          nc.sbuf_tensor("zt2c0", [128, 1024], BF16) as zt0,
          nc.sbuf_tensor("zt2c1", [128, 1024], BF16) as zt1,
          nc.sbuf_tensor("szt2c", [128, 1024], F32) as szt,
          nc.sbuf_tensor("spf2c", [128, 512], BF16) as spf,
          nc.psum_tensor("psPa", [128, 512], F32) as psPa,
          nc.psum_tensor("psPb", [128, 512], F32) as psPb,
          nc.psum_tensor("trQa", [128, 128], BF16) as trQa,
          nc.psum_tensor("trQb", [128, 128], BF16) as trQb):
        wop = [wop0, wop1]
        yt = [yt0, yt1]
        zt = [zt0, zt1]
        psP = [psPa, psPb]
        with nc.Block() as blk:
            q = Seq(nc, VE, SC, GP, spp, pe_count, dma_counts)
            tr = mk_trh(q, trQa, trQb)
            gidx = 0
            last_cp = {}
            last_mm_of = {}
            prev_half_tr = []
            for hh in range(2):
                last_rd = {}
                vlast = None
                for i in range(16):
                    p = i % 2
                    deps = [last_rd[i - 2]] if i >= 2 else list(prev_half_tr)
                    dly = q.step("SD", deps, (lambda e, i=i, p=p, hh=hh: e.dma_start(
                        out=yt[p][:, :],
                        in_=bass.AP(yT_d, i * 128 * TC + hh * 1024, [[TC, 128], [1, 1024]]))),
                        dma_sem=lyy[p])
                    dlz = q.step("SD", deps, (lambda e, i=i, p=p, hh=hh: e.dma_start(
                        out=zt[p][:, :],
                        in_=bass.AP(zT_d, i * 128 * TC + hh * 1024, [[TC, 128], [1, 1024]]))),
                        dma_sem=lzz[p])
                    sA = q.step("S", [dlz] + ([vlast] if vlast else []), (lambda e, p=p: e.op(
                        "activation", out=szt[:, :], in_=zt[p][:, :], func=AF.Silu)))
                    vlast = q.step("V", [sA, dly], (lambda e, i=i, p=p: e.op(
                        "tensor_tensor", out=yz[:, i * 1024:(i + 1) * 1024], in0=yt[p][:, :],
                        in1=szt[:, :], op=ALU.mult)))
                    last_rd[i] = vlast
                for j in range(8):
                    p = j % 2
                    deps = [vlast]
                    if (hh, j - 2) in last_mm_of:
                        deps.append(last_mm_of[(hh, j - 2)])
                    if hh == 1:
                        deps.append(last_mm_of[(0, j)] if (0, j) in last_mm_of else vlast)
                    dlw = q.step("GD", deps, (lambda e, j=j, p=p: e.dma_start(
                        out=wop[p][:, :],
                        in_=bass.AP(opT_d, j * 128, [[1024, 128], [128 * 1024, 16], [1, 128]]))),
                        dma_sem=lop[p])
                    for cgt in range(2):
                        pb = gidx % 2
                        deps2 = [dlw]
                        if gidx >= 2:
                            deps2.append(last_cp[gidx - 2])

                        def mk(e, p=p, cgt=cgt, pb=pb):
                            last = None
                            for k in range(16):
                                last = e.matmul(psP[pb][:, 0:512],
                                                wop[p][:, k * 128:(k + 1) * 128],
                                                yz[:, k * 1024 + cgt * 512:k * 1024 + cgt * 512 + 512],
                                                start=(k == 0), stop=(k == 15))
                            return last
                        t_m = q.step("T", deps2, mk)
                        prev_tr = last_mm_of.get(("tr", gidx - 1))
                        v_c = q.step("V", [t_m] + ([prev_tr] if prev_tr else []),
                                     (lambda e, pb=pb: e.op(
                            "tensor_copy", out=spf[:, :], in_=psP[pb][:, 0:512])))
                        last_cp[gidx] = v_c
                        trts = []
                        for ss in range(4):
                            jt = hh * 8 + cgt * 4 + ss
                            t_, v_ = tr(spf[:, ss * 128:(ss + 1) * 128],
                                        s_tok[:, jt * D + j * 128:jt * D + (j + 1) * 128],
                                        [v_c])
                            trts.append(t_)
                        last_mm_of[("tr", gidx)] = trts[-1]
                        last_mm_of[(hh, j)] = t_m
                        gidx += 1
                prev_half_tr = [last_mm_of[("tr", gidx - 1)]]
            q.emit(blk)

    es2 = contextlib.ExitStack()
    _e = es2.enter_context
    h_fm = _e(nc.sbuf_tensor("hfm2c", [128, NFH * QT], BF16))
    wg = [_e(nc.sbuf_tensor(f"wgsb{i}", [128, 8 * 128], BF16)) for i in range(2)]
    wu = [_e(nc.sbuf_tensor(f"wusb{i}", [128, 8 * 128], BF16)) for i in range(2)]
    wd0 = _e(nc.sbuf_tensor("wdsb0", [128, NFH * 128], BF16))
    wd = [wd0, wd0]
    h_tok = _e(nc.sbuf_tensor("htok2c", [128, FH], F32))
    hq_tok = _e(nc.sbuf_tensor("hqtok2c", [128, FH], BF16))
    gsil = _e(nc.sbuf_tensor("gsil2c", [128, 512], F32))
    psG = [_e(nc.psum_tensor(f"psG{i}", [128, 512], F32)) for i in range(2)]
    psU = [_e(nc.psum_tensor(f"psU{i}", [128, 512], F32)) for i in range(2)]
    psD = [_e(nc.psum_tensor(f"psD{i}", [128, 512], F32)) for i in range(2)]
    trRa = _e(nc.psum_tensor("trRa", [128, 128], BF16))
    trRb = _e(nc.psum_tensor("trRb", [128, 128], BF16))
    if True:
        with nc.Block() as blk:
            q = Seq(nc, VE, SC, GP, spp, pe_count, dma_counts)
            state_f = {"last": None}
            state_b = {"last": None}

            def trf(src_ap, dst_ap, deps=()):
                d = list(deps)
                if state_f["last"] is not None:
                    d.append(state_f["last"])
                t = q.step("T", d, (lambda e, src_ap=src_ap: e.transpose(
                    trRa[0:128, 0:128], src_ap, ident_bf[:, :])))
                v = q.step("V", [t], (lambda e, dst_ap=dst_ap: e.op(
                    "tensor_copy", out=dst_ap, in_=trRa[0:128, 0:128])))
                state_f["last"] = v
                return t, v

            def trb(src_ap, dst_ap, deps=()):
                d = list(deps)
                if state_b["last"] is not None:
                    d.append(state_b["last"])
                t = q.step("T", d, (lambda e, src_ap=src_ap: e.transpose(
                    trRb[0:128, 0:128], src_ap, ident_bf[:, :])))
                v = q.step("V", [t], (lambda e, dst_ap=dst_ap: e.op(
                    "tensor_copy", out=dst_ap, in_=trRb[0:128, 0:128])))
                state_b["last"] = v
                return t, v

            last_dn_t = []
            lw_rd = {}
            lu_rd = {}
            ld_rd = {}
            kq = 0
            for qq in range(4):
                vmul_prev = None
                last_hfm_wr = []
                for kh in range(NFH):
                    p = kh % 2
                    depw = [lw_rd[(qq, kh - 2)]] if kh >= 2 else (
                        [lw_rd[(qq - 1, NFH - 2 + (kh % 2))]] if qq >= 1 else [])
                    dlg = q.step("SD", depw, (lambda e, kh=kh, p=p: e.dma_start(
                        out=wg[p][:, :],
                        in_=bass.AP(wgT_d, kh * 128, [[FH, 128], [128 * FH, 8], [1, 128]]))),
                        dma_sem=lwg[p])
                    depu = [lu_rd[(qq, kh - 2)]] if kh >= 2 else (
                        [lu_rd[(qq - 1, NFH - 2 + (kh % 2))]] if qq >= 1 else [])
                    dlu = q.step("SD", depu, (lambda e, kh=kh, p=p: e.dma_start(
                        out=wu[p][:, :],
                        in_=bass.AP(wuT_d, kh * 128, [[FH, 128], [128 * FH, 8], [1, 128]]))),
                        dma_sem=lwu[p])

                    def mkg(e, p=p, qq=qq):
                        last = None
                        for k in range(8):
                            last = e.matmul(psG[p][:, 0:512], wg[p][:, k * 128:(k + 1) * 128],
                                            xqT[:, k * TC + qq * QT:k * TC + qq * QT + QT],
                                            start=(k == 0), stop=(k == 7))
                        return last
                    depg = [dlg] + last_dn_t
                    if kh >= 2:
                        depg.append(lw_rd[(qq, kh - 2, "gs")])
                    t_g = q.step("T", depg, mkg)
                    lw_rd[(qq, kh)] = t_g
                    s_g = q.step("S", [t_g] + ([vmul_prev] if vmul_prev else []),
                                 (lambda e, p=p: e.op(
                        "activation", out=gsil[:, :], in_=psG[p][:, 0:512], func=AF.Silu)))
                    lw_rd[(qq, kh, "gs")] = s_g

                    def mku(e, p=p, qq=qq):
                        last = None
                        for k in range(8):
                            last = e.matmul(psU[p][:, 0:512], wu[p][:, k * 128:(k + 1) * 128],
                                            xqT[:, k * TC + qq * QT:k * TC + qq * QT + QT],
                                            start=(k == 0), stop=(k == 7))
                        return last
                    depu2 = [dlu] + last_dn_t
                    if kh >= 2:
                        depu2.append(lw_rd[(qq, kh - 2, "um")])
                    t_u = q.step("T", depu2, mku)
                    lu_rd[(qq, kh)] = t_u
                    vmul = q.step("V", [s_g, t_u] + last_dn_t, (lambda e, kh=kh, p=p: e.op(
                        "tensor_tensor", out=h_fm[:, kh * QT:(kh + 1) * QT], in0=gsil[:, :],
                        in1=psU[p][:, 0:512], op=ALU.mult)))
                    lw_rd[(qq, kh, "um")] = vmul
                    vmul_prev = vmul
                    last_hfm_wr.append(vmul)
                last_dn_t = []
                prev_wb_t = []
                for jt in range(4):
                    rts = []
                    for kf in range(NFH):
                        t_, v_ = trf(h_fm[:, kf * QT + jt * 128:kf * QT + (jt + 1) * 128],
                                     h_tok[:, kf * 128:(kf + 1) * 128],
                                     [last_hfm_wr[kf]])
                        rts.append(v_)
                    vqs = []
                    for (off, wid) in ((0, 1024), (1024, 1024), (2048, 768)):
                        vq = _emit_fp4_quant(q, h_tok[:, off:off + wid],
                                             hq_tok[:, off:off + wid], qscl, qsgn, qabs,
                                             qcqt, wid // 16, wid,
                                             [rts[-1]] + prev_wb_t
                                             + ([vqs[-1]] if vqs else []),
                                             accq_w=D, scl_w=128)
                        vqs.append(vq)
                    prev_wb_t = []
                    wb_vs = []
                    for kf in range(NFH):
                        t_, v_ = trb(hq_tok[:, kf * 128:(kf + 1) * 128],
                                     h_fm[:, kf * QT + jt * 128:kf * QT + (jt + 1) * 128],
                                     [vqs[min(kf * 128 // 1024, 2)], rts[kf]])
                        prev_wb_t.append(t_)
                        wb_vs.append(v_)
                    prev_wb_t = [prev_wb_t[-1]]
                wb_done = [wb_vs[-1]]
                for j in range(8):
                    p = j % 2
                    depw = [ld_rd[(qq, j - 1)]] if j >= 1 else (
                        [ld_rd[(qq - 1, 7)]] if qq >= 1 else [])
                    dld = q.step("SD", depw, (lambda e, j=j, p=p: e.dma_start(
                        out=wd[p][:, :],
                        in_=bass.AP(wdT_d, j * 128, [[1024, 128], [128 * 1024, NFH], [1, 128]]))),
                        dma_sem=lwd[p])

                    def mkd(e, p=p):
                        last = None
                        for k in range(NFH):
                            last = e.matmul(psD[p][:, 0:512], wd[p][:, k * 128:(k + 1) * 128],
                                            h_fm[:, k * QT:k * QT + QT],
                                            start=(k == 0), stop=(k == NFH - 1))
                        return last
                    depd = [dld] + wb_done
                    if j >= 2:
                        depd.append(ld_rd[(qq, j - 2, "cp")])
                    t_d = q.step("T", depd, mkd)
                    ld_rd[(qq, j)] = t_d
                    v_d = q.step("V", [t_d], (lambda e, j=j, qq=qq, p=p: e.op(
                        "tensor_copy", out=fdown[:, j * TC + qq * QT:j * TC + qq * QT + QT],
                        in_=psD[p][:, 0:512])))
                    ld_rd[(qq, j, "cp")] = v_d
                    last_dn_t.append(t_d)
                last_dn_t = [last_dn_t[-1]]
            q.emit(blk)
    es2.close()

    with (nc.sbuf_tensor("ffntok", [128, D], F32) as ffn_tok,
          nc.sbuf_tensor("g2t2c", [128, D], F32) as g2t,
          nc.sbuf_tensor("s2t2c", [128, D], F32) as s2t,
          nc.sbuf_tensor("x3t0", [128, D], F32) as x3t0,
          nc.sbuf_tensor("x3t1", [128, D], F32) as x3t1,
          nc.psum_tensor("trSa", [128, 128], BF16) as trSa,
          nc.psum_tensor("trSb", [128, 128], BF16) as trSb):
        x3t = [x3t0, x3t1]
        with nc.Block() as blk:
            q = Seq(nc, VE, SC, GP, spp, pe_count, dma_counts)
            tr = mk_trh(q, trSa, trSb)
            last_st = {}
            last_x2_rd = {}
            for j in range(NT):
                p = j % 2
                trvs = []
                for kd in range(8):
                    t_, v_ = tr(fdown[:, kd * TC + j * 128:kd * TC + (j + 1) * 128],
                                ffn_tok[:, kd * 128:(kd + 1) * 128], [])
                    trvs.append(v_)
                s_g = q.step("S", [trvs[-1]], lambda e: e.op(
                    "activation", out=g2t[:, :], in_=ffn_tok[:, :], func=AF.Silu))
                v_s2 = q.step("V", [s_g], (lambda e, j=j: e.op(
                    "tensor_tensor", out=s2t[:, :], in0=xn2bf[:, j * D:(j + 1) * D],
                    in1=s_tok[:, j * D:(j + 1) * D], op=ALU.add)))
                deps = [last_x2_rd[j - 2]] if j >= 2 else []
                dl = q.step("SD", deps, (lambda e, j=j, p=p: e.dma_start(
                    out=x2t[p][:, :], in_=bass.AP(x2_d, j * 128 * D, [[D, 128], [1, D]]))),
                    dma_sem=lx2[p])
                v_m = q.step("V", [v_s2, s_g], lambda e: e.op(
                    "tensor_tensor", out=s2t[:, :], in0=s2t[:, :], in1=g2t[:, :], op=ALU.mult))
                depst = [last_st[j - 2]] if j >= 2 else []
                v_x3 = q.step("V", [v_m, dl] + depst, (lambda e, p=p: e.op(
                    "tensor_tensor", out=x3t[p][:, :], in0=x2t[p][:, :], in1=s2t[:, :],
                    op=ALU.add)))
                last_x2_rd[j] = v_x3
                stp = q.step("GD", [v_x3], (lambda e, j=j, p=p: e.dma_start(
                    out=bass.AP(x3_d, j * 128 * D, [[D, 128], [1, D]]), in_=x3t[p][:, :])),
                    dma_sem=sst[p])
                last_st[j] = stp
            q.emit(blk)

    with nc.Block() as blk:
        @blk.gpsimd
        def _(e):
            for s_ in (sst[0], sst[1]):
                if id(s_) in dma_counts:
                    e.wait_ge(s_, dma_counts[id(s_)])

    es.close()
    _BUILD_CACHE["l2c"] = nc
    return nc




def build_l3a():
    if "l3a" in _BUILD_CACHE:
        return _BUILD_CACHE["l3a"]
    nc = _nc()
    x_d = nc.dram_tensor("x", [TC, D], F32, kind="ExternalInput")
    wq_d = nc.dram_tensor("wqkvT", [D, 3072], F32, kind="ExternalInput")
    bq_d = nc.dram_tensor("bqkv", [3072], F32, kind="ExternalInput")
    wsw_d = nc.dram_tensor("wsw", [D, 8], F32, kind="ExternalInput")
    bsw_d = nc.dram_tensor("bsw", [8], F32, kind="ExternalInput")
    cos_d = nc.dram_tensor("cosr", [TC, 32], F32, kind="ExternalInput")
    sin_d = nc.dram_tensor("sinr", [TC, 32], F32, kind="ExternalInput")
    qg_d = nc.dram_tensor("qgam", [D], F32, kind="ExternalInput")
    kg_d = nc.dram_tensor("kgam", [D], F32, kind="ExternalInput")
    qr_d = nc.dram_tensor("qr", [TC, D], F32, kind="ExternalOutput")
    kr_d = nc.dram_tensor("kr", [TC, D], F32, kind="ExternalOutput")
    vr_d = nc.dram_tensor("vr", [TC, D], F32, kind="ExternalOutput")
    lg_d = nc.dram_tensor("logits2d", [128, NT * 8], F32, kind="ExternalOutput")

    es, sb, sem, VE, SC, GP = _mk_ctx(nc)
    wq_all = sb("wq_all", [128, 8 * 3072], BF16)
    bqb = sb("bqb", [128, 3072])
    wsw_sb = sb("wsw_sb", [128, 8 * 8], BF16)
    bswb = sb("bswb", [128, 8])
    qgb = sb("qgb", [128, D])
    kgb = sb("kgb", [128, D])
    cosb = [sb(f"cosb{p}", [128, 32]) for p in range(2)]
    sinb = [sb(f"sinb{p}", [128, 32]) for p in range(2)]
    x2t = [sb(f"x3t{p}", [128, D]) for p in range(2)]
    xn_t = sb("xn_t3", [128, D])
    xnc = sb("xnc3", [128, D], BF16)
    xnT_t = sb("xnT_t3", [128, 8 * 128], BF16)
    qkv_t = sb("qkv_t3", [128, 3072])
    rt = sb("rt3", [128, D])
    tmp = sb("tmp3", [128, D])
    sqr = sb("sqr3", [128, D])
    rs16 = sb("rs16", [128, 16])
    rr16 = sb("rr16", [128, 16])
    qob = [sb(f"qob{p}", [128, D]) for p in range(2)]
    kob = [sb(f"kob{p}", [128, D]) for p in range(2)]
    vob = [sb(f"vob{p}", [128, D]) for p in range(2)]
    lg2d = sb("lg2d3", [128, NT * 8])
    lnt = sb("lnt3", [128, 8])
    eps_t = sb("eps3", [128, 1])
    eps6 = sb("eps6", [128, 1])
    ident_bf = sb("ident3", [128, 128], BF16)

    ld = sem("ld")
    ldg = sem("ldg")
    ldg2 = sem("ldg2")
    lx = [sem("lxa"), sem("lxb")]
    lco = [sem("lcoa"), sem("lcob")]
    lsi = [sem("lsia"), sem("lsib")]
    sq_ = [sem("sqa"), sem("sqb")]
    sk_ = [sem("ska"), sem("skb")]
    sv_ = [sem("sva"), sem("svb")]
    slg = sem("slg")
    spp = sem("spp")
    pe_count = [0]
    dma_counts = {}

    with nc.Block() as blk:
        @blk.sync
        def _(e):
            e.dma_start(out=bqb[:, :], in_=_bcast_row_ap(bq_d, 3072)).then_inc(ld, 16)
            e.dma_start(out=bswb[:, :], in_=_bcast_row_ap(bsw_d, 8)).then_inc(ld, 16)
            e.dma_start(out=qgb[:, :], in_=_bcast_row_ap(qg_d, D)).then_inc(ld, 16)
            e.dma_start(out=kgb[:, :], in_=_bcast_row_ap(kg_d, D)).then_inc(ld, 16)

        @blk.gpsimd
        def _(e):
            _build_ident(GP, ident_bf)
            GP.op("memset", eps_t[:, :], 1e-5)
            GP.op("memset", eps6[:, :], 1e-6)
            e.dma_start(out=wq_all[:, :],
                        in_=bass.AP(wq_d, 0, [[3072, 128], [128 * 3072, 8], [1, 3072]])
                        ).then_inc(ldg, 16)
            e.dma_start(out=wsw_sb[:, :],
                        in_=bass.AP(wsw_d, 0, [[8, 128], [128 * 8, 8], [1, 8]])
                        ).then_inc(ldg2, 16)

    def v3d(handle_or_ap, pstep, off, step2_off):
        t = handle_or_ap
        return bass.AP(t, off + step2_off, [[pstep, 128], [64, 16], [2, 32]])

    def vcs(tb):
        return bass.AP(tb, 0, [[32, 128], [0, 16], [1, 32]])

    with (nc.psum_tensor("trXa", [128, 128], BF16) as trXa,
          nc.psum_tensor("trXb", [128, 128], BF16) as trXb,
          nc.psum_tensor("psQ0", [128, 512], F32) as psQ0,
          nc.psum_tensor("psQ1", [128, 512], F32) as psQ1,
          nc.psum_tensor("psL3", [128, 8], F32) as psL3):
        psQ = [psQ0, psQ1]
        with nc.Block() as blk:
            q = Seq(nc, VE, SC, GP, spp, pe_count, dma_counts)
            trcnt = [0]
            last_tr = [None, None]
            last_x_rd = {}
            last_cs_rd = {}
            last_mmL = None
            last_st = {}
            for j in range(NT):
                p = j % 2
                xdeps = [last_x_rd[j - 2]] if j >= 2 else []
                dx = q.step("SD", xdeps, (lambda e, j=j, p=p: e.dma_start(
                    out=x2t[p][:, :], in_=bass.AP(x_d, j * 128 * D, [[D, 128], [1, D]]))),
                    dma_sem=lx[p])
                csdeps = [last_cs_rd[j - 2]] if j >= 2 else []
                dco = q.step("SD", csdeps, (lambda e, j=j, p=p: e.dma_start(
                    out=cosb[p][:, :], in_=bass.AP(cos_d, j * 128 * 32, [[32, 128], [1, 32]]))),
                    dma_sem=lco[p])
                dsi = q.step("SD", csdeps, (lambda e, j=j, p=p: e.dma_start(
                    out=sinb[p][:, :], in_=bass.AP(sin_d, j * 128 * 32, [[32, 128], [1, 32]]))),
                    dma_sem=lsi[p])
                v1 = q.step("V", [dx], (lambda e, p=p: e.op(
                    "reduce_sum", out=lnt[:, 0:1], in_=x2t[p][:, :], axis=AX.X)))
                s1 = q.step("S", [v1], lambda e: e.op("mul", lnt[:, 0:1], lnt[:, 0:1], 1.0 / D))
                v2 = q.step("V", [s1], (lambda e, p=p: e.op(
                    "tensor_scalar", out=xn_t[:, :], in0=x2t[p][:, :], scalar1=lnt[:, 0:1],
                    scalar2=None, op0=ALU.subtract)))
                last_x_rd[j] = v2
                v3 = q.step("V", [v2], lambda e: e.op(
                    "tensor_tensor", out=sqr[:, :], in0=xn_t[:, :], in1=xn_t[:, :], op=ALU.mult))
                v4 = q.step("V", [v3], lambda e: e.op(
                    "reduce_sum", out=lnt[:, 1:2], in_=sqr[:, :], axis=AX.X))
                s2 = q.step("S", [v4], lambda e: e.op(
                    "activation", out=lnt[:, 2:3], in_=lnt[:, 1:2], func=AF.Sqrt,
                    bias=eps_t[:, 0:1], scale=1.0 / D))
                v5 = q.step("V", [s2], lambda e: e.op(
                    "reciprocal", out=lnt[:, 3:4], in_=lnt[:, 2:3]))
                v6 = q.step("V", [v5], lambda e: e.op(
                    "tensor_scalar", out=xn_t[:, :], in0=xn_t[:, :], scalar1=lnt[:, 3:4],
                    scalar2=None, op0=ALU.mult))
                v7 = q.step("V", [v6], lambda e: e.op(
                    "tensor_copy", out=xnc[:, :], in_=xn_t[:, :]))
                trvs = []
                for kd in range(8):
                    b = trcnt[0] % 2
                    trcnt[0] += 1
                    trp = (trXa, trXb)[b]
                    d = [v7]
                    if last_tr[b] is not None:
                        d.append(last_tr[b])
                    tt = q.step("T", d, (lambda e, kd=kd, trp=trp: e.transpose(
                        trp[:, 0:128], xnc[:, kd * 128:(kd + 1) * 128], ident_bf[:, :])))
                    dd = [tt] + ([last_mmL] if last_mmL is not None and kd == 0 else [])
                    vv = q.step("V", dd, (lambda e, kd=kd, trp=trp: e.op(
                        "tensor_copy", out=xnT_t[:, kd * 128:(kd + 1) * 128],
                        in_=trp[:, 0:128])))
                    last_tr[b] = vv
                    trvs.append(vv)
                vadds = []
                for fg in range(6):
                    pb = fg % 2

                    def mkq(e, fg=fg, pb=pb):
                        last = None
                        for kd in range(8):
                            last = e.matmul(psQ[pb][:, 0:512],
                                            xnT_t[:, kd * 128:(kd + 1) * 128],
                                            wq_all[:, kd * 3072 + fg * 512:kd * 3072 + (fg + 1) * 512],
                                            start=(kd == 0), stop=(kd == 7))
                        return last
                    deps = [trvs[-1], (ldg, 16), (ld, 64)]
                    if fg >= 2:
                        deps.append(vadds[fg - 2])
                    t_m = q.step("T", deps, mkq)
                    v_a = q.step("V", [t_m], (lambda e, fg=fg, pb=pb: e.op(
                        "tensor_tensor", out=qkv_t[:, fg * 512:(fg + 1) * 512],
                        in0=psQ[pb][:, 0:512], in1=bqb[:, fg * 512:(fg + 1) * 512], op=ALU.add)))
                    vadds.append(v_a)

                def mkl(e):
                    last = None
                    for kd in range(8):
                        last = e.matmul(psL3[:, 0:8], xnT_t[:, kd * 128:(kd + 1) * 128],
                                        wsw_sb[:, kd * 8:(kd + 1) * 8],
                                        start=(kd == 0), stop=(kd == 7))
                    return last
                depl = [trvs[-1], (ldg2, 16)] + ([last_mmL] if last_mmL is not None else [])
                t_l = q.step("T", depl, mkl)
                last_mmL = t_l
                last_lgv = q.step("V", [t_l, (ld, 64)], (lambda e, j=j: e.op(
                    "tensor_tensor", out=lg2d[:, j * 8:(j + 1) * 8], in0=psL3[:, 0:8],
                    in1=bswb[:, :], op=ALU.add)))

                for (off, gam, outb, osem, od) in ((0, qgb, qob, sq_, qr_d),
                                                   (1024, kgb, kob, sk_, kr_d)):
                    qa_e = v3d(qkv_t, 3072, off, 0)
                    qa_o = v3d(qkv_t, 3072, off, 1)
                    rt_e = v3d(rt, 1024, 0, 0)
                    rt_o = v3d(rt, 1024, 0, 1)
                    tp_e = v3d(tmp, 1024, 0, 0)
                    cosv = vcs(cosb[p])
                    sinv = vcs(sinb[p])
                    ve1 = q.step("V", [vadds[-1], dco], (lambda e, qa_e=qa_e, rt_e=rt_e, cosv=cosv: e.op(
                        "tensor_tensor", out=rt_e, in0=qa_e, in1=cosv, op=ALU.mult)))
                    ve2 = q.step("V", [ve1, dsi], (lambda e, qa_o=qa_o, tp_e=tp_e, sinv=sinv: e.op(
                        "tensor_tensor", out=tp_e, in0=qa_o, in1=sinv, op=ALU.mult)))
                    ve3 = q.step("V", [ve2], (lambda e, rt_e=rt_e, tp_e=tp_e: e.op(
                        "tensor_tensor", out=rt_e, in0=rt_e, in1=tp_e, op=ALU.subtract)))
                    vo1 = q.step("V", [ve3], (lambda e, qa_e=qa_e, rt_o=rt_o, sinv=sinv: e.op(
                        "tensor_tensor", out=rt_o, in0=qa_e, in1=sinv, op=ALU.mult)))
                    vo2 = q.step("V", [vo1], (lambda e, qa_o=qa_o, tp_e=tp_e, cosv=cosv: e.op(
                        "tensor_tensor", out=tp_e, in0=qa_o, in1=cosv, op=ALU.mult)))
                    vo3 = q.step("V", [vo2], (lambda e, rt_o=rt_o, tp_e=tp_e: e.op(
                        "tensor_tensor", out=rt_o, in0=rt_o, in1=tp_e, op=ALU.add)))
                    last_cs_rd[j] = vo2
                    vs1 = q.step("V", [vo3], lambda e: e.op(
                        "tensor_tensor", out=sqr[:, :], in0=rt[:, :], in1=rt[:, :], op=ALU.mult))
                    vs2 = q.step("V", [vs1], lambda e: e.op(
                        "tensor_reduce", out=rs16[:, 0:16],
                        in_=bass.AP(sqr, 0, [[1024, 128], [64, 16], [1, 64]]),
                        axis=AX.X, op=ALU.add))
                    ss1 = q.step("S", [vs2], lambda e: e.op(
                        "activation", out=rs16[:, 0:16], in_=rs16[:, 0:16], func=AF.Sqrt,
                        bias=eps6[:, 0:1], scale=1.0 / 64))
                    vr1 = q.step("V", [ss1], lambda e: e.op(
                        "reciprocal", out=rr16[:, 0:16], in_=rs16[:, 0:16]))
                    vg1 = q.step("V", [vr1], lambda e: e.op(
                        "tensor_tensor", out=rt[:, :], in0=rt[:, :],
                        in1=bass.AP(rr16, 0, [[16, 128], [1, 16], [0, 64]]), op=ALU.mult))
                    stdep = [last_st[(off, j - 2)]] if (off, j - 2) in last_st else []
                    vg2 = q.step("V", [vg1, (ld, 64)] + stdep, (lambda e, gam=gam, outb=outb, p=p: e.op(
                        "tensor_tensor", out=outb[p][:, :], in0=rt[:, :], in1=gam[:, :],
                        op=ALU.mult)))
                    stp = q.step("GD", [vg2], (lambda e, j=j, p=p, outb=outb, od=od: e.dma_start(
                        out=bass.AP(od, j * 128 * D, [[D, 128], [1, D]]), in_=outb[p][:, :])),
                        dma_sem=osem[p])
                    last_st[(off, j)] = stp
                vdep = [last_st[("v", j - 2)]] if ("v", j - 2) in last_st else []
                vcv = q.step("V", [vadds[-1]] + vdep, (lambda e, p=p: e.op(
                    "tensor_copy", out=vob[p][:, :], in_=qkv_t[:, 2048:3072])))
                stv = q.step("GD", [vcv], (lambda e, j=j, p=p: e.dma_start(
                    out=bass.AP(vr_d, j * 128 * D, [[D, 128], [1, D]]), in_=vob[p][:, :])),
                    dma_sem=sv_[p])
                last_st[("v", j)] = stv
            q.step("GD", [last_lgv],
                   lambda e: e.dma_start(out=lg_d.ap(), in_=lg2d[:, :]), dma_sem=slg)
            q.emit(blk)

    with nc.Block() as blk:
        @blk.gpsimd
        def _(e):
            for s_ in (sq_[0], sq_[1], sk_[0], sk_[1], sv_[0], sv_[1], slg):
                if id(s_) in dma_counts:
                    e.wait_ge(s_, dma_counts[id(s_)])

    es.close()
    _BUILD_CACHE["l3a"] = nc
    return nc




def build_l3b():
    if "l3b" in _BUILD_CACHE:
        return _BUILD_CACHE["l3b"]
    H = 16
    DK = 64
    NKT = L // 128
    nc = _nc()
    qT_d = nc.dram_tensor("qTs", [D, TC], F32, kind="ExternalInput")
    kT_d = nc.dram_tensor("kTb", [D, L], F32, kind="ExternalInput")
    vx_d = nc.dram_tensor("vextb", [L, H * 65], F32, kind="ExternalInput")
    oT_d = nc.dram_tensor("oTs", [D, TC], F32, kind="ExternalOutput")

    es, sb, sem, VE, SC, GP = _mk_ctx(nc)
    kb = sb("kb3", [64, L], BF16)
    k2b = sb("k2b3", [64, L], BF16)
    qb = sb("qb3", [64, TC], BF16)
    vb = sb("vb3", [128, NKT * 65], BF16)
    expS = [sb(f"expS{p}", [128, 512], BF16) for p in range(2)]
    dsb = sb("dsb3", [128, 512])
    dbb = sb("dbb3", [128, 512])
    ob = [sb(f"ob3{p}", [64, 512]) for p in range(2)]
    nmx = sb("nmx3", [1, 16])
    kqm = sb("kqm3", [1, 8])
    nbb = sb("nbb3", [128, 1])
    ones_sb = sb("ones3", [128, 128])
    ones_bf = sb("ones3bf", [128, 1], BF16)

    lk = sem("lk")
    lq = sem("lq")
    lv = sem("lv")
    so = [sem("soa"), sem("sob")]
    spp = sem("spp")
    pe_count = [0]
    dma_counts = {}

    with nc.Block() as blk:
        @blk.gpsimd
        def _(e):
            GP.op("memset", ones_sb[:, :], 1.0)
            GP.op("memset", ones_bf[:, :], 1.0)

    with (nc.psum_tensor("psSa", [128, 512], F32) as psSa,
          nc.psum_tensor("psSb", [128, 512], F32) as psSb,
          nc.psum_tensor("psO3", [128, 512], F32) as psO,
          nc.psum_tensor("psB3", [128, 512], F32) as psB2,
          nc.psum_tensor("psNa", [128, 512], F32) as psNa,
          nc.psum_tensor("psNb", [128, 512], F32) as psNb,
          nc.psum_tensor("psBB", [128, 1], F32) as psBB):
        psS = [psSa, psSb]
        psN = [psNa, psNb]
        with nc.Block() as blk:
            q = Seq(nc, VE, SC, GP, spp, pe_count, dma_counts)
            last_k_rd = []
            last_q_rd = []
            last_v_rd = []
            last_exp = []
            last_st = {}
            for h in range(H):
                dk = q.step("GD", last_k_rd, (lambda e, h=h: e.dma_start(
                    out=kb[:, :], in_=bass.AP(kT_d, h * 64 * L, [[L, 64], [1, L]]))),
                    dma_sem=lk)
                dq = q.step("GD", last_q_rd, (lambda e, h=h: e.dma_start(
                    out=qb[:, :], in_=bass.AP(qT_d, h * 64 * TC, [[TC, 64], [1, TC]]))),
                    dma_sem=lq)
                dv = q.step("GD", last_v_rd, (lambda e, h=h: e.dma_start(
                    out=vb[:, :],
                    in_=bass.AP(vx_d, h * 65, [[H * 65, 128], [128 * H * 65, NKT], [1, 65]]))),
                    dma_sem=lv)
                vk2 = q.step("V", [dk], lambda e: e.op(
                    "tensor_tensor", out=k2b[:, :], in0=kb[:, :], in1=kb[:, :], op=ALU.mult))
                nred = []
                for ch in range(16):
                    pb = ch % 2
                    d = [vk2]
                    if ch >= 2:
                        d.append(nred[ch - 2])
                    t_n = q.step("T", d, (lambda e, ch=ch, pb=pb: e.matmul(
                        psN[pb][0:1, 0:512], ones_bf[0:64, 0:1],
                        k2b[:, ch * 512:(ch + 1) * 512], start=True, stop=True)))
                    v_n = q.step("V", [t_n], (lambda e, ch=ch, pb=pb: e.op(
                        "tensor_reduce", out=nmx[0:1, ch:ch + 1], in_=psN[pb][0:1, 0:512],
                        axis=AX.X, op=ALU.max)))
                    nred.append(v_n)
                v_km = q.step("V", [nred[-1]], lambda e: e.op(
                    "tensor_reduce", out=kqm[0:1, 0:1], in_=nmx[0:1, 0:16],
                    axis=AX.X, op=ALU.max))
                vq2 = q.step("V", [dq, v_km], lambda e: e.op(
                    "tensor_tensor", out=k2b[:, 0:TC], in0=qb[:, :], in1=qb[:, :], op=ALU.mult))
                qred = []
                for ch in range(4):
                    pb = ch % 2
                    d = [vq2]
                    if ch >= 2:
                        d.append(qred[ch - 2])
                    t_n = q.step("T", d, (lambda e, ch=ch, pb=pb: e.matmul(
                        psN[pb][0:1, 0:512], ones_bf[0:64, 0:1],
                        k2b[:, ch * 512:(ch + 1) * 512], start=True, stop=True)))
                    v_n = q.step("V", [t_n], (lambda e, ch=ch, pb=pb: e.op(
                        "tensor_reduce", out=nmx[0:1, ch:ch + 1], in_=psN[pb][0:1, 0:512],
                        axis=AX.X, op=ALU.max)))
                    qred.append(v_n)
                v_qm = q.step("V", [qred[-1]], lambda e: e.op(
                    "tensor_reduce", out=kqm[0:1, 1:2], in_=nmx[0:1, 0:4],
                    axis=AX.X, op=ALU.max))
                v_pr = q.step("V", [v_qm], lambda e: e.op(
                    "tensor_tensor", out=kqm[0:1, 2:3], in0=kqm[0:1, 0:1],
                    in1=kqm[0:1, 1:2], op=ALU.mult))
                s_sq = q.step("S", [v_pr], lambda e: e.op(
                    "activation", out=kqm[0:1, 3:4], in_=kqm[0:1, 2:3], func=AF.Sqrt))
                s_ng = q.step("S", [s_sq], lambda e: e.op(
                    "mul", kqm[0:1, 4:5], kqm[0:1, 3:4], -0.125))
                t_bb = q.step("T", [s_ng] + last_exp, lambda e: e.matmul(
                    psBB[0:128, 0:1], ones_sb[0:1, 0:128], kqm[0:1, 4:5],
                    start=True, stop=True))
                v_bb = q.step("V", [t_bb], lambda e: e.op(
                    "tensor_copy", out=nbb[:, :], in_=psBB[0:128, 0:1]))
                last_av = None
                last_psO_rd = []
                last_dbb = []
                for qg in range(4):
                    sexp = {}
                    avs = {}
                    for kt in range(NKT):
                        pb = kt % 2
                        d = [v_bb]
                        if kt >= 2:
                            d.append(sexp[kt - 2])
                        t_s = q.step("T", d + (last_psO_rd if kt == 0 else []),
                                     (lambda e, kt=kt, qg=qg, pb=pb: e.matmul(
                            psS[pb][:, 0:512], kb[:, kt * 128:(kt + 1) * 128],
                            qb[:, qg * 512:(qg + 1) * 512], start=True, stop=True)))
                        d2 = [t_s]
                        if kt >= 2:
                            d2.append(avs[kt - 2])
                        s_e = q.step("S", d2, (lambda e, kt=kt, pb=pb: e.op(
                            "activation", out=expS[pb][:, :], in_=psS[pb][:, 0:512],
                            func=AF.Exp, scale=0.125, bias=nbb[:, 0:1])))
                        sexp[kt] = s_e
                        d3 = [s_e]
                        if kt == 0:
                            d3 += last_psO_rd
                            if qg == 0:
                                d3.append(dv)
                        t_a = q.step("T", d3, (lambda e, kt=kt, pb=pb: e.matmul(
                            psO[0:65, 0:512], vb[:, kt * 65:(kt + 1) * 65], expS[pb][:, :],
                            start=(kt == 0), stop=(kt == NKT - 1))))
                        avs[kt] = t_a
                        last_av = t_a
                    v_rc = q.step("V", [last_av], lambda e: e.op(
                        "reciprocal", out=dsb[64:65, 0:512], in_=psO[64:65, 0:512]))
                    t_b2 = q.step("T", [v_rc] + last_dbb, lambda e: e.matmul(
                        psB2[0:64, 0:512], ones_sb[64:65, 0:64], dsb[64:65, 0:512],
                        start=True, stop=True))
                    v_db = q.step("V", [t_b2], lambda e: e.op(
                        "tensor_copy", out=dbb[0:64, 0:512], in_=psB2[0:64, 0:512]))
                    last_dbb = [v_db]
                    p = qg % 2
                    stdep = []
                    if ("st", h, qg - 2) in dma_counts:
                        pass
                    prev_st = getattr(build_l3b, "_dummy", None)
                    v_ml = q.step("V", [v_db] + ([last_st[(h, qg - 2)]] if (h, qg - 2) in last_st else ([last_st[(h - 1, qg + 2)]] if (h - 1, qg + 2) in last_st else [])),
                                  (lambda e, p=p: e.op(
                        "tensor_tensor", out=ob[p][:, :], in0=psO[0:64, 0:512],
                        in1=dbb[0:64, 0:512], op=ALU.mult)))
                    last_psO_rd = [v_ml]
                    stp = q.step("GD", [v_ml], (lambda e, h=h, qg=qg, p=p: e.dma_start(
                        out=bass.AP(oT_d, h * 64 * TC + qg * 512, [[TC, 64], [1, 512]]),
                        in_=ob[p][:, :])), dma_sem=so[p])
                    last_st[(h, qg)] = stp
                last_k_rd = [sexp[NKT - 1], avs[NKT - 1]]
                last_k_rd = [avs[NKT - 1]]
                last_q_rd = [avs[NKT - 1]]
                last_v_rd = [avs[NKT - 1]]
                last_exp = [sexp[NKT - 1]]
            q.emit(blk)

    with nc.Block() as blk:
        @blk.gpsimd
        def _(e):
            for s_ in (so[0], so[1]):
                if id(s_) in dma_counts:
                    e.wait_ge(s_, dma_counts[id(s_)])

    es.close()
    _BUILD_CACHE["l3b"] = nc
    return nc




def build_l3c():
    if "l3c" in _BUILD_CACHE:
        return _BUILD_CACHE["l3c"]
    FF = 4096
    NDC = D // 128
    NHG = FF // 512
    NHC = FF // 128
    nc = _nc()
    x_d = nc.dram_tensor("x", [TC, D], F32, kind="ExternalInput")
    oT_d = nc.dram_tensor("oTall", [D, TC], F32, kind="ExternalInput")
    wo_d = nc.dram_tensor("wo", [D, D], F32, kind="ExternalInput")
    bo_d = nc.dram_tensor("bout", [D], F32, kind="ExternalInput")
    lgall_d = nc.dram_tensor("lgall2d", [128, 128 * 8], F32, kind="ExternalInput")
    lgown_d = nc.dram_tensor("lgown2d", [128, NT * 8], F32, kind="ExternalInput")
    mcpre_d = nc.dram_tensor("mcpre", [T], F32, kind="ExternalInput")
    w1_d = nc.dram_tensor("w1", [8, D, FF], F32, kind="ExternalInput")
    b1_d = nc.dram_tensor("b1", [8, FF], F32, kind="ExternalInput")
    w2_d = nc.dram_tensor("w2", [8, FF, D], F32, kind="ExternalInput")
    b2_d = nc.dram_tensor("b2", [8, D], F32, kind="ExternalInput")
    x4_d = nc.dram_tensor("x4", [TC, D], F32, kind="ExternalOutput")

    es, sb, sem, VE, SC, GP = _mk_ctx(nc)
    xn_bf = sb("xn_bf", [128, NDC * TC], BF16)
    xn_tok = sb("xn_tok", [128, NT * D], BF16)
    acc = sb("acc", [128, NT * D], BF16)
    lgall = sb("lgall", [128, 128 * 8])
    lgown = sb("lgown", [128, NT * 8])
    mxg = sb("mxg", [128, 128])
    mxo = sb("mxo", [128, NT])
    keepcol = sb("keepcol", [128, NT])
    invk = sb("invk", [128, NT])
    ohk = sb("ohk", [128, NT * 8])
    ohtile_bf = sb("ohtile_bf", [128, 8], BF16)
    ident_bf = sb("ident_bf", [128, 128], BF16)
    eps_t = sb("eps_t", [128, 1])
    prefix8 = sb("prefix8", [8, 2])
    pm8 = sb("pm8", [8, 512])

    ld = sem("ld")
    ldm = sem("ldm")
    lwe = sem("lwe")
    lbc = sem("lbc")
    lht = sem("lht")
    lhtb = sem("lhtb")
    lto = sem("lto")
    lx = [sem("lx0"), sem("lx1")]
    sst = [sem("sst0"), sem("sst1")]
    spp = sem("spp")
    pe_count = [0]
    dma_counts = {}

    with nc.Block() as blk:
        @blk.sync
        def _(e):
            e.dma_start(out=lgall[:, :], in_=lgall_d.ap()).then_inc(ld, 16)
            e.dma_start(out=lgown[:, :], in_=lgown_d.ap()).then_inc(ld, 16)

        @blk.gpsimd
        def _(e):
            _build_ident(GP, ident_bf)
            GP.op("memset", eps_t[:, :], 1e-5)
            GP.op("memset", acc[:, :], 0.0)

    with (nc.sbuf_tensor("xnt3c", [128, D], F32) as xn_t,
          nc.sbuf_tensor("xnc3c", [128, D], BF16) as xnc,
          nc.sbuf_tensor("sqr3c", [128, D], F32) as sqr,
          nc.sbuf_tensor("lnt3c", [128, 8], F32) as lnt,
          nc.sbuf_tensor("x3t0c", [128, D], F32) as x3t0,
          nc.sbuf_tensor("x3t1c", [128, D], F32) as x3t1,
          nc.psum_tensor("trYa", [128, 128], BF16) as trYa,
          nc.psum_tensor("trYb", [128, 128], BF16) as trYb):
        x3t = [x3t0, x3t1]
        with nc.Block() as blk:
            q = Seq(nc, VE, SC, GP, spp, pe_count, dma_counts)
            last_x_rd = {}
            last_tr = [None, None]
            trcnt = [0]
            for j in range(NT):
                p = j % 2
                xdeps = [last_x_rd[j - 2]] if j >= 2 else []
                dx = q.step("SD", xdeps, (lambda e, j=j, p=p: e.dma_start(
                    out=x3t[p][:, :], in_=bass.AP(x_d, j * 128 * D, [[D, 128], [1, D]]))),
                    dma_sem=lx[p])
                v1 = q.step("V", [dx], (lambda e, p=p: e.op(
                    "reduce_sum", out=lnt[:, 0:1], in_=x3t[p][:, :], axis=AX.X)))
                s1 = q.step("S", [v1], lambda e: e.op("mul", lnt[:, 0:1], lnt[:, 0:1], 1.0 / D))
                v2 = q.step("V", [s1], (lambda e, p=p: e.op(
                    "tensor_scalar", out=xn_t[:, :], in0=x3t[p][:, :], scalar1=lnt[:, 0:1],
                    scalar2=None, op0=ALU.subtract)))
                last_x_rd[j] = v2
                v3 = q.step("V", [v2], lambda e: e.op(
                    "tensor_tensor", out=sqr[:, :], in0=xn_t[:, :], in1=xn_t[:, :], op=ALU.mult))
                v4 = q.step("V", [v3], lambda e: e.op(
                    "reduce_sum", out=lnt[:, 1:2], in_=sqr[:, :], axis=AX.X))
                s2 = q.step("S", [v4], lambda e: e.op(
                    "activation", out=lnt[:, 2:3], in_=lnt[:, 1:2], func=AF.Sqrt,
                    bias=eps_t[:, 0:1], scale=1.0 / D))
                v5 = q.step("V", [s2], lambda e: e.op(
                    "reciprocal", out=lnt[:, 3:4], in_=lnt[:, 2:3]))
                v6 = q.step("V", [v5], lambda e: e.op(
                    "tensor_scalar", out=xn_t[:, :], in0=xn_t[:, :], scalar1=lnt[:, 3:4],
                    scalar2=None, op0=ALU.mult))
                v7 = q.step("V", [v6], (lambda e, j=j: e.op(
                    "tensor_copy", out=xn_tok[:, j * D:(j + 1) * D], in_=xn_t[:, :])))
                v8 = q.step("V", [v7], lambda e: e.op(
                    "tensor_copy", out=xnc[:, :], in_=xn_t[:, :]))
                for kd in range(NDC):
                    b = trcnt[0] % 2
                    trcnt[0] += 1
                    trp = (trYa, trYb)[b]
                    d = [v8]
                    if last_tr[b] is not None:
                        d.append(last_tr[b])
                    tt = q.step("T", d, (lambda e, kd=kd, trp=trp: e.transpose(
                        trp[:, 0:128], xnc[:, kd * 128:(kd + 1) * 128], ident_bf[:, :])))
                    vv = q.step("V", [tt], (lambda e, kd=kd, j=j, trp=trp: e.op(
                        "tensor_copy", out=xn_bf[:, kd * TC + j * 128:kd * TC + (j + 1) * 128],
                        in_=trp[:, 0:128])))
                    last_tr[b] = vv
            q.emit(blk)

    with (nc.sbuf_tensor("ohT", [8, T], BF16) as ohT,
          nc.sbuf_tensor("ohTown", [8, TC], BF16) as ohTown,
          nc.sbuf_tensor("postokT", [8, TC], F32) as postokT,
          nc.sbuf_tensor("mcpre8", [8, T], BF16) as mcpre8,
          nc.sbuf_tensor("keeprow", [1, TC], F32) as keeprow,
          nc.sbuf_tensor("keeprow_bf", [1, TC], BF16) as keeprow_bf,
          nc.sbuf_tensor("ones64r", [8, TC], F32) as ones64r,
          nc.psum_tensor("psT8a", [8, 128], BF16) as psT8a,
          nc.psum_tensor("psT8b", [8, 128], BF16) as psT8b,
          nc.psum_tensor("psS", [1, 512], F32) as psS,
          nc.psum_tensor("psKa", [128, 1], BF16) as psKa,
          nc.psum_tensor("psKb", [128, 1], BF16) as psKb):
        with nc.Block() as blk:
            @blk.gpsimd
            def _(e):
                GP.op("memset", ones64r[:, :], 1.0)
                e.dma_start(out=mcpre8[0:8, :],
                            in_=bass.AP(mcpre_d, 0, [[0, 8], [1, T]])).then_inc(ldm, 16)

        with nc.Block() as blk:
            q = Seq(nc, VE, SC, GP, spp, pe_count, dma_counts)
            _emit_top1_routing(nc, q, VE, SC, GP, spp, NT, 2048.0,
                               lgall, lgown, mcpre8, ohT, ohTown, postokT, pm8,
                               mxg, mxo, prefix8, keeprow, keeprow_bf, keepcol, ohk,
                               ohtile_bf, ones64r, (psT8a, psT8b), psS, (psKa, psKb), ident_bf,
                               [(ld, 32), (ldm, 16)])
            q.emit(blk)

    with nc.Block() as blk:
        @blk.vector
        def _(e):
            VE.op("tensor_scalar", out=invk[:, :], in0=keepcol[:, :],
                  scalar1=-1.0, scalar2=1.0, op0=ALU.mult, op1=ALU.add)

    hT_scr = nc.dram_tensor("hT_scr", [NT, NHC * 128 * 128], BF16, kind="Internal")
    with (nc.sbuf_tensor("wbuf", [128, NDC * FF], BF16) as wbuf,
          nc.sbuf_tensor("b1b", [128, FF], BF16) as b1b,
          nc.sbuf_tensor("b2b", [128, D], BF16) as b2b,
          nc.sbuf_tensor("h_bf", [128, FF], BF16) as h_bf,
          nc.sbuf_tensor("hT_j", [128, NHC * 128], BF16) as hT_j,
          nc.psum_tensor("h_ps0", [128, 512], F32) as hps0,
          nc.psum_tensor("h_ps1", [128, 512], F32) as hps1,
          nc.psum_tensor("trhA", [128, 128], BF16) as trhA,
          nc.psum_tensor("trhB", [128, 128], BF16) as trhB,
          nc.psum_tensor("y_ps", [128, D], F32) as y_ps):
        h_ps = [hps0, hps1]
        assert NDC * FF == NHC * D
        for eix in range(8):
            with nc.Block() as blk:
                @blk.gpsimd
                def _(e, eix=eix):
                    e.dma_start(out=wbuf[:, :],
                                in_=bass.AP(w1_d, eix * D * FF, [[FF, 128], [128 * FF, NDC], [1, FF]])
                                ).then_inc(lwe, 16)
                    e.dma_start(out=b1b[:, :], in_=_bcast_row_ap(b1_d, FF, eix * FF)).then_inc(lbc, 16)

            with nc.Block() as blk:
                q = Seq(nc, VE, SC, GP, spp, pe_count, dma_counts)
                vh_prev = None
                vt_prev = None
                for j in range(NT):
                    wdep = [(lwe, 32 * eix + 16), (lbc, 32 * eix + 16)]
                    vh = {}
                    for hg in range(NHG):
                        pp = hg % 2
                        deps = list(wdep)
                        if hg >= 2:
                            deps.append(vh[hg - 2])
                        elif vh_prev is not None:
                            deps.append(vh_prev[NHG - 2 + hg])

                        def mk_h(e, hg=hg, pp=pp, j=j):
                            last = None
                            for dc in range(NDC):
                                last = e.matmul(h_ps[pp][:, :],
                                                xn_bf[:, dc * TC + j * 128:dc * TC + (j + 1) * 128],
                                                wbuf[:, dc * FF + hg * 512:dc * FF + (hg + 1) * 512],
                                                start=(dc == 0), stop=(dc == NDC - 1))
                            return last
                        th = q.step("T", deps, mk_h)
                        vh[hg] = q.step("V", [th], (lambda e, hg=hg, pp=pp: e.op(
                            "tensor_tensor", out=h_bf[:, hg * 512:(hg + 1) * 512],
                            in0=h_ps[pp][:, :], in1=b1b[:, hg * 512:(hg + 1) * 512], op=ALU.add)))
                    s_gel = q.step("S", [vh[hg] for hg in range(NHG)], lambda e: e.op(
                        "activation", out=h_bf[:, :], in_=h_bf[:, :], func=AF.Gelu))
                    vt = {}
                    nstores = eix * NT + j
                    stdep = [(lht, 16 * nstores)] if nstores >= 1 else []
                    for hc in range(NHC):
                        trh = trhA if hc % 2 == 0 else trhB
                        deps = [s_gel]
                        if hc >= 2:
                            deps.append(vt[hc - 2])
                        elif vt_prev is not None:
                            deps.append(vt_prev[NHC - 2 + hc])
                        tt = q.step("T", deps, (lambda e, hc=hc, trh=trh: e.transpose(
                            trh[:, 0:128], h_bf[:, hc * 128:(hc + 1) * 128],
                            ident_bf[:, :])))
                        vt[hc] = q.step("V", [tt] + (stdep if hc < 2 else []),
                                        (lambda e, hc=hc, trh=trh: e.op(
                            "tensor_copy", out=hT_j[:, hc * 128:(hc + 1) * 128],
                            in_=trh[:, 0:128])))
                    q.step("GD", [vt[hc] for hc in range(NHC)],
                           (lambda e, j=j: e.dma_start(
                               out=bass.AP(hT_scr, j * NHC * 128 * 128,
                                           [[NHC * 128, 128], [1, NHC * 128]]),
                               in_=hT_j[:, :])), dma_sem=lht)
                    vh_prev = vh
                    vt_prev = vt
                q.emit(blk)

            with nc.Block() as blk:
                @blk.gpsimd
                def _(e, eix=eix):
                    e.wait_ge(lht, dma_counts[id(lht)])
                    e.dma_start(out=wbuf[:, :],
                                in_=bass.AP(w2_d, eix * FF * D, [[D, 128], [128 * D, NHC], [1, D]])
                                ).then_inc(lwe, 16)
                    e.dma_start(out=b2b[:, :], in_=_bcast_row_ap(b2_d, D, eix * D)).then_inc(lbc, 16)

            with nc.Block() as blk:
                q = Seq(nc, VE, SC, GP, spp, pe_count, dma_counts)
                prev_ty = None
                prev_va = None
                for j in range(NT):
                    dh = q.step("GD", ([prev_ty] if prev_ty is not None else []),
                                (lambda e, j=j: e.dma_start(
                        out=hT_j[:, :],
                        in_=bass.AP(hT_scr, j * NHC * 128 * 128,
                                    [[NHC * 128, 128], [1, NHC * 128]]))), dma_sem=lhtb)
                    wdep = [(lwe, 32 * (eix + 1)), (lbc, 32 * eix + 32), dh]
                    if prev_va is not None:
                        wdep.append(prev_va)

                    def mk_y(e, j=j):
                        last = None
                        for hc in range(NHC):
                            e.matmul(y_ps[:, 0:512], hT_j[:, hc * 128:(hc + 1) * 128],
                                     wbuf[:, hc * D:hc * D + 512],
                                     start=(hc == 0), stop=(hc == NHC - 1))
                            last = e.matmul(y_ps[:, 512:1024], hT_j[:, hc * 128:(hc + 1) * 128],
                                            wbuf[:, hc * D + 512:(hc + 1) * D],
                                            start=(hc == 0), stop=(hc == NHC - 1))
                        return last
                    t_y = q.step("T", wdep, mk_y)
                    v_a1 = q.step("V", [t_y], (lambda e, j=j, eix=eix: e.op(
                        "scalar_tensor_tensor", out=acc[:, j * D:(j + 1) * D],
                        in0=y_ps[:, :], scalar=ohk[:, j * 8 + eix:j * 8 + eix + 1],
                        in1=acc[:, j * D:(j + 1) * D], op0=ALU.mult, op1=ALU.add)))
                    q.step("V", [v_a1], (lambda e, j=j, eix=eix: e.op(
                        "scalar_tensor_tensor", out=acc[:, j * D:(j + 1) * D],
                        in0=b2b[:, :], scalar=ohk[:, j * 8 + eix:j * 8 + eix + 1],
                        in1=acc[:, j * D:(j + 1) * D], op0=ALU.mult, op1=ALU.add)))
                    prev_ty = t_y
                    prev_va = v_a1
                q.emit(blk)

    with (nc.sbuf_tensor("oT_bf", [128, NDC * TC], BF16) as oT_bf,
          nc.sbuf_tensor("wo_sb", [128, NDC * D], BF16) as wo_sb,
          nc.sbuf_tensor("bob", [128, D], F32) as bob,
          nc.sbuf_tensor("fin", [128, D], F32) as fin,
          nc.sbuf_tensor("gsil", [128, D], F32) as gsil,
          nc.sbuf_tensor("sres", [128, D], F32) as sres,
          nc.sbuf_tensor("x1t0", [128, D], F32) as x1t0,
          nc.sbuf_tensor("x1t1", [128, D], F32) as x1t1,
          nc.psum_tensor("s_ps", [128, D], F32) as s_ps):
        x1t = [x1t0, x1t1]
        with nc.Block() as blk:
            @blk.gpsimd
            def _(e):
                e.dma_start(out=oT_bf[:, :],
                            in_=bass.AP(oT_d, 0, [[TC, 128], [128 * TC, NDC], [1, TC]])
                            ).then_inc(lto, 16)
                e.dma_start(out=wo_sb[:, :],
                            in_=bass.AP(wo_d, 0, [[D, 128], [128 * D, NDC], [1, D]])
                            ).then_inc(lto, 16)

            @blk.sync
            def _(e):
                e.dma_start(out=bob[:, :], in_=_bcast_row_ap(bo_d, D)).then_inc(ld, 16)

        with nc.Block() as blk:
            q = Seq(nc, VE, SC, GP, spp, pe_count, dma_counts)
            ldp = [(lto, 32), (ld, 48)]
            prev_out = {}
            st_steps = {}
            for j in range(NT):
                pp = j % 2
                xdeps = []
                if j >= 2:
                    xdeps = [prev_out[j - 2], st_steps[j - 2]]
                dx = q.step("SD", xdeps, (lambda e, j=j, pp=pp: e.dma_start(
                    out=x1t[pp][:, :], in_=_ap3_tokmaj(x_d, 1, D, offset=j * 128 * D))),
                    dma_sem=lx[pp])
                v_f = q.step("V", [], (lambda e, j=j: e.op(
                    "scalar_tensor_tensor", out=fin[:, :],
                    in0=xn_tok[:, j * D:(j + 1) * D], scalar=invk[:, j:j + 1],
                    in1=acc[:, j * D:(j + 1) * D], op0=ALU.mult, op1=ALU.add)))
                s_sil = q.step("S", [v_f], lambda e: e.op(
                    "activation", out=gsil[:, :], in_=fin[:, :], func=AF.Silu))

                def mk_s(e, j=j):
                    last = None
                    for hv in range(NDC):
                        e.matmul(s_ps[:, 0:512], oT_bf[:, hv * TC + j * 128:hv * TC + (j + 1) * 128],
                                 wo_sb[:, hv * D:hv * D + 512],
                                 start=(hv == 0), stop=(hv == NDC - 1))
                        last = e.matmul(s_ps[:, 512:1024],
                                        oT_bf[:, hv * TC + j * 128:hv * TC + (j + 1) * 128],
                                        wo_sb[:, hv * D + 512:(hv + 1) * D],
                                        start=(hv == 0), stop=(hv == NDC - 1))
                    return last
                sdeps = ldp + ([prev_out[j - 1]] if j >= 1 else [])
                t_s = q.step("T", sdeps, mk_s)
                v_b = q.step("V", [t_s], lambda e: e.op(
                    "tensor_tensor", out=sres[:, :], in0=s_ps[:, :], in1=bob[:, :], op=ALU.add))
                prev_out[j] = v_b
                v_sg = q.step("V", [v_b, s_sil], lambda e: e.op(
                    "tensor_tensor", out=gsil[:, :], in0=gsil[:, :], in1=sres[:, :], op=ALU.mult))
                v_o = q.step("V", [v_sg, dx], (lambda e, pp=pp: e.op(
                    "tensor_tensor", out=x1t[pp][:, :], in0=x1t[pp][:, :],
                    in1=gsil[:, :], op=ALU.add)))
                st_steps[j] = q.step("GD", [v_o], (lambda e, j=j, pp=pp: e.dma_start(
                    out=_ap3_tokmaj(x4_d, 1, D, offset=j * 128 * D), in_=x1t[pp][:, :])),
                    dma_sem=sst[pp])
            q.emit(blk)

    with nc.Block() as blk:
        @blk.gpsimd
        def _(e):
            e.wait_ge(sst[0], dma_counts[id(sst[0])])
            e.wait_ge(sst[1], dma_counts[id(sst[1])])

    es.close()
    _BUILD_CACHE["l3c"] = nc
    return nc



def _run(nc, in_maps):
    return run_bass_kernel_spmd(nc, in_maps, core_ids=CORE_IDS)


def run_l0(ins):
    x = np.ascontiguousarray(ins["L0_attn_x"].reshape(T, D))
    nc = build_l0a()
    im = [{"x": x[c * TC:(c + 1) * TC],
           "rwv": np.ascontiguousarray(ins["L0_attn_router_w"][0]),
           "rbv": np.ascontiguousarray(ins["L0_attn_router_b"])} for c in range(NC)]
    res = _run(nc, im)
    xn = np.concatenate([r["xn"] for r in res.results], axis=0)
    rw = np.concatenate([r["rw2d"].T.reshape(-1) for r in res.results])

    xn3 = xn.reshape(B, L, D)
    xnrev = np.ascontiguousarray(xn3[:, ::-1, :].reshape(T, D))
    rwb = rw.reshape(B, L)
    nc2 = build_l0b()
    im2 = []
    for c in range(NC):
        bsel = c // (NC // B)
        im2.append({
            "x": x[c * TC:(c + 1) * TC],
            "xn": xn[c * TC:(c + 1) * TC],
            "xnrev": xnrev[c * TC:(c + 1) * TC],
            "rwb": np.ascontiguousarray(rwb[bsel]),
            "rwown2d": np.ascontiguousarray(rw[c * TC:(c + 1) * TC].reshape(NT, 128).T),
            "wg": np.ascontiguousarray(ins["L0_ffn_w_gate"]),
            "w1": np.ascontiguousarray(ins["L0_ffn_w1"]),
            "w3": np.ascontiguousarray(ins["L0_ffn_w3"]),
            "w2": np.ascontiguousarray(ins["L0_ffn_w2"]),
        })
    res2 = _run(nc2, im2)
    x1 = np.concatenate([r["x1"] for r in res2.results], axis=0)
    return x1, xn, rw



def run_l1(ins, x1):
    H, DK = 16, 64
    nc = build_l1a()
    im = []
    for c in range(NC):
        im.append({
            "x": np.ascontiguousarray(x1[c * TC:(c + 1) * TC]),
            "wq": np.ascontiguousarray(ins["L1_attn_W_q"]),
            "wk": np.ascontiguousarray(ins["L1_attn_W_k"]),
            "wv": np.ascontiguousarray(ins["L1_attn_W_v"]),
            "wgk": np.ascontiguousarray(ins["L1_attn_Wg_K"]),
            "wgc": np.ascontiguousarray(ins["L1_attn_Wg_C"]),
            "wproj": np.ascontiguousarray(ins["L1_ffn_W_proj"]),
            "bgk2d": np.ascontiguousarray(ins["L1_attn_bg_K"].reshape(8, 128).T),
            "bgc2d": np.ascontiguousarray(ins["L1_attn_bg_C"].reshape(8, 128).T),
            "bproj2d": np.ascontiguousarray(ins["L1_ffn_b_proj"].reshape(8, 128).T),
            "wgate": np.ascontiguousarray(ins["L1_ffn_W_gate"]),
        })
    res = _run(nc, im)
    full = {}
    for nm in ["qT", "kT", "vT", "aKT", "aCT", "xpT"]:
        full[nm] = np.concatenate([res.results[c][nm] for c in range(NC)], axis=1)
    lg = np.concatenate([res.results[c]["logits2d"].reshape(128, NT, 8).transpose(1, 0, 2).reshape(TC, 8)
                         for c in range(NC)], axis=0)

    nc2 = build_l1b()
    im2 = []
    for c in range(NC):
        packs = []
        for i in range(4):
            bhg = c * 4 + i
            b, h = bhg // H, bhg % H
            rows = slice(h * DK, (h + 1) * DK)
            cols = slice(b * L, (b + 1) * L)
            arrs = [full[nm][rows, cols] for nm in ["qT", "kT", "vT", "aKT", "aCT"]]
            packs.append(np.stack([np.ascontiguousarray(a) for a in arrs], 0))
        im2.append({"qkvab": np.ascontiguousarray(np.stack(packs, 0))})
    res2 = _run(nc2, im2)
    oT = np.zeros((D, T), np.float32)
    for c in range(NC):
        o4 = res2.results[c]["oT"].reshape(4, L, DK)
        for i in range(4):
            bhg = c * 4 + i
            b, h = bhg // H, bhg % H
            oT[h * DK:(h + 1) * DK, b * L:(b + 1) * L] = o4[i].T

    nc3 = build_l1c()
    lgall2d = np.ascontiguousarray(lg.reshape(128, 128, 8).transpose(1, 0, 2).reshape(128, 128 * 8))
    im3 = []
    for c in range(NC):
        sl = slice(c * TC, (c + 1) * TC)
        lgown = lg[sl]
        mcpre = np.zeros(T, np.float32)
        mcpre[:c * TC] = 1.0
        im3.append({
            "x": np.ascontiguousarray(x1[sl]),
            "oTall": np.ascontiguousarray(oT[:, sl]),
            "wo": np.ascontiguousarray(ins["L1_attn_W_o"]),
            "xpT": np.ascontiguousarray(full["xpT"][:, sl]),
            "lgall2d": lgall2d,
            "lgown2d": np.ascontiguousarray(lgown.reshape(NT, 128, 8).transpose(1, 0, 2).reshape(128, NT * 8)),
            "mcpre": mcpre,
            "w1": ins["L1_ffn_W1"], "b1": ins["L1_ffn_b1"],
            "g1": ins["L1_ffn_ln1_g"], "bb1": ins["L1_ffn_ln1_b"],
            "w2": ins["L1_ffn_W2"], "b2": ins["L1_ffn_b2"],
            "lng": ins["L1_ffn_ln_g"], "lnb": ins["L1_ffn_ln_b"],
        })
    res3 = _run(nc3, im3)
    x2 = np.concatenate([res3.results[c]["x2"] for c in range(NC)], axis=0)
    return x2


def run_l2(ins, x2):
    DI = 2048
    nc = build_l2a()
    cw = ins["L2_attn_conv_w"][:, 0, :]
    cw2d = np.zeros((128, 64), np.float32)
    cb2d = np.zeros((128, 16), np.float32)
    for ch in range(16):
        for j in range(4):
            cw2d[:, ch * 4 + j] = cw[ch * 128:(ch + 1) * 128, j]
        cb2d[:, ch] = ins["L2_attn_conv_b"][ch * 128:(ch + 1) * 128]
    ipT = np.ascontiguousarray(ins["L2_attn_in_proj_w"].T)
    xpwT = np.ascontiguousarray(ins["L2_attn_x_proj_w"].T)
    wg = ins["L2_ffn_w_gate"]; wu = ins["L2_ffn_w_up"]; wd = ins["L2_ffn_w_down"]
    im = []
    for c in range(NC):
        xh = np.zeros((TC + 128, D), np.float32)
        xh[128:] = x2[c * TC:(c + 1) * TC]
        if c % 4:
            xh[:128] = x2[c * TC - 128:c * TC]
        wgs = np.zeros((384, D), np.float32); wgs[:352] = wg[c * 352:(c + 1) * 352]
        wus = np.zeros((384, D), np.float32); wus[:352] = wu[c * 352:(c + 1) * 352]
        im.append({
            "xh": xh, "nw": ins["L2_attn_norm_w"], "nb": ins["L2_attn_norm_b"],
            "ipT": ipT, "cw2d": cw2d, "cb2d": cb2d, "xpwT": xpwT,
            "halom": np.array([0.0 if c % 4 == 0 else 1.0], np.float32),
            "wgs": wgs, "wus": wus,
            "wds": np.ascontiguousarray(wd[c * 128:(c + 1) * 128]).astype(np.float32),
        })
    res = _run(nc, im)
    zT = [res.results[c]["zT"] for c in range(NC)]
    xcT = [res.results[c]["xcT"] for c in range(NC)]
    xp_full = np.concatenate([np.asarray(res.results[c]["xp33"], np.float32)
                              for c in range(NC)], axis=1)
    wgq = np.concatenate([np.asarray(res.results[c]["wgq"], np.float32)[:352]
                          for c in range(NC)], axis=0)
    wuq = np.concatenate([np.asarray(res.results[c]["wuq"], np.float32)[:352]
                          for c in range(NC)], axis=0)
    wdq = np.concatenate([np.asarray(res.results[c]["wdq"], np.float32)
                          for c in range(NC)], axis=0)

    nc2 = build_l2b()
    xc_fm = np.concatenate([np.asarray(x, np.float32) for x in xcT], axis=1)
    A = -np.exp(ins["L2_attn_A_log"])
    xp0 = np.ascontiguousarray(xp_full[0].reshape(B, L))
    bpT = np.ascontiguousarray(xp_full[1:17].reshape(16, B, L).transpose(1, 0, 2))
    cpT = np.ascontiguousarray(xp_full[17:33].reshape(16, B, L).transpose(1, 0, 2))
    im2 = []
    for c in range(NC):
        xcg = np.zeros((4, 128, L), np.float32)
        dtw2d = np.zeros((128, 2), np.float32); dtb2d = np.zeros((128, 2), np.float32)
        A2d = np.zeros((128, 32), np.float32); dsk2d = np.zeros((128, 2), np.float32)
        for g in range(4):
            bb, cg = g // 2, g % 2
            xcg[g] = xc_fm[c * 256 + cg * 128:c * 256 + (cg + 1) * 128, bb * L:(bb + 1) * L]
        for cg in range(2):
            sl = slice(c * 256 + cg * 128, c * 256 + (cg + 1) * 128)
            dtw2d[:, cg] = ins["L2_attn_dt_w"][sl, 0]
            dtb2d[:, cg] = ins["L2_attn_dt_b"][sl]
            A2d[:, cg * 16:(cg + 1) * 16] = A[sl]
            dsk2d[:, cg] = ins["L2_attn_D_skip"][sl]
        im2.append({"xcg": xcg.astype(ml_dtypes.bfloat16), "xp0": xp0, "bpT": bpT, "cpT": cpT,
                    "dtw2d": dtw2d, "dtb2d": dtb2d, "A2d": A2d, "dsk2d": dsk2d})
    res2 = _run(nc2, im2)

    yall = np.zeros((B, 2048, L), np.float32)
    for c in range(NC):
        y4 = np.asarray(res2.results[c]["yT"], np.float32)
        for g in range(4):
            bb, cg = g // 2, g % 2
            yall[bb, c * 256 + cg * 128:c * 256 + (cg + 1) * 128, :] = y4[g]

    nc3 = build_l2c()
    opT = np.ascontiguousarray(ins["L2_attn_out_proj_w"].T)
    wgqT = np.ascontiguousarray(wgq.T).astype(ml_dtypes.bfloat16)
    wuqT = np.ascontiguousarray(wuq.T).astype(ml_dtypes.bfloat16)
    wdqT = np.ascontiguousarray(wdq.T).astype(ml_dtypes.bfloat16)
    im3 = []
    for c in range(NC):
        b = c // 4
        t0 = (c % 4) * TC
        im3.append({
            "x": np.ascontiguousarray(x2[c * TC:(c + 1) * TC]),
            "yTg": np.ascontiguousarray(yall[b, :, t0:t0 + TC]),
            "zTg": np.ascontiguousarray(np.asarray(zT[c], ml_dtypes.bfloat16)),
            "opT": opT, "wgqT": wgqT, "wuqT": wuqT, "wdqT": wdqT,
        })
    res3 = _run(nc3, im3)
    x3 = np.concatenate([res3.results[c]["x3"] for c in range(NC)], axis=0)
    return x3


def run_l3(ins, x3):
    nc = build_l3a()
    half = 32
    inv_freq = 1.0 / 10000.0 ** (np.arange(half) / half)
    ang = np.arange(L)[:, None] * inv_freq[None, :]
    cost = np.cos(ang).astype(np.float32)
    sint = np.sin(ang).astype(np.float32)
    wqkvT = np.ascontiguousarray(ins["L3_attn_w_qkv"].T)
    im = []
    for c in range(NC):
        l0 = (c % 4) * TC
        im.append({
            "x": np.ascontiguousarray(x3[c * TC:(c + 1) * TC]),
            "wqkvT": wqkvT,
            "bqkv": ins["L3_attn_b_qkv"],
            "wsw": np.ascontiguousarray(ins["L3_ffn_W_sw"]),
            "bsw": ins["L3_ffn_b_sw"],
            "cosr": np.ascontiguousarray(cost[l0:l0 + TC]),
            "sinr": np.ascontiguousarray(sint[l0:l0 + TC]),
            "qgam": np.ascontiguousarray(ins["L3_attn_q_gamma"].reshape(-1)),
            "kgam": np.ascontiguousarray(ins["L3_attn_k_gamma"].reshape(-1)),
        })
    res = _run(nc, im)
    qf = np.concatenate([res.results[c]["qr"] for c in range(NC)], axis=0)
    kf = np.concatenate([res.results[c]["kr"] for c in range(NC)], axis=0)
    vf = np.concatenate([res.results[c]["vr"] for c in range(NC)], axis=0)
    lg = np.concatenate([np.asarray(res.results[c]["logits2d"], np.float32)
                         .reshape(128, NT, 8).transpose(1, 0, 2).reshape(TC, 8)
                         for c in range(NC)], axis=0)

    nc2 = build_l3b()
    kTb = [np.ascontiguousarray(kf[b * L:(b + 1) * L].T) for b in range(B)]
    veb = []
    for b in range(B):
        ve = np.ones((L, 16 * 65), np.float32)
        for h in range(16):
            ve[:, h * 65:h * 65 + 64] = vf[b * L:(b + 1) * L, h * 64:(h + 1) * 64]
        veb.append(ve)
    im2 = []
    for c in range(NC):
        b = c // 4
        im2.append({
            "qTs": np.ascontiguousarray(qf[c * TC:(c + 1) * TC].T),
            "kTb": kTb[b],
            "vextb": veb[b],
        })
    res2 = _run(nc2, im2)

    nc3 = build_l3c()
    lgall2d = np.ascontiguousarray(lg.reshape(128, 128, 8).transpose(1, 0, 2).reshape(128, 128 * 8))
    woT = np.ascontiguousarray(ins["L3_attn_w_out"].T)
    im3 = []
    for c in range(NC):
        sl = slice(c * TC, (c + 1) * TC)
        lgown = lg[sl]
        mcpre = np.zeros(T, np.float32)
        mcpre[:c * TC] = 1.0
        im3.append({
            "x": np.ascontiguousarray(x3[sl]),
            "oTall": np.asarray(res2.results[c]["oTs"], np.float32),
            "wo": woT,
            "bout": ins["L3_attn_b_out"],
            "lgall2d": lgall2d,
            "lgown2d": np.ascontiguousarray(lgown.reshape(NT, 128, 8).transpose(1, 0, 2).reshape(128, NT * 8)),
            "mcpre": mcpre,
            "w1": ins["L3_ffn_W1"], "b1": ins["L3_ffn_b1"],
            "w2": ins["L3_ffn_W2"], "b2": ins["L3_ffn_b2"],
        })
    res3 = _run(nc3, im3)
    x4 = np.concatenate([res3.results[c]["x4"] for c in range(NC)], axis=0)
    return x4


def kernel(**ins):
    x1, xn0, rw0 = run_l0(ins)
    x2 = run_l1(ins, x1)
    x3 = run_l2(ins, x2)
    x4 = run_l3(ins, x3)
    return np.ascontiguousarray(x4.reshape(B, L, D)).astype(np.float32)
